# version 03
# baseline (speedup 1.0000x reference)
import contextlib
import numpy as np
import ml_dtypes

import concourse.bass as bass
import concourse.mybir as mybir
import concourse.bass_utils as bass_utils
from concourse.aot_env import allow_ici_all_to_all

F32 = mybir.dt.float32
BF16 = mybir.dt.bfloat16
I32 = mybir.dt.int32
AF = mybir.ActivationFunctionType
ALU = mybir.AluOpType
AX = mybir.AxisListType

DIM = 4096
NH = 32
NKV = 8
HD = 128
NCORES = 8
HPC = 4
THETA = 10000.0

ENGS = ("tensor", "vector", "scalar", "gpsimd", "sync")


class Ctr:

    def __init__(self, stack, nc, name, step=1):
        self.sem = stack.enter_context(nc.semaphore(name))
        self.n = 0
        self.step = step

    def bump(self):
        self.n += self.step
        return (self.sem, self.step)


class Prog:
    def __init__(self):
        self.ops = {e: [] for e in ENGS}

    def emit(self, eng, fn=None, waits=(), inc=None):
        self.ops[eng].append((fn, tuple((s, v) for (s, v) in waits if v > 0), inc))


def build_program(S, upto='full', p1_tgs=None, p1_lvl=6):
    To = S // NCORES
    NTG = S // 512
    NQC = S // 512
    NTH = (To + 511) // 512
    n_ft = DIM // 128

    nc = bass.Bass("TRN2", target_bir_lowering=False)

    xT = nc.dram_tensor("xT", [DIM, S], F32, kind="ExternalInput").ap()
    xoT = nc.dram_tensor("xoT", [DIM, To], F32, kind="ExternalInput").ap()
    wq = nc.dram_tensor("wq", [DIM, HPC * HD], F32, kind="ExternalInput").ap()
    wk = nc.dram_tensor("wk", [DIM, HD], F32, kind="ExternalInput").ap()
    wv = nc.dram_tensor("wv", [DIM, HD], F32, kind="ExternalInput").ap()
    wo = nc.dram_tensor("wo", [DIM, DIM], F32, kind="ExternalInput").ap()
    w2T = nc.dram_tensor("w2T", [DIM, DIM], I32, kind="ExternalInput").ap()
    bias2d = nc.dram_tensor("bias2d", [128, n_ft], F32, kind="ExternalInput").ap()
    wscale = nc.dram_tensor("wscale", [1, 1], F32, kind="ExternalInput").ap()
    cosT = nc.dram_tensor("cosT", [128, S], F32, kind="ExternalInput").ap()
    sinT = nc.dram_tensor("sinT", [128, S], F32, kind="ExternalInput").ap()
    masks = nc.dram_tensor("masks", [4, 128, 512], BF16, kind="ExternalInput").ap()
    pswap = nc.dram_tensor("pswap", [128, 128], BF16, kind="ExternalInput").ap()
    ident = nc.dram_tensor("ident", [128, 128], BF16, kind="ExternalInput").ap()
    onescb = nc.dram_tensor("onescb", [128, 1], BF16, kind="ExternalInput").ap()
    onesrb = nc.dram_tensor("onesrb", [1, 128], BF16, kind="ExternalInput").ap()
    onescf = nc.dram_tensor("onescf", [128, 1], F32, kind="ExternalInput").ap()
    onesrf = nc.dram_tensor("onesrf", [1, 128], F32, kind="ExternalInput").ap()
    outT = nc.dram_tensor("outT", [DIM, To], F32, kind="ExternalOutput").ap()

    a2a_in = nc.dram_tensor("a2a_in", [NCORES, HPC * HD, To], BF16)
    a2a_out = nc.dram_tensor("a2a_out", [NCORES, HPC * HD, To], BF16)
    s_dram = nc.dram_tensor("s_dram", [DIM, To], F32)
    r2_dram = nc.dram_tensor("r2_dram", [DIM, To], F32)

    stk = contextlib.ExitStack()

    sW = Ctr(stk, nc, "sW", 16)
    sX = [Ctr(stk, nc, "sX%d" % i, 16) for i in range(8)]
    sCS = [Ctr(stk, nc, "sCS%d" % i, 16) for i in range(2)]
    sWo = [Ctr(stk, nc, "sWo%d" % i, 16) for i in range(2)]
    sFf = [Ctr(stk, nc, "sFf%d" % i, 16) for i in range(3)]
    sXo = [Ctr(stk, nc, "sXo%d" % i, 16) for i in range(2)]
    sSw = [Ctr(stk, nc, "sSw%d" % i, 16) for i in range(2)]
    sSr = [Ctr(stk, nc, "sSr%d" % i, 16) for i in range(4)]
    sS2 = [Ctr(stk, nc, "sS2%d" % i, 16) for i in range(4)]
    sR2w = [Ctr(stk, nc, "sR2w%d" % i, 16) for i in range(2)]
    sR2r = [Ctr(stk, nc, "sR2r%d" % i, 16) for i in range(4)]
    sA2 = [Ctr(stk, nc, "sA2_%d" % i, 16) for i in range(8)]
    sAo = Ctr(stk, nc, "sAo", 16)
    sOut = [Ctr(stk, nc, "sOut%d" % i, 16) for i in range(2)]
    ccs = Ctr(stk, nc, "ccs", 1)
    peM = Ctr(stk, nc, "peM", 1)
    acM = Ctr(stk, nc, "acM", 1)
    dvM = Ctr(stk, nc, "dvM", 1)

    P = Prog()

    masks_sb = stk.enter_context(nc.sbuf_tensor("masks_sb", [128, 4, 512], BF16))
    pswap_sb = stk.enter_context(nc.sbuf_tensor("pswap_sb", [128, 128], BF16))
    ident_sb = stk.enter_context(nc.sbuf_tensor("ident_sb", [128, 128], BF16))
    onescb_sb = stk.enter_context(nc.sbuf_tensor("onescb_sb", [128, 1], BF16))
    onesrb_sb = stk.enter_context(nc.sbuf_tensor("onesrb_sb", [1, 128], BF16))
    onescf_sb = stk.enter_context(nc.sbuf_tensor("onescf_sb", [128, 1], F32))
    onesrf_sb = stk.enter_context(nc.sbuf_tensor("onesrf_sb", [1, 128], F32))
    bias2d_sb = stk.enter_context(nc.sbuf_tensor("bias2d_sb", [128, n_ft], F32))
    wscale_sb = stk.enter_context(nc.sbuf_tensor("wscale_sb", [1, 1], F32))
    cmax = stk.enter_context(nc.sbuf_tensor("cmax", [1, 4 * NQC], F32))
    kmaxs = stk.enter_context(nc.sbuf_tensor("kmaxs", [1, NTG + 2], F32))
    cneg = stk.enter_context(nc.sbuf_tensor("cneg", [1, 4 * NQC], BF16))
    rowbuf = stk.enter_context(nc.sbuf_tensor("rowbuf", [1, 3 * 1024], F32))
    consts = stk.enter_context(nc.sbuf_tensor("consts", [1, 8], F32))

    ph12 = contextlib.ExitStack()
    q4 = ph12.enter_context(nc.sbuf_tensor("q4", [128, HPC, S], BF16))
    KT = ph12.enter_context(nc.sbuf_tensor("KT", [128, S], BF16))
    Vtok = ph12.enter_context(nc.sbuf_tensor("Vtok", [128, S], BF16))

    nc.const_aps.register(F32, 1e-5, consts[0:1, 0:1])
    nc.const_aps.register(F32, 0.5, consts[0:1, 1:2])
    nc.const_aps.register(F32, -0.5, consts[0:1, 2:3])

    P.emit("vector", lambda v: v.memset(consts[0:1, 0:1], 1e-5), inc=dvM.bump())
    P.emit("vector", lambda v: v.memset(consts[0:1, 1:2], 0.5), inc=dvM.bump())
    P.emit("vector", lambda v: v.memset(consts[0:1, 2:3], -0.5), inc=dvM.bump())
    consts_ready = dvM.n

    def g_load(dst, src):
        P.emit("gpsimd", lambda g, d=dst, s=src: g.dma_start(out=d, in_=s), inc=sW.bump())

    g_load(masks_sb[:, :, :], masks.rearrange("m p t -> p m t"))
    g_load(pswap_sb[:, :], pswap)
    g_load(ident_sb[:, :], ident)
    g_load(onescb_sb[:, :], onescb)
    g_load(onesrb_sb[:, :], onesrb)
    g_load(onescf_sb[:, :], onescf)
    g_load(onesrf_sb[:, :], onesrf)
    g_load(bias2d_sb[:, :], bias2d)
    g_load(wscale_sb[:, :], wscale)

    ph1 = contextlib.ExitStack()
    wq_sb = ph1.enter_context(nc.sbuf_tensor("wq_sb", [128, n_ft, HPC * HD], BF16))
    wk_sb = ph1.enter_context(nc.sbuf_tensor("wk_sb", [128, n_ft, HD], BF16))
    wv_sb = ph1.enter_context(nc.sbuf_tensor("wv_sb", [128, n_ft, HD], BF16))
    xbuf = [ph1.enter_context(nc.sbuf_tensor("xbuf%d" % i, [128, 512], BF16)) for i in range(8)]
    csbuf = [ph1.enter_context(nc.sbuf_tensor("csbuf%d" % i, [128, 2, 512], F32)) for i in range(2)]
    qkstage = ph1.enter_context(nc.sbuf_tensor("qkstage", [128, 512], BF16))
    vstage = ph1.enter_context(nc.sbuf_tensor("vstage", [128, 512], BF16))
    sqstage = ph1.enter_context(nc.sbuf_tensor("sqstage", [128, 512], BF16))
    t1b = ph1.enter_context(nc.sbuf_tensor("t1b", [128, 512], F32))
    t2b = ph1.enter_context(nc.sbuf_tensor("t2b", [128, 512], F32))

    pp1 = contextlib.ExitStack()
    q_ps = pp1.enter_context(nc.psum_tensor("q_ps", [128, 4 * 512], F32))
    kv_ps = pp1.enter_context(nc.psum_tensor("kv_ps", [128, 2 * 512], F32))
    sw_ps = pp1.enter_context(nc.psum_tensor("sw_ps", [128, 512], F32))
    vt_ps = pp1.enter_context(nc.psum_tensor("vt_ps", [128, 128], BF16))

    g_load(wq_sb[:, :, :], wq.rearrange("(i p) f -> p i f", p=128))
    g_load(wk_sb[:, :, :], wk.rearrange("(i p) f -> p i f", p=128))
    g_load(wv_sb[:, :, :], wv.rearrange("(i p) f -> p i f", p=128))
    w_loaded = sW.n

    total_tiles = NTG * n_ft
    xt_consumed_at = []
    cs_consumed_at = []

    if upto == "p0":
        sT = Ctr(stk, nc, "sT", 16)
        P.emit("gpsimd",
               lambda g: g.dma_start(out=outT[0:128, 0:512], in_=wq_sb[:, 0, :]),
               waits=[(sW.sem, sW.n)], inc=sT.bump())
        P.emit("gpsimd", None, waits=[(sT.sem, sT.n)])
        def finish0():
            def runner(eng_ops):
                def _run(eng):
                    for fn, waits, inc in eng_ops:
                        for (sem, val) in waits:
                            eng.wait_ge(sem, val)
                        if fn is None:
                            continue
                        inst = fn(eng)
                        if inc is not None:
                            inst.then_inc(inc[0], inc[1])
                return _run
            with nc.Block() as block:
                block.tensor(runner(P.ops["tensor"]))
                block.vector(runner(P.ops["vector"]))
                block.scalar(runner(P.ops["scalar"]))
                block.gpsimd(runner(P.ops["gpsimd"]))
                block.sync(runner(P.ops["sync"]))
            return nc
        nc_out = finish0()
        ph1.close(); pp1.close(); ph12.close(); stk.close()
        return nc_out

    def emit_xt_load(j):
        tg, d = divmod(j, n_ft)
        b = j % 8
        waits = [(sW.sem, w_loaded)]
        if j >= 8:
            waits.append((peM.sem, xt_consumed_at[j - 8]))
        P.emit("gpsimd",
               lambda g, b=b, tg=tg, d=d: g.dma_start(
                   out=xbuf[b][:, :], in_=xT[d * 128:(d + 1) * 128, tg * 512:(tg + 1) * 512]),
               waits=waits, inc=sX[b].bump())

    def emit_cs_load(tg):
        b = tg % 2
        waits = []
        if tg >= 2:
            waits.append((dvM.sem, cs_consumed_at[tg - 2]))
        P.emit("sync",
               lambda sy, b=b, tg=tg: sy.dma_start(
                   out=csbuf[b][:, 0, :], in_=cosT[:, tg * 512:(tg + 1) * 512]),
               waits=waits, inc=sCS[b].bump())
        P.emit("sync",
               lambda sy, b=b, tg=tg: sy.dma_start(
                   out=csbuf[b][:, 1, :], in_=sinT[:, tg * 512:(tg + 1) * 512]),
               inc=sCS[b].bump())

    NTG_RUN = NTG if p1_tgs is None else min(NTG, p1_tgs)
    total_tiles = NTG_RUN * n_ft
    xt_next = 0
    for j in range(min(8, total_tiles)):
        emit_xt_load(j)
        xt_next += 1
    emit_cs_load(0)
    if NTG > 1:
        emit_cs_load(1)

    prev_ep_act = 0
    prev_ep_dve = 0

    for tg in range(NTG_RUN):
        for d in range(n_ft):
            j = tg * n_ft + d
            b = j % 8
            waits = [(sX[b].sem, 16 * (j // 8 + 1))]
            if d == 0:
                waits.append((acM.sem, prev_ep_act))
                waits.append((dvM.sem, prev_ep_dve))
            st = (d == 0)
            sp = (d == n_ft - 1)
            for jj in range(HPC):
                P.emit("tensor",
                       lambda t, d=d, jj=jj, b=b, st=st, sp=sp: t.matmul(
                           q_ps[:, jj * 512:jj * 512 + 512],
                           wq_sb[:, d, jj * 128:jj * 128 + 128],
                           xbuf[b][:, :], start=st, stop=sp),
                       waits=waits if jj == 0 else ())
            P.emit("tensor",
                   lambda t, d=d, b=b, st=st, sp=sp: t.matmul(
                       kv_ps[:, 0:512], wk_sb[:, d, :], xbuf[b][:, :], start=st, stop=sp))
            P.emit("tensor",
                   lambda t, d=d, b=b, st=st, sp=sp: t.matmul(
                       kv_ps[:, 512:1024], wv_sb[:, d, :], xbuf[b][:, :], start=st, stop=sp),
                   inc=peM.bump())
            xt_consumed_at.append(peM.n)
            if xt_next < total_tiles:
                emit_xt_load(xt_next)
                xt_next += 1
        dloop_done = peM.n

        csb = tg % 2
        cs_ready = 32 * (tg // 2 + 1)
        tgs = slice(tg * 512, (tg + 1) * 512)
        prev_t2 = 0
        prev_red = 0
        prev_norm_pe = 0
        for hx in range(HPC + 1):
            if p1_lvl < 1:
                break
            is_q = hx < HPC
            src_ps = q_ps[:, hx * 512:hx * 512 + 512] if is_q else kv_ps[:, 0:512]
            P.emit("scalar",
                   lambda s, src=src_ps: s.copy(qkstage[:, :], src),
                   waits=[(peM.sem, dloop_done), (dvM.sem, prev_t2), (acM.sem, acM.n)],
                   inc=acM.bump())
            evac_done = acM.n
            if p1_lvl < 2:
                continue
            P.emit("tensor",
                   lambda t: t.matmul(sw_ps[:, :], pswap_sb[:, :], qkstage[:, :],
                                      start=True, stop=True),
                   waits=[(acM.sem, evac_done), (dvM.sem, max(prev_t2, prev_red))],
                   inc=peM.bump())
            swap_done = peM.n
            if p1_lvl < 3:
                continue
            P.emit("vector",
                   lambda v, src=src_ps, csb=csb: v.tensor_mul(t1b[:, :], src, csbuf[csb][:, 0, :]),
                   waits=[(sCS[csb].sem, cs_ready), (peM.sem, dloop_done), (dvM.sem, dvM.n),
                          (acM.sem, evac_done)])
            P.emit("vector",
                   lambda v, csb=csb: v.tensor_mul(t2b[:, :], sw_ps[:, :], csbuf[csb][:, 1, :]),
                   waits=[(peM.sem, swap_done), (dvM.sem, dvM.n)], inc=dvM.bump())
            prev_t2 = dvM.n
            dst = q4[:, hx, tgs] if is_q else KT[:, tgs]
            P.emit("vector",
                   lambda v, dst=dst: v.tensor_add(dst, t1b[:, :], t2b[:, :]),
                   waits=[(dvM.sem, dvM.n)], inc=dvM.bump())
            rope_done = dvM.n
            if p1_lvl < 4:
                continue
            P.emit("scalar",
                   lambda s, dst=dst: s.activation(sqstage[:, :], dst, AF.Square),
                   waits=[(dvM.sem, rope_done), (peM.sem, prev_norm_pe), (acM.sem, acM.n)],
                   inc=acM.bump())
            sq_done = acM.n
            P.emit("tensor",
                   lambda t: t.matmul(sw_ps[0:1, :], onescb_sb[:, :], sqstage[:, :],
                                      start=True, stop=True),
                   waits=[(acM.sem, sq_done), (dvM.sem, prev_t2)],
                   inc=peM.bump())
            prev_norm_pe = peM.n
            if is_q:
                dst_red = cmax[0:1, (tg * 4 + hx):(tg * 4 + hx + 1)]
            else:
                dst_red = kmaxs[0:1, tg:tg + 1]
            P.emit("vector",
                   lambda v, dst_red=dst_red: v.reduce_max(dst_red, sw_ps[0:1, :], axis=AX.X),
                   waits=[(peM.sem, prev_norm_pe), (dvM.sem, dvM.n)], inc=dvM.bump())
            prev_red = dvM.n
        if p1_lvl >= 5:
            P.emit("scalar",
                   lambda s: s.copy(vstage[:, :], kv_ps[:, 512:1024]),
                   waits=[(peM.sem, dloop_done), (acM.sem, acM.n)], inc=acM.bump())
            vevac_done = acM.n
            prev_vt_copy = 0
            for i in range(4):
                P.emit("tensor",
                       lambda t, i=i: t.transpose(vt_ps[:, :], vstage[:, i * 128:i * 128 + 128],
                                                  ident_sb[:, :]),
                       waits=[(acM.sem, vevac_done), (dvM.sem, prev_vt_copy)],
                       inc=peM.bump())
                tr_done = peM.n
                P.emit("vector",
                       lambda v, i=i, tg=tg: v.tensor_copy(
                           Vtok[:, (tg * 4 + i) * 128:(tg * 4 + i) * 128 + 128], vt_ps[:, :]),
                       waits=[(peM.sem, tr_done), (dvM.sem, dvM.n)], inc=dvM.bump())
                prev_vt_copy = dvM.n
        cs_consumed_at.append(dvM.n)
        if tg + 2 < NTG:
            emit_cs_load(tg + 2)
        prev_ep_act = acM.n
        prev_ep_dve = dvM.n

    phase1_dve = dvM.n
    phase1_act = acM.n

    if p1_tgs is not None:
        sT = Ctr(stk, nc, "sT", 16)
        P.emit("gpsimd",
               lambda g: g.dma_start(out=outT[0:128, :], in_=KT[:, 0:To]),
               waits=[(dvM.sem, phase1_dve), (acM.sem, phase1_act)], inc=sT.bump())
        P.emit("gpsimd",
               lambda g: g.dma_start(out=outT[128:256, :], in_=q4[:, 0, 0:To]),
               inc=sT.bump())
        P.emit("gpsimd",
               lambda g: g.dma_start(out=outT[256:384, :], in_=Vtok[:, 0:To]),
               inc=sT.bump())
        P.emit("gpsimd", None, waits=[(sT.sem, sT.n)])
        def finish1():
            def runner(eng_ops):
                def _run(eng):
                    for fn, waits, inc in eng_ops:
                        for (sem, val) in waits:
                            eng.wait_ge(sem, val)
                        if fn is None:
                            continue
                        inst = fn(eng)
                        if inc is not None:
                            inst.then_inc(inc[0], inc[1])
                return _run
            with nc.Block() as block:
                block.tensor(runner(P.ops["tensor"]))
                block.vector(runner(P.ops["vector"]))
                block.scalar(runner(P.ops["scalar"]))
                block.gpsimd(runner(P.ops["gpsimd"]))
                block.sync(runner(P.ops["sync"]))
            return nc
        nc_out = finish1()
        ph1.close(); pp1.close(); ph12.close(); stk.close()
        return nc_out

    P.emit("vector",
           lambda v: v.reduce_max(kmaxs[0:1, NTG:NTG + 1], kmaxs[0:1, 0:NTG], axis=AX.X),
           waits=[(dvM.sem, phase1_dve)], inc=dvM.bump())
    P.emit("vector",
           lambda v: v.tensor_scalar(rowbuf[0:1, 0:4 * NQC], cmax[0:1, :],
                                     kmaxs[0:1, NTG:NTG + 1], 1.21, ALU.mult, ALU.mult),
           waits=[(dvM.sem, dvM.n)], inc=dvM.bump())
    u_done = dvM.n
    P.emit("scalar",
           lambda s: s.activation(rowbuf[0:1, 1024:1024 + 4 * NQC], rowbuf[0:1, 0:4 * NQC],
                                  AF.Ln, bias=1e-5),
           waits=[(dvM.sem, u_done), (sW.sem, sW.n), (dvM.sem, consts_ready),
                  (acM.sem, acM.n)],
           inc=acM.bump())
    P.emit("scalar",
           lambda s: s.activation(rowbuf[0:1, 2048:2048 + 4 * NQC],
                                  rowbuf[0:1, 1024:1024 + 4 * NQC], AF.Exp, scale=0.5),
           waits=[(acM.sem, acM.n)], inc=acM.bump())
    c_done = acM.n
    P.emit("vector",
           lambda v: v.tensor_scalar_mul(cneg[0:1, :], rowbuf[0:1, 2048:2048 + 4 * NQC], -1.0),
           waits=[(acM.sem, c_done), (dvM.sem, dvM.n)], inc=dvM.bump())
    cneg_done = dvM.n

    dbg_tail = []

    def finish():
        def runner(eng_ops):
            def _run(eng):
                for fn, waits, inc in eng_ops:
                    for (sem, val) in waits:
                        eng.wait_ge(sem, val)
                    if fn is None:
                        continue
                    inst = fn(eng)
                    if inc is not None:
                        inst.then_inc(inc[0], inc[1])
            return _run

        with allow_ici_all_to_all():
            with nc.Block() as block:
                block.tensor(runner(P.ops["tensor"]))
                block.vector(runner(P.ops["vector"]))
                block.scalar(runner(P.ops["scalar"]))
                block.gpsimd(runner(P.ops["gpsimd"]))
                block.sync(runner(P.ops["sync"]))
        return nc

    if upto == "p1":
        sT = Ctr(stk, nc, "sT", 16)
        P.emit("gpsimd",
               lambda g: g.dma_start(out=outT[0:128, :], in_=KT[:, 0:To]),
               waits=[(dvM.sem, cneg_done)], inc=sT.bump())
        P.emit("gpsimd",
               lambda g: g.dma_start(out=outT[128:256, :], in_=q4[:, 0, 0:To]),
               inc=sT.bump())
        P.emit("gpsimd",
               lambda g: g.dma_start(out=outT[256:384, :], in_=Vtok[:, 0:To]),
               inc=sT.bump())
        P.emit("gpsimd", None, waits=[(sT.sem, sT.n)])
        nc_out = finish()
        ph1.close(); pp1.close(); ph12.close(); stk.close()
        return nc_out

    ph1.close()
    pp1.close()

    ph2 = contextlib.ExitStack()
    pt = [[ph2.enter_context(nc.sbuf_tensor("pt%d_%d" % (hp, par), [128, 2, 1024], BF16))
           for par in range(2)] for hp in range(2)]
    bias_sb = ph2.enter_context(nc.sbuf_tensor("biasc_sb", [128, 4], F32))
    o_sb = [ph2.enter_context(nc.sbuf_tensor("o_sb%d" % i, [128, 512], F32)) for i in range(2)]
    linv_row = [ph2.enter_context(nc.sbuf_tensor("linv_row%d" % i, [1, 512], F32))
                for i in range(2)]
    linv_bc = [ph2.enter_context(nc.sbuf_tensor("linv_bc%d" % i, [128, 512], F32))
               for i in range(2)]
    atst = [ph2.enter_context(nc.sbuf_tensor("atst%d" % i, [128, 512], BF16)) for i in range(8)]

    pp2 = contextlib.ExitStack()
    s_ps = [pp2.enter_context(nc.psum_tensor("s_ps%d" % i, [128, 1024], F32)) for i in range(2)]
    o_ps = [pp2.enter_context(nc.psum_tensor("o_ps%d" % i, [128, 512], F32)) for i in range(2)]
    l_ps = [pp2.enter_context(nc.psum_tensor("l_ps%d" % i, [128, 512], F32)) for i in range(2)]

    pair_idx = 0
    exp_half_at = {}
    group_l_done = {}
    ghp_count = [0, 0]
    atst_stored_at = [0] * 8
    o_free_at = [0, 0]
    o_sb_free = [0, 0]
    linv_bc_free = [0, 0]
    bank_guard = [0, 0]

    for qc in range(NQC):
        pro_waits = [(acM.sem, acM.n), (dvM.sem, cneg_done)]
        if qc == 0:
            pro_waits += [(dvM.sem, max(phase1_dve, cneg_done)), (acM.sem, phase1_act)]
        P.emit("tensor",
               lambda t, qc=qc: t.matmul(s_ps[0][:, 0:4], onesrb_sb[:, :],
                                         cneg[0:1, 4 * qc:4 * qc + 4], start=True, stop=True),
               waits=pro_waits, inc=peM.bump())
        bias_mm = peM.n
        P.emit("scalar",
               lambda s: s.copy(bias_sb[:, :], s_ps[0][:, 0:4]),
               waits=[(peM.sem, bias_mm), (acM.sem, acM.n)], inc=acM.bump())
        bias_ready = acM.n
        bank_guard[0] = bias_ready

        for pas in range(2):
            for kc in range(qc + 1):
                diag = (kc == qc)
                for hp in range(2):
                    h = pas * 2 + hp
                    gpar = ghp_count[hp] % 2
                    pair_avs = []
                    for pj in range(2):
                        pg = pair_idx
                        bank = pg % 2
                        sdone = [0, 0]
                        for sj in range(2):
                            ks = pj * 2 + sj
                            w = []
                            if pg >= 2:
                                w.append((acM.sem, exp_half_at[(pg - 2, sj)]))
                            if sj == 0:
                                w.append((acM.sem, bank_guard[bank]))
                            P.emit("tensor",
                                   lambda t, kc=kc, ks=ks, h=h, qc=qc, bank=bank, sj=sj:
                                   t.matmul(s_ps[bank][:, sj * 512:sj * 512 + 512],
                                            KT[:, kc * 512 + ks * 128:kc * 512 + ks * 128 + 128],
                                            q4[:, h, qc * 512:qc * 512 + 512],
                                            start=True, stop=True),
                                   waits=w, inc=peM.bump())
                            sdone[sj] = peM.n
                        av_waits = [None, None]
                        for sj in range(2):
                            ks = pj * 2 + sj
                            ew = [(peM.sem, sdone[sj]), (acM.sem, acM.n)]
                            if (hp, gpar) in group_l_done:
                                ew.append((peM.sem, group_l_done[(hp, gpar)]))
                            P.emit("scalar",
                                   lambda s, bank=bank, hp=hp, gpar=gpar, pj=pj, sj=sj,
                                   hcol=pas * 2 + hp:
                                   s.activation(pt[hp][gpar][:, pj, sj * 512:sj * 512 + 512],
                                                s_ps[bank][:, sj * 512:sj * 512 + 512],
                                                AF.Exp, bias=bias_sb[:, hcol:hcol + 1]),
                                   waits=ew, inc=acM.bump())
                            exp_half_at[(pg, sj)] = acM.n
                            av_waits[sj] = (acM.sem, acM.n)
                            if diag:
                                P.emit("vector",
                                       lambda v, hp=hp, gpar=gpar, pj=pj, ks=ks, sj=sj:
                                       v.tensor_mul(pt[hp][gpar][:, pj, sj * 512:sj * 512 + 512],
                                                    pt[hp][gpar][:, pj, sj * 512:sj * 512 + 512],
                                                    masks_sb[:, ks, :]),
                                       waits=[(acM.sem, acM.n), (dvM.sem, dvM.n)],
                                       inc=dvM.bump())
                                av_waits[sj] = (dvM.sem, dvM.n)
                        pair_avs.append((pj, av_waits))
                        pair_idx += 1
                    for (ppj, av_waits) in pair_avs:
                        st0 = (kc == 0 and ppj == 0)
                        sp0 = (diag and ppj == 1)
                        for sj in range(2):
                            ks = ppj * 2 + sj
                            kslice = slice((kc * 4 + ks) * 128, (kc * 4 + ks) * 128 + 128)
                            avkw = [av_waits[sj]]
                            if st0 and sj == 0:
                                avkw.append((dvM.sem, o_free_at[hp]))
                            P.emit("tensor",
                                   lambda t, gpar=gpar, ppj=ppj, sj=sj, kslice=kslice, hp=hp,
                                   st=(st0 and sj == 0), sp=(sp0 and sj == 1):
                                   t.matmul(o_ps[hp][:, :], Vtok[:, kslice],
                                            pt[hp][gpar][:, ppj, sj * 512:sj * 512 + 512],
                                            start=st, stop=sp),
                                   waits=avkw)
                        for sj in range(2):
                            P.emit("tensor",
                                   lambda t, gpar=gpar, ppj=ppj, sj=sj, hp=hp,
                                   st=(st0 and sj == 0), sp=(sp0 and sj == 1):
                                   t.matmul(l_ps[hp][0:1, :], onescb_sb[:, :],
                                            pt[hp][gpar][:, ppj, sj * 512:sj * 512 + 512],
                                            start=st, stop=sp),
                                   inc=peM.bump() if (ppj == 1 and sj == 1) else None)
                    group_l_done[(hp, ghp_count[hp] % 2)] = peM.n
                    ghp_count[hp] += 1
            for hp in range(2):
                h = pas * 2 + hp
                gl = group_l_done[(hp, (ghp_count[hp] - 1) % 2)]
                P.emit("vector",
                       lambda v, hp=hp: v.tensor_copy(o_sb[hp][:, :], o_ps[hp][:, :]),
                       waits=[(peM.sem, gl), (dvM.sem, max(dvM.n, o_sb_free[hp]))],
                       inc=dvM.bump())
                P.emit("vector",
                       lambda v, hp=hp: v.reciprocal(linv_row[hp][:, :], l_ps[hp][0:1, :]),
                       waits=[(peM.sem, gl), (dvM.sem, dvM.n)], inc=dvM.bump())
                recip_done = dvM.n
                o_free_at[hp] = recip_done
                P.emit("tensor",
                       lambda t, hp=hp: t.matmul(s_ps[hp][:, 0:512], onesrf_sb[:, :],
                                                 linv_row[hp][:, :], start=True, stop=True),
                       waits=[(dvM.sem, recip_done)], inc=peM.bump())
                bc_done = peM.n
                P.emit("scalar",
                       lambda s, hp=hp: s.copy(linv_bc[hp][:, :], s_ps[hp][:, 0:512]),
                       waits=[(peM.sem, bc_done), (acM.sem, acM.n),
                              (dvM.sem, linv_bc_free[hp])],
                       inc=acM.bump())
                bank_guard[hp] = acM.n
                bcev_done = acM.n
                sti = (qc * 4 + h) % 8
                P.emit("vector",
                       lambda v, hp=hp, sti=sti: v.tensor_mul(atst[sti][:, :], o_sb[hp][:, :],
                                                              linv_bc[hp][:, :]),
                       waits=[(acM.sem, bcev_done), (sA2[sti].sem, atst_stored_at[sti]),
                              (dvM.sem, dvM.n)],
                       inc=dvM.bump())
                o_sb_free[hp] = dvM.n
                linv_bc_free[hp] = dvM.n
                mul_done = dvM.n
                d_dest = qc // NTH
                tof = (qc % NTH) * 512
                P.emit("sync",
                       lambda sy, h=h, sti=sti, d_dest=d_dest, tof=tof: sy.dma_start(
                           out=a2a_in.ap()[d_dest, h * 128:h * 128 + 128, tof:tof + 512],
                           in_=atst[sti][:, :]),
                       waits=[(dvM.sem, mul_done)], inc=sA2[sti].bump())
                atst_stored_at[sti] = sA2[sti].n

    P.emit("gpsimd",
           lambda g: g.collective_compute(
               "AllToAll", ALU.bypass, replica_groups=[list(range(NCORES))],
               ins=[a2a_in.ap().opt()], outs=[a2a_out.ap().opt()]),
           waits=[(s.sem, s.n) for s in sA2], inc=ccs.bump())

    if upto == "p2":
        sT = Ctr(stk, nc, "sT", 16)
        P.emit("gpsimd",
               lambda g: g.dma_start(out=outT[:, :],
                                     in_=a2a_out.ap().rearrange("r f t -> (r f) t")),
               waits=[(ccs.sem, 1)], inc=sT.bump())
        P.emit("gpsimd", None, waits=[(sT.sem, sT.n)])
        nc_out = finish()
        ph2.close(); pp2.close(); ph12.close(); stk.close()
        return nc_out

    ph2.close()
    pp2.close()
    ph12.close()

    ph34 = contextlib.ExitStack()
    xo_sb = [ph34.enter_context(nc.sbuf_tensor("xo_sb%d" % i, [128, To], F32)) for i in range(2)]
    st_sb = [ph34.enter_context(nc.sbuf_tensor("st_sb%d" % i, [128, To], F32)) for i in range(4)]
    r2st = [ph34.enter_context(nc.sbuf_tensor("r2st%d" % i, [128, To], F32)) for i in range(2)]
    sq_sb = ph34.enter_context(nc.sbuf_tensor("sq_sb", [128, To], F32))
    acc_s = ph34.enter_context(nc.sbuf_tensor("acc_s", [128, To], F32))
    acc_q = ph34.enter_context(nc.sbuf_tensor("acc_q", [128, To], F32))
    m_bc = ph34.enter_context(nc.sbuf_tensor("m_bc", [128, To], F32))
    rstd_bc = ph34.enter_context(nc.sbuf_tensor("rstd_bc", [128, To], F32))
    sc_bc = ph34.enter_context(nc.sbuf_tensor("sc_bc", [128, To], F32))

    ph25 = contextlib.ExitStack()
    at_own = ph25.enter_context(nc.sbuf_tensor("at_own", [128, n_ft, To], BF16))
    wo_sb = [ph25.enter_context(nc.sbuf_tensor("wo_sb%d" % i, [128, n_ft, 256], BF16))
             for i in range(2)]

    pp3 = contextlib.ExitStack()
    psA = [pp3.enter_context(nc.psum_tensor("psA%d" % i, [128, To], F32)) for i in range(4)]

    P.emit("gpsimd",
           lambda g: g.dma_start(out=at_own[:, :, :],
                                 in_=a2a_out.ap().rearrange("r (i p) t -> p (r i) t", p=128)),
           waits=[(ccs.sem, 1)], inc=sAo.bump())
    at_own_ready = sAo.n

    n_grp = n_ft // 2
    wo_grp_free_at = [0, 0]
    wo_stripe_free_at = [0, 0]
    st_free_at = [0, 0]
    st_free_acm = [0, 0]
    xo_used_at = [0, 0]
    acc_chain = 0

    def emit_wo_stripe(g):
        b = g % 2
        w = [(ccs.sem, 1)]
        if g >= 2:
            w.append((peM.sem, wo_stripe_free_at[b]))
        P.emit("gpsimd",
               lambda gp, g=g, b=b: gp.dma_start(
                   out=wo_sb[b][:, :, :],
                   in_=wo[:, g * 256:(g + 1) * 256].rearrange("(i p) f -> p i f", p=128)),
               waits=w, inc=sWo[b].bump())

    emit_wo_stripe(0)
    emit_wo_stripe(1)
    for ft in range(min(2, n_ft)):
        P.emit("sync",
               lambda sy, ft=ft: sy.dma_start(
                   out=xo_sb[ft % 2][:, :], in_=xoT[ft * 128:(ft + 1) * 128, :]),
               waits=[(ccs.sem, 1)], inc=sXo[ft % 2].bump())

    for g in range(n_grp):
        b = g % 2
        mmw = [(sWo[b].sem, 16 * (g // 2 + 1)), (sAo.sem, at_own_ready)]
        if g >= 2:
            mmw.append((dvM.sem, wo_grp_free_at[b]))
        for d in range(n_ft):
            for jj in range(2):
                for th in range(NTH):
                    P.emit("tensor",
                           lambda t, b=b, d=d, jj=jj, th=th, st=(d == 0), sp=(d == n_ft - 1):
                           t.matmul(psA[2 * b + jj][:, th * 512:th * 512 + 512],
                                    wo_sb[b][:, d, jj * 128:jj * 128 + 128],
                                    at_own[:, d, th * 512:th * 512 + 512],
                                    start=st, stop=sp),
                           waits=mmw if (d == 0 and jj == 0 and th == 0) else (),
                           inc=peM.bump() if (d == n_ft - 1 and jj == 1 and th == NTH - 1) else None)
        grp_done = peM.n
        wo_stripe_free_at[b] = grp_done
        if g + 2 < n_grp:
            emit_wo_stripe(g + 2)
        for jj in range(2):
            ft = 2 * g + jj
            sb = ft % 2
            P.emit("vector",
                   lambda v, b=b, jj=jj, sb=sb: v.tensor_add(
                       st_sb[sb][:, :], psA[2 * b + jj][:, 0:To], xo_sb[sb][:, :]),
                   waits=[(peM.sem, grp_done), (sXo[sb].sem, 16 * (ft // 2 + 1)),
                          (sSw[sb].sem, st_free_at[sb]), (acM.sem, st_free_acm[sb]),
                          (dvM.sem, dvM.n)],
                   inc=dvM.bump())
            add_done = dvM.n
            xo_used_at[sb] = add_done
            if jj == 1:
                wo_grp_free_at[b] = add_done
            if ft + 2 < n_ft:
                P.emit("sync",
                       lambda sy, ft=ft, sb=sb: sy.dma_start(
                           out=xo_sb[sb][:, :], in_=xoT[(ft + 2) * 128:(ft + 3) * 128, :]),
                       waits=[(dvM.sem, add_done)], inc=sXo[sb].bump())
            P.emit("sync",
                   lambda sy, ft=ft, sb=sb: sy.dma_start(
                       out=s_dram.ap()[ft * 128:(ft + 1) * 128, :], in_=st_sb[sb][:, :]),
                   waits=[(dvM.sem, add_done)], inc=sSw[sb].bump())
            st_free_at[sb] = sSw[sb].n
            if ft == 0:
                P.emit("vector", lambda v, sb=sb: v.tensor_copy(acc_s[:, :], st_sb[sb][:, :]),
                       waits=[(dvM.sem, add_done)], inc=dvM.bump())
            else:
                P.emit("vector", lambda v, sb=sb: v.tensor_add(acc_s[:, :], acc_s[:, :],
                                                               st_sb[sb][:, :]),
                       waits=[(dvM.sem, max(add_done, acc_chain))], inc=dvM.bump())
            P.emit("scalar", lambda s, sb=sb: s.activation(sq_sb[:, :], st_sb[sb][:, :],
                                                           AF.Square),
                   waits=[(dvM.sem, add_done), (dvM.sem, acc_chain), (acM.sem, acM.n)],
                   inc=acM.bump())
            sqd = acM.n
            st_free_acm[sb] = sqd
            if ft == 0:
                P.emit("vector", lambda v: v.tensor_copy(acc_q[:, :], sq_sb[:, :]),
                       waits=[(acM.sem, sqd), (dvM.sem, dvM.n)], inc=dvM.bump())
            else:
                P.emit("vector", lambda v: v.tensor_add(acc_q[:, :], acc_q[:, :], sq_sb[:, :]),
                       waits=[(acM.sem, sqd), (dvM.sem, dvM.n)], inc=dvM.bump())
            acc_chain = dvM.n

    def emit_ln_finalize(m_dst, r_dst, roff, acc_ready_dvM, sc_dst=None):
        psum, psumq = psA[3], psA[2]
        for half in range(NTH):
            P.emit("tensor",
                   lambda t, half=half: t.matmul(
                       psum[0:1, half * 512:half * 512 + 512], onescf_sb[:, :],
                       acc_s[:, half * 512:half * 512 + 512], start=True, stop=True),
                   waits=[(dvM.sem, acc_ready_dvM), (acM.sem, acM.n)],
                   inc=peM.bump())
            P.emit("tensor",
                   lambda t, half=half: t.matmul(
                       psumq[0:1, half * 512:half * 512 + 512], onescf_sb[:, :],
                       acc_q[:, half * 512:half * 512 + 512], start=True, stop=True),
                   inc=peM.bump())
        mm_done = peM.n
        r0, r1, r2o = roff, roff + 1024, roff + 2048
        P.emit("scalar",
               lambda s: s.copy(rowbuf[0:1, r0:r0 + To], psum[0:1, 0:To]),
               waits=[(peM.sem, mm_done), (acM.sem, acM.n)], inc=acM.bump())
        P.emit("scalar",
               lambda s: s.copy(rowbuf[0:1, r1:r1 + To], psumq[0:1, 0:To]),
               waits=[(acM.sem, acM.n)], inc=acM.bump())
        evr = acM.n
        P.emit("vector",
               lambda v: v.tensor_scalar_mul(rowbuf[0:1, r0:r0 + To],
                                             rowbuf[0:1, r0:r0 + To], 1.0 / DIM),
               waits=[(acM.sem, evr), (dvM.sem, dvM.n)], inc=dvM.bump())
        m_row_done = dvM.n
        P.emit("vector",
               lambda v: v.tensor_scalar_mul(rowbuf[0:1, r1:r1 + To],
                                             rowbuf[0:1, r1:r1 + To], 1.0 / DIM),
               waits=[(dvM.sem, dvM.n)], inc=dvM.bump())
        P.emit("vector",
               lambda v: v.tensor_mul(rowbuf[0:1, r2o:r2o + To],
                                      rowbuf[0:1, r0:r0 + To], rowbuf[0:1, r0:r0 + To]),
               waits=[(dvM.sem, dvM.n)], inc=dvM.bump())
        P.emit("vector",
               lambda v: v.tensor_sub(rowbuf[0:1, r1:r1 + To], rowbuf[0:1, r1:r1 + To],
                                      rowbuf[0:1, r2o:r2o + To]),
               waits=[(dvM.sem, dvM.n)], inc=dvM.bump())
        var_done = dvM.n
        P.emit("scalar",
               lambda s: s.activation(rowbuf[0:1, r2o:r2o + To], rowbuf[0:1, r1:r1 + To],
                                      AF.Ln, bias=1e-5),
               waits=[(dvM.sem, var_done), (acM.sem, acM.n)], inc=acM.bump())
        P.emit("scalar",
               lambda s: s.activation(rowbuf[0:1, r1:r1 + To], rowbuf[0:1, r2o:r2o + To],
                                      AF.Exp, scale=-0.5),
               waits=[(acM.sem, acM.n)], inc=acM.bump())
        rstd_done = acM.n
        sc_done = dvM.n
        if sc_dst is not None:
            P.emit("vector",
                   lambda v: v.tensor_scalar(rowbuf[0:1, r2o:r2o + To],
                                             rowbuf[0:1, r1:r1 + To],
                                             wscale_sb[0:1, 0:1], None, ALU.mult),
                   waits=[(acM.sem, rstd_done), (dvM.sem, dvM.n)], inc=dvM.bump())
            sc_done = dvM.n
        bc_list = [(m_dst, r0), (r_dst, r1)]
        if sc_dst is not None:
            bc_list.append((sc_dst, r2o))
        prev_ev = 0
        for (dst, off) in bc_list:
            for half in range(NTH):
                P.emit("tensor",
                       lambda t, off=off, half=half: t.matmul(
                           psum[:, half * 512:half * 512 + 512], onesrf_sb[:, :],
                           rowbuf[0:1, off + half * 512:off + half * 512 + 512],
                           start=True, stop=True),
                       waits=[(acM.sem, max(rstd_done, prev_ev)), (dvM.sem, sc_done)],
                       inc=peM.bump())
                bmm = peM.n
                P.emit("scalar",
                       lambda s, dst=dst, half=half: s.copy(
                           dst[:, half * 512:half * 512 + 512],
                           psum[:, half * 512:half * 512 + 512]),
                       waits=[(peM.sem, bmm), (acM.sem, acM.n)], inc=acM.bump())
                prev_ev = acM.n
        return acM.n

    bc1_done = emit_ln_finalize(m_bc, rstd_bc, 0, acc_chain, sc_dst=sc_bc)

    if upto == "p25":
        sT = Ctr(stk, nc, "sT", 16)
        P.emit("sync",
               lambda sy: sy.dma_start(out=outT[:, :], in_=s_dram.ap()),
               waits=[(sSw[0].sem, sSw[0].n), (sSw[1].sem, sSw[1].n),
                      (acM.sem, bc1_done)], inc=sT.bump())
        P.emit("sync", None, waits=[(sT.sem, sT.n)])
        nc_out = finish()
        ph25.close(); ph34.close(); pp3.close(); stk.close()
        return nc_out

    ph25.close()
    ph3 = contextlib.ExitStack()
    d1_sb = ph3.enter_context(nc.sbuf_tensor("d1_sb", [128, n_ft, To], BF16))
    w2r = [ph3.enter_context(nc.sbuf_tensor("w2r%d" % i, [128, n_ft, 128], BF16))
           for i in range(3)]
    wf_sb = [ph3.enter_context(nc.sbuf_tensor("wf_sb%d" % i, [128, n_ft, 128], BF16))
             for i in range(3)]

    d1_done_at = []
    for ft in range(n_ft):
        sb = ft % 4
        P.emit("sync",
               lambda sy, ft=ft, sb=sb: sy.dma_start(
                   out=st_sb[sb][:, :], in_=s_dram.ap()[ft * 128:(ft + 1) * 128, :]),
               waits=[(sSw[0].sem, sSw[0].n), (sSw[1].sem, sSw[1].n),
                      (acM.sem, bc1_done),
                      (dvM.sem, d1_done_at[ft - 4] if ft >= 4 else 0)],
               inc=sSr[sb].bump())
        P.emit("vector",
               lambda v, ft=ft, sb=sb: v.tensor_sub(d1_sb[:, ft, :], st_sb[sb][:, :],
                                                    m_bc[:, :]),
               waits=[(sSr[sb].sem, 16 * (ft // 4 + 1)), (acM.sem, bc1_done),
                      (dvM.sem, dvM.n)],
               inc=dvM.bump())
        d1_done_at.append(dvM.n)
    d1_all = dvM.n

    ff_grp_free_at = [0, 0, 0]
    ff_stripe_free_at = [0, 0, 0]
    wf_conv_at = [0, 0, 0]
    st3_free_dv = [0, 0, 0, 0]
    r2st_free_dv = [0, 0]
    r2st_free_ac = [0, 0]
    r2st_spill_at = [0, 0]
    acc2_chain = 0

    for g in range(n_ft):
        b = g % 3
        w = [(sSw[0].sem, sSw[0].n), (sSw[1].sem, sSw[1].n)]
        if g >= 3:
            w.append((dvM.sem, wf_conv_at[b]))
        P.emit("gpsimd",
               lambda gp, g=g, b=b: gp.dma_start(
                   out=w2r[b][:, :, :],
                   in_=w2T[:, g * 128:(g + 1) * 128].rearrange("(i p) f -> p i f", p=128)),
               waits=w, inc=sFf[b].bump())
        P.emit("vector",
               lambda v, b=b: v.tensor_scalar_sub(wf_sb[b][:, :, :], w2r[b][:, :, :], 1.5),
               waits=[(sFf[b].sem, 16 * (g // 3 + 1)), (peM.sem, ff_stripe_free_at[b]),
                      (dvM.sem, dvM.n)],
               inc=dvM.bump())
        wf_conv_at[b] = dvM.n
        conv_done = dvM.n
        mmw = [(dvM.sem, max(conv_done, d1_all))]
        if g >= 3:
            mmw.append((dvM.sem, ff_grp_free_at[b]))
        for d in range(n_ft):
            for th in range(NTH):
                P.emit("tensor",
                       lambda t, b=b, d=d, th=th, st=(d == 0), sp=(d == n_ft - 1):
                       t.matmul(psA[b][:, th * 512:th * 512 + 512],
                                wf_sb[b][:, d, :], d1_sb[:, d, th * 512:th * 512 + 512],
                                start=st, stop=sp),
                       waits=mmw if (d == 0 and th == 0) else (),
                       inc=peM.bump() if (d == n_ft - 1 and th == NTH - 1) else None)
        grp_done = peM.n
        ff_stripe_free_at[b] = grp_done
        sb = g % 4
        sb2 = g % 2
        P.emit("sync",
               lambda sy, g=g, sb=sb: sy.dma_start(
                   out=st_sb[sb][:, :], in_=s_dram.ap()[g * 128:(g + 1) * 128, :]),
               waits=[(dvM.sem, max(d1_all, st3_free_dv[sb]))],
               inc=sS2[sb].bump())
        P.emit("vector",
               lambda v, sb=sb: v.tensor_sub(sq_sb[:, :], st_sb[sb][:, :], m_bc[:, :]),
               waits=[(sS2[sb].sem, 16 * (g // 4 + 1)), (dvM.sem, dvM.n)],
               inc=dvM.bump())
        st3_free_dv[sb] = dvM.n
        P.emit("vector",
               lambda v, sb=sb: v.tensor_mul(sq_sb[:, :], sq_sb[:, :], rstd_bc[:, :]),
               waits=[(dvM.sem, dvM.n)], inc=dvM.bump())
        hh_done = dvM.n
        P.emit("vector",
               lambda v, b=b, sb2=sb2: v.tensor_mul(r2st[sb2][:, :], psA[b][:, 0:To],
                                                    sc_bc[:, :]),
               waits=[(peM.sem, grp_done), (dvM.sem, max(hh_done, r2st_free_dv[sb2])),
                      (acM.sem, r2st_free_ac[sb2]), (sR2w[sb2].sem, r2st_spill_at[sb2])],
               inc=dvM.bump())
        ff_grp_free_at[b] = dvM.n
        P.emit("vector",
               lambda v, g=g, sb2=sb2: v.scalar_tensor_tensor(
                   r2st[sb2][:, :], r2st[sb2][:, :], bias2d_sb[:, g:g + 1], sq_sb[:, :],
                   ALU.add, ALU.add),
               waits=[(dvM.sem, dvM.n)], inc=dvM.bump())
        r2_done = dvM.n
        P.emit("scalar",
               lambda s, g=g, sb2=sb2: s.dma_start(
                   out=r2_dram.ap()[g * 128:(g + 1) * 128, :], in_=r2st[sb2][:, :]),
               waits=[(dvM.sem, r2_done)], inc=sR2w[sb2].bump())
        r2st_spill_at[sb2] = sR2w[sb2].n
        if g == 0:
            P.emit("vector", lambda v, sb2=sb2: v.tensor_copy(acc_s[:, :], r2st[sb2][:, :]),
                   waits=[(dvM.sem, r2_done)], inc=dvM.bump())
        else:
            P.emit("vector", lambda v, sb2=sb2: v.tensor_add(acc_s[:, :], acc_s[:, :],
                                                             r2st[sb2][:, :]),
                   waits=[(dvM.sem, max(r2_done, acc2_chain))], inc=dvM.bump())
        P.emit("scalar", lambda s, sb2=sb2: s.activation(sq_sb[:, :], r2st[sb2][:, :], AF.Square),
               waits=[(dvM.sem, dvM.n), (acM.sem, acM.n)], inc=acM.bump())
        sqd = acM.n
        r2st_free_ac[sb2] = sqd
        if g == 0:
            P.emit("vector", lambda v: v.tensor_copy(acc_q[:, :], sq_sb[:, :]),
                   waits=[(acM.sem, sqd), (dvM.sem, dvM.n)], inc=dvM.bump())
        else:
            P.emit("vector", lambda v: v.tensor_add(acc_q[:, :], acc_q[:, :], sq_sb[:, :]),
                   waits=[(acM.sem, sqd), (dvM.sem, dvM.n)], inc=dvM.bump())
        acc2_chain = dvM.n
        r2st_free_dv[sb2] = acc2_chain

    bc2_done = emit_ln_finalize(m_bc, rstd_bc, 0, acc2_chain)
    dv_after_bc2 = dvM.n

    out_done_at = [dv_after_bc2] * 4
    out_stage_free = [dv_after_bc2, dv_after_bc2]
    out_store_at = [0, 0]
    for ft in range(n_ft):
        sb = ft % 4
        sb2 = ft % 2
        P.emit("sync",
               lambda sy, ft=ft, sb=sb: sy.dma_start(
                   out=st_sb[sb][:, :], in_=r2_dram.ap()[ft * 128:(ft + 1) * 128, :]),
               waits=[(sR2w[0].sem, sR2w[0].n), (sR2w[1].sem, sR2w[1].n),
                      (dvM.sem, out_done_at[sb]), (acM.sem, bc2_done)],
               inc=sR2r[sb].bump())
        P.emit("vector",
               lambda v, sb=sb: v.tensor_sub(sq_sb[:, :], st_sb[sb][:, :], m_bc[:, :]),
               waits=[(sR2r[sb].sem, 16 * (ft // 4 + 1)), (acM.sem, bc2_done),
                      (dvM.sem, dvM.n)],
               inc=dvM.bump())
        out_done_at[sb] = dvM.n
        P.emit("vector",
               lambda v, sb2=sb2: v.tensor_mul(r2st[sb2][:, :], sq_sb[:, :], rstd_bc[:, :]),
               waits=[(dvM.sem, max(dvM.n, out_stage_free[sb2])),
                      (sOut[sb2].sem, out_store_at[sb2])],
               inc=dvM.bump())
        out_stage_free[sb2] = dvM.n
        P.emit("scalar",
               lambda s, ft=ft, sb2=sb2: s.dma_start(
                   out=outT[ft * 128:(ft + 1) * 128, :], in_=r2st[sb2][:, :]),
               waits=[(dvM.sem, dvM.n)], inc=sOut[sb2].bump())
        out_store_at[sb2] = sOut[sb2].n
    P.emit("sync", None, waits=[(sOut[0].sem, sOut[0].n), (sOut[1].sem, sOut[1].n)])

    nc_out = finish()
    ph3.close()
    ph34.close()
    pp3.close()
    stk.close()
    return nc_out


def prep_inputs(L0_attn_x, L0_attn_wq, L0_attn_wk, L0_attn_wv, L0_attn_wo,
                L0_ffn_weight_2bit, L0_ffn_weight_scale, L0_ffn_bias):
    x = np.asarray(L0_attn_x)
    S = x.shape[1]
    x2 = np.ascontiguousarray(x.reshape(S, DIM))
    xT = np.ascontiguousarray(x2.T)

    perm = np.concatenate([np.arange(0, HD, 2), np.arange(1, HD, 2)])
    wq = np.asarray(L0_attn_wq).reshape(DIM, NH, HD)[:, :, perm].reshape(DIM, NH * HD)
    wk = np.asarray(L0_attn_wk).reshape(DIM, NKV, HD)[:, :, perm].reshape(DIM, NKV * HD)
    wv = np.asarray(L0_attn_wv)
    wo = np.ascontiguousarray(np.asarray(L0_attn_wo))
    w2T = np.ascontiguousarray(np.asarray(L0_ffn_weight_2bit).T.astype(np.int32))
    bias2d = np.ascontiguousarray(np.asarray(L0_ffn_bias).reshape(DIM // 128, 128).T)
    wsc = np.ascontiguousarray(np.asarray(L0_ffn_weight_scale).reshape(1, 1).astype(np.float32))

    sc128 = np.float64(HD) ** -0.25
    freqs = 1.0 / THETA ** (np.arange(0, HD, 2, dtype=np.float64) / HD)
    ang = np.outer(freqs, np.arange(S, dtype=np.float64))
    C = (np.concatenate([np.cos(ang), np.cos(ang)], 0) * sc128).astype(np.float32)
    Ssg = (np.concatenate([-np.sin(ang), np.sin(ang)], 0) * sc128).astype(np.float32)
    kk = np.arange(128)[:, None]
    qq = np.arange(512)[None, :]
    masks = np.stack([(128 * s + kk <= qq) for s in range(4)]).astype(ml_dtypes.bfloat16)
    pswap = np.zeros((128, 128), np.float32)
    for i in range(128):
        pswap[i, (i + 64) % 128] = 1.0
    pswap = pswap.astype(ml_dtypes.bfloat16)
    identm = np.eye(128, dtype=ml_dtypes.bfloat16)

    To = S // NCORES
    maps = []
    for c in range(NCORES):
        maps.append({
            "xT": xT,
            "xoT": np.ascontiguousarray(xT[:, c * To:(c + 1) * To]),
            "wq": np.ascontiguousarray(wq[:, c * HPC * HD:(c + 1) * HPC * HD]),
            "wk": np.ascontiguousarray(wk[:, c * HD:(c + 1) * HD]),
            "wv": np.ascontiguousarray(np.asarray(wv)[:, c * HD:(c + 1) * HD]),
            "wo": wo,
            "w2T": w2T,
            "bias2d": bias2d,
            "wscale": wsc,
            "cosT": C,
            "sinT": Ssg,
            "masks": masks,
            "pswap": pswap,
            "ident": identm,
            "onescb": np.ones((128, 1), ml_dtypes.bfloat16),
            "onesrb": np.ones((1, 128), ml_dtypes.bfloat16),
            "onescf": np.ones((128, 1), np.float32),
            "onesrf": np.ones((1, 128), np.float32),
        })
    return maps


def kernel(L0_attn_x, L0_attn_wq, L0_attn_wk, L0_attn_wv, L0_attn_wo,
           L0_ffn_weight_2bit, L0_ffn_weight_scale, L0_ffn_bias):
    x = np.asarray(L0_attn_x)
    S = x.shape[1]
    maps = prep_inputs(L0_attn_x, L0_attn_wq, L0_attn_wk, L0_attn_wv, L0_attn_wo,
                       L0_ffn_weight_2bit, L0_ffn_weight_scale, L0_ffn_bias)
    nc = build_program(S)
    with allow_ici_all_to_all():
        res = bass_utils.run_bass_kernel_spmd(nc, maps, core_ids=list(range(NCORES)))
    outs = [np.asarray(r["outT"]).T for r in res.results]
    return np.ascontiguousarray(np.concatenate(outs, axis=0)).reshape(1, S, DIM).astype(np.float32)


# version 04
# speedup vs baseline: 1.0405x; 1.0405x over previous
import contextlib
import numpy as np
import ml_dtypes

import concourse.bass as bass
import concourse.mybir as mybir
import concourse.bass_utils as bass_utils
from concourse.aot_env import allow_ici_all_to_all

F32 = mybir.dt.float32
BF16 = mybir.dt.bfloat16
I32 = mybir.dt.int32
AF = mybir.ActivationFunctionType
ALU = mybir.AluOpType
AX = mybir.AxisListType

DIM = 4096
NH = 32
NKV = 8
HD = 128
NCORES = 8
HPC = 4
THETA = 10000.0

ENGS = ("tensor", "vector", "scalar", "gpsimd", "sync")


class Ctr:

    def __init__(self, stack, nc, name, step=1):
        self.sem = stack.enter_context(nc.semaphore(name))
        self.n = 0
        self.step = step

    def bump(self):
        self.n += self.step
        return (self.sem, self.step)


class Prog:
    def __init__(self):
        self.ops = {e: [] for e in ENGS}

    def emit(self, eng, fn=None, waits=(), inc=None):
        self.ops[eng].append((fn, tuple((s, v) for (s, v) in waits if v > 0), inc))


def build_program(S, upto='full', p1_tgs=None, p1_lvl=6):
    To = S // NCORES
    NTG = S // 512
    NQC = S // 512
    NTH = (To + 511) // 512
    n_ft = DIM // 128

    nc = bass.Bass("TRN2", target_bir_lowering=False)

    xT = nc.dram_tensor("xT", [DIM, S], F32, kind="ExternalInput").ap()
    xoT = nc.dram_tensor("xoT", [DIM, To], F32, kind="ExternalInput").ap()
    wq = nc.dram_tensor("wq", [DIM, HPC * HD], F32, kind="ExternalInput").ap()
    wk = nc.dram_tensor("wk", [DIM, HD], F32, kind="ExternalInput").ap()
    wv = nc.dram_tensor("wv", [DIM, HD], F32, kind="ExternalInput").ap()
    wo = nc.dram_tensor("wo", [DIM, DIM], F32, kind="ExternalInput").ap()
    w2T = nc.dram_tensor("w2T", [DIM, DIM], I32, kind="ExternalInput").ap()
    bias2d = nc.dram_tensor("bias2d", [128, n_ft], F32, kind="ExternalInput").ap()
    wscale = nc.dram_tensor("wscale", [1, 1], F32, kind="ExternalInput").ap()
    cosT = nc.dram_tensor("cosT", [128, S], F32, kind="ExternalInput").ap()
    sinT = nc.dram_tensor("sinT", [128, S], F32, kind="ExternalInput").ap()
    masks = nc.dram_tensor("masks", [4, 128, 512], BF16, kind="ExternalInput").ap()
    pswap = nc.dram_tensor("pswap", [128, 128], BF16, kind="ExternalInput").ap()
    ident = nc.dram_tensor("ident", [128, 128], BF16, kind="ExternalInput").ap()
    onescb = nc.dram_tensor("onescb", [128, 1], BF16, kind="ExternalInput").ap()
    onesrb = nc.dram_tensor("onesrb", [1, 128], BF16, kind="ExternalInput").ap()
    onescf = nc.dram_tensor("onescf", [128, 1], F32, kind="ExternalInput").ap()
    onesrf = nc.dram_tensor("onesrf", [1, 128], F32, kind="ExternalInput").ap()
    outT = nc.dram_tensor("outT", [DIM, To], F32, kind="ExternalOutput").ap()

    a2a_in = nc.dram_tensor("a2a_in", [NCORES, HPC * HD, To], BF16)
    a2a_out = nc.dram_tensor("a2a_out", [NCORES, HPC * HD, To], BF16)
    s_dram = nc.dram_tensor("s_dram", [DIM, To], F32)
    r2_dram = nc.dram_tensor("r2_dram", [DIM, To], F32)

    stk = contextlib.ExitStack()

    sW = Ctr(stk, nc, "sW", 16)
    sX = [Ctr(stk, nc, "sX%d" % i, 16) for i in range(8)]
    sCS = [Ctr(stk, nc, "sCS%d" % i, 16) for i in range(2)]
    sWo = [Ctr(stk, nc, "sWo%d" % i, 16) for i in range(2)]
    sFf = [Ctr(stk, nc, "sFf%d" % i, 16) for i in range(3)]
    sXo = [Ctr(stk, nc, "sXo%d" % i, 16) for i in range(2)]
    sSw = [Ctr(stk, nc, "sSw%d" % i, 16) for i in range(2)]
    sSr = [Ctr(stk, nc, "sSr%d" % i, 16) for i in range(4)]
    sS2 = [Ctr(stk, nc, "sS2%d" % i, 16) for i in range(4)]
    sR2w = [Ctr(stk, nc, "sR2w%d" % i, 16) for i in range(2)]
    sR2r = [Ctr(stk, nc, "sR2r%d" % i, 16) for i in range(4)]
    sA2 = [Ctr(stk, nc, "sA2_%d" % i, 16) for i in range(8)]
    sAo = Ctr(stk, nc, "sAo", 16)
    sOut = [Ctr(stk, nc, "sOut%d" % i, 16) for i in range(2)]
    ccs = Ctr(stk, nc, "ccs", 1)
    peM = Ctr(stk, nc, "peM", 1)
    acM = Ctr(stk, nc, "acM", 1)
    dvM = Ctr(stk, nc, "dvM", 1)

    P = Prog()

    masks_sb = stk.enter_context(nc.sbuf_tensor("masks_sb", [128, 4, 512], BF16))
    pswap_sb = stk.enter_context(nc.sbuf_tensor("pswap_sb", [128, 128], BF16))
    ident_sb = stk.enter_context(nc.sbuf_tensor("ident_sb", [128, 128], BF16))
    onescb_sb = stk.enter_context(nc.sbuf_tensor("onescb_sb", [128, 1], BF16))
    onesrb_sb = stk.enter_context(nc.sbuf_tensor("onesrb_sb", [1, 128], BF16))
    onescf_sb = stk.enter_context(nc.sbuf_tensor("onescf_sb", [128, 1], F32))
    onesrf_sb = stk.enter_context(nc.sbuf_tensor("onesrf_sb", [1, 128], F32))
    bias2d_sb = stk.enter_context(nc.sbuf_tensor("bias2d_sb", [128, n_ft], F32))
    wscale_sb = stk.enter_context(nc.sbuf_tensor("wscale_sb", [1, 1], F32))
    cmax = stk.enter_context(nc.sbuf_tensor("cmax", [1, 4 * NQC], F32))
    kmaxs = stk.enter_context(nc.sbuf_tensor("kmaxs", [1, NTG + 2], F32))
    cneg = stk.enter_context(nc.sbuf_tensor("cneg", [1, 4 * NQC], BF16))
    rowbuf = stk.enter_context(nc.sbuf_tensor("rowbuf", [1, 3 * 1024], F32))
    consts = stk.enter_context(nc.sbuf_tensor("consts", [1, 8], F32))

    xo_sb = [stk.enter_context(nc.sbuf_tensor("xo_sb%d" % i, [128, To], F32)) for i in range(2)]

    ph12 = contextlib.ExitStack()
    q4 = ph12.enter_context(nc.sbuf_tensor("q4", [128, HPC, S], BF16))
    KT = ph12.enter_context(nc.sbuf_tensor("KT", [128, S], BF16))
    Vtok = ph12.enter_context(nc.sbuf_tensor("Vtok", [128, S], BF16))

    nc.const_aps.register(F32, 1e-5, consts[0:1, 0:1])
    nc.const_aps.register(F32, 0.5, consts[0:1, 1:2])
    nc.const_aps.register(F32, -0.5, consts[0:1, 2:3])
    nc.const_aps.register(F32, -1.0, consts[0:1, 3:4])

    P.emit("vector", lambda v: v.memset(consts[0:1, 0:1], 1e-5), inc=dvM.bump())
    P.emit("vector", lambda v: v.memset(consts[0:1, 1:2], 0.5), inc=dvM.bump())
    P.emit("vector", lambda v: v.memset(consts[0:1, 2:3], -0.5), inc=dvM.bump())
    P.emit("vector", lambda v: v.memset(consts[0:1, 3:4], -1.0), inc=dvM.bump())
    consts_ready = dvM.n

    def g_load(dst, src):
        P.emit("gpsimd", lambda g, d=dst, s=src: g.dma_start(out=d, in_=s), inc=sW.bump())

    g_load(masks_sb[:, :, :], masks.rearrange("m p t -> p m t"))
    g_load(pswap_sb[:, :], pswap)
    g_load(ident_sb[:, :], ident)
    g_load(onescb_sb[:, :], onescb)
    g_load(onesrb_sb[:, :], onesrb)
    g_load(onescf_sb[:, :], onescf)
    g_load(onesrf_sb[:, :], onesrf)
    g_load(bias2d_sb[:, :], bias2d)
    g_load(wscale_sb[:, :], wscale)

    ph1 = contextlib.ExitStack()
    wq_sb = ph1.enter_context(nc.sbuf_tensor("wq_sb", [128, n_ft, HPC * HD], BF16))
    wk_sb = ph1.enter_context(nc.sbuf_tensor("wk_sb", [128, n_ft, HD], BF16))
    wv_sb = ph1.enter_context(nc.sbuf_tensor("wv_sb", [128, n_ft, HD], BF16))
    xbuf = [ph1.enter_context(nc.sbuf_tensor("xbuf%d" % i, [128, 512], BF16)) for i in range(8)]
    csbuf = [ph1.enter_context(nc.sbuf_tensor("csbuf%d" % i, [128, 2, 512], F32)) for i in range(2)]
    qkstage = ph1.enter_context(nc.sbuf_tensor("qkstage", [128, 512], BF16))
    vstage = ph1.enter_context(nc.sbuf_tensor("vstage", [128, 512], BF16))
    sqstage = ph1.enter_context(nc.sbuf_tensor("sqstage", [128, 512], BF16))
    t1b = ph1.enter_context(nc.sbuf_tensor("t1b", [128, 512], F32))
    t2b = ph1.enter_context(nc.sbuf_tensor("t2b", [128, 512], F32))

    pp1 = contextlib.ExitStack()
    q_ps = pp1.enter_context(nc.psum_tensor("q_ps", [128, 4 * 512], F32))
    kv_ps = pp1.enter_context(nc.psum_tensor("kv_ps", [128, 2 * 512], F32))
    sw_ps = pp1.enter_context(nc.psum_tensor("sw_ps", [128, 512], F32))
    vt_ps = pp1.enter_context(nc.psum_tensor("vt_ps", [128, 128], BF16))

    g_load(wq_sb[:, :, :], wq.rearrange("(i p) f -> p i f", p=128))
    g_load(wk_sb[:, :, :], wk.rearrange("(i p) f -> p i f", p=128))
    g_load(wv_sb[:, :, :], wv.rearrange("(i p) f -> p i f", p=128))
    w_loaded = sW.n

    total_tiles = NTG * n_ft
    xt_consumed_at = []
    cs_consumed_at = []

    if upto == "p0":
        sT = Ctr(stk, nc, "sT", 16)
        P.emit("gpsimd",
               lambda g: g.dma_start(out=outT[0:128, 0:512], in_=wq_sb[:, 0, :]),
               waits=[(sW.sem, sW.n)], inc=sT.bump())
        P.emit("gpsimd", None, waits=[(sT.sem, sT.n)])
        def finish0():
            def runner(eng_ops):
                def _run(eng):
                    for fn, waits, inc in eng_ops:
                        for (sem, val) in waits:
                            eng.wait_ge(sem, val)
                        if fn is None:
                            continue
                        inst = fn(eng)
                        if inc is not None:
                            inst.then_inc(inc[0], inc[1])
                return _run
            with nc.Block() as block:
                block.tensor(runner(P.ops["tensor"]))
                block.vector(runner(P.ops["vector"]))
                block.scalar(runner(P.ops["scalar"]))
                block.gpsimd(runner(P.ops["gpsimd"]))
                block.sync(runner(P.ops["sync"]))
            return nc
        nc_out = finish0()
        ph1.close(); pp1.close(); ph12.close(); stk.close()
        return nc_out

    def emit_xt_load(j):
        tg, d = divmod(j, n_ft)
        b = j % 8
        waits = [(sW.sem, w_loaded)]
        if j >= 8:
            waits.append((peM.sem, xt_consumed_at[j - 8]))
        P.emit("gpsimd",
               lambda g, b=b, tg=tg, d=d: g.dma_start(
                   out=xbuf[b][:, :], in_=xT[d * 128:(d + 1) * 128, tg * 512:(tg + 1) * 512]),
               waits=waits, inc=sX[b].bump())

    def emit_cs_load(tg):
        b = tg % 2
        waits = []
        if tg >= 2:
            waits.append((dvM.sem, cs_consumed_at[tg - 2]))
        P.emit("sync",
               lambda sy, b=b, tg=tg: sy.dma_start(
                   out=csbuf[b][:, 0, :], in_=cosT[:, tg * 512:(tg + 1) * 512]),
               waits=waits, inc=sCS[b].bump())
        P.emit("sync",
               lambda sy, b=b, tg=tg: sy.dma_start(
                   out=csbuf[b][:, 1, :], in_=sinT[:, tg * 512:(tg + 1) * 512]),
               inc=sCS[b].bump())

    NTG_RUN = NTG if p1_tgs is None else min(NTG, p1_tgs)
    total_tiles = NTG_RUN * n_ft
    xt_next = 0
    for j in range(min(8, total_tiles)):
        emit_xt_load(j)
        xt_next += 1
    emit_cs_load(0)
    if NTG > 1:
        emit_cs_load(1)

    prev_ep_act = 0
    prev_ep_dve = 0

    for tg in range(NTG_RUN):
        for d in range(n_ft):
            j = tg * n_ft + d
            b = j % 8
            waits = [(sX[b].sem, 16 * (j // 8 + 1))]
            if d == 0:
                waits.append((acM.sem, prev_ep_act))
                waits.append((dvM.sem, prev_ep_dve))
            st = (d == 0)
            sp = (d == n_ft - 1)
            for jj in range(HPC):
                P.emit("tensor",
                       lambda t, d=d, jj=jj, b=b, st=st, sp=sp: t.matmul(
                           q_ps[:, jj * 512:jj * 512 + 512],
                           wq_sb[:, d, jj * 128:jj * 128 + 128],
                           xbuf[b][:, :], start=st, stop=sp),
                       waits=waits if jj == 0 else ())
            P.emit("tensor",
                   lambda t, d=d, b=b, st=st, sp=sp: t.matmul(
                       kv_ps[:, 0:512], wk_sb[:, d, :], xbuf[b][:, :], start=st, stop=sp))
            P.emit("tensor",
                   lambda t, d=d, b=b, st=st, sp=sp: t.matmul(
                       kv_ps[:, 512:1024], wv_sb[:, d, :], xbuf[b][:, :], start=st, stop=sp),
                   inc=peM.bump())
            xt_consumed_at.append(peM.n)
            if xt_next < total_tiles:
                emit_xt_load(xt_next)
                xt_next += 1
        dloop_done = peM.n

        csb = tg % 2
        cs_ready = 32 * (tg // 2 + 1)
        tgs = slice(tg * 512, (tg + 1) * 512)
        prev_t2 = 0
        prev_red = 0
        prev_norm_pe = 0
        for hx in range(HPC + 1):
            if p1_lvl < 1:
                break
            is_q = hx < HPC
            src_ps = q_ps[:, hx * 512:hx * 512 + 512] if is_q else kv_ps[:, 0:512]
            P.emit("scalar",
                   lambda s, src=src_ps: s.copy(qkstage[:, :], src),
                   waits=[(peM.sem, dloop_done), (dvM.sem, prev_t2), (acM.sem, acM.n)],
                   inc=acM.bump())
            evac_done = acM.n
            if p1_lvl < 2:
                continue
            P.emit("tensor",
                   lambda t: t.matmul(sw_ps[:, :], pswap_sb[:, :], qkstage[:, :],
                                      start=True, stop=True),
                   waits=[(acM.sem, evac_done), (dvM.sem, max(prev_t2, prev_red))],
                   inc=peM.bump())
            swap_done = peM.n
            if p1_lvl < 3:
                continue
            P.emit("vector",
                   lambda v, src=src_ps, csb=csb: v.tensor_mul(t1b[:, :], src, csbuf[csb][:, 0, :]),
                   waits=[(sCS[csb].sem, cs_ready), (peM.sem, dloop_done), (dvM.sem, dvM.n),
                          (acM.sem, evac_done)])
            P.emit("vector",
                   lambda v, csb=csb: v.tensor_mul(t2b[:, :], sw_ps[:, :], csbuf[csb][:, 1, :]),
                   waits=[(peM.sem, swap_done), (dvM.sem, dvM.n)], inc=dvM.bump())
            prev_t2 = dvM.n
            dst = q4[:, hx, tgs] if is_q else KT[:, tgs]
            P.emit("vector",
                   lambda v, dst=dst: v.tensor_add(dst, t1b[:, :], t2b[:, :]),
                   waits=[(dvM.sem, dvM.n)], inc=dvM.bump())
            rope_done = dvM.n
            if p1_lvl < 4:
                continue
            P.emit("scalar",
                   lambda s, dst=dst: s.activation(sqstage[:, :], dst, AF.Square),
                   waits=[(dvM.sem, rope_done), (peM.sem, prev_norm_pe), (acM.sem, acM.n)],
                   inc=acM.bump())
            sq_done = acM.n
            P.emit("tensor",
                   lambda t: t.matmul(sw_ps[0:1, :], onescb_sb[:, :], sqstage[:, :],
                                      start=True, stop=True),
                   waits=[(acM.sem, sq_done), (dvM.sem, prev_t2)],
                   inc=peM.bump())
            prev_norm_pe = peM.n
            if is_q:
                dst_red = cmax[0:1, (tg * 4 + hx):(tg * 4 + hx + 1)]
            else:
                dst_red = kmaxs[0:1, tg:tg + 1]
            P.emit("vector",
                   lambda v, dst_red=dst_red: v.reduce_max(dst_red, sw_ps[0:1, :], axis=AX.X),
                   waits=[(peM.sem, prev_norm_pe), (dvM.sem, dvM.n)], inc=dvM.bump())
            prev_red = dvM.n
        if p1_lvl >= 5:
            P.emit("scalar",
                   lambda s: s.copy(vstage[:, :], kv_ps[:, 512:1024]),
                   waits=[(peM.sem, dloop_done), (acM.sem, acM.n)], inc=acM.bump())
            vevac_done = acM.n
            prev_vt_copy = 0
            for i in range(4):
                P.emit("tensor",
                       lambda t, i=i: t.transpose(vt_ps[:, :], vstage[:, i * 128:i * 128 + 128],
                                                  ident_sb[:, :]),
                       waits=[(acM.sem, vevac_done), (dvM.sem, prev_vt_copy)],
                       inc=peM.bump())
                tr_done = peM.n
                P.emit("vector",
                       lambda v, i=i, tg=tg: v.tensor_copy(
                           Vtok[:, (tg * 4 + i) * 128:(tg * 4 + i) * 128 + 128], vt_ps[:, :]),
                       waits=[(peM.sem, tr_done), (dvM.sem, dvM.n)], inc=dvM.bump())
                prev_vt_copy = dvM.n
        cs_consumed_at.append(dvM.n)
        if tg + 2 < NTG:
            emit_cs_load(tg + 2)
        prev_ep_act = acM.n
        prev_ep_dve = dvM.n

    phase1_dve = dvM.n
    phase1_act = acM.n

    if p1_tgs is not None:
        sT = Ctr(stk, nc, "sT", 16)
        P.emit("gpsimd",
               lambda g: g.dma_start(out=outT[0:128, :], in_=KT[:, 0:To]),
               waits=[(dvM.sem, phase1_dve), (acM.sem, phase1_act)], inc=sT.bump())
        P.emit("gpsimd",
               lambda g: g.dma_start(out=outT[128:256, :], in_=q4[:, 0, 0:To]),
               inc=sT.bump())
        P.emit("gpsimd",
               lambda g: g.dma_start(out=outT[256:384, :], in_=Vtok[:, 0:To]),
               inc=sT.bump())
        P.emit("gpsimd", None, waits=[(sT.sem, sT.n)])
        def finish1():
            def runner(eng_ops):
                def _run(eng):
                    for fn, waits, inc in eng_ops:
                        for (sem, val) in waits:
                            eng.wait_ge(sem, val)
                        if fn is None:
                            continue
                        inst = fn(eng)
                        if inc is not None:
                            inst.then_inc(inc[0], inc[1])
                return _run
            with nc.Block() as block:
                block.tensor(runner(P.ops["tensor"]))
                block.vector(runner(P.ops["vector"]))
                block.scalar(runner(P.ops["scalar"]))
                block.gpsimd(runner(P.ops["gpsimd"]))
                block.sync(runner(P.ops["sync"]))
            return nc
        nc_out = finish1()
        ph1.close(); pp1.close(); ph12.close(); stk.close()
        return nc_out

    P.emit("vector",
           lambda v: v.reduce_max(kmaxs[0:1, NTG:NTG + 1], kmaxs[0:1, 0:NTG], axis=AX.X),
           waits=[(dvM.sem, phase1_dve)], inc=dvM.bump())
    P.emit("vector",
           lambda v: v.tensor_scalar(rowbuf[0:1, 0:4 * NQC], cmax[0:1, :],
                                     kmaxs[0:1, NTG:NTG + 1], 1.21, ALU.mult, ALU.mult),
           waits=[(dvM.sem, dvM.n)], inc=dvM.bump())
    u_done = dvM.n
    P.emit("scalar",
           lambda s: s.activation(rowbuf[0:1, 1024:1024 + 4 * NQC], rowbuf[0:1, 0:4 * NQC],
                                  AF.Ln, bias=1e-5),
           waits=[(dvM.sem, u_done), (sW.sem, sW.n), (dvM.sem, consts_ready),
                  (acM.sem, acM.n)],
           inc=acM.bump())
    P.emit("scalar",
           lambda s: s.activation(rowbuf[0:1, 2048:2048 + 4 * NQC],
                                  rowbuf[0:1, 1024:1024 + 4 * NQC], AF.Exp, scale=0.5),
           waits=[(acM.sem, acM.n)], inc=acM.bump())
    c_done = acM.n
    P.emit("vector",
           lambda v: v.tensor_scalar_mul(cneg[0:1, :], rowbuf[0:1, 2048:2048 + 4 * NQC], -1.0),
           waits=[(acM.sem, c_done), (dvM.sem, dvM.n)], inc=dvM.bump())
    cneg_done = dvM.n

    dbg_tail = []

    def finish():
        def runner(eng_ops):
            def _run(eng):
                for fn, waits, inc in eng_ops:
                    for (sem, val) in waits:
                        eng.wait_ge(sem, val)
                    if fn is None:
                        continue
                    inst = fn(eng)
                    if inc is not None:
                        inst.then_inc(inc[0], inc[1])
            return _run

        with allow_ici_all_to_all():
            with nc.Block() as block:
                block.tensor(runner(P.ops["tensor"]))
                block.vector(runner(P.ops["vector"]))
                block.scalar(runner(P.ops["scalar"]))
                block.gpsimd(runner(P.ops["gpsimd"]))
                block.sync(runner(P.ops["sync"]))
        return nc

    if upto == "p1":
        sT = Ctr(stk, nc, "sT", 16)
        P.emit("gpsimd",
               lambda g: g.dma_start(out=outT[0:128, :], in_=KT[:, 0:To]),
               waits=[(dvM.sem, cneg_done)], inc=sT.bump())
        P.emit("gpsimd",
               lambda g: g.dma_start(out=outT[128:256, :], in_=q4[:, 0, 0:To]),
               inc=sT.bump())
        P.emit("gpsimd",
               lambda g: g.dma_start(out=outT[256:384, :], in_=Vtok[:, 0:To]),
               inc=sT.bump())
        P.emit("gpsimd", None, waits=[(sT.sem, sT.n)])
        nc_out = finish()
        ph1.close(); pp1.close(); ph12.close(); stk.close()
        return nc_out

    ph1.close()
    pp1.close()

    ph2 = contextlib.ExitStack()
    pt = [[ph2.enter_context(nc.sbuf_tensor("pt%d_%d" % (hp, par), [128, 2, 1024], BF16))
           for par in range(2)] for hp in range(2)]
    bias_sb = ph2.enter_context(nc.sbuf_tensor("biasc_sb", [128, 4], F32))
    o_sb = [ph2.enter_context(nc.sbuf_tensor("o_sb%d" % i, [128, 512], F32)) for i in range(2)]
    linv_row = [ph2.enter_context(nc.sbuf_tensor("linv_row%d" % i, [1, 512], F32))
                for i in range(2)]
    linv_bc = [ph2.enter_context(nc.sbuf_tensor("linv_bc%d" % i, [128, 512], F32))
               for i in range(2)]
    atst = [ph2.enter_context(nc.sbuf_tensor("atst%d" % i, [128, 512], BF16)) for i in range(8)]

    pp2 = contextlib.ExitStack()
    s_ps = [pp2.enter_context(nc.psum_tensor("s_ps%d" % i, [128, 1024], F32)) for i in range(2)]
    o_ps = [pp2.enter_context(nc.psum_tensor("o_ps%d" % i, [128, 512], F32)) for i in range(2)]
    l_ps = [pp2.enter_context(nc.psum_tensor("l_ps%d" % i, [128, 512], F32)) for i in range(2)]

    pair_idx = 0
    pending_norm = []
    l_free_at = [0, 0]
    linv_pe_guard = [0, 0]
    exp_half_at = {}
    group_l_done = {}
    ghp_count = [0, 0]
    atst_stored_at = [0] * 8
    o_free_at = [0, 0]
    o_sb_free = [0, 0]
    linv_bc_free = [0, 0]
    bank_guard = [0, 0]

    for qc in range(NQC):
        pro_waits = [(acM.sem, acM.n), (dvM.sem, cneg_done)]
        if qc == 0:
            pro_waits += [(dvM.sem, max(phase1_dve, cneg_done)), (acM.sem, phase1_act)]
        P.emit("tensor",
               lambda t, qc=qc: t.matmul(s_ps[0][:, 0:4], onesrb_sb[:, :],
                                         cneg[0:1, 4 * qc:4 * qc + 4], start=True, stop=True),
               waits=pro_waits, inc=peM.bump())
        bias_mm = peM.n
        P.emit("scalar",
               lambda s: s.copy(bias_sb[:, :], s_ps[0][:, 0:4]),
               waits=[(peM.sem, bias_mm), (acM.sem, acM.n)], inc=acM.bump())
        bias_ready = acM.n
        bank_guard[0] = bias_ready

        for pas in range(2):
            for kc in range(qc + 1):
                diag = (kc == qc)
                for hp in range(2):
                    h = pas * 2 + hp
                    gpar = ghp_count[hp] % 2
                    pair_avs = []
                    for pj in range(2):
                        pg = pair_idx
                        bank = pg % 2
                        sdone = [0, 0]
                        for sj in range(2):
                            ks = pj * 2 + sj
                            w = []
                            if pg >= 2:
                                w.append((acM.sem, exp_half_at[(pg - 2, sj)]))
                            if sj == 0:
                                w.append((acM.sem, bank_guard[bank]))
                            P.emit("tensor",
                                   lambda t, kc=kc, ks=ks, h=h, qc=qc, bank=bank, sj=sj:
                                   t.matmul(s_ps[bank][:, sj * 512:sj * 512 + 512],
                                            KT[:, kc * 512 + ks * 128:kc * 512 + ks * 128 + 128],
                                            q4[:, h, qc * 512:qc * 512 + 512],
                                            start=True, stop=True),
                                   waits=w, inc=peM.bump())
                            sdone[sj] = peM.n
                        av_waits = [None, None]
                        for sj in range(2):
                            ks = pj * 2 + sj
                            ew = [(peM.sem, sdone[sj]), (acM.sem, acM.n)]
                            if (hp, gpar) in group_l_done:
                                ew.append((peM.sem, group_l_done[(hp, gpar)]))
                            P.emit("scalar",
                                   lambda s, bank=bank, hp=hp, gpar=gpar, pj=pj, sj=sj,
                                   hcol=pas * 2 + hp:
                                   s.activation(pt[hp][gpar][:, pj, sj * 512:sj * 512 + 512],
                                                s_ps[bank][:, sj * 512:sj * 512 + 512],
                                                AF.Exp, bias=bias_sb[:, hcol:hcol + 1]),
                                   waits=ew, inc=acM.bump())
                            exp_half_at[(pg, sj)] = acM.n
                            av_waits[sj] = (acM.sem, acM.n)
                            if diag:
                                P.emit("vector",
                                       lambda v, hp=hp, gpar=gpar, pj=pj, ks=ks, sj=sj:
                                       v.tensor_mul(pt[hp][gpar][:, pj, sj * 512:sj * 512 + 512],
                                                    pt[hp][gpar][:, pj, sj * 512:sj * 512 + 512],
                                                    masks_sb[:, ks, :]),
                                       waits=[(acM.sem, acM.n), (dvM.sem, dvM.n)],
                                       inc=dvM.bump())
                                av_waits[sj] = (dvM.sem, dvM.n)
                        pair_avs.append((pj, av_waits))
                        pair_idx += 1
                    for (ppj, av_waits) in pair_avs:
                        st0 = (kc == 0 and ppj == 0)
                        sp0 = (diag and ppj == 1)
                        for sj in range(2):
                            ks = ppj * 2 + sj
                            kslice = slice((kc * 4 + ks) * 128, (kc * 4 + ks) * 128 + 128)
                            avkw = [av_waits[sj]]
                            if st0 and sj == 0:
                                avkw.append((dvM.sem, o_free_at[hp]))
                            P.emit("tensor",
                                   lambda t, gpar=gpar, ppj=ppj, sj=sj, kslice=kslice, hp=hp,
                                   st=(st0 and sj == 0), sp=(sp0 and sj == 1):
                                   t.matmul(o_ps[hp][:, :], Vtok[:, kslice],
                                            pt[hp][gpar][:, ppj, sj * 512:sj * 512 + 512],
                                            start=st, stop=sp),
                                   waits=avkw)
                        for sj in range(2):
                            lw = []
                            if st0 and sj == 0:
                                lw.append((acM.sem, l_free_at[hp]))
                            P.emit("tensor",
                                   lambda t, gpar=gpar, ppj=ppj, sj=sj, hp=hp,
                                   st=(st0 and sj == 0), sp=(sp0 and sj == 1):
                                   t.matmul(l_ps[hp][0:1, :], onescb_sb[:, :],
                                            pt[hp][gpar][:, ppj, sj * 512:sj * 512 + 512],
                                            start=st, stop=sp),
                                   waits=lw,
                                   inc=peM.bump() if (ppj == 1 and sj == 1) else None)
                    group_l_done[(hp, ghp_count[hp] % 2)] = peM.n
                    ghp_count[hp] += 1
            for fn in pending_norm:
                fn()
            pending_norm = []
            for hp in range(2):
                h = pas * 2 + hp
                gl = group_l_done[(hp, (ghp_count[hp] - 1) % 2)]
                P.emit("scalar",
                       lambda s, hp=hp: s.activation(linv_row[hp][:, :], l_ps[hp][0:1, :],
                                                     AF.Ln),
                       waits=[(peM.sem, max(gl, linv_pe_guard[hp])), (acM.sem, acM.n)],
                       inc=acM.bump())
                ln_done = acM.n
                P.emit("scalar",
                       lambda s, hp=hp: s.activation(linv_row[hp][:, :], linv_row[hp][:, :],
                                                     AF.Exp, scale=-1.0),
                       waits=[(acM.sem, acM.n)], inc=acM.bump())
                linv_done = acM.n
                P.emit("vector",
                       lambda v, hp=hp: v.tensor_copy(o_sb[hp][:, :], o_ps[hp][:, :]),
                       waits=[(peM.sem, gl), (dvM.sem, max(dvM.n, o_sb_free[hp]))],
                       inc=dvM.bump())
                o_free_at[hp] = dvM.n
                l_free_at[hp] = linv_done
                def _norm(hp=hp, h=h, qc=qc, linv_done=linv_done):
                    P.emit("tensor",
                           lambda t, hp=hp: t.matmul(s_ps[hp][:, 0:512], onesrf_sb[:, :],
                                                     linv_row[hp][:, :], start=True, stop=True),
                           waits=[(acM.sem, max(linv_done, acM.n))], inc=peM.bump())
                    bc_done = peM.n
                    linv_pe_guard[hp] = bc_done
                    P.emit("scalar",
                           lambda s, hp=hp: s.copy(linv_bc[hp][:, :], s_ps[hp][:, 0:512]),
                           waits=[(peM.sem, bc_done), (acM.sem, acM.n),
                                  (dvM.sem, linv_bc_free[hp])],
                           inc=acM.bump())
                    bank_guard[hp] = acM.n
                    bcev_done = acM.n
                    sti = (qc * 4 + h) % 8
                    P.emit("vector",
                           lambda v, hp=hp, sti=sti: v.tensor_mul(atst[sti][:, :],
                                                                  o_sb[hp][:, :],
                                                                  linv_bc[hp][:, :]),
                           waits=[(acM.sem, bcev_done), (sA2[sti].sem, atst_stored_at[sti]),
                                  (dvM.sem, dvM.n)],
                           inc=dvM.bump())
                    o_sb_free[hp] = dvM.n
                    linv_bc_free[hp] = dvM.n
                    mul_done = dvM.n
                    d_dest = qc // NTH
                    tof = (qc % NTH) * 512
                    P.emit("sync",
                           lambda sy, h=h, sti=sti, d_dest=d_dest, tof=tof: sy.dma_start(
                               out=a2a_in.ap()[d_dest, h * 128:h * 128 + 128, tof:tof + 512],
                               in_=atst[sti][:, :]),
                           waits=[(dvM.sem, mul_done)], inc=sA2[sti].bump())
                    atst_stored_at[sti] = sA2[sti].n
                pending_norm.append(_norm)

    for fn in pending_norm:
        fn()
    pending_norm = []

    P.emit("gpsimd",
           lambda g: g.collective_compute(
               "AllToAll", ALU.bypass, replica_groups=[list(range(NCORES))],
               ins=[a2a_in.ap().opt()], outs=[a2a_out.ap().opt()]),
           waits=[(s.sem, s.n) for s in sA2], inc=ccs.bump())

    if upto == "p2":
        sT = Ctr(stk, nc, "sT", 16)
        P.emit("gpsimd",
               lambda g: g.dma_start(out=outT[:, :],
                                     in_=a2a_out.ap().rearrange("r f t -> (r f) t")),
               waits=[(ccs.sem, 1)], inc=sT.bump())
        P.emit("gpsimd", None, waits=[(sT.sem, sT.n)])
        nc_out = finish()
        ph2.close(); pp2.close(); ph12.close(); stk.close()
        return nc_out

    ph2.close()
    pp2.close()
    ph12.close()

    ph34 = contextlib.ExitStack()
    st_sb = [ph34.enter_context(nc.sbuf_tensor("st_sb%d" % i, [128, To], F32)) for i in range(4)]
    r2st = [ph34.enter_context(nc.sbuf_tensor("r2st%d" % i, [128, To], F32)) for i in range(2)]
    sq_sb = ph34.enter_context(nc.sbuf_tensor("sq_sb", [128, To], F32))
    acc_s = ph34.enter_context(nc.sbuf_tensor("acc_s", [128, To], F32))
    acc_q = ph34.enter_context(nc.sbuf_tensor("acc_q", [128, To], F32))
    m_bc = ph34.enter_context(nc.sbuf_tensor("m_bc", [128, To], F32))
    rstd_bc = ph34.enter_context(nc.sbuf_tensor("rstd_bc", [128, To], F32))
    sc_bc = ph34.enter_context(nc.sbuf_tensor("sc_bc", [128, To], F32))

    ph25 = contextlib.ExitStack()
    at_own = ph25.enter_context(nc.sbuf_tensor("at_own", [128, n_ft, To], BF16))
    wo_sb = [ph25.enter_context(nc.sbuf_tensor("wo_sb%d" % i, [128, n_ft, 256], BF16))
             for i in range(2)]

    pp3 = contextlib.ExitStack()
    psA = [pp3.enter_context(nc.psum_tensor("psA%d" % i, [128, To], F32)) for i in range(4)]

    P.emit("sync",
           lambda sy: sy.dma_start(out=at_own[:, :, :],
                                   in_=a2a_out.ap().rearrange("r (i p) t -> p (r i) t", p=128)),
           waits=[(ccs.sem, 1)], inc=sAo.bump())
    at_own_ready = sAo.n

    n_grp = n_ft // 2
    wo_grp_free_at = [0, 0]
    wo_stripe_free_at = [0, 0]
    st_free_at = [0, 0]
    st_free_acm = [0, 0]
    xo_used_at = [0, 0]
    acc_chain = 0

    def emit_wo_stripe(g):
        b = g % 2
        w = [(ccs.sem, 1)] if g < 2 else []
        if g >= 2:
            w.append((peM.sem, wo_stripe_free_at[b]))
        P.emit("gpsimd",
               lambda gp, g=g, b=b: gp.dma_start(
                   out=wo_sb[b][:, :, :],
                   in_=wo[:, g * 256:(g + 1) * 256].rearrange("(i p) f -> p i f", p=128)),
               waits=w, inc=sWo[b].bump())

    emit_wo_stripe(0)
    emit_wo_stripe(1)
    for ft in range(min(2, n_ft)):
        P.emit("sync",
               lambda sy, ft=ft: sy.dma_start(
                   out=xo_sb[ft % 2][:, :], in_=xoT[ft * 128:(ft + 1) * 128, :]),
               inc=sXo[ft % 2].bump())

    for g in range(n_grp):
        b = g % 2
        mmw = [(sWo[b].sem, 16 * (g // 2 + 1)), (sAo.sem, at_own_ready)]
        if g >= 2:
            mmw.append((dvM.sem, wo_grp_free_at[b]))
        for d in range(n_ft):
            for jj in range(2):
                for th in range(NTH):
                    P.emit("tensor",
                           lambda t, b=b, d=d, jj=jj, th=th, st=(d == 0), sp=(d == n_ft - 1):
                           t.matmul(psA[2 * b + jj][:, th * 512:th * 512 + 512],
                                    wo_sb[b][:, d, jj * 128:jj * 128 + 128],
                                    at_own[:, d, th * 512:th * 512 + 512],
                                    start=st, stop=sp),
                           waits=mmw if (d == 0 and jj == 0 and th == 0) else (),
                           inc=peM.bump() if (d == n_ft - 1 and jj == 1 and th == NTH - 1) else None)
        grp_done = peM.n
        wo_stripe_free_at[b] = grp_done
        if g + 2 < n_grp:
            emit_wo_stripe(g + 2)
        for jj in range(2):
            ft = 2 * g + jj
            sb = ft % 2
            P.emit("vector",
                   lambda v, b=b, jj=jj, sb=sb: v.tensor_add(
                       st_sb[sb][:, :], psA[2 * b + jj][:, 0:To], xo_sb[sb][:, :]),
                   waits=[(peM.sem, grp_done), (sXo[sb].sem, 16 * (ft // 2 + 1)),
                          (sSw[sb].sem, st_free_at[sb]), (acM.sem, st_free_acm[sb]),
                          (dvM.sem, dvM.n)],
                   inc=dvM.bump())
            add_done = dvM.n
            xo_used_at[sb] = add_done
            if jj == 1:
                wo_grp_free_at[b] = add_done
            if ft + 2 < n_ft:
                P.emit("sync",
                       lambda sy, ft=ft, sb=sb: sy.dma_start(
                           out=xo_sb[sb][:, :], in_=xoT[(ft + 2) * 128:(ft + 3) * 128, :]),
                       waits=[(dvM.sem, add_done)], inc=sXo[sb].bump())
            P.emit("sync",
                   lambda sy, ft=ft, sb=sb: sy.dma_start(
                       out=s_dram.ap()[ft * 128:(ft + 1) * 128, :], in_=st_sb[sb][:, :]),
                   waits=[(dvM.sem, add_done)], inc=sSw[sb].bump())
            st_free_at[sb] = sSw[sb].n
            if ft == 0:
                P.emit("vector", lambda v, sb=sb: v.tensor_copy(acc_s[:, :], st_sb[sb][:, :]),
                       waits=[(dvM.sem, add_done)], inc=dvM.bump())
            else:
                P.emit("vector", lambda v, sb=sb: v.tensor_add(acc_s[:, :], acc_s[:, :],
                                                               st_sb[sb][:, :]),
                       waits=[(dvM.sem, max(add_done, acc_chain))], inc=dvM.bump())
            P.emit("scalar", lambda s, sb=sb: s.activation(sq_sb[:, :], st_sb[sb][:, :],
                                                           AF.Square),
                   waits=[(dvM.sem, add_done), (dvM.sem, acc_chain), (acM.sem, acM.n)],
                   inc=acM.bump())
            sqd = acM.n
            st_free_acm[sb] = sqd
            if ft == 0:
                P.emit("vector", lambda v: v.tensor_copy(acc_q[:, :], sq_sb[:, :]),
                       waits=[(acM.sem, sqd), (dvM.sem, dvM.n)], inc=dvM.bump())
            else:
                P.emit("vector", lambda v: v.tensor_add(acc_q[:, :], acc_q[:, :], sq_sb[:, :]),
                       waits=[(acM.sem, sqd), (dvM.sem, dvM.n)], inc=dvM.bump())
            acc_chain = dvM.n

    def emit_ln_finalize(m_dst, r_dst, roff, acc_ready_dvM, sc_dst=None):
        psum, psumq = psA[3], psA[2]
        for half in range(NTH):
            P.emit("tensor",
                   lambda t, half=half: t.matmul(
                       psum[0:1, half * 512:half * 512 + 512], onescf_sb[:, :],
                       acc_s[:, half * 512:half * 512 + 512], start=True, stop=True),
                   waits=[(dvM.sem, acc_ready_dvM), (acM.sem, acM.n)],
                   inc=peM.bump())
            P.emit("tensor",
                   lambda t, half=half: t.matmul(
                       psumq[0:1, half * 512:half * 512 + 512], onescf_sb[:, :],
                       acc_q[:, half * 512:half * 512 + 512], start=True, stop=True),
                   inc=peM.bump())
        mm_done = peM.n
        r0, r1, r2o = roff, roff + 1024, roff + 2048
        P.emit("scalar",
               lambda s: s.copy(rowbuf[0:1, r0:r0 + To], psum[0:1, 0:To]),
               waits=[(peM.sem, mm_done), (acM.sem, acM.n)], inc=acM.bump())
        P.emit("scalar",
               lambda s: s.copy(rowbuf[0:1, r1:r1 + To], psumq[0:1, 0:To]),
               waits=[(acM.sem, acM.n)], inc=acM.bump())
        evr = acM.n
        P.emit("vector",
               lambda v: v.tensor_scalar_mul(rowbuf[0:1, r0:r0 + To],
                                             rowbuf[0:1, r0:r0 + To], 1.0 / DIM),
               waits=[(acM.sem, evr), (dvM.sem, dvM.n)], inc=dvM.bump())
        m_row_done = dvM.n
        P.emit("vector",
               lambda v: v.tensor_scalar_mul(rowbuf[0:1, r1:r1 + To],
                                             rowbuf[0:1, r1:r1 + To], 1.0 / DIM),
               waits=[(dvM.sem, dvM.n)], inc=dvM.bump())
        P.emit("vector",
               lambda v: v.tensor_mul(rowbuf[0:1, r2o:r2o + To],
                                      rowbuf[0:1, r0:r0 + To], rowbuf[0:1, r0:r0 + To]),
               waits=[(dvM.sem, dvM.n)], inc=dvM.bump())
        P.emit("vector",
               lambda v: v.tensor_sub(rowbuf[0:1, r1:r1 + To], rowbuf[0:1, r1:r1 + To],
                                      rowbuf[0:1, r2o:r2o + To]),
               waits=[(dvM.sem, dvM.n)], inc=dvM.bump())
        var_done = dvM.n
        P.emit("scalar",
               lambda s: s.activation(rowbuf[0:1, r2o:r2o + To], rowbuf[0:1, r1:r1 + To],
                                      AF.Ln, bias=1e-5),
               waits=[(dvM.sem, var_done), (acM.sem, acM.n)], inc=acM.bump())
        P.emit("scalar",
               lambda s: s.activation(rowbuf[0:1, r1:r1 + To], rowbuf[0:1, r2o:r2o + To],
                                      AF.Exp, scale=-0.5),
               waits=[(acM.sem, acM.n)], inc=acM.bump())
        rstd_done = acM.n
        sc_done = dvM.n
        if sc_dst is not None:
            P.emit("vector",
                   lambda v: v.tensor_scalar(rowbuf[0:1, r2o:r2o + To],
                                             rowbuf[0:1, r1:r1 + To],
                                             wscale_sb[0:1, 0:1], None, ALU.mult),
                   waits=[(acM.sem, rstd_done), (dvM.sem, dvM.n)], inc=dvM.bump())
            sc_done = dvM.n
        bc_list = [(m_dst, r0), (r_dst, r1)]
        if sc_dst is not None:
            bc_list.append((sc_dst, r2o))
        prev_ev = 0
        for (dst, off) in bc_list:
            for half in range(NTH):
                P.emit("tensor",
                       lambda t, off=off, half=half: t.matmul(
                           psum[:, half * 512:half * 512 + 512], onesrf_sb[:, :],
                           rowbuf[0:1, off + half * 512:off + half * 512 + 512],
                           start=True, stop=True),
                       waits=[(acM.sem, max(rstd_done, prev_ev)), (dvM.sem, sc_done)],
                       inc=peM.bump())
                bmm = peM.n
                P.emit("scalar",
                       lambda s, dst=dst, half=half: s.copy(
                           dst[:, half * 512:half * 512 + 512],
                           psum[:, half * 512:half * 512 + 512]),
                       waits=[(peM.sem, bmm), (acM.sem, acM.n)], inc=acM.bump())
                prev_ev = acM.n
        return acM.n

    bc1_done = emit_ln_finalize(m_bc, rstd_bc, 0, acc_chain, sc_dst=sc_bc)

    if upto == "p25":
        sT = Ctr(stk, nc, "sT", 16)
        P.emit("sync",
               lambda sy: sy.dma_start(out=outT[:, :], in_=s_dram.ap()),
               waits=[(sSw[0].sem, sSw[0].n), (sSw[1].sem, sSw[1].n),
                      (acM.sem, bc1_done)], inc=sT.bump())
        P.emit("sync", None, waits=[(sT.sem, sT.n)])
        nc_out = finish()
        ph25.close(); ph34.close(); pp3.close(); stk.close()
        return nc_out

    ph25.close()
    ph3 = contextlib.ExitStack()
    d1_sb = ph3.enter_context(nc.sbuf_tensor("d1_sb", [128, n_ft, To], BF16))
    w2r = [ph3.enter_context(nc.sbuf_tensor("w2r%d" % i, [128, n_ft, 128], BF16))
           for i in range(3)]
    wf_sb = [ph3.enter_context(nc.sbuf_tensor("wf_sb%d" % i, [128, n_ft, 128], BF16))
             for i in range(3)]

    d1_done_at = []
    for ft in range(n_ft):
        sb = ft % 4
        P.emit("sync",
               lambda sy, ft=ft, sb=sb: sy.dma_start(
                   out=st_sb[sb][:, :], in_=s_dram.ap()[ft * 128:(ft + 1) * 128, :]),
               waits=[(sSw[0].sem, sSw[0].n), (sSw[1].sem, sSw[1].n),
                      (acM.sem, bc1_done),
                      (dvM.sem, d1_done_at[ft - 4] if ft >= 4 else 0)],
               inc=sSr[sb].bump())
        P.emit("vector",
               lambda v, ft=ft, sb=sb: v.tensor_sub(d1_sb[:, ft, :], st_sb[sb][:, :],
                                                    m_bc[:, :]),
               waits=[(sSr[sb].sem, 16 * (ft // 4 + 1)), (acM.sem, bc1_done),
                      (dvM.sem, dvM.n)],
               inc=dvM.bump())
        d1_done_at.append(dvM.n)
    d1_all = dvM.n

    ff_grp_free_at = [0, 0, 0]
    ff_stripe_free_at = [0, 0, 0]
    wf_conv_at = [0, 0, 0]
    st3_free_dv = [0, 0, 0, 0]
    r2st_free_dv = [0, 0]
    r2st_free_ac = [0, 0]
    r2st_spill_at = [0, 0]
    acc2_chain = 0

    for g in range(n_ft):
        b = g % 3
        w = [(sSw[0].sem, sSw[0].n), (sSw[1].sem, sSw[1].n)]
        if g >= 3:
            w.append((dvM.sem, wf_conv_at[b]))
        P.emit("gpsimd",
               lambda gp, g=g, b=b: gp.dma_start(
                   out=w2r[b][:, :, :],
                   in_=w2T[:, g * 128:(g + 1) * 128].rearrange("(i p) f -> p i f", p=128)),
               waits=w, inc=sFf[b].bump())
        P.emit("vector",
               lambda v, b=b: v.tensor_scalar_sub(wf_sb[b][:, :, :], w2r[b][:, :, :], 1.5),
               waits=[(sFf[b].sem, 16 * (g // 3 + 1)), (peM.sem, ff_stripe_free_at[b]),
                      (dvM.sem, dvM.n)],
               inc=dvM.bump())
        wf_conv_at[b] = dvM.n
        conv_done = dvM.n
        mmw = [(dvM.sem, max(conv_done, d1_all))]
        if g >= 3:
            mmw.append((dvM.sem, ff_grp_free_at[b]))
        for d in range(n_ft):
            for th in range(NTH):
                P.emit("tensor",
                       lambda t, b=b, d=d, th=th, st=(d == 0), sp=(d == n_ft - 1):
                       t.matmul(psA[b][:, th * 512:th * 512 + 512],
                                wf_sb[b][:, d, :], d1_sb[:, d, th * 512:th * 512 + 512],
                                start=st, stop=sp),
                       waits=mmw if (d == 0 and th == 0) else (),
                       inc=peM.bump() if (d == n_ft - 1 and th == NTH - 1) else None)
        grp_done = peM.n
        ff_stripe_free_at[b] = grp_done
        sb = g % 4
        sb2 = g % 2
        P.emit("sync",
               lambda sy, g=g, sb=sb: sy.dma_start(
                   out=st_sb[sb][:, :], in_=s_dram.ap()[g * 128:(g + 1) * 128, :]),
               waits=[(dvM.sem, max(d1_all, st3_free_dv[sb]))],
               inc=sS2[sb].bump())
        P.emit("vector",
               lambda v, sb=sb: v.tensor_sub(sq_sb[:, :], st_sb[sb][:, :], m_bc[:, :]),
               waits=[(sS2[sb].sem, 16 * (g // 4 + 1)), (dvM.sem, dvM.n)],
               inc=dvM.bump())
        st3_free_dv[sb] = dvM.n
        P.emit("vector",
               lambda v, sb=sb: v.tensor_mul(sq_sb[:, :], sq_sb[:, :], rstd_bc[:, :]),
               waits=[(dvM.sem, dvM.n)], inc=dvM.bump())
        hh_done = dvM.n
        P.emit("vector",
               lambda v, b=b, sb2=sb2: v.tensor_mul(r2st[sb2][:, :], psA[b][:, 0:To],
                                                    sc_bc[:, :]),
               waits=[(peM.sem, grp_done), (dvM.sem, max(hh_done, r2st_free_dv[sb2])),
                      (acM.sem, r2st_free_ac[sb2]), (sR2w[sb2].sem, r2st_spill_at[sb2])],
               inc=dvM.bump())
        ff_grp_free_at[b] = dvM.n
        P.emit("vector",
               lambda v, g=g, sb2=sb2: v.scalar_tensor_tensor(
                   r2st[sb2][:, :], r2st[sb2][:, :], bias2d_sb[:, g:g + 1], sq_sb[:, :],
                   ALU.add, ALU.add),
               waits=[(dvM.sem, dvM.n)], inc=dvM.bump())
        r2_done = dvM.n
        P.emit("scalar",
               lambda s, g=g, sb2=sb2: s.dma_start(
                   out=r2_dram.ap()[g * 128:(g + 1) * 128, :], in_=r2st[sb2][:, :]),
               waits=[(dvM.sem, r2_done)], inc=sR2w[sb2].bump())
        r2st_spill_at[sb2] = sR2w[sb2].n
        if g == 0:
            P.emit("vector", lambda v, sb2=sb2: v.tensor_copy(acc_s[:, :], r2st[sb2][:, :]),
                   waits=[(dvM.sem, r2_done)], inc=dvM.bump())
        else:
            P.emit("vector", lambda v, sb2=sb2: v.tensor_add(acc_s[:, :], acc_s[:, :],
                                                             r2st[sb2][:, :]),
                   waits=[(dvM.sem, max(r2_done, acc2_chain))], inc=dvM.bump())
        P.emit("scalar", lambda s, sb2=sb2: s.activation(sq_sb[:, :], r2st[sb2][:, :], AF.Square),
               waits=[(dvM.sem, dvM.n), (acM.sem, acM.n)], inc=acM.bump())
        sqd = acM.n
        r2st_free_ac[sb2] = sqd
        if g == 0:
            P.emit("vector", lambda v: v.tensor_copy(acc_q[:, :], sq_sb[:, :]),
                   waits=[(acM.sem, sqd), (dvM.sem, dvM.n)], inc=dvM.bump())
        else:
            P.emit("vector", lambda v: v.tensor_add(acc_q[:, :], acc_q[:, :], sq_sb[:, :]),
                   waits=[(acM.sem, sqd), (dvM.sem, dvM.n)], inc=dvM.bump())
        acc2_chain = dvM.n
        r2st_free_dv[sb2] = acc2_chain

    bc2_done = emit_ln_finalize(m_bc, rstd_bc, 0, acc2_chain)
    dv_after_bc2 = dvM.n

    out_done_at = [dv_after_bc2] * 4
    out_stage_free = [dv_after_bc2, dv_after_bc2]
    out_store_at = [0, 0]
    for ft in range(n_ft):
        sb = ft % 4
        sb2 = ft % 2
        P.emit("sync",
               lambda sy, ft=ft, sb=sb: sy.dma_start(
                   out=st_sb[sb][:, :], in_=r2_dram.ap()[ft * 128:(ft + 1) * 128, :]),
               waits=[(sR2w[0].sem, sR2w[0].n), (sR2w[1].sem, sR2w[1].n),
                      (dvM.sem, out_done_at[sb]), (acM.sem, bc2_done)],
               inc=sR2r[sb].bump())
        P.emit("vector",
               lambda v, sb=sb: v.tensor_sub(sq_sb[:, :], st_sb[sb][:, :], m_bc[:, :]),
               waits=[(sR2r[sb].sem, 16 * (ft // 4 + 1)), (acM.sem, bc2_done),
                      (dvM.sem, dvM.n)],
               inc=dvM.bump())
        out_done_at[sb] = dvM.n
        P.emit("vector",
               lambda v, sb2=sb2: v.tensor_mul(r2st[sb2][:, :], sq_sb[:, :], rstd_bc[:, :]),
               waits=[(dvM.sem, max(dvM.n, out_stage_free[sb2])),
                      (sOut[sb2].sem, out_store_at[sb2])],
               inc=dvM.bump())
        out_stage_free[sb2] = dvM.n
        P.emit("scalar",
               lambda s, ft=ft, sb2=sb2: s.dma_start(
                   out=outT[ft * 128:(ft + 1) * 128, :], in_=r2st[sb2][:, :]),
               waits=[(dvM.sem, dvM.n)], inc=sOut[sb2].bump())
        out_store_at[sb2] = sOut[sb2].n
    P.emit("sync", None, waits=[(sOut[0].sem, sOut[0].n), (sOut[1].sem, sOut[1].n)])

    nc_out = finish()
    ph3.close()
    ph34.close()
    pp3.close()
    stk.close()
    return nc_out


def prep_inputs(L0_attn_x, L0_attn_wq, L0_attn_wk, L0_attn_wv, L0_attn_wo,
                L0_ffn_weight_2bit, L0_ffn_weight_scale, L0_ffn_bias):
    x = np.asarray(L0_attn_x)
    S = x.shape[1]
    x2 = np.ascontiguousarray(x.reshape(S, DIM))
    xT = np.ascontiguousarray(x2.T)

    perm = np.concatenate([np.arange(0, HD, 2), np.arange(1, HD, 2)])
    wq = np.asarray(L0_attn_wq).reshape(DIM, NH, HD)[:, :, perm].reshape(DIM, NH * HD)
    wk = np.asarray(L0_attn_wk).reshape(DIM, NKV, HD)[:, :, perm].reshape(DIM, NKV * HD)
    wv = np.asarray(L0_attn_wv)
    wo = np.ascontiguousarray(np.asarray(L0_attn_wo))
    w2T = np.ascontiguousarray(np.asarray(L0_ffn_weight_2bit).T.astype(np.int32))
    bias2d = np.ascontiguousarray(np.asarray(L0_ffn_bias).reshape(DIM // 128, 128).T)
    wsc = np.ascontiguousarray(np.asarray(L0_ffn_weight_scale).reshape(1, 1).astype(np.float32))

    sc128 = np.float64(HD) ** -0.25
    freqs = 1.0 / THETA ** (np.arange(0, HD, 2, dtype=np.float64) / HD)
    ang = np.outer(freqs, np.arange(S, dtype=np.float64))
    C = (np.concatenate([np.cos(ang), np.cos(ang)], 0) * sc128).astype(np.float32)
    Ssg = (np.concatenate([-np.sin(ang), np.sin(ang)], 0) * sc128).astype(np.float32)
    kk = np.arange(128)[:, None]
    qq = np.arange(512)[None, :]
    masks = np.stack([(128 * s + kk <= qq) for s in range(4)]).astype(ml_dtypes.bfloat16)
    pswap = np.zeros((128, 128), np.float32)
    for i in range(128):
        pswap[i, (i + 64) % 128] = 1.0
    pswap = pswap.astype(ml_dtypes.bfloat16)
    identm = np.eye(128, dtype=ml_dtypes.bfloat16)

    To = S // NCORES
    maps = []
    for c in range(NCORES):
        maps.append({
            "xT": xT,
            "xoT": np.ascontiguousarray(xT[:, c * To:(c + 1) * To]),
            "wq": np.ascontiguousarray(wq[:, c * HPC * HD:(c + 1) * HPC * HD]),
            "wk": np.ascontiguousarray(wk[:, c * HD:(c + 1) * HD]),
            "wv": np.ascontiguousarray(np.asarray(wv)[:, c * HD:(c + 1) * HD]),
            "wo": wo,
            "w2T": w2T,
            "bias2d": bias2d,
            "wscale": wsc,
            "cosT": C,
            "sinT": Ssg,
            "masks": masks,
            "pswap": pswap,
            "ident": identm,
            "onescb": np.ones((128, 1), ml_dtypes.bfloat16),
            "onesrb": np.ones((1, 128), ml_dtypes.bfloat16),
            "onescf": np.ones((128, 1), np.float32),
            "onesrf": np.ones((1, 128), np.float32),
        })
    return maps


def kernel(L0_attn_x, L0_attn_wq, L0_attn_wk, L0_attn_wv, L0_attn_wo,
           L0_ffn_weight_2bit, L0_ffn_weight_scale, L0_ffn_bias):
    x = np.asarray(L0_attn_x)
    S = x.shape[1]
    maps = prep_inputs(L0_attn_x, L0_attn_wq, L0_attn_wk, L0_attn_wv, L0_attn_wo,
                       L0_ffn_weight_2bit, L0_ffn_weight_scale, L0_ffn_bias)
    nc = build_program(S)
    with allow_ici_all_to_all():
        res = bass_utils.run_bass_kernel_spmd(nc, maps, core_ids=list(range(NCORES)))
    outs = [np.asarray(r["outT"]).T for r in res.results]
    return np.ascontiguousarray(np.concatenate(outs, axis=0)).reshape(1, S, DIM).astype(np.float32)


# version 05
# speedup vs baseline: 1.0715x; 1.0298x over previous
import contextlib
import numpy as np
import ml_dtypes

import concourse.bass as bass
import concourse.mybir as mybir
import concourse.bass_utils as bass_utils
from concourse.aot_env import allow_ici_all_to_all

F32 = mybir.dt.float32
BF16 = mybir.dt.bfloat16
I32 = mybir.dt.int32
AF = mybir.ActivationFunctionType
ALU = mybir.AluOpType
AX = mybir.AxisListType

DIM = 4096
NH = 32
NKV = 8
HD = 128
NCORES = 8
HPC = 4
THETA = 10000.0

ENGS = ("tensor", "vector", "scalar", "gpsimd", "sync")


class Ctr:

    def __init__(self, stack, nc, name, step=1):
        self.sem = stack.enter_context(nc.semaphore(name))
        self.n = 0
        self.step = step

    def bump(self):
        self.n += self.step
        return (self.sem, self.step)


class Prog:
    def __init__(self):
        self.ops = {e: [] for e in ENGS}

    def emit(self, eng, fn=None, waits=(), inc=None):
        self.ops[eng].append((fn, tuple((s, v) for (s, v) in waits if v > 0), inc))


def build_program(S, upto='full', p1_tgs=None, p1_lvl=6):
    To = S // NCORES
    NTG = S // 512
    NQC = S // 512
    NTH = (To + 511) // 512
    n_ft = DIM // 128

    nc = bass.Bass("TRN2", target_bir_lowering=False)

    xT = nc.dram_tensor("xT", [DIM, S], F32, kind="ExternalInput").ap()
    xoT = nc.dram_tensor("xoT", [DIM, To], F32, kind="ExternalInput").ap()
    wq = nc.dram_tensor("wq", [DIM, HPC * HD], F32, kind="ExternalInput").ap()
    wk = nc.dram_tensor("wk", [DIM, HD], F32, kind="ExternalInput").ap()
    wv = nc.dram_tensor("wv", [DIM, HD], F32, kind="ExternalInput").ap()
    wo = nc.dram_tensor("wo", [DIM, DIM], F32, kind="ExternalInput").ap()
    w2T = nc.dram_tensor("w2T", [DIM, DIM], I32, kind="ExternalInput").ap()
    bias2d = nc.dram_tensor("bias2d", [128, n_ft], F32, kind="ExternalInput").ap()
    wscale = nc.dram_tensor("wscale", [1, 1], F32, kind="ExternalInput").ap()
    cosT = nc.dram_tensor("cosT", [128, S], F32, kind="ExternalInput").ap()
    sinT = nc.dram_tensor("sinT", [128, S], F32, kind="ExternalInput").ap()
    masks = nc.dram_tensor("masks", [4, 128, 512], BF16, kind="ExternalInput").ap()
    pswap = nc.dram_tensor("pswap", [128, 128], BF16, kind="ExternalInput").ap()
    ident = nc.dram_tensor("ident", [128, 128], BF16, kind="ExternalInput").ap()
    onescb = nc.dram_tensor("onescb", [128, 1], BF16, kind="ExternalInput").ap()
    onesrb = nc.dram_tensor("onesrb", [1, 128], BF16, kind="ExternalInput").ap()
    onescf = nc.dram_tensor("onescf", [128, 1], F32, kind="ExternalInput").ap()
    onesrf = nc.dram_tensor("onesrf", [1, 128], F32, kind="ExternalInput").ap()
    outT = nc.dram_tensor("outT", [DIM, To], F32, kind="ExternalOutput").ap()

    a2a_in = nc.dram_tensor("a2a_in", [NCORES, HPC * HD, To], BF16)
    a2a_out = nc.dram_tensor("a2a_out", [NCORES, HPC * HD, To], BF16)
    s_dram = nc.dram_tensor("s_dram", [DIM, To], F32)
    r2_dram = nc.dram_tensor("r2_dram", [DIM, To], F32)

    stk = contextlib.ExitStack()

    sW = Ctr(stk, nc, "sW", 16)
    sX = [Ctr(stk, nc, "sX%d" % i, 16) for i in range(8)]
    sCS = [Ctr(stk, nc, "sCS%d" % i, 16) for i in range(2)]
    sWo = [Ctr(stk, nc, "sWo%d" % i, 16) for i in range(2)]
    sFf = [Ctr(stk, nc, "sFf%d" % i, 16) for i in range(3)]
    sXo = [Ctr(stk, nc, "sXo%d" % i, 16) for i in range(2)]
    sSw = [Ctr(stk, nc, "sSw%d" % i, 16) for i in range(2)]
    sSr = [Ctr(stk, nc, "sSr%d" % i, 16) for i in range(4)]
    sS2 = [Ctr(stk, nc, "sS2%d" % i, 16) for i in range(4)]
    sR2w = [Ctr(stk, nc, "sR2w%d" % i, 16) for i in range(2)]
    sR2r = [Ctr(stk, nc, "sR2r%d" % i, 16) for i in range(4)]
    sA2 = [Ctr(stk, nc, "sA2_%d" % i, 16) for i in range(8)]
    sAo = Ctr(stk, nc, "sAo", 16)
    sOut = [Ctr(stk, nc, "sOut%d" % i, 16) for i in range(2)]
    ccs = Ctr(stk, nc, "ccs", 1)
    peM = Ctr(stk, nc, "peM", 1)
    acM = Ctr(stk, nc, "acM", 1)
    dvM = Ctr(stk, nc, "dvM", 1)

    P = Prog()

    masks_sb = stk.enter_context(nc.sbuf_tensor("masks_sb", [128, 4, 512], BF16))
    pswap_sb = stk.enter_context(nc.sbuf_tensor("pswap_sb", [128, 128], BF16))
    ident_sb = stk.enter_context(nc.sbuf_tensor("ident_sb", [128, 128], BF16))
    onescb_sb = stk.enter_context(nc.sbuf_tensor("onescb_sb", [128, 1], BF16))
    onesrb_sb = stk.enter_context(nc.sbuf_tensor("onesrb_sb", [1, 128], BF16))
    onescf_sb = stk.enter_context(nc.sbuf_tensor("onescf_sb", [128, 1], F32))
    onesrf_sb = stk.enter_context(nc.sbuf_tensor("onesrf_sb", [1, 128], F32))
    bias2d_sb = stk.enter_context(nc.sbuf_tensor("bias2d_sb", [128, n_ft], F32))
    wscale_sb = stk.enter_context(nc.sbuf_tensor("wscale_sb", [1, 1], F32))
    cmax = stk.enter_context(nc.sbuf_tensor("cmax", [1, 4 * NQC], F32))
    kmaxs = stk.enter_context(nc.sbuf_tensor("kmaxs", [1, NTG + 2], F32))
    cneg = stk.enter_context(nc.sbuf_tensor("cneg", [1, 4 * NQC], BF16))
    rowbuf = stk.enter_context(nc.sbuf_tensor("rowbuf", [1, 3 * 1024], F32))
    consts = stk.enter_context(nc.sbuf_tensor("consts", [1, 8], F32))

    xo_sb = [stk.enter_context(nc.sbuf_tensor("xo_sb%d" % i, [128, To], F32)) for i in range(2)]

    ph12 = contextlib.ExitStack()
    q4 = ph12.enter_context(nc.sbuf_tensor("q4", [128, HPC, S], BF16))
    KT = ph12.enter_context(nc.sbuf_tensor("KT", [128, S], BF16))
    Vtok = ph12.enter_context(nc.sbuf_tensor("Vtok", [128, S], BF16))

    nc.const_aps.register(F32, 1e-5, consts[0:1, 0:1])
    nc.const_aps.register(F32, 0.5, consts[0:1, 1:2])
    nc.const_aps.register(F32, -0.5, consts[0:1, 2:3])
    nc.const_aps.register(F32, -1.0, consts[0:1, 3:4])

    P.emit("vector", lambda v: v.memset(consts[0:1, 0:1], 1e-5), inc=dvM.bump())
    P.emit("vector", lambda v: v.memset(consts[0:1, 1:2], 0.5), inc=dvM.bump())
    P.emit("vector", lambda v: v.memset(consts[0:1, 2:3], -0.5), inc=dvM.bump())
    P.emit("vector", lambda v: v.memset(consts[0:1, 3:4], -1.0), inc=dvM.bump())
    consts_ready = dvM.n

    def g_load(dst, src):
        P.emit("gpsimd", lambda g, d=dst, s=src: g.dma_start(out=d, in_=s), inc=sW.bump())

    g_load(masks_sb[:, :, :], masks.rearrange("m p t -> p m t"))
    g_load(pswap_sb[:, :], pswap)
    g_load(ident_sb[:, :], ident)
    g_load(onescb_sb[:, :], onescb)
    g_load(onesrb_sb[:, :], onesrb)
    g_load(onescf_sb[:, :], onescf)
    g_load(onesrf_sb[:, :], onesrf)
    g_load(bias2d_sb[:, :], bias2d)
    g_load(wscale_sb[:, :], wscale)

    ph1 = contextlib.ExitStack()
    wq_sb = ph1.enter_context(nc.sbuf_tensor("wq_sb", [128, n_ft, HPC * HD], BF16))
    wk_sb = ph1.enter_context(nc.sbuf_tensor("wk_sb", [128, n_ft, HD], BF16))
    wv_sb = ph1.enter_context(nc.sbuf_tensor("wv_sb", [128, n_ft, HD], BF16))
    xbuf = [ph1.enter_context(nc.sbuf_tensor("xbuf%d" % i, [128, 512], BF16)) for i in range(8)]
    csbuf = [ph1.enter_context(nc.sbuf_tensor("csbuf%d" % i, [128, 2, 512], F32)) for i in range(2)]
    qkstage = ph1.enter_context(nc.sbuf_tensor("qkstage", [128, 512], BF16))
    vstage = ph1.enter_context(nc.sbuf_tensor("vstage", [128, 512], BF16))
    sqstage = ph1.enter_context(nc.sbuf_tensor("sqstage", [128, 512], BF16))
    t1b = ph1.enter_context(nc.sbuf_tensor("t1b", [128, 512], F32))
    t2b = ph1.enter_context(nc.sbuf_tensor("t2b", [128, 512], F32))

    pp1 = contextlib.ExitStack()
    q_ps = pp1.enter_context(nc.psum_tensor("q_ps", [128, 4 * 512], F32))
    kv_ps = pp1.enter_context(nc.psum_tensor("kv_ps", [128, 2 * 512], F32))
    sw_ps = pp1.enter_context(nc.psum_tensor("sw_ps", [128, 512], F32))
    vt_ps = pp1.enter_context(nc.psum_tensor("vt_ps", [128, 128], BF16))

    g_load(wq_sb[:, :, :], wq.rearrange("(i p) f -> p i f", p=128))
    g_load(wk_sb[:, :, :], wk.rearrange("(i p) f -> p i f", p=128))
    g_load(wv_sb[:, :, :], wv.rearrange("(i p) f -> p i f", p=128))
    w_loaded = sW.n

    total_tiles = NTG * n_ft
    xt_consumed_at = []
    cs_consumed_at = []

    if upto == "p0":
        sT = Ctr(stk, nc, "sT", 16)
        P.emit("gpsimd",
               lambda g: g.dma_start(out=outT[0:128, 0:512], in_=wq_sb[:, 0, :]),
               waits=[(sW.sem, sW.n)], inc=sT.bump())
        P.emit("gpsimd", None, waits=[(sT.sem, sT.n)])
        def finish0():
            def runner(eng_ops):
                def _run(eng):
                    for fn, waits, inc in eng_ops:
                        if fn is None:
                            for (sem, val) in waits:
                                eng.wait_ge(sem, val)
                            continue
                        if len(waits) > 1:
                            for (sem, val) in waits[:-1]:
                                eng.wait_ge(sem, val)
                        inst = fn(eng)
                        if waits:
                            inst.wait_op(waits[-1][0], waits[-1][1], "sem-ge")
                        if inc is not None:
                            inst.then_inc(inc[0], inc[1])
                return _run
            with nc.Block() as block:
                block.tensor(runner(P.ops["tensor"]))
                block.vector(runner(P.ops["vector"]))
                block.scalar(runner(P.ops["scalar"]))
                block.gpsimd(runner(P.ops["gpsimd"]))
                block.sync(runner(P.ops["sync"]))
            return nc
        nc_out = finish0()
        ph1.close(); pp1.close(); ph12.close(); stk.close()
        return nc_out

    def emit_xt_load(j):
        tg, d = divmod(j, n_ft)
        b = j % 8
        waits = [(sW.sem, w_loaded)]
        if j >= 8:
            waits.append((peM.sem, xt_consumed_at[j - 8]))
        P.emit("gpsimd",
               lambda g, b=b, tg=tg, d=d: g.dma_start(
                   out=xbuf[b][:, :], in_=xT[d * 128:(d + 1) * 128, tg * 512:(tg + 1) * 512]),
               waits=waits, inc=sX[b].bump())

    def emit_cs_load(tg):
        b = tg % 2
        waits = []
        if tg >= 2:
            waits.append((dvM.sem, cs_consumed_at[tg - 2]))
        P.emit("sync",
               lambda sy, b=b, tg=tg: sy.dma_start(
                   out=csbuf[b][:, 0, :], in_=cosT[:, tg * 512:(tg + 1) * 512]),
               waits=waits, inc=sCS[b].bump())
        P.emit("sync",
               lambda sy, b=b, tg=tg: sy.dma_start(
                   out=csbuf[b][:, 1, :], in_=sinT[:, tg * 512:(tg + 1) * 512]),
               inc=sCS[b].bump())

    NTG_RUN = NTG if p1_tgs is None else min(NTG, p1_tgs)
    total_tiles = NTG_RUN * n_ft
    xt_next = 0
    for j in range(min(8, total_tiles)):
        emit_xt_load(j)
        xt_next += 1
    emit_cs_load(0)
    if NTG > 1:
        emit_cs_load(1)

    prev_ep_act = 0
    prev_ep_dve = 0

    for tg in range(NTG_RUN):
        for d in range(n_ft):
            j = tg * n_ft + d
            b = j % 8
            waits = [(sX[b].sem, 16 * (j // 8 + 1))]
            if d == 0:
                waits.append((acM.sem, prev_ep_act))
                waits.append((dvM.sem, prev_ep_dve))
            st = (d == 0)
            sp = (d == n_ft - 1)
            for jj in range(HPC):
                P.emit("tensor",
                       lambda t, d=d, jj=jj, b=b, st=st, sp=sp: t.matmul(
                           q_ps[:, jj * 512:jj * 512 + 512],
                           wq_sb[:, d, jj * 128:jj * 128 + 128],
                           xbuf[b][:, :], start=st, stop=sp),
                       waits=waits if jj == 0 else ())
            P.emit("tensor",
                   lambda t, d=d, b=b, st=st, sp=sp: t.matmul(
                       kv_ps[:, 0:512], wk_sb[:, d, :], xbuf[b][:, :], start=st, stop=sp))
            P.emit("tensor",
                   lambda t, d=d, b=b, st=st, sp=sp: t.matmul(
                       kv_ps[:, 512:1024], wv_sb[:, d, :], xbuf[b][:, :], start=st, stop=sp),
                   inc=peM.bump())
            xt_consumed_at.append(peM.n)
            if xt_next < total_tiles:
                emit_xt_load(xt_next)
                xt_next += 1
        dloop_done = peM.n

        csb = tg % 2
        cs_ready = 32 * (tg // 2 + 1)
        tgs = slice(tg * 512, (tg + 1) * 512)
        prev_t2 = 0
        prev_red = 0
        prev_norm_pe = 0
        for hx in range(HPC + 1):
            if p1_lvl < 1:
                break
            is_q = hx < HPC
            src_ps = q_ps[:, hx * 512:hx * 512 + 512] if is_q else kv_ps[:, 0:512]
            P.emit("scalar",
                   lambda s, src=src_ps: s.copy(qkstage[:, :], src),
                   waits=[(peM.sem, dloop_done), (dvM.sem, prev_t2), (acM.sem, acM.n)],
                   inc=acM.bump())
            evac_done = acM.n
            if p1_lvl < 2:
                continue
            P.emit("tensor",
                   lambda t: t.matmul(sw_ps[:, :], pswap_sb[:, :], qkstage[:, :],
                                      start=True, stop=True),
                   waits=[(acM.sem, evac_done), (dvM.sem, max(prev_t2, prev_red))],
                   inc=peM.bump())
            swap_done = peM.n
            if p1_lvl < 3:
                continue
            P.emit("vector",
                   lambda v, src=src_ps, csb=csb: v.tensor_mul(t1b[:, :], src, csbuf[csb][:, 0, :]),
                   waits=[(sCS[csb].sem, cs_ready), (peM.sem, dloop_done), (dvM.sem, dvM.n),
                          (acM.sem, evac_done)])
            P.emit("vector",
                   lambda v, csb=csb: v.tensor_mul(t2b[:, :], sw_ps[:, :], csbuf[csb][:, 1, :]),
                   waits=[(peM.sem, swap_done), (dvM.sem, dvM.n)], inc=dvM.bump())
            prev_t2 = dvM.n
            dst = q4[:, hx, tgs] if is_q else KT[:, tgs]
            P.emit("vector",
                   lambda v, dst=dst: v.tensor_add(dst, t1b[:, :], t2b[:, :]),
                   waits=[(dvM.sem, dvM.n)], inc=dvM.bump())
            rope_done = dvM.n
            if p1_lvl < 4:
                continue
            P.emit("scalar",
                   lambda s, dst=dst: s.activation(sqstage[:, :], dst, AF.Square),
                   waits=[(dvM.sem, rope_done), (peM.sem, prev_norm_pe), (acM.sem, acM.n)],
                   inc=acM.bump())
            sq_done = acM.n
            P.emit("tensor",
                   lambda t: t.matmul(sw_ps[0:1, :], onescb_sb[:, :], sqstage[:, :],
                                      start=True, stop=True),
                   waits=[(acM.sem, sq_done), (dvM.sem, prev_t2)],
                   inc=peM.bump())
            prev_norm_pe = peM.n
            if is_q:
                dst_red = cmax[0:1, (tg * 4 + hx):(tg * 4 + hx + 1)]
            else:
                dst_red = kmaxs[0:1, tg:tg + 1]
            P.emit("vector",
                   lambda v, dst_red=dst_red: v.reduce_max(dst_red, sw_ps[0:1, :], axis=AX.X),
                   waits=[(peM.sem, prev_norm_pe), (dvM.sem, dvM.n)], inc=dvM.bump())
            prev_red = dvM.n
        if p1_lvl >= 5:
            P.emit("scalar",
                   lambda s: s.copy(vstage[:, :], kv_ps[:, 512:1024]),
                   waits=[(peM.sem, dloop_done), (acM.sem, acM.n)], inc=acM.bump())
            vevac_done = acM.n
            prev_vt_copy = 0
            for i in range(4):
                P.emit("tensor",
                       lambda t, i=i: t.transpose(vt_ps[:, :], vstage[:, i * 128:i * 128 + 128],
                                                  ident_sb[:, :]),
                       waits=[(acM.sem, vevac_done), (dvM.sem, prev_vt_copy)],
                       inc=peM.bump())
                tr_done = peM.n
                P.emit("vector",
                       lambda v, i=i, tg=tg: v.tensor_copy(
                           Vtok[:, (tg * 4 + i) * 128:(tg * 4 + i) * 128 + 128], vt_ps[:, :]),
                       waits=[(peM.sem, tr_done), (dvM.sem, dvM.n)], inc=dvM.bump())
                prev_vt_copy = dvM.n
        cs_consumed_at.append(dvM.n)
        if tg + 2 < NTG:
            emit_cs_load(tg + 2)
        prev_ep_act = acM.n
        prev_ep_dve = dvM.n

    phase1_dve = dvM.n
    phase1_act = acM.n

    if p1_tgs is not None:
        sT = Ctr(stk, nc, "sT", 16)
        P.emit("gpsimd",
               lambda g: g.dma_start(out=outT[0:128, :], in_=KT[:, 0:To]),
               waits=[(dvM.sem, phase1_dve), (acM.sem, phase1_act)], inc=sT.bump())
        P.emit("gpsimd",
               lambda g: g.dma_start(out=outT[128:256, :], in_=q4[:, 0, 0:To]),
               inc=sT.bump())
        P.emit("gpsimd",
               lambda g: g.dma_start(out=outT[256:384, :], in_=Vtok[:, 0:To]),
               inc=sT.bump())
        P.emit("gpsimd", None, waits=[(sT.sem, sT.n)])
        def finish1():
            def runner(eng_ops):
                def _run(eng):
                    for fn, waits, inc in eng_ops:
                        if fn is None:
                            for (sem, val) in waits:
                                eng.wait_ge(sem, val)
                            continue
                        if len(waits) > 1:
                            for (sem, val) in waits[:-1]:
                                eng.wait_ge(sem, val)
                        inst = fn(eng)
                        if waits:
                            inst.wait_op(waits[-1][0], waits[-1][1], "sem-ge")
                        if inc is not None:
                            inst.then_inc(inc[0], inc[1])
                return _run
            with nc.Block() as block:
                block.tensor(runner(P.ops["tensor"]))
                block.vector(runner(P.ops["vector"]))
                block.scalar(runner(P.ops["scalar"]))
                block.gpsimd(runner(P.ops["gpsimd"]))
                block.sync(runner(P.ops["sync"]))
            return nc
        nc_out = finish1()
        ph1.close(); pp1.close(); ph12.close(); stk.close()
        return nc_out

    P.emit("vector",
           lambda v: v.reduce_max(kmaxs[0:1, NTG:NTG + 1], kmaxs[0:1, 0:NTG], axis=AX.X),
           waits=[(dvM.sem, phase1_dve)], inc=dvM.bump())
    P.emit("vector",
           lambda v: v.tensor_scalar(rowbuf[0:1, 0:4 * NQC], cmax[0:1, :],
                                     kmaxs[0:1, NTG:NTG + 1], 1.21, ALU.mult, ALU.mult),
           waits=[(dvM.sem, dvM.n)], inc=dvM.bump())
    u_done = dvM.n
    P.emit("scalar",
           lambda s: s.activation(rowbuf[0:1, 1024:1024 + 4 * NQC], rowbuf[0:1, 0:4 * NQC],
                                  AF.Ln, bias=1e-5),
           waits=[(dvM.sem, u_done), (sW.sem, sW.n), (dvM.sem, consts_ready),
                  (acM.sem, acM.n)],
           inc=acM.bump())
    P.emit("scalar",
           lambda s: s.activation(rowbuf[0:1, 2048:2048 + 4 * NQC],
                                  rowbuf[0:1, 1024:1024 + 4 * NQC], AF.Exp, scale=0.5),
           waits=[(acM.sem, acM.n)], inc=acM.bump())
    c_done = acM.n
    P.emit("vector",
           lambda v: v.tensor_scalar_mul(cneg[0:1, :], rowbuf[0:1, 2048:2048 + 4 * NQC], -1.0),
           waits=[(acM.sem, c_done), (dvM.sem, dvM.n)], inc=dvM.bump())
    cneg_done = dvM.n

    dbg_tail = []

    def finish():
        def runner(eng_ops):
            def _run(eng):
                for fn, waits, inc in eng_ops:
                    if fn is None:
                        for (sem, val) in waits:
                            eng.wait_ge(sem, val)
                        continue
                    if len(waits) > 1:
                        for (sem, val) in waits[:-1]:
                            eng.wait_ge(sem, val)
                    inst = fn(eng)
                    if waits:
                        inst.wait_op(waits[-1][0], waits[-1][1], "sem-ge")
                    if inc is not None:
                        inst.then_inc(inc[0], inc[1])
            return _run

        with allow_ici_all_to_all():
            with nc.Block() as block:
                block.tensor(runner(P.ops["tensor"]))
                block.vector(runner(P.ops["vector"]))
                block.scalar(runner(P.ops["scalar"]))
                block.gpsimd(runner(P.ops["gpsimd"]))
                block.sync(runner(P.ops["sync"]))
        return nc

    if upto == "p1":
        sT = Ctr(stk, nc, "sT", 16)
        P.emit("gpsimd",
               lambda g: g.dma_start(out=outT[0:128, :], in_=KT[:, 0:To]),
               waits=[(dvM.sem, cneg_done)], inc=sT.bump())
        P.emit("gpsimd",
               lambda g: g.dma_start(out=outT[128:256, :], in_=q4[:, 0, 0:To]),
               inc=sT.bump())
        P.emit("gpsimd",
               lambda g: g.dma_start(out=outT[256:384, :], in_=Vtok[:, 0:To]),
               inc=sT.bump())
        P.emit("gpsimd", None, waits=[(sT.sem, sT.n)])
        nc_out = finish()
        ph1.close(); pp1.close(); ph12.close(); stk.close()
        return nc_out

    ph1.close()
    pp1.close()

    ph2 = contextlib.ExitStack()
    pt = [[ph2.enter_context(nc.sbuf_tensor("pt%d_%d" % (hp, par), [128, 2, 1024], BF16))
           for par in range(2)] for hp in range(2)]
    bias_sb = ph2.enter_context(nc.sbuf_tensor("biasc_sb", [128, 4], F32))
    o_sb = [ph2.enter_context(nc.sbuf_tensor("o_sb%d" % i, [128, 512], F32)) for i in range(2)]
    linv_row = [ph2.enter_context(nc.sbuf_tensor("linv_row%d" % i, [1, 512], F32))
                for i in range(2)]
    linv_bc = [ph2.enter_context(nc.sbuf_tensor("linv_bc%d" % i, [128, 512], F32))
               for i in range(2)]
    atst = [ph2.enter_context(nc.sbuf_tensor("atst%d" % i, [128, 512], BF16)) for i in range(8)]

    pp2 = contextlib.ExitStack()
    s_ps = [pp2.enter_context(nc.psum_tensor("s_ps%d" % i, [128, 1024], F32)) for i in range(2)]
    o_ps = [pp2.enter_context(nc.psum_tensor("o_ps%d" % i, [128, 512], F32)) for i in range(2)]
    l_ps = [pp2.enter_context(nc.psum_tensor("l_ps%d" % i, [128, 512], F32)) for i in range(2)]

    pair_idx = 0
    pending_norm = []
    l_free_at = [0, 0]
    linv_pe_guard = [0, 0]
    exp_half_at = {}
    group_l_done = {}
    ghp_count = [0, 0]
    atst_stored_at = [0] * 8
    o_free_at = [0, 0]
    o_sb_free = [0, 0]
    linv_bc_free = [0, 0]
    bank_guard = [0, 0]

    for qc in range(NQC):
        pro_waits = [(acM.sem, acM.n), (dvM.sem, cneg_done)]
        if qc == 0:
            pro_waits += [(dvM.sem, max(phase1_dve, cneg_done)), (acM.sem, phase1_act)]
        P.emit("tensor",
               lambda t, qc=qc: t.matmul(s_ps[0][:, 0:4], onesrb_sb[:, :],
                                         cneg[0:1, 4 * qc:4 * qc + 4], start=True, stop=True),
               waits=pro_waits, inc=peM.bump())
        bias_mm = peM.n
        P.emit("scalar",
               lambda s: s.copy(bias_sb[:, :], s_ps[0][:, 0:4]),
               waits=[(peM.sem, bias_mm), (acM.sem, acM.n)], inc=acM.bump())
        bias_ready = acM.n
        bank_guard[0] = bias_ready

        for pas in range(2):
            for kc in range(qc + 1):
                diag = (kc == qc)
                for hp in range(2):
                    h = pas * 2 + hp
                    gpar = ghp_count[hp] % 2
                    pair_avs = []
                    for pj in range(2):
                        pg = pair_idx
                        bank = pg % 2
                        for sj in range(2):
                            ks = pj * 2 + sj
                            w = []
                            if sj == 0 and pg >= 2:
                                w.append((acM.sem, exp_half_at[(pg - 2, 1)]))
                            if sj == 0:
                                w.append((acM.sem, bank_guard[bank]))
                            P.emit("tensor",
                                   lambda t, kc=kc, ks=ks, h=h, qc=qc, bank=bank, sj=sj:
                                   t.matmul(s_ps[bank][:, sj * 512:sj * 512 + 512],
                                            KT[:, kc * 512 + ks * 128:kc * 512 + ks * 128 + 128],
                                            q4[:, h, qc * 512:qc * 512 + 512],
                                            start=True, stop=True),
                                   waits=w, inc=peM.bump() if sj == 1 else None)
                        sdone = [peM.n, peM.n]
                        av_waits = [None, None]
                        for sj in range(2):
                            ks = pj * 2 + sj
                            ew = [(peM.sem, sdone[sj]), (acM.sem, acM.n)]
                            if (hp, gpar) in group_l_done:
                                ew.append((peM.sem, group_l_done[(hp, gpar)]))
                            P.emit("scalar",
                                   lambda s, bank=bank, hp=hp, gpar=gpar, pj=pj, sj=sj,
                                   hcol=pas * 2 + hp:
                                   s.activation(pt[hp][gpar][:, pj, sj * 512:sj * 512 + 512],
                                                s_ps[bank][:, sj * 512:sj * 512 + 512],
                                                AF.Exp, bias=bias_sb[:, hcol:hcol + 1]),
                                   waits=ew, inc=acM.bump())
                            exp_half_at[(pg, sj)] = acM.n
                            av_waits[sj] = (acM.sem, acM.n)
                            if diag:
                                P.emit("vector",
                                       lambda v, hp=hp, gpar=gpar, pj=pj, ks=ks, sj=sj:
                                       v.tensor_mul(pt[hp][gpar][:, pj, sj * 512:sj * 512 + 512],
                                                    pt[hp][gpar][:, pj, sj * 512:sj * 512 + 512],
                                                    masks_sb[:, ks, :]),
                                       waits=[(acM.sem, acM.n), (dvM.sem, dvM.n)],
                                       inc=dvM.bump())
                                av_waits[sj] = (dvM.sem, dvM.n)
                        pair_avs.append((pj, av_waits))
                        pair_idx += 1
                    for (ppj, av_waits) in pair_avs:
                        st0 = (kc == 0 and ppj == 0)
                        sp0 = (diag and ppj == 1)
                        for sj in range(2):
                            ks = ppj * 2 + sj
                            kslice = slice((kc * 4 + ks) * 128, (kc * 4 + ks) * 128 + 128)
                            avkw = [av_waits[sj]]
                            if st0 and sj == 0:
                                avkw.append((dvM.sem, o_free_at[hp]))
                            P.emit("tensor",
                                   lambda t, gpar=gpar, ppj=ppj, sj=sj, kslice=kslice, hp=hp,
                                   st=(st0 and sj == 0), sp=(sp0 and sj == 1):
                                   t.matmul(o_ps[hp][:, :], Vtok[:, kslice],
                                            pt[hp][gpar][:, ppj, sj * 512:sj * 512 + 512],
                                            start=st, stop=sp),
                                   waits=avkw)
                        for sj in range(2):
                            lw = []
                            if st0 and sj == 0:
                                lw.append((acM.sem, l_free_at[hp]))
                            P.emit("tensor",
                                   lambda t, gpar=gpar, ppj=ppj, sj=sj, hp=hp,
                                   st=(st0 and sj == 0), sp=(sp0 and sj == 1):
                                   t.matmul(l_ps[hp][0:1, :], onescb_sb[:, :],
                                            pt[hp][gpar][:, ppj, sj * 512:sj * 512 + 512],
                                            start=st, stop=sp),
                                   waits=lw,
                                   inc=peM.bump() if (ppj == 1 and sj == 1) else None)
                    group_l_done[(hp, ghp_count[hp] % 2)] = peM.n
                    ghp_count[hp] += 1
            for fn in pending_norm:
                fn()
            pending_norm = []
            for hp in range(2):
                h = pas * 2 + hp
                gl = group_l_done[(hp, (ghp_count[hp] - 1) % 2)]
                P.emit("scalar",
                       lambda s, hp=hp: s.activation(linv_row[hp][:, :], l_ps[hp][0:1, :],
                                                     AF.Ln),
                       waits=[(peM.sem, max(gl, linv_pe_guard[hp])), (acM.sem, acM.n)],
                       inc=acM.bump())
                ln_done = acM.n
                P.emit("scalar",
                       lambda s, hp=hp: s.activation(linv_row[hp][:, :], linv_row[hp][:, :],
                                                     AF.Exp, scale=-1.0),
                       waits=[(acM.sem, acM.n)], inc=acM.bump())
                linv_done = acM.n
                P.emit("vector",
                       lambda v, hp=hp: v.tensor_copy(o_sb[hp][:, :], o_ps[hp][:, :]),
                       waits=[(peM.sem, gl), (dvM.sem, max(dvM.n, o_sb_free[hp]))],
                       inc=dvM.bump())
                o_free_at[hp] = dvM.n
                l_free_at[hp] = linv_done
                def _norm(hp=hp, h=h, qc=qc, linv_done=linv_done):
                    P.emit("tensor",
                           lambda t, hp=hp: t.matmul(s_ps[hp][:, 0:512], onesrf_sb[:, :],
                                                     linv_row[hp][:, :], start=True, stop=True),
                           waits=[(acM.sem, max(linv_done, acM.n))], inc=peM.bump())
                    bc_done = peM.n
                    linv_pe_guard[hp] = bc_done
                    P.emit("scalar",
                           lambda s, hp=hp: s.copy(linv_bc[hp][:, :], s_ps[hp][:, 0:512]),
                           waits=[(peM.sem, bc_done), (acM.sem, acM.n),
                                  (dvM.sem, linv_bc_free[hp])],
                           inc=acM.bump())
                    bank_guard[hp] = acM.n
                    bcev_done = acM.n
                    sti = (qc * 4 + h) % 8
                    P.emit("vector",
                           lambda v, hp=hp, sti=sti: v.tensor_mul(atst[sti][:, :],
                                                                  o_sb[hp][:, :],
                                                                  linv_bc[hp][:, :]),
                           waits=[(acM.sem, bcev_done), (sA2[sti].sem, atst_stored_at[sti]),
                                  (dvM.sem, dvM.n)],
                           inc=dvM.bump())
                    o_sb_free[hp] = dvM.n
                    linv_bc_free[hp] = dvM.n
                    mul_done = dvM.n
                    d_dest = qc // NTH
                    tof = (qc % NTH) * 512
                    P.emit("sync",
                           lambda sy, h=h, sti=sti, d_dest=d_dest, tof=tof: sy.dma_start(
                               out=a2a_in.ap()[d_dest, h * 128:h * 128 + 128, tof:tof + 512],
                               in_=atst[sti][:, :]),
                           waits=[(dvM.sem, mul_done)], inc=sA2[sti].bump())
                    atst_stored_at[sti] = sA2[sti].n
                pending_norm.append(_norm)

    for fn in pending_norm:
        fn()
    pending_norm = []

    P.emit("gpsimd",
           lambda g: g.collective_compute(
               "AllToAll", ALU.bypass, replica_groups=[list(range(NCORES))],
               ins=[a2a_in.ap().opt()], outs=[a2a_out.ap().opt()]),
           waits=[(s.sem, s.n) for s in sA2], inc=ccs.bump())

    if upto == "p2":
        sT = Ctr(stk, nc, "sT", 16)
        P.emit("gpsimd",
               lambda g: g.dma_start(out=outT[:, :],
                                     in_=a2a_out.ap().rearrange("r f t -> (r f) t")),
               waits=[(ccs.sem, 1)], inc=sT.bump())
        P.emit("gpsimd", None, waits=[(sT.sem, sT.n)])
        nc_out = finish()
        ph2.close(); pp2.close(); ph12.close(); stk.close()
        return nc_out

    ph2.close()
    pp2.close()
    ph12.close()

    ph34 = contextlib.ExitStack()
    st_sb = [ph34.enter_context(nc.sbuf_tensor("st_sb%d" % i, [128, To], F32)) for i in range(4)]
    r2st = [ph34.enter_context(nc.sbuf_tensor("r2st%d" % i, [128, To], F32)) for i in range(2)]
    sq_sb = ph34.enter_context(nc.sbuf_tensor("sq_sb", [128, To], F32))
    acc_s = ph34.enter_context(nc.sbuf_tensor("acc_s", [128, To], F32))
    acc_q = ph34.enter_context(nc.sbuf_tensor("acc_q", [128, To], F32))
    m_bc = ph34.enter_context(nc.sbuf_tensor("m_bc", [128, To], F32))
    rstd_bc = ph34.enter_context(nc.sbuf_tensor("rstd_bc", [128, To], F32))
    sc_bc = ph34.enter_context(nc.sbuf_tensor("sc_bc", [128, To], F32))

    ph25 = contextlib.ExitStack()
    at_own = ph25.enter_context(nc.sbuf_tensor("at_own", [128, n_ft, To], BF16))
    wo_sb = [ph25.enter_context(nc.sbuf_tensor("wo_sb%d" % i, [128, n_ft, 256], BF16))
             for i in range(2)]

    pp3 = contextlib.ExitStack()
    psA = [pp3.enter_context(nc.psum_tensor("psA%d" % i, [128, To], F32)) for i in range(4)]

    P.emit("sync",
           lambda sy: sy.dma_start(out=at_own[:, :, :],
                                   in_=a2a_out.ap().rearrange("r (i p) t -> p (r i) t", p=128)),
           waits=[(ccs.sem, 1)], inc=sAo.bump())
    at_own_ready = sAo.n

    n_grp = n_ft // 2
    wo_grp_free_at = [0, 0]
    wo_stripe_free_at = [0, 0]
    st_free_at = [0, 0]
    st_free_acm = [0, 0]
    xo_used_at = [0, 0]
    acc_chain = 0

    def emit_wo_stripe(g):
        b = g % 2
        w = [(ccs.sem, 1)] if g < 2 else []
        if g >= 2:
            w.append((peM.sem, wo_stripe_free_at[b]))
        P.emit("gpsimd",
               lambda gp, g=g, b=b: gp.dma_start(
                   out=wo_sb[b][:, :, :],
                   in_=wo[:, g * 256:(g + 1) * 256].rearrange("(i p) f -> p i f", p=128)),
               waits=w, inc=sWo[b].bump())

    emit_wo_stripe(0)
    emit_wo_stripe(1)
    for ft in range(min(2, n_ft)):
        P.emit("sync",
               lambda sy, ft=ft: sy.dma_start(
                   out=xo_sb[ft % 2][:, :], in_=xoT[ft * 128:(ft + 1) * 128, :]),
               inc=sXo[ft % 2].bump())

    for g in range(n_grp):
        b = g % 2
        mmw = [(sWo[b].sem, 16 * (g // 2 + 1)), (sAo.sem, at_own_ready)]
        if g >= 2:
            mmw.append((dvM.sem, wo_grp_free_at[b]))
        for d in range(n_ft):
            for jj in range(2):
                for th in range(NTH):
                    P.emit("tensor",
                           lambda t, b=b, d=d, jj=jj, th=th, st=(d == 0), sp=(d == n_ft - 1):
                           t.matmul(psA[2 * b + jj][:, th * 512:th * 512 + 512],
                                    wo_sb[b][:, d, jj * 128:jj * 128 + 128],
                                    at_own[:, d, th * 512:th * 512 + 512],
                                    start=st, stop=sp),
                           waits=mmw if (d == 0 and jj == 0 and th == 0) else (),
                           inc=peM.bump() if (d == n_ft - 1 and jj == 1 and th == NTH - 1) else None)
        grp_done = peM.n
        wo_stripe_free_at[b] = grp_done
        if g + 2 < n_grp:
            emit_wo_stripe(g + 2)
        for jj in range(2):
            ft = 2 * g + jj
            sb = ft % 2
            P.emit("vector",
                   lambda v, b=b, jj=jj, sb=sb: v.tensor_add(
                       st_sb[sb][:, :], psA[2 * b + jj][:, 0:To], xo_sb[sb][:, :]),
                   waits=[(peM.sem, grp_done), (sXo[sb].sem, 16 * (ft // 2 + 1)),
                          (sSw[sb].sem, st_free_at[sb]), (acM.sem, st_free_acm[sb]),
                          (dvM.sem, dvM.n)],
                   inc=dvM.bump())
            add_done = dvM.n
            xo_used_at[sb] = add_done
            if jj == 1:
                wo_grp_free_at[b] = add_done
            if ft + 2 < n_ft:
                P.emit("sync",
                       lambda sy, ft=ft, sb=sb: sy.dma_start(
                           out=xo_sb[sb][:, :], in_=xoT[(ft + 2) * 128:(ft + 3) * 128, :]),
                       waits=[(dvM.sem, add_done)], inc=sXo[sb].bump())
            P.emit("sync",
                   lambda sy, ft=ft, sb=sb: sy.dma_start(
                       out=s_dram.ap()[ft * 128:(ft + 1) * 128, :], in_=st_sb[sb][:, :]),
                   waits=[(dvM.sem, add_done)], inc=sSw[sb].bump())
            st_free_at[sb] = sSw[sb].n
            if ft == 0:
                P.emit("vector", lambda v, sb=sb: v.tensor_copy(acc_s[:, :], st_sb[sb][:, :]),
                       waits=[(dvM.sem, add_done)], inc=dvM.bump())
            else:
                P.emit("vector", lambda v, sb=sb: v.tensor_add(acc_s[:, :], acc_s[:, :],
                                                               st_sb[sb][:, :]),
                       waits=[(dvM.sem, max(add_done, acc_chain))], inc=dvM.bump())
            P.emit("scalar", lambda s, sb=sb: s.activation(sq_sb[:, :], st_sb[sb][:, :],
                                                           AF.Square),
                   waits=[(dvM.sem, add_done), (dvM.sem, acc_chain), (acM.sem, acM.n)],
                   inc=acM.bump())
            sqd = acM.n
            st_free_acm[sb] = sqd
            if ft == 0:
                P.emit("vector", lambda v: v.tensor_copy(acc_q[:, :], sq_sb[:, :]),
                       waits=[(acM.sem, sqd), (dvM.sem, dvM.n)], inc=dvM.bump())
            else:
                P.emit("vector", lambda v: v.tensor_add(acc_q[:, :], acc_q[:, :], sq_sb[:, :]),
                       waits=[(acM.sem, sqd), (dvM.sem, dvM.n)], inc=dvM.bump())
            acc_chain = dvM.n

    def emit_ln_finalize(m_dst, r_dst, roff, acc_ready_dvM, sc_dst=None):
        psum, psumq = psA[3], psA[2]
        for half in range(NTH):
            P.emit("tensor",
                   lambda t, half=half: t.matmul(
                       psum[0:1, half * 512:half * 512 + 512], onescf_sb[:, :],
                       acc_s[:, half * 512:half * 512 + 512], start=True, stop=True),
                   waits=[(dvM.sem, acc_ready_dvM), (acM.sem, acM.n)],
                   inc=peM.bump())
            P.emit("tensor",
                   lambda t, half=half: t.matmul(
                       psumq[0:1, half * 512:half * 512 + 512], onescf_sb[:, :],
                       acc_q[:, half * 512:half * 512 + 512], start=True, stop=True),
                   inc=peM.bump())
        mm_done = peM.n
        r0, r1, r2o = roff, roff + 1024, roff + 2048
        P.emit("scalar",
               lambda s: s.copy(rowbuf[0:1, r0:r0 + To], psum[0:1, 0:To]),
               waits=[(peM.sem, mm_done), (acM.sem, acM.n)], inc=acM.bump())
        P.emit("scalar",
               lambda s: s.copy(rowbuf[0:1, r1:r1 + To], psumq[0:1, 0:To]),
               waits=[(acM.sem, acM.n)], inc=acM.bump())
        evr = acM.n
        P.emit("vector",
               lambda v: v.tensor_scalar_mul(rowbuf[0:1, r0:r0 + To],
                                             rowbuf[0:1, r0:r0 + To], 1.0 / DIM),
               waits=[(acM.sem, evr), (dvM.sem, dvM.n)], inc=dvM.bump())
        m_row_done = dvM.n
        P.emit("vector",
               lambda v: v.tensor_scalar_mul(rowbuf[0:1, r1:r1 + To],
                                             rowbuf[0:1, r1:r1 + To], 1.0 / DIM),
               waits=[(dvM.sem, dvM.n)], inc=dvM.bump())
        P.emit("vector",
               lambda v: v.tensor_mul(rowbuf[0:1, r2o:r2o + To],
                                      rowbuf[0:1, r0:r0 + To], rowbuf[0:1, r0:r0 + To]),
               waits=[(dvM.sem, dvM.n)], inc=dvM.bump())
        P.emit("vector",
               lambda v: v.tensor_sub(rowbuf[0:1, r1:r1 + To], rowbuf[0:1, r1:r1 + To],
                                      rowbuf[0:1, r2o:r2o + To]),
               waits=[(dvM.sem, dvM.n)], inc=dvM.bump())
        var_done = dvM.n
        P.emit("scalar",
               lambda s: s.activation(rowbuf[0:1, r2o:r2o + To], rowbuf[0:1, r1:r1 + To],
                                      AF.Ln, bias=1e-5),
               waits=[(dvM.sem, var_done), (acM.sem, acM.n)], inc=acM.bump())
        P.emit("scalar",
               lambda s: s.activation(rowbuf[0:1, r1:r1 + To], rowbuf[0:1, r2o:r2o + To],
                                      AF.Exp, scale=-0.5),
               waits=[(acM.sem, acM.n)], inc=acM.bump())
        rstd_done = acM.n
        sc_done = dvM.n
        if sc_dst is not None:
            P.emit("vector",
                   lambda v: v.tensor_scalar(rowbuf[0:1, r2o:r2o + To],
                                             rowbuf[0:1, r1:r1 + To],
                                             wscale_sb[0:1, 0:1], None, ALU.mult),
                   waits=[(acM.sem, rstd_done), (dvM.sem, dvM.n)], inc=dvM.bump())
            sc_done = dvM.n
        bc_list = [(m_dst, r0), (r_dst, r1)]
        if sc_dst is not None:
            bc_list.append((sc_dst, r2o))
        prev_ev = 0
        for (dst, off) in bc_list:
            for half in range(NTH):
                P.emit("tensor",
                       lambda t, off=off, half=half: t.matmul(
                           psum[:, half * 512:half * 512 + 512], onesrf_sb[:, :],
                           rowbuf[0:1, off + half * 512:off + half * 512 + 512],
                           start=True, stop=True),
                       waits=[(acM.sem, max(rstd_done, prev_ev)), (dvM.sem, sc_done)],
                       inc=peM.bump())
                bmm = peM.n
                P.emit("scalar",
                       lambda s, dst=dst, half=half: s.copy(
                           dst[:, half * 512:half * 512 + 512],
                           psum[:, half * 512:half * 512 + 512]),
                       waits=[(peM.sem, bmm), (acM.sem, acM.n)], inc=acM.bump())
                prev_ev = acM.n
        return acM.n

    bc1_done = emit_ln_finalize(m_bc, rstd_bc, 0, acc_chain, sc_dst=sc_bc)

    if upto == "p25":
        sT = Ctr(stk, nc, "sT", 16)
        P.emit("sync",
               lambda sy: sy.dma_start(out=outT[:, :], in_=s_dram.ap()),
               waits=[(sSw[0].sem, sSw[0].n), (sSw[1].sem, sSw[1].n),
                      (acM.sem, bc1_done)], inc=sT.bump())
        P.emit("sync", None, waits=[(sT.sem, sT.n)])
        nc_out = finish()
        ph25.close(); ph34.close(); pp3.close(); stk.close()
        return nc_out

    ph25.close()
    ph3 = contextlib.ExitStack()
    d1_sb = ph3.enter_context(nc.sbuf_tensor("d1_sb", [128, n_ft, To], BF16))
    w2r = [ph3.enter_context(nc.sbuf_tensor("w2r%d" % i, [128, n_ft, 128], BF16))
           for i in range(3)]
    wf_sb = [ph3.enter_context(nc.sbuf_tensor("wf_sb%d" % i, [128, n_ft, 128], BF16))
             for i in range(3)]

    d1_done_at = []
    for ft in range(n_ft):
        sb = ft % 4
        P.emit("sync",
               lambda sy, ft=ft, sb=sb: sy.dma_start(
                   out=st_sb[sb][:, :], in_=s_dram.ap()[ft * 128:(ft + 1) * 128, :]),
               waits=[(sSw[0].sem, sSw[0].n), (sSw[1].sem, sSw[1].n),
                      (acM.sem, bc1_done),
                      (dvM.sem, d1_done_at[ft - 4] if ft >= 4 else 0)],
               inc=sSr[sb].bump())
        P.emit("vector",
               lambda v, ft=ft, sb=sb: v.tensor_sub(d1_sb[:, ft, :], st_sb[sb][:, :],
                                                    m_bc[:, :]),
               waits=[(sSr[sb].sem, 16 * (ft // 4 + 1)), (acM.sem, bc1_done),
                      (dvM.sem, dvM.n)],
               inc=dvM.bump())
        d1_done_at.append(dvM.n)
    d1_all = dvM.n

    ff_grp_free_at = [0, 0, 0]
    ff_stripe_free_at = [0, 0, 0]
    wf_conv_at = [0, 0, 0]
    st3_free_dv = [0, 0, 0, 0]
    r2st_free_dv = [0, 0]
    r2st_free_ac = [0, 0]
    r2st_spill_at = [0, 0]
    acc2_chain = 0

    for g in range(n_ft):
        b = g % 3
        w = [(sSw[0].sem, sSw[0].n), (sSw[1].sem, sSw[1].n)]
        if g >= 3:
            w.append((dvM.sem, wf_conv_at[b]))
        P.emit("gpsimd",
               lambda gp, g=g, b=b: gp.dma_start(
                   out=w2r[b][:, :, :],
                   in_=w2T[:, g * 128:(g + 1) * 128].rearrange("(i p) f -> p i f", p=128)),
               waits=w, inc=sFf[b].bump())
        P.emit("vector",
               lambda v, b=b: v.tensor_scalar_sub(wf_sb[b][:, :, :], w2r[b][:, :, :], 1.5),
               waits=[(sFf[b].sem, 16 * (g // 3 + 1)), (peM.sem, ff_stripe_free_at[b]),
                      (dvM.sem, dvM.n)],
               inc=dvM.bump())
        wf_conv_at[b] = dvM.n
        conv_done = dvM.n
        mmw = [(dvM.sem, max(conv_done, d1_all))]
        if g >= 3:
            mmw.append((dvM.sem, ff_grp_free_at[b]))
        for d in range(n_ft):
            for th in range(NTH):
                P.emit("tensor",
                       lambda t, b=b, d=d, th=th, st=(d == 0), sp=(d == n_ft - 1):
                       t.matmul(psA[b][:, th * 512:th * 512 + 512],
                                wf_sb[b][:, d, :], d1_sb[:, d, th * 512:th * 512 + 512],
                                start=st, stop=sp),
                       waits=mmw if (d == 0 and th == 0) else (),
                       inc=peM.bump() if (d == n_ft - 1 and th == NTH - 1) else None)
        grp_done = peM.n
        ff_stripe_free_at[b] = grp_done
        sb = g % 4
        sb2 = g % 2
        P.emit("sync",
               lambda sy, g=g, sb=sb: sy.dma_start(
                   out=st_sb[sb][:, :], in_=s_dram.ap()[g * 128:(g + 1) * 128, :]),
               waits=[(dvM.sem, max(d1_all, st3_free_dv[sb]))],
               inc=sS2[sb].bump())
        P.emit("vector",
               lambda v, sb=sb: v.tensor_sub(sq_sb[:, :], st_sb[sb][:, :], m_bc[:, :]),
               waits=[(sS2[sb].sem, 16 * (g // 4 + 1)), (dvM.sem, dvM.n)],
               inc=dvM.bump())
        st3_free_dv[sb] = dvM.n
        P.emit("vector",
               lambda v, sb=sb: v.tensor_mul(sq_sb[:, :], sq_sb[:, :], rstd_bc[:, :]),
               waits=[(dvM.sem, dvM.n)], inc=dvM.bump())
        hh_done = dvM.n
        P.emit("vector",
               lambda v, b=b, sb2=sb2: v.tensor_mul(r2st[sb2][:, :], psA[b][:, 0:To],
                                                    sc_bc[:, :]),
               waits=[(peM.sem, grp_done), (dvM.sem, max(hh_done, r2st_free_dv[sb2])),
                      (acM.sem, r2st_free_ac[sb2]), (sR2w[sb2].sem, r2st_spill_at[sb2])],
               inc=dvM.bump())
        ff_grp_free_at[b] = dvM.n
        P.emit("vector",
               lambda v, g=g, sb2=sb2: v.scalar_tensor_tensor(
                   r2st[sb2][:, :], r2st[sb2][:, :], bias2d_sb[:, g:g + 1], sq_sb[:, :],
                   ALU.add, ALU.add),
               waits=[(dvM.sem, dvM.n)], inc=dvM.bump())
        r2_done = dvM.n
        P.emit("scalar",
               lambda s, g=g, sb2=sb2: s.dma_start(
                   out=r2_dram.ap()[g * 128:(g + 1) * 128, :], in_=r2st[sb2][:, :]),
               waits=[(dvM.sem, r2_done)], inc=sR2w[sb2].bump())
        r2st_spill_at[sb2] = sR2w[sb2].n
        if g == 0:
            P.emit("vector", lambda v, sb2=sb2: v.tensor_copy(acc_s[:, :], r2st[sb2][:, :]),
                   waits=[(dvM.sem, r2_done)], inc=dvM.bump())
        else:
            P.emit("vector", lambda v, sb2=sb2: v.tensor_add(acc_s[:, :], acc_s[:, :],
                                                             r2st[sb2][:, :]),
                   waits=[(dvM.sem, max(r2_done, acc2_chain))], inc=dvM.bump())
        P.emit("scalar", lambda s, sb2=sb2: s.activation(sq_sb[:, :], r2st[sb2][:, :], AF.Square),
               waits=[(dvM.sem, dvM.n), (acM.sem, acM.n)], inc=acM.bump())
        sqd = acM.n
        r2st_free_ac[sb2] = sqd
        if g == 0:
            P.emit("vector", lambda v: v.tensor_copy(acc_q[:, :], sq_sb[:, :]),
                   waits=[(acM.sem, sqd), (dvM.sem, dvM.n)], inc=dvM.bump())
        else:
            P.emit("vector", lambda v: v.tensor_add(acc_q[:, :], acc_q[:, :], sq_sb[:, :]),
                   waits=[(acM.sem, sqd), (dvM.sem, dvM.n)], inc=dvM.bump())
        acc2_chain = dvM.n
        r2st_free_dv[sb2] = acc2_chain

    bc2_done = emit_ln_finalize(m_bc, rstd_bc, 0, acc2_chain)
    dv_after_bc2 = dvM.n

    out_done_at = [dv_after_bc2] * 4
    out_stage_free = [dv_after_bc2, dv_after_bc2]
    out_store_at = [0, 0]
    for ft in range(n_ft):
        sb = ft % 4
        sb2 = ft % 2
        P.emit("sync",
               lambda sy, ft=ft, sb=sb: sy.dma_start(
                   out=st_sb[sb][:, :], in_=r2_dram.ap()[ft * 128:(ft + 1) * 128, :]),
               waits=[(sR2w[0].sem, sR2w[0].n), (sR2w[1].sem, sR2w[1].n),
                      (dvM.sem, out_done_at[sb]), (acM.sem, bc2_done)],
               inc=sR2r[sb].bump())
        P.emit("vector",
               lambda v, sb=sb: v.tensor_sub(sq_sb[:, :], st_sb[sb][:, :], m_bc[:, :]),
               waits=[(sR2r[sb].sem, 16 * (ft // 4 + 1)), (acM.sem, bc2_done),
                      (dvM.sem, dvM.n)],
               inc=dvM.bump())
        out_done_at[sb] = dvM.n
        P.emit("vector",
               lambda v, sb2=sb2: v.tensor_mul(r2st[sb2][:, :], sq_sb[:, :], rstd_bc[:, :]),
               waits=[(dvM.sem, max(dvM.n, out_stage_free[sb2])),
                      (sOut[sb2].sem, out_store_at[sb2])],
               inc=dvM.bump())
        out_stage_free[sb2] = dvM.n
        P.emit("scalar",
               lambda s, ft=ft, sb2=sb2: s.dma_start(
                   out=outT[ft * 128:(ft + 1) * 128, :], in_=r2st[sb2][:, :]),
               waits=[(dvM.sem, dvM.n)], inc=sOut[sb2].bump())
        out_store_at[sb2] = sOut[sb2].n
    P.emit("sync", None, waits=[(sOut[0].sem, sOut[0].n), (sOut[1].sem, sOut[1].n)])

    nc_out = finish()
    ph3.close()
    ph34.close()
    pp3.close()
    stk.close()
    return nc_out


def prep_inputs(L0_attn_x, L0_attn_wq, L0_attn_wk, L0_attn_wv, L0_attn_wo,
                L0_ffn_weight_2bit, L0_ffn_weight_scale, L0_ffn_bias):
    x = np.asarray(L0_attn_x)
    S = x.shape[1]
    x2 = np.ascontiguousarray(x.reshape(S, DIM))
    xT = np.ascontiguousarray(x2.T)

    perm = np.concatenate([np.arange(0, HD, 2), np.arange(1, HD, 2)])
    wq = np.asarray(L0_attn_wq).reshape(DIM, NH, HD)[:, :, perm].reshape(DIM, NH * HD)
    wk = np.asarray(L0_attn_wk).reshape(DIM, NKV, HD)[:, :, perm].reshape(DIM, NKV * HD)
    wv = np.asarray(L0_attn_wv)
    wo = np.ascontiguousarray(np.asarray(L0_attn_wo))
    w2T = np.ascontiguousarray(np.asarray(L0_ffn_weight_2bit).T.astype(np.int32))
    bias2d = np.ascontiguousarray(np.asarray(L0_ffn_bias).reshape(DIM // 128, 128).T)
    wsc = np.ascontiguousarray(np.asarray(L0_ffn_weight_scale).reshape(1, 1).astype(np.float32))

    sc128 = np.float64(HD) ** -0.25
    freqs = 1.0 / THETA ** (np.arange(0, HD, 2, dtype=np.float64) / HD)
    ang = np.outer(freqs, np.arange(S, dtype=np.float64))
    C = (np.concatenate([np.cos(ang), np.cos(ang)], 0) * sc128).astype(np.float32)
    Ssg = (np.concatenate([-np.sin(ang), np.sin(ang)], 0) * sc128).astype(np.float32)
    kk = np.arange(128)[:, None]
    qq = np.arange(512)[None, :]
    masks = np.stack([(128 * s + kk <= qq) for s in range(4)]).astype(ml_dtypes.bfloat16)
    pswap = np.zeros((128, 128), np.float32)
    for i in range(128):
        pswap[i, (i + 64) % 128] = 1.0
    pswap = pswap.astype(ml_dtypes.bfloat16)
    identm = np.eye(128, dtype=ml_dtypes.bfloat16)

    To = S // NCORES
    maps = []
    for c in range(NCORES):
        maps.append({
            "xT": xT,
            "xoT": np.ascontiguousarray(xT[:, c * To:(c + 1) * To]),
            "wq": np.ascontiguousarray(wq[:, c * HPC * HD:(c + 1) * HPC * HD]),
            "wk": np.ascontiguousarray(wk[:, c * HD:(c + 1) * HD]),
            "wv": np.ascontiguousarray(np.asarray(wv)[:, c * HD:(c + 1) * HD]),
            "wo": wo,
            "w2T": w2T,
            "bias2d": bias2d,
            "wscale": wsc,
            "cosT": C,
            "sinT": Ssg,
            "masks": masks,
            "pswap": pswap,
            "ident": identm,
            "onescb": np.ones((128, 1), ml_dtypes.bfloat16),
            "onesrb": np.ones((1, 128), ml_dtypes.bfloat16),
            "onescf": np.ones((128, 1), np.float32),
            "onesrf": np.ones((1, 128), np.float32),
        })
    return maps


def kernel(L0_attn_x, L0_attn_wq, L0_attn_wk, L0_attn_wv, L0_attn_wo,
           L0_ffn_weight_2bit, L0_ffn_weight_scale, L0_ffn_bias):
    x = np.asarray(L0_attn_x)
    S = x.shape[1]
    maps = prep_inputs(L0_attn_x, L0_attn_wq, L0_attn_wk, L0_attn_wv, L0_attn_wo,
                       L0_ffn_weight_2bit, L0_ffn_weight_scale, L0_ffn_bias)
    nc = build_program(S)
    with allow_ici_all_to_all():
        res = bass_utils.run_bass_kernel_spmd(nc, maps, core_ids=list(range(NCORES)))
    outs = [np.asarray(r["outT"]).T for r in res.results]
    return np.ascontiguousarray(np.concatenate(outs, axis=0)).reshape(1, S, DIM).astype(np.float32)


# version 06
# speedup vs baseline: 1.1223x; 1.0474x over previous
import contextlib
import numpy as np
import ml_dtypes

import concourse.bass as bass
import concourse.mybir as mybir
import concourse.bass_utils as bass_utils
from concourse.aot_env import allow_ici_all_to_all

F32 = mybir.dt.float32
BF16 = mybir.dt.bfloat16
I32 = mybir.dt.int32
AF = mybir.ActivationFunctionType
ALU = mybir.AluOpType
AX = mybir.AxisListType

DIM = 4096
NH = 32
NKV = 8
HD = 128
NCORES = 8
HPC = 4
THETA = 10000.0

ENGS = ("tensor", "vector", "scalar", "gpsimd", "sync")


class Ctr:

    def __init__(self, stack, nc, name, step=1):
        self.sem = stack.enter_context(nc.semaphore(name))
        self.n = 0
        self.step = step

    def bump(self):
        self.n += self.step
        return (self.sem, self.step)


class Prog:
    def __init__(self):
        self.ops = {e: [] for e in ENGS}

    def emit(self, eng, fn=None, waits=(), inc=None):
        self.ops[eng].append((fn, tuple((s, v) for (s, v) in waits if v > 0), inc))


def build_program(S, upto='full', p1_tgs=None, p1_lvl=6):
    To = S // NCORES
    NTG = S // 512
    NQC = S // 512
    NTH = (To + 511) // 512
    n_ft = DIM // 128

    nc = bass.Bass("TRN2", target_bir_lowering=False)

    xT = nc.dram_tensor("xT", [DIM, S], F32, kind="ExternalInput").ap()
    xoT = nc.dram_tensor("xoT", [DIM, To], F32, kind="ExternalInput").ap()
    wq = nc.dram_tensor("wq", [DIM, HPC * HD], F32, kind="ExternalInput").ap()
    wk = nc.dram_tensor("wk", [DIM, HD], F32, kind="ExternalInput").ap()
    wv = nc.dram_tensor("wv", [DIM, HD], F32, kind="ExternalInput").ap()
    wo = nc.dram_tensor("wo", [DIM, DIM], F32, kind="ExternalInput").ap()
    w2T = nc.dram_tensor("w2T", [DIM, DIM], I32, kind="ExternalInput").ap()
    bias2d = nc.dram_tensor("bias2d", [128, n_ft], F32, kind="ExternalInput").ap()
    wscale = nc.dram_tensor("wscale", [1, 1], F32, kind="ExternalInput").ap()
    cosT = nc.dram_tensor("cosT", [128, S], F32, kind="ExternalInput").ap()
    sinT = nc.dram_tensor("sinT", [128, S], F32, kind="ExternalInput").ap()
    masks = nc.dram_tensor("masks", [128, 4 * 512], BF16, kind="ExternalInput").ap()
    pswap = nc.dram_tensor("pswap", [128, 128], BF16, kind="ExternalInput").ap()
    ident = nc.dram_tensor("ident", [128, 128], BF16, kind="ExternalInput").ap()
    onescb = nc.dram_tensor("onescb", [128, 1], BF16, kind="ExternalInput").ap()
    onesrb = nc.dram_tensor("onesrb", [1, 128], BF16, kind="ExternalInput").ap()
    onescf = nc.dram_tensor("onescf", [128, 1], F32, kind="ExternalInput").ap()
    onesrf = nc.dram_tensor("onesrf", [1, 128], F32, kind="ExternalInput").ap()
    ones4 = nc.dram_tensor("ones4", [128, 1], F32, kind="ExternalInput").ap()
    outT = nc.dram_tensor("outT", [DIM, To], F32, kind="ExternalOutput").ap()

    a2a_in = nc.dram_tensor("a2a_in", [NCORES, HPC * HD, To], BF16)
    a2a_out = nc.dram_tensor("a2a_out", [NCORES, HPC * HD, To], BF16)
    s_dram = nc.dram_tensor("s_dram", [DIM, To], F32)
    r2_dram = nc.dram_tensor("r2_dram", [DIM, To], F32)

    stk = contextlib.ExitStack()

    sW = Ctr(stk, nc, "sW", 16)
    sX = [Ctr(stk, nc, "sX%d" % i, 16) for i in range(8)]
    sCS = [Ctr(stk, nc, "sCS%d" % i, 16) for i in range(2)]
    sWo = [Ctr(stk, nc, "sWo%d" % i, 16) for i in range(2)]
    sFf = [Ctr(stk, nc, "sFf%d" % i, 16) for i in range(3)]
    sXo = [Ctr(stk, nc, "sXo%d" % i, 16) for i in range(2)]
    sSw = [Ctr(stk, nc, "sSw%d" % i, 16) for i in range(2)]
    sSr = [Ctr(stk, nc, "sSr%d" % i, 16) for i in range(4)]
    sS2 = [Ctr(stk, nc, "sS2%d" % i, 16) for i in range(4)]
    sR2w = [Ctr(stk, nc, "sR2w%d" % i, 16) for i in range(2)]
    sR2r = [Ctr(stk, nc, "sR2r%d" % i, 16) for i in range(4)]
    sA2 = [Ctr(stk, nc, "sA2_%d" % i, 16) for i in range(8)]
    sAo = Ctr(stk, nc, "sAo", 16)
    sOut = [Ctr(stk, nc, "sOut%d" % i, 16) for i in range(2)]
    ccs = Ctr(stk, nc, "ccs", 1)
    peM = Ctr(stk, nc, "peM", 1)
    acM = Ctr(stk, nc, "acM", 1)
    dvM = Ctr(stk, nc, "dvM", 1)

    P = Prog()

    masks_sb = stk.enter_context(nc.sbuf_tensor("masks_sb", [128, 4 * 512], BF16))
    pswap_sb = stk.enter_context(nc.sbuf_tensor("pswap_sb", [128, 128], BF16))
    ident_sb = stk.enter_context(nc.sbuf_tensor("ident_sb", [128, 128], BF16))
    onescb_sb = stk.enter_context(nc.sbuf_tensor("onescb_sb", [128, 1], BF16))
    onesrb_sb = stk.enter_context(nc.sbuf_tensor("onesrb_sb", [1, 128], BF16))
    onescf_sb = stk.enter_context(nc.sbuf_tensor("onescf_sb", [128, 1], F32))
    onesrf_sb = stk.enter_context(nc.sbuf_tensor("onesrf_sb", [1, 128], F32))
    ones4_sb = stk.enter_context(nc.sbuf_tensor("ones4_sb", [128, 1], F32))
    bias2d_sb = stk.enter_context(nc.sbuf_tensor("bias2d_sb", [128, n_ft], F32))
    wscale_sb = stk.enter_context(nc.sbuf_tensor("wscale_sb", [1, 1], F32))
    cmax = stk.enter_context(nc.sbuf_tensor("cmax", [1, 4 * NQC], F32))
    kmaxs = stk.enter_context(nc.sbuf_tensor("kmaxs", [1, NTG + 2], F32))
    cneg = stk.enter_context(nc.sbuf_tensor("cneg", [1, 4 * NQC], BF16))
    rowbuf = stk.enter_context(nc.sbuf_tensor("rowbuf", [1, 3 * 1024], F32))
    consts = stk.enter_context(nc.sbuf_tensor("consts", [1, 8], F32))

    xo_sb = [stk.enter_context(nc.sbuf_tensor("xo_sb%d" % i, [128, To], F32)) for i in range(2)]

    ph12 = contextlib.ExitStack()
    q4 = ph12.enter_context(nc.sbuf_tensor("q4", [128, HPC, S], BF16))
    KT = ph12.enter_context(nc.sbuf_tensor("KT", [128, S], BF16))
    Vtok = ph12.enter_context(nc.sbuf_tensor("Vtok", [128, S], BF16))

    nc.const_aps.register(F32, 1e-5, consts[0:1, 0:1])
    nc.const_aps.register(F32, 0.5, consts[0:1, 1:2])
    nc.const_aps.register(F32, -0.5, consts[0:1, 2:3])
    nc.const_aps.register(F32, -1.0, consts[0:1, 3:4])

    P.emit("vector", lambda v: v.memset(consts[0:1, 0:1], 1e-5), inc=dvM.bump())
    P.emit("vector", lambda v: v.memset(consts[0:1, 1:2], 0.5), inc=dvM.bump())
    P.emit("vector", lambda v: v.memset(consts[0:1, 2:3], -0.5), inc=dvM.bump())
    P.emit("vector", lambda v: v.memset(consts[0:1, 3:4], -1.0), inc=dvM.bump())
    consts_ready = dvM.n

    def g_load(dst, src):
        P.emit("gpsimd", lambda g, d=dst, s=src: g.dma_start(out=d, in_=s), inc=sW.bump())

    g_load(masks_sb[:, :], masks)
    g_load(pswap_sb[:, :], pswap)
    g_load(ident_sb[:, :], ident)
    g_load(onescb_sb[:, :], onescb)
    g_load(onesrb_sb[:, :], onesrb)
    g_load(onescf_sb[:, :], onescf)
    g_load(onesrf_sb[:, :], onesrf)
    g_load(ones4_sb[:, :], ones4)
    g_load(bias2d_sb[:, :], bias2d)
    g_load(wscale_sb[:, :], wscale)

    ph1 = contextlib.ExitStack()
    wq_sb = ph1.enter_context(nc.sbuf_tensor("wq_sb", [128, n_ft, HPC * HD], BF16))
    wk_sb = ph1.enter_context(nc.sbuf_tensor("wk_sb", [128, n_ft, HD], BF16))
    wv_sb = ph1.enter_context(nc.sbuf_tensor("wv_sb", [128, n_ft, HD], BF16))
    xbuf = [ph1.enter_context(nc.sbuf_tensor("xbuf%d" % i, [128, 512], BF16)) for i in range(8)]
    csbuf = [ph1.enter_context(nc.sbuf_tensor("csbuf%d" % i, [128, 2, 512], F32)) for i in range(2)]
    qkstage = ph1.enter_context(nc.sbuf_tensor("qkstage", [128, 512], BF16))
    vstage = ph1.enter_context(nc.sbuf_tensor("vstage", [128, 512], BF16))
    sqstage = ph1.enter_context(nc.sbuf_tensor("sqstage", [128, 512], BF16))
    t1b = ph1.enter_context(nc.sbuf_tensor("t1b", [128, 512], F32))
    t2b = ph1.enter_context(nc.sbuf_tensor("t2b", [128, 512], F32))

    pp1 = contextlib.ExitStack()
    q_ps = pp1.enter_context(nc.psum_tensor("q_ps", [128, 4 * 512], F32))
    kv_ps = pp1.enter_context(nc.psum_tensor("kv_ps", [128, 2 * 512], F32))
    sw_ps = pp1.enter_context(nc.psum_tensor("sw_ps", [128, 512], F32))
    vt_ps = pp1.enter_context(nc.psum_tensor("vt_ps", [128, 128], BF16))

    g_load(wq_sb[:, :, :], wq.rearrange("(i p) f -> p i f", p=128))
    g_load(wk_sb[:, :, :], wk.rearrange("(i p) f -> p i f", p=128))
    g_load(wv_sb[:, :, :], wv.rearrange("(i p) f -> p i f", p=128))
    w_loaded = sW.n

    total_tiles = NTG * n_ft
    xt_consumed_at = []
    cs_consumed_at = []

    if upto == "p0":
        sT = Ctr(stk, nc, "sT", 16)
        P.emit("gpsimd",
               lambda g: g.dma_start(out=outT[0:128, 0:512], in_=wq_sb[:, 0, :]),
               waits=[(sW.sem, sW.n)], inc=sT.bump())
        P.emit("gpsimd", None, waits=[(sT.sem, sT.n)])
        def finish0():
            def runner(eng_ops):
                def _run(eng):
                    for fn, waits, inc in eng_ops:
                        if fn is None:
                            for (sem, val) in waits:
                                eng.wait_ge(sem, val)
                            continue
                        if len(waits) > 1:
                            for (sem, val) in waits[:-1]:
                                eng.wait_ge(sem, val)
                        inst = fn(eng)
                        if waits:
                            inst.wait_op(waits[-1][0], waits[-1][1], "sem-ge")
                        if inc is not None:
                            inst.then_inc(inc[0], inc[1])
                return _run
            with nc.Block() as block:
                block.tensor(runner(P.ops["tensor"]))
                block.vector(runner(P.ops["vector"]))
                block.scalar(runner(P.ops["scalar"]))
                block.gpsimd(runner(P.ops["gpsimd"]))
                block.sync(runner(P.ops["sync"]))
            return nc
        nc_out = finish0()
        ph1.close(); pp1.close(); ph12.close(); stk.close()
        return nc_out

    def emit_xt_load(j):
        tg, d = divmod(j, n_ft)
        b = j % 8
        waits = [(sW.sem, w_loaded)]
        if j >= 8:
            waits.append((peM.sem, xt_consumed_at[j - 8]))
        P.emit("gpsimd",
               lambda g, b=b, tg=tg, d=d: g.dma_start(
                   out=xbuf[b][:, :], in_=xT[d * 128:(d + 1) * 128, tg * 512:(tg + 1) * 512]),
               waits=waits, inc=sX[b].bump())

    def emit_cs_load(tg):
        b = tg % 2
        waits = []
        if tg >= 2:
            waits.append((dvM.sem, cs_consumed_at[tg - 2]))
        P.emit("sync",
               lambda sy, b=b, tg=tg: sy.dma_start(
                   out=csbuf[b][:, 0, :], in_=cosT[:, tg * 512:(tg + 1) * 512]),
               waits=waits, inc=sCS[b].bump())
        P.emit("sync",
               lambda sy, b=b, tg=tg: sy.dma_start(
                   out=csbuf[b][:, 1, :], in_=sinT[:, tg * 512:(tg + 1) * 512]),
               inc=sCS[b].bump())

    NTG_RUN = NTG if p1_tgs is None else min(NTG, p1_tgs)
    total_tiles = NTG_RUN * n_ft
    xt_next = 0
    for j in range(min(8, total_tiles)):
        emit_xt_load(j)
        xt_next += 1
    emit_cs_load(0)
    if NTG > 1:
        emit_cs_load(1)

    prev_ep_act = 0
    prev_ep_dve = 0

    for tg in range(NTG_RUN):
        for d in range(n_ft):
            j = tg * n_ft + d
            b = j % 8
            waits = [(sX[b].sem, 16 * (j // 8 + 1))]
            if d == 0:
                waits.append((acM.sem, prev_ep_act))
                waits.append((dvM.sem, prev_ep_dve))
            st = (d == 0)
            sp = (d == n_ft - 1)
            for jj in range(HPC):
                P.emit("tensor",
                       lambda t, d=d, jj=jj, b=b, st=st, sp=sp: t.matmul(
                           q_ps[:, jj * 512:jj * 512 + 512],
                           wq_sb[:, d, jj * 128:jj * 128 + 128],
                           xbuf[b][:, :], start=st, stop=sp),
                       waits=waits if jj == 0 else ())
            P.emit("tensor",
                   lambda t, d=d, b=b, st=st, sp=sp: t.matmul(
                       kv_ps[:, 0:512], wk_sb[:, d, :], xbuf[b][:, :], start=st, stop=sp))
            P.emit("tensor",
                   lambda t, d=d, b=b, st=st, sp=sp: t.matmul(
                       kv_ps[:, 512:1024], wv_sb[:, d, :], xbuf[b][:, :], start=st, stop=sp),
                   inc=peM.bump())
            xt_consumed_at.append(peM.n)
            if xt_next < total_tiles:
                emit_xt_load(xt_next)
                xt_next += 1
        dloop_done = peM.n

        csb = tg % 2
        cs_ready = 32 * (tg // 2 + 1)
        tgs = slice(tg * 512, (tg + 1) * 512)
        prev_t2 = 0
        prev_red = 0
        prev_norm_pe = 0
        for hx in range(HPC + 1):
            if p1_lvl < 1:
                break
            is_q = hx < HPC
            src_ps = q_ps[:, hx * 512:hx * 512 + 512] if is_q else kv_ps[:, 0:512]
            P.emit("scalar",
                   lambda s, src=src_ps: s.copy(qkstage[:, :], src),
                   waits=[(peM.sem, dloop_done), (dvM.sem, prev_t2), (acM.sem, acM.n)],
                   inc=acM.bump())
            evac_done = acM.n
            if p1_lvl < 2:
                continue
            P.emit("tensor",
                   lambda t: t.matmul(sw_ps[:, :], pswap_sb[:, :], qkstage[:, :],
                                      start=True, stop=True),
                   waits=[(acM.sem, evac_done), (dvM.sem, max(prev_t2, prev_red))],
                   inc=peM.bump())
            swap_done = peM.n
            if p1_lvl < 3:
                continue
            P.emit("vector",
                   lambda v, src=src_ps, csb=csb: v.tensor_mul(t1b[:, :], src, csbuf[csb][:, 0, :]),
                   waits=[(sCS[csb].sem, cs_ready), (peM.sem, dloop_done), (dvM.sem, dvM.n),
                          (acM.sem, evac_done)])
            P.emit("vector",
                   lambda v, csb=csb: v.tensor_mul(t2b[:, :], sw_ps[:, :], csbuf[csb][:, 1, :]),
                   waits=[(peM.sem, swap_done), (dvM.sem, dvM.n)], inc=dvM.bump())
            prev_t2 = dvM.n
            dst = q4[:, hx, tgs] if is_q else KT[:, tgs]
            P.emit("vector",
                   lambda v, dst=dst: v.tensor_add(dst, t1b[:, :], t2b[:, :]),
                   waits=[(dvM.sem, dvM.n)], inc=dvM.bump())
            rope_done = dvM.n
            if p1_lvl < 4:
                continue
            P.emit("scalar",
                   lambda s, dst=dst: s.activation(sqstage[:, :], dst, AF.Square),
                   waits=[(dvM.sem, rope_done), (peM.sem, prev_norm_pe), (acM.sem, acM.n)],
                   inc=acM.bump())
            sq_done = acM.n
            P.emit("tensor",
                   lambda t: t.matmul(sw_ps[0:1, :], onescb_sb[:, :], sqstage[:, :],
                                      start=True, stop=True),
                   waits=[(acM.sem, sq_done), (dvM.sem, prev_t2)],
                   inc=peM.bump())
            prev_norm_pe = peM.n
            if is_q:
                dst_red = cmax[0:1, (tg * 4 + hx):(tg * 4 + hx + 1)]
            else:
                dst_red = kmaxs[0:1, tg:tg + 1]
            P.emit("vector",
                   lambda v, dst_red=dst_red: v.reduce_max(dst_red, sw_ps[0:1, :], axis=AX.X),
                   waits=[(peM.sem, prev_norm_pe), (dvM.sem, dvM.n)], inc=dvM.bump())
            prev_red = dvM.n
        if p1_lvl >= 5:
            P.emit("scalar",
                   lambda s: s.copy(vstage[:, :], kv_ps[:, 512:1024]),
                   waits=[(peM.sem, dloop_done), (acM.sem, acM.n)], inc=acM.bump())
            vevac_done = acM.n
            prev_vt_copy = 0
            for i in range(4):
                P.emit("tensor",
                       lambda t, i=i: t.transpose(vt_ps[:, :], vstage[:, i * 128:i * 128 + 128],
                                                  ident_sb[:, :]),
                       waits=[(acM.sem, vevac_done), (dvM.sem, prev_vt_copy)],
                       inc=peM.bump())
                tr_done = peM.n
                P.emit("vector",
                       lambda v, i=i, tg=tg: v.tensor_copy(
                           Vtok[:, (tg * 4 + i) * 128:(tg * 4 + i) * 128 + 128], vt_ps[:, :]),
                       waits=[(peM.sem, tr_done), (dvM.sem, dvM.n)], inc=dvM.bump())
                prev_vt_copy = dvM.n
        cs_consumed_at.append(dvM.n)
        if tg + 2 < NTG:
            emit_cs_load(tg + 2)
        prev_ep_act = acM.n
        prev_ep_dve = dvM.n

    phase1_dve = dvM.n
    phase1_act = acM.n

    if p1_tgs is not None:
        sT = Ctr(stk, nc, "sT", 16)
        P.emit("gpsimd",
               lambda g: g.dma_start(out=outT[0:128, :], in_=KT[:, 0:To]),
               waits=[(dvM.sem, phase1_dve), (acM.sem, phase1_act)], inc=sT.bump())
        P.emit("gpsimd",
               lambda g: g.dma_start(out=outT[128:256, :], in_=q4[:, 0, 0:To]),
               inc=sT.bump())
        P.emit("gpsimd",
               lambda g: g.dma_start(out=outT[256:384, :], in_=Vtok[:, 0:To]),
               inc=sT.bump())
        P.emit("gpsimd", None, waits=[(sT.sem, sT.n)])
        def finish1():
            def runner(eng_ops):
                def _run(eng):
                    for fn, waits, inc in eng_ops:
                        if fn is None:
                            for (sem, val) in waits:
                                eng.wait_ge(sem, val)
                            continue
                        if len(waits) > 1:
                            for (sem, val) in waits[:-1]:
                                eng.wait_ge(sem, val)
                        inst = fn(eng)
                        if waits:
                            inst.wait_op(waits[-1][0], waits[-1][1], "sem-ge")
                        if inc is not None:
                            inst.then_inc(inc[0], inc[1])
                return _run
            with nc.Block() as block:
                block.tensor(runner(P.ops["tensor"]))
                block.vector(runner(P.ops["vector"]))
                block.scalar(runner(P.ops["scalar"]))
                block.gpsimd(runner(P.ops["gpsimd"]))
                block.sync(runner(P.ops["sync"]))
            return nc
        nc_out = finish1()
        ph1.close(); pp1.close(); ph12.close(); stk.close()
        return nc_out

    P.emit("vector",
           lambda v: v.reduce_max(kmaxs[0:1, NTG:NTG + 1], kmaxs[0:1, 0:NTG], axis=AX.X),
           waits=[(dvM.sem, phase1_dve)], inc=dvM.bump())
    P.emit("vector",
           lambda v: v.tensor_scalar(rowbuf[0:1, 0:4 * NQC], cmax[0:1, :],
                                     kmaxs[0:1, NTG:NTG + 1], 1.21, ALU.mult, ALU.mult),
           waits=[(dvM.sem, dvM.n)], inc=dvM.bump())
    u_done = dvM.n
    P.emit("scalar",
           lambda s: s.activation(rowbuf[0:1, 1024:1024 + 4 * NQC], rowbuf[0:1, 0:4 * NQC],
                                  AF.Ln, bias=1e-5),
           waits=[(dvM.sem, u_done), (sW.sem, sW.n), (dvM.sem, consts_ready),
                  (acM.sem, acM.n)],
           inc=acM.bump())
    P.emit("scalar",
           lambda s: s.activation(rowbuf[0:1, 2048:2048 + 4 * NQC],
                                  rowbuf[0:1, 1024:1024 + 4 * NQC], AF.Exp, scale=0.5),
           waits=[(acM.sem, acM.n)], inc=acM.bump())
    c_done = acM.n
    P.emit("vector",
           lambda v: v.tensor_scalar_mul(cneg[0:1, :], rowbuf[0:1, 2048:2048 + 4 * NQC], -1.0),
           waits=[(acM.sem, c_done), (dvM.sem, dvM.n)], inc=dvM.bump())
    cneg_done = dvM.n

    dbg_tail = []

    def finish():
        def runner(eng_ops):
            def _run(eng):
                for fn, waits, inc in eng_ops:
                    if fn is None:
                        for (sem, val) in waits:
                            eng.wait_ge(sem, val)
                        continue
                    if len(waits) > 1:
                        for (sem, val) in waits[:-1]:
                            eng.wait_ge(sem, val)
                    inst = fn(eng)
                    if waits:
                        inst.wait_op(waits[-1][0], waits[-1][1], "sem-ge")
                    if inc is not None:
                        inst.then_inc(inc[0], inc[1])
            return _run

        with allow_ici_all_to_all():
            with nc.Block() as block:
                block.tensor(runner(P.ops["tensor"]))
                block.vector(runner(P.ops["vector"]))
                block.scalar(runner(P.ops["scalar"]))
                block.gpsimd(runner(P.ops["gpsimd"]))
                block.sync(runner(P.ops["sync"]))
        return nc

    if upto == "p1":
        sT = Ctr(stk, nc, "sT", 16)
        P.emit("gpsimd",
               lambda g: g.dma_start(out=outT[0:128, :], in_=KT[:, 0:To]),
               waits=[(dvM.sem, cneg_done)], inc=sT.bump())
        P.emit("gpsimd",
               lambda g: g.dma_start(out=outT[128:256, :], in_=q4[:, 0, 0:To]),
               inc=sT.bump())
        P.emit("gpsimd",
               lambda g: g.dma_start(out=outT[256:384, :], in_=Vtok[:, 0:To]),
               inc=sT.bump())
        P.emit("gpsimd", None, waits=[(sT.sem, sT.n)])
        nc_out = finish()
        ph1.close(); pp1.close(); ph12.close(); stk.close()
        return nc_out

    ph1.close()
    pp1.close()

    ph2 = contextlib.ExitStack()
    pt = [[ph2.enter_context(nc.sbuf_tensor("pt%d_%d" % (hp, par), [128, 2, 1024], BF16))
           for par in range(2)] for hp in range(2)]
    bias_sb = ph2.enter_context(nc.sbuf_tensor("biasc_sb", [128, 4], F32))
    o_sb = [ph2.enter_context(nc.sbuf_tensor("o_sb%d" % i, [128, 512], F32)) for i in range(2)]
    l_sb = [ph2.enter_context(nc.sbuf_tensor("l_sb%d" % i, [128, 512], F32)) for i in range(2)]
    linv_row = [ph2.enter_context(nc.sbuf_tensor("linv_row%d" % i, [1, 512], F32))
                for i in range(2)]
    linv_bc = [ph2.enter_context(nc.sbuf_tensor("linv_bc%d" % i, [128, 512], F32))
               for i in range(2)]
    atst = [ph2.enter_context(nc.sbuf_tensor("atst%d" % i, [128, 512], BF16)) for i in range(8)]

    pp2 = contextlib.ExitStack()
    s_ps = [pp2.enter_context(nc.psum_tensor("s_ps%d" % i, [128, 1024], F32)) for i in range(2)]
    o_ps = [pp2.enter_context(nc.psum_tensor("o_ps%d" % i, [128, 512], F32)) for i in range(2)]
    l_ps = [pp2.enter_context(nc.psum_tensor("l_ps%d" % i, [128, 512], F32)) for i in range(2)]

    pair_idx = 0
    pending_norm = []
    l_free_at = [0, 0]
    l_zero_at = [0, 0]
    linv_pe_guard = [0, 0]
    exp_half_at = {}
    group_l_done = {}
    ghp_count = [0, 0]
    atst_stored_at = [0] * 8
    o_free_at = [0, 0]
    o_sb_free = [0, 0]
    linv_bc_free = [0, 0]
    bank_guard = [0, 0]

    for qc in range(NQC):
        pro_waits = [(acM.sem, acM.n), (dvM.sem, cneg_done)]
        if qc == 0:
            pro_waits += [(dvM.sem, max(phase1_dve, cneg_done)), (acM.sem, phase1_act)]
        P.emit("tensor",
               lambda t, qc=qc: t.matmul(s_ps[0][:, 0:4], onesrb_sb[:, :],
                                         cneg[0:1, 4 * qc:4 * qc + 4], start=True, stop=True),
               waits=pro_waits, inc=peM.bump())
        bias_mm = peM.n
        P.emit("scalar",
               lambda s: s.copy(bias_sb[:, :], s_ps[0][:, 0:4]),
               waits=[(peM.sem, bias_mm), (acM.sem, acM.n)], inc=acM.bump())
        bias_ready = acM.n
        bank_guard[0] = bias_ready

        for pas in range(2):
            for hp in range(2):
                P.emit("vector",
                       lambda v, hp=hp: v.memset(l_ps[hp][:, :], 0.0),
                       waits=[(acM.sem, l_free_at[hp]), (dvM.sem, dvM.n)],
                       inc=dvM.bump())
                l_zero_at[hp] = dvM.n
            for kc in range(qc + 1):
                diag = (kc == qc)
                for hp in range(2):
                    h = pas * 2 + hp
                    gpar = ghp_count[hp] % 2
                    pair_avs = []
                    for pj in range(2):
                        pg = pair_idx
                        bank = pg % 2
                        for sj in range(2):
                            ks = pj * 2 + sj
                            w = []
                            if sj == 0 and pg >= 2:
                                w.append((acM.sem, max(exp_half_at[(pg - 2, 1)],
                                                       bank_guard[bank])))
                            elif sj == 0:
                                w.append((acM.sem, bank_guard[bank]))
                            P.emit("tensor",
                                   lambda t, kc=kc, ks=ks, h=h, qc=qc, bank=bank, sj=sj:
                                   t.matmul(s_ps[bank][:, sj * 512:sj * 512 + 512],
                                            KT[:, kc * 512 + ks * 128:kc * 512 + ks * 128 + 128],
                                            q4[:, h, qc * 512:qc * 512 + 512],
                                            start=True, stop=True),
                                   waits=w, inc=peM.bump() if sj == 1 else None)
                        sdone = peM.n
                        ew = [(peM.sem, sdone), (acM.sem, acM.n)]
                        if (hp, gpar) in group_l_done:
                            ew.append((peM.sem, group_l_done[(hp, gpar)]))
                        P.emit("scalar",
                               lambda s, bank=bank, hp=hp, gpar=gpar, pj=pj,
                               hcol=pas * 2 + hp:
                               s.activation(pt[hp][gpar][:, pj, :], s_ps[bank][:, :],
                                            AF.Exp, bias=bias_sb[:, hcol:hcol + 1]),
                               waits=ew, inc=acM.bump())
                        exp_half_at[(pg, 1)] = acM.n
                        avw = (acM.sem, acM.n)
                        if diag:
                            P.emit("vector",
                                   lambda v, hp=hp, gpar=gpar, pj=pj:
                                   v.tensor_mul(pt[hp][gpar][:, pj, :],
                                                pt[hp][gpar][:, pj, :],
                                                masks_sb[:, pj * 1024:pj * 1024 + 1024]),
                                   waits=[(acM.sem, acM.n), (dvM.sem, dvM.n)],
                                   inc=dvM.bump())
                            avw = (dvM.sem, dvM.n)
                        pair_avs.append((pj, avw))
                        pair_idx += 1
                    for (ppj, avw) in pair_avs:
                        st0 = (kc == 0 and ppj == 0)
                        sp0 = (diag and ppj == 1)
                        for sj in range(2):
                            ks = ppj * 2 + sj
                            kslice = slice((kc * 4 + ks) * 128, (kc * 4 + ks) * 128 + 128)
                            avkw = []
                            if sj == 0:
                                avkw.append(avw)
                            if st0 and sj == 0:
                                avkw.append((dvM.sem, o_free_at[hp]))
                            P.emit("tensor",
                                   lambda t, gpar=gpar, ppj=ppj, sj=sj, kslice=kslice, hp=hp,
                                   st=(st0 and sj == 0), sp=(sp0 and sj == 1):
                                   t.matmul(o_ps[hp][:, :], Vtok[:, kslice],
                                            pt[hp][gpar][:, ppj, sj * 512:sj * 512 + 512],
                                            start=st, stop=sp),
                                   waits=avkw)
                    for ksl in range(4):
                        ppj, sj = divmod(ksl, 2)
                        lw = []
                        if ksl == 0 and kc == 0:
                            lw.append((dvM.sem, l_zero_at[hp]))
                        P.emit("tensor",
                               lambda t, gpar=gpar, ppj=ppj, sj=sj, hp=hp, ksl=ksl, sp=diag:
                               t.matmul(l_ps[hp][32 * ksl:32 * ksl + 1, :], onescb_sb[:, :],
                                        pt[hp][gpar][:, ppj, sj * 512:sj * 512 + 512],
                                        start=False, stop=sp, skip_group_check=True,
                                        tile_position=(0, 32 * ksl)),
                               waits=lw,
                               inc=peM.bump() if ksl == 3 else None)
                    group_l_done[(hp, ghp_count[hp] % 2)] = peM.n
                    ghp_count[hp] += 1
            for fn in pending_norm:
                fn()
            pending_norm = []
            for hp in range(2):
                h = pas * 2 + hp
                gl = group_l_done[(hp, (ghp_count[hp] - 1) % 2)]
                P.emit("vector",
                       lambda v, hp=hp: v.tensor_copy(l_sb[hp][:, :], l_ps[hp][:, :]),
                       waits=[(peM.sem, gl), (dvM.sem, dvM.n)], inc=dvM.bump())
                levac_done = dvM.n
                P.emit("tensor",
                       lambda t, hp=hp: t.matmul(l_ps[hp][0:1, :], ones4_sb[:, :],
                                                 l_sb[hp][:, :], start=True, stop=True),
                       waits=[(dvM.sem, levac_done)], inc=peM.bump())
                lsum_done = peM.n
                P.emit("scalar",
                       lambda s, hp=hp: s.activation(linv_row[hp][:, :], l_ps[hp][0:1, :],
                                                     AF.Ln),
                       waits=[(peM.sem, max(lsum_done, linv_pe_guard[hp])), (acM.sem, acM.n)],
                       inc=acM.bump())
                ln_done = acM.n
                P.emit("scalar",
                       lambda s, hp=hp: s.activation(linv_row[hp][:, :], linv_row[hp][:, :],
                                                     AF.Exp, scale=-1.0),
                       waits=[(acM.sem, acM.n)], inc=acM.bump())
                linv_done = acM.n
                P.emit("vector",
                       lambda v, hp=hp: v.tensor_copy(o_sb[hp][:, :], o_ps[hp][:, :]),
                       waits=[(peM.sem, gl), (dvM.sem, max(dvM.n, o_sb_free[hp]))],
                       inc=dvM.bump())
                o_free_at[hp] = dvM.n
                l_free_at[hp] = ln_done
                def _norm(hp=hp, h=h, qc=qc, linv_done=linv_done):
                    P.emit("tensor",
                           lambda t, hp=hp: t.matmul(s_ps[hp][:, 0:512], onesrf_sb[:, :],
                                                     linv_row[hp][:, :], start=True, stop=True),
                           waits=[(acM.sem, max(linv_done, acM.n))], inc=peM.bump())
                    bc_done = peM.n
                    linv_pe_guard[hp] = bc_done
                    P.emit("scalar",
                           lambda s, hp=hp: s.copy(linv_bc[hp][:, :], s_ps[hp][:, 0:512]),
                           waits=[(peM.sem, bc_done), (acM.sem, acM.n),
                                  (dvM.sem, linv_bc_free[hp])],
                           inc=acM.bump())
                    bank_guard[hp] = acM.n
                    bcev_done = acM.n
                    sti = (qc * 4 + h) % 8
                    P.emit("vector",
                           lambda v, hp=hp, sti=sti: v.tensor_mul(atst[sti][:, :],
                                                                  o_sb[hp][:, :],
                                                                  linv_bc[hp][:, :]),
                           waits=[(acM.sem, bcev_done), (sA2[sti].sem, atst_stored_at[sti]),
                                  (dvM.sem, dvM.n)],
                           inc=dvM.bump())
                    o_sb_free[hp] = dvM.n
                    linv_bc_free[hp] = dvM.n
                    mul_done = dvM.n
                    d_dest = qc // NTH
                    tof = (qc % NTH) * 512
                    P.emit("sync",
                           lambda sy, h=h, sti=sti, d_dest=d_dest, tof=tof: sy.dma_start(
                               out=a2a_in.ap()[d_dest, h * 128:h * 128 + 128, tof:tof + 512],
                               in_=atst[sti][:, :]),
                           waits=[(dvM.sem, mul_done)], inc=sA2[sti].bump())
                    atst_stored_at[sti] = sA2[sti].n
                pending_norm.append(_norm)

    for fn in pending_norm:
        fn()
    pending_norm = []

    P.emit("gpsimd",
           lambda g: g.collective_compute(
               "AllToAll", ALU.bypass, replica_groups=[list(range(NCORES))],
               ins=[a2a_in.ap().opt()], outs=[a2a_out.ap().opt()]),
           waits=[(s.sem, s.n) for s in sA2], inc=ccs.bump())

    if upto == "p2":
        sT = Ctr(stk, nc, "sT", 16)
        P.emit("gpsimd",
               lambda g: g.dma_start(out=outT[:, :],
                                     in_=a2a_out.ap().rearrange("r f t -> (r f) t")),
               waits=[(ccs.sem, 1)], inc=sT.bump())
        P.emit("gpsimd", None, waits=[(sT.sem, sT.n)])
        nc_out = finish()
        ph2.close(); pp2.close(); ph12.close(); stk.close()
        return nc_out

    ph2.close()
    pp2.close()
    ph12.close()

    ph34 = contextlib.ExitStack()
    st_sb = [ph34.enter_context(nc.sbuf_tensor("st_sb%d" % i, [128, To], F32)) for i in range(4)]
    r2st = [ph34.enter_context(nc.sbuf_tensor("r2st%d" % i, [128, To], F32)) for i in range(2)]
    sq_sb = ph34.enter_context(nc.sbuf_tensor("sq_sb", [128, To], F32))
    acc_s = ph34.enter_context(nc.sbuf_tensor("acc_s", [128, To], F32))
    acc_q = ph34.enter_context(nc.sbuf_tensor("acc_q", [128, To], F32))
    m_bc = ph34.enter_context(nc.sbuf_tensor("m_bc", [128, To], F32))
    rstd_bc = ph34.enter_context(nc.sbuf_tensor("rstd_bc", [128, To], F32))
    sc_bc = ph34.enter_context(nc.sbuf_tensor("sc_bc", [128, To], F32))

    ph25 = contextlib.ExitStack()
    at_own = ph25.enter_context(nc.sbuf_tensor("at_own", [128, n_ft, To], BF16))
    wo_sb = [ph25.enter_context(nc.sbuf_tensor("wo_sb%d" % i, [128, n_ft, 256], BF16))
             for i in range(2)]

    pp3 = contextlib.ExitStack()
    psA = [pp3.enter_context(nc.psum_tensor("psA%d" % i, [128, To], F32)) for i in range(4)]

    P.emit("sync",
           lambda sy: sy.dma_start(out=at_own[:, :, :],
                                   in_=a2a_out.ap().rearrange("r (i p) t -> p (r i) t", p=128)),
           waits=[(ccs.sem, 1)], inc=sAo.bump())
    at_own_ready = sAo.n

    n_grp = n_ft // 2
    wo_grp_free_at = [0, 0]
    wo_stripe_free_at = [0, 0]
    st_free_at = [0, 0]
    st_free_acm = [0, 0]
    xo_used_at = [0, 0]
    acc_chain = 0

    def emit_wo_stripe(g):
        b = g % 2
        w = [(ccs.sem, 1)] if g < 2 else []
        if g >= 2:
            w.append((peM.sem, wo_stripe_free_at[b]))
        P.emit("gpsimd",
               lambda gp, g=g, b=b: gp.dma_start(
                   out=wo_sb[b][:, :, :],
                   in_=wo[:, g * 256:(g + 1) * 256].rearrange("(i p) f -> p i f", p=128)),
               waits=w, inc=sWo[b].bump())

    emit_wo_stripe(0)
    emit_wo_stripe(1)
    for ft in range(min(2, n_ft)):
        P.emit("sync",
               lambda sy, ft=ft: sy.dma_start(
                   out=xo_sb[ft % 2][:, :], in_=xoT[ft * 128:(ft + 1) * 128, :]),
               inc=sXo[ft % 2].bump())

    for g in range(n_grp):
        b = g % 2
        mmw = [(sWo[b].sem, 16 * (g // 2 + 1)), (sAo.sem, at_own_ready)]
        if g >= 2:
            mmw.append((dvM.sem, wo_grp_free_at[b]))
        for d in range(n_ft):
            for jj in range(2):
                for th in range(NTH):
                    P.emit("tensor",
                           lambda t, b=b, d=d, jj=jj, th=th, st=(d == 0), sp=(d == n_ft - 1):
                           t.matmul(psA[2 * b + jj][:, th * 512:th * 512 + 512],
                                    wo_sb[b][:, d, jj * 128:jj * 128 + 128],
                                    at_own[:, d, th * 512:th * 512 + 512],
                                    start=st, stop=sp),
                           waits=mmw if (d == 0 and jj == 0 and th == 0) else (),
                           inc=peM.bump() if (d == n_ft - 1 and jj == 1 and th == NTH - 1) else None)
        grp_done = peM.n
        wo_stripe_free_at[b] = grp_done
        if g + 2 < n_grp:
            emit_wo_stripe(g + 2)
        for jj in range(2):
            ft = 2 * g + jj
            sb = ft % 2
            P.emit("vector",
                   lambda v, b=b, jj=jj, sb=sb: v.tensor_add(
                       st_sb[sb][:, :], psA[2 * b + jj][:, 0:To], xo_sb[sb][:, :]),
                   waits=[(peM.sem, grp_done), (sXo[sb].sem, 16 * (ft // 2 + 1)),
                          (sSw[sb].sem, st_free_at[sb]), (acM.sem, st_free_acm[sb]),
                          (dvM.sem, dvM.n)],
                   inc=dvM.bump())
            add_done = dvM.n
            xo_used_at[sb] = add_done
            if jj == 1:
                wo_grp_free_at[b] = add_done
            if ft + 2 < n_ft:
                P.emit("sync",
                       lambda sy, ft=ft, sb=sb: sy.dma_start(
                           out=xo_sb[sb][:, :], in_=xoT[(ft + 2) * 128:(ft + 3) * 128, :]),
                       waits=[(dvM.sem, add_done)], inc=sXo[sb].bump())
            P.emit("sync",
                   lambda sy, ft=ft, sb=sb: sy.dma_start(
                       out=s_dram.ap()[ft * 128:(ft + 1) * 128, :], in_=st_sb[sb][:, :]),
                   waits=[(dvM.sem, add_done)], inc=sSw[sb].bump())
            st_free_at[sb] = sSw[sb].n
            if ft == 0:
                P.emit("vector", lambda v, sb=sb: v.tensor_copy(acc_s[:, :], st_sb[sb][:, :]),
                       waits=[(dvM.sem, add_done)], inc=dvM.bump())
            else:
                P.emit("vector", lambda v, sb=sb: v.tensor_add(acc_s[:, :], acc_s[:, :],
                                                               st_sb[sb][:, :]),
                       waits=[(dvM.sem, max(add_done, acc_chain))], inc=dvM.bump())
            P.emit("scalar", lambda s, sb=sb: s.activation(sq_sb[:, :], st_sb[sb][:, :],
                                                           AF.Square),
                   waits=[(dvM.sem, add_done), (dvM.sem, acc_chain), (acM.sem, acM.n)],
                   inc=acM.bump())
            sqd = acM.n
            st_free_acm[sb] = sqd
            if ft == 0:
                P.emit("vector", lambda v: v.tensor_copy(acc_q[:, :], sq_sb[:, :]),
                       waits=[(acM.sem, sqd), (dvM.sem, dvM.n)], inc=dvM.bump())
            else:
                P.emit("vector", lambda v: v.tensor_add(acc_q[:, :], acc_q[:, :], sq_sb[:, :]),
                       waits=[(acM.sem, sqd), (dvM.sem, dvM.n)], inc=dvM.bump())
            acc_chain = dvM.n

    def emit_ln_finalize(m_dst, r_dst, roff, acc_ready_dvM, sc_dst=None):
        psum, psumq = psA[3], psA[2]
        for half in range(NTH):
            P.emit("tensor",
                   lambda t, half=half: t.matmul(
                       psum[0:1, half * 512:half * 512 + 512], onescf_sb[:, :],
                       acc_s[:, half * 512:half * 512 + 512], start=True, stop=True),
                   waits=[(dvM.sem, acc_ready_dvM), (acM.sem, acM.n)],
                   inc=peM.bump())
            P.emit("tensor",
                   lambda t, half=half: t.matmul(
                       psumq[0:1, half * 512:half * 512 + 512], onescf_sb[:, :],
                       acc_q[:, half * 512:half * 512 + 512], start=True, stop=True),
                   inc=peM.bump())
        mm_done = peM.n
        r0, r1, r2o = roff, roff + 1024, roff + 2048
        P.emit("scalar",
               lambda s: s.copy(rowbuf[0:1, r0:r0 + To], psum[0:1, 0:To]),
               waits=[(peM.sem, mm_done), (acM.sem, acM.n)], inc=acM.bump())
        P.emit("scalar",
               lambda s: s.copy(rowbuf[0:1, r1:r1 + To], psumq[0:1, 0:To]),
               waits=[(acM.sem, acM.n)], inc=acM.bump())
        evr = acM.n
        P.emit("vector",
               lambda v: v.tensor_scalar_mul(rowbuf[0:1, r0:r0 + To],
                                             rowbuf[0:1, r0:r0 + To], 1.0 / DIM),
               waits=[(acM.sem, evr), (dvM.sem, dvM.n)], inc=dvM.bump())
        m_row_done = dvM.n
        P.emit("vector",
               lambda v: v.tensor_scalar_mul(rowbuf[0:1, r1:r1 + To],
                                             rowbuf[0:1, r1:r1 + To], 1.0 / DIM),
               waits=[(dvM.sem, dvM.n)], inc=dvM.bump())
        P.emit("vector",
               lambda v: v.tensor_mul(rowbuf[0:1, r2o:r2o + To],
                                      rowbuf[0:1, r0:r0 + To], rowbuf[0:1, r0:r0 + To]),
               waits=[(dvM.sem, dvM.n)], inc=dvM.bump())
        P.emit("vector",
               lambda v: v.tensor_sub(rowbuf[0:1, r1:r1 + To], rowbuf[0:1, r1:r1 + To],
                                      rowbuf[0:1, r2o:r2o + To]),
               waits=[(dvM.sem, dvM.n)], inc=dvM.bump())
        var_done = dvM.n
        P.emit("scalar",
               lambda s: s.activation(rowbuf[0:1, r2o:r2o + To], rowbuf[0:1, r1:r1 + To],
                                      AF.Ln, bias=1e-5),
               waits=[(dvM.sem, var_done), (acM.sem, acM.n)], inc=acM.bump())
        P.emit("scalar",
               lambda s: s.activation(rowbuf[0:1, r1:r1 + To], rowbuf[0:1, r2o:r2o + To],
                                      AF.Exp, scale=-0.5),
               waits=[(acM.sem, acM.n)], inc=acM.bump())
        rstd_done = acM.n
        sc_done = dvM.n
        if sc_dst is not None:
            P.emit("vector",
                   lambda v: v.tensor_scalar(rowbuf[0:1, r2o:r2o + To],
                                             rowbuf[0:1, r1:r1 + To],
                                             wscale_sb[0:1, 0:1], None, ALU.mult),
                   waits=[(acM.sem, rstd_done), (dvM.sem, dvM.n)], inc=dvM.bump())
            sc_done = dvM.n
        bc_list = [(m_dst, r0), (r_dst, r1)]
        if sc_dst is not None:
            bc_list.append((sc_dst, r2o))
        prev_ev = 0
        for (dst, off) in bc_list:
            for half in range(NTH):
                P.emit("tensor",
                       lambda t, off=off, half=half: t.matmul(
                           psum[:, half * 512:half * 512 + 512], onesrf_sb[:, :],
                           rowbuf[0:1, off + half * 512:off + half * 512 + 512],
                           start=True, stop=True),
                       waits=[(acM.sem, max(rstd_done, prev_ev)), (dvM.sem, sc_done)],
                       inc=peM.bump())
                bmm = peM.n
                P.emit("scalar",
                       lambda s, dst=dst, half=half: s.copy(
                           dst[:, half * 512:half * 512 + 512],
                           psum[:, half * 512:half * 512 + 512]),
                       waits=[(peM.sem, bmm), (acM.sem, acM.n)], inc=acM.bump())
                prev_ev = acM.n
        return acM.n

    bc1_done = emit_ln_finalize(m_bc, rstd_bc, 0, acc_chain, sc_dst=sc_bc)

    if upto == "p25":
        sT = Ctr(stk, nc, "sT", 16)
        P.emit("sync",
               lambda sy: sy.dma_start(out=outT[:, :], in_=s_dram.ap()),
               waits=[(sSw[0].sem, sSw[0].n), (sSw[1].sem, sSw[1].n),
                      (acM.sem, bc1_done)], inc=sT.bump())
        P.emit("sync", None, waits=[(sT.sem, sT.n)])
        nc_out = finish()
        ph25.close(); ph34.close(); pp3.close(); stk.close()
        return nc_out

    ph25.close()
    ph3 = contextlib.ExitStack()
    d1_sb = ph3.enter_context(nc.sbuf_tensor("d1_sb", [128, n_ft, To], BF16))
    w2r = [ph3.enter_context(nc.sbuf_tensor("w2r%d" % i, [128, n_ft, 128], BF16))
           for i in range(3)]
    wf_sb = [ph3.enter_context(nc.sbuf_tensor("wf_sb%d" % i, [128, n_ft, 128], BF16))
             for i in range(3)]

    d1_done_at = []
    for ft in range(n_ft):
        sb = ft % 4
        P.emit("sync",
               lambda sy, ft=ft, sb=sb: sy.dma_start(
                   out=st_sb[sb][:, :], in_=s_dram.ap()[ft * 128:(ft + 1) * 128, :]),
               waits=[(sSw[0].sem, sSw[0].n), (sSw[1].sem, sSw[1].n),
                      (acM.sem, bc1_done),
                      (dvM.sem, d1_done_at[ft - 4] if ft >= 4 else 0)],
               inc=sSr[sb].bump())
        P.emit("vector",
               lambda v, ft=ft, sb=sb: v.tensor_sub(d1_sb[:, ft, :], st_sb[sb][:, :],
                                                    m_bc[:, :]),
               waits=[(sSr[sb].sem, 16 * (ft // 4 + 1)), (acM.sem, bc1_done),
                      (dvM.sem, dvM.n)],
               inc=dvM.bump())
        d1_done_at.append(dvM.n)
    d1_all = dvM.n

    ff_grp_free_at = [0, 0, 0]
    ff_stripe_free_at = [0, 0, 0]
    wf_conv_at = [0, 0, 0]
    st3_free_dv = [0, 0, 0, 0]
    r2st_free_dv = [0, 0]
    r2st_free_ac = [0, 0]
    r2st_spill_at = [0, 0]
    acc2_chain = 0

    for g in range(n_ft):
        b = g % 3
        w = [(sSw[0].sem, sSw[0].n), (sSw[1].sem, sSw[1].n)]
        if g >= 3:
            w.append((dvM.sem, wf_conv_at[b]))
        P.emit("gpsimd",
               lambda gp, g=g, b=b: gp.dma_start(
                   out=w2r[b][:, :, :],
                   in_=w2T[:, g * 128:(g + 1) * 128].rearrange("(i p) f -> p i f", p=128)),
               waits=w, inc=sFf[b].bump())
        P.emit("vector",
               lambda v, b=b: v.tensor_scalar_sub(wf_sb[b][:, :, :], w2r[b][:, :, :], 1.5),
               waits=[(sFf[b].sem, 16 * (g // 3 + 1)), (peM.sem, ff_stripe_free_at[b]),
                      (dvM.sem, dvM.n)],
               inc=dvM.bump())
        wf_conv_at[b] = dvM.n
        conv_done = dvM.n
        mmw = [(dvM.sem, max(conv_done, d1_all))]
        if g >= 3:
            mmw.append((dvM.sem, ff_grp_free_at[b]))
        for d in range(n_ft):
            for th in range(NTH):
                P.emit("tensor",
                       lambda t, b=b, d=d, th=th, st=(d == 0), sp=(d == n_ft - 1):
                       t.matmul(psA[b][:, th * 512:th * 512 + 512],
                                wf_sb[b][:, d, :], d1_sb[:, d, th * 512:th * 512 + 512],
                                start=st, stop=sp),
                       waits=mmw if (d == 0 and th == 0) else (),
                       inc=peM.bump() if (d == n_ft - 1 and th == NTH - 1) else None)
        grp_done = peM.n
        ff_stripe_free_at[b] = grp_done
        sb = g % 4
        sb2 = g % 2
        P.emit("sync",
               lambda sy, g=g, sb=sb: sy.dma_start(
                   out=st_sb[sb][:, :], in_=s_dram.ap()[g * 128:(g + 1) * 128, :]),
               waits=[(dvM.sem, max(d1_all, st3_free_dv[sb]))],
               inc=sS2[sb].bump())
        P.emit("vector",
               lambda v, sb=sb: v.tensor_sub(sq_sb[:, :], st_sb[sb][:, :], m_bc[:, :]),
               waits=[(sS2[sb].sem, 16 * (g // 4 + 1)), (dvM.sem, dvM.n)],
               inc=dvM.bump())
        st3_free_dv[sb] = dvM.n
        P.emit("vector",
               lambda v, sb=sb: v.tensor_mul(sq_sb[:, :], sq_sb[:, :], rstd_bc[:, :]),
               waits=[(dvM.sem, dvM.n)], inc=dvM.bump())
        hh_done = dvM.n
        P.emit("vector",
               lambda v, b=b, sb2=sb2: v.tensor_mul(r2st[sb2][:, :], psA[b][:, 0:To],
                                                    sc_bc[:, :]),
               waits=[(peM.sem, grp_done), (dvM.sem, max(hh_done, r2st_free_dv[sb2])),
                      (acM.sem, r2st_free_ac[sb2]), (sR2w[sb2].sem, r2st_spill_at[sb2])],
               inc=dvM.bump())
        ff_grp_free_at[b] = dvM.n
        P.emit("vector",
               lambda v, g=g, sb2=sb2: v.scalar_tensor_tensor(
                   r2st[sb2][:, :], r2st[sb2][:, :], bias2d_sb[:, g:g + 1], sq_sb[:, :],
                   ALU.add, ALU.add),
               waits=[(dvM.sem, dvM.n)], inc=dvM.bump())
        r2_done = dvM.n
        P.emit("scalar",
               lambda s, g=g, sb2=sb2: s.dma_start(
                   out=r2_dram.ap()[g * 128:(g + 1) * 128, :], in_=r2st[sb2][:, :]),
               waits=[(dvM.sem, r2_done)], inc=sR2w[sb2].bump())
        r2st_spill_at[sb2] = sR2w[sb2].n
        if g == 0:
            P.emit("vector", lambda v, sb2=sb2: v.tensor_copy(acc_s[:, :], r2st[sb2][:, :]),
                   waits=[(dvM.sem, r2_done)], inc=dvM.bump())
        else:
            P.emit("vector", lambda v, sb2=sb2: v.tensor_add(acc_s[:, :], acc_s[:, :],
                                                             r2st[sb2][:, :]),
                   waits=[(dvM.sem, max(r2_done, acc2_chain))], inc=dvM.bump())
        P.emit("scalar", lambda s, sb2=sb2: s.activation(sq_sb[:, :], r2st[sb2][:, :], AF.Square),
               waits=[(dvM.sem, dvM.n), (acM.sem, acM.n)], inc=acM.bump())
        sqd = acM.n
        r2st_free_ac[sb2] = sqd
        if g == 0:
            P.emit("vector", lambda v: v.tensor_copy(acc_q[:, :], sq_sb[:, :]),
                   waits=[(acM.sem, sqd), (dvM.sem, dvM.n)], inc=dvM.bump())
        else:
            P.emit("vector", lambda v: v.tensor_add(acc_q[:, :], acc_q[:, :], sq_sb[:, :]),
                   waits=[(acM.sem, sqd), (dvM.sem, dvM.n)], inc=dvM.bump())
        acc2_chain = dvM.n
        r2st_free_dv[sb2] = acc2_chain

    bc2_done = emit_ln_finalize(m_bc, rstd_bc, 0, acc2_chain)
    dv_after_bc2 = dvM.n

    out_done_at = [dv_after_bc2] * 4
    out_stage_free = [dv_after_bc2, dv_after_bc2]
    out_store_at = [0, 0]
    for ft in range(n_ft):
        sb = ft % 4
        sb2 = ft % 2
        P.emit("sync",
               lambda sy, ft=ft, sb=sb: sy.dma_start(
                   out=st_sb[sb][:, :], in_=r2_dram.ap()[ft * 128:(ft + 1) * 128, :]),
               waits=[(sR2w[0].sem, sR2w[0].n), (sR2w[1].sem, sR2w[1].n),
                      (dvM.sem, out_done_at[sb]), (acM.sem, bc2_done)],
               inc=sR2r[sb].bump())
        P.emit("vector",
               lambda v, sb=sb: v.tensor_sub(sq_sb[:, :], st_sb[sb][:, :], m_bc[:, :]),
               waits=[(sR2r[sb].sem, 16 * (ft // 4 + 1)), (acM.sem, bc2_done),
                      (dvM.sem, dvM.n)],
               inc=dvM.bump())
        out_done_at[sb] = dvM.n
        P.emit("vector",
               lambda v, sb2=sb2: v.tensor_mul(r2st[sb2][:, :], sq_sb[:, :], rstd_bc[:, :]),
               waits=[(dvM.sem, max(dvM.n, out_stage_free[sb2])),
                      (sOut[sb2].sem, out_store_at[sb2])],
               inc=dvM.bump())
        out_stage_free[sb2] = dvM.n
        P.emit("scalar",
               lambda s, ft=ft, sb2=sb2: s.dma_start(
                   out=outT[ft * 128:(ft + 1) * 128, :], in_=r2st[sb2][:, :]),
               waits=[(dvM.sem, dvM.n)], inc=sOut[sb2].bump())
        out_store_at[sb2] = sOut[sb2].n
    P.emit("sync", None, waits=[(sOut[0].sem, sOut[0].n), (sOut[1].sem, sOut[1].n)])

    nc_out = finish()
    ph3.close()
    ph34.close()
    pp3.close()
    stk.close()
    return nc_out


def prep_inputs(L0_attn_x, L0_attn_wq, L0_attn_wk, L0_attn_wv, L0_attn_wo,
                L0_ffn_weight_2bit, L0_ffn_weight_scale, L0_ffn_bias):
    x = np.asarray(L0_attn_x)
    S = x.shape[1]
    x2 = np.ascontiguousarray(x.reshape(S, DIM))
    xT = np.ascontiguousarray(x2.T)

    perm = np.concatenate([np.arange(0, HD, 2), np.arange(1, HD, 2)])
    wq = np.asarray(L0_attn_wq).reshape(DIM, NH, HD)[:, :, perm].reshape(DIM, NH * HD)
    wk = np.asarray(L0_attn_wk).reshape(DIM, NKV, HD)[:, :, perm].reshape(DIM, NKV * HD)
    wv = np.asarray(L0_attn_wv)
    wo = np.ascontiguousarray(np.asarray(L0_attn_wo))
    w2T = np.ascontiguousarray(np.asarray(L0_ffn_weight_2bit).T.astype(np.int32))
    bias2d = np.ascontiguousarray(np.asarray(L0_ffn_bias).reshape(DIM // 128, 128).T)
    wsc = np.ascontiguousarray(np.asarray(L0_ffn_weight_scale).reshape(1, 1).astype(np.float32))

    sc128 = np.float64(HD) ** -0.25
    freqs = 1.0 / THETA ** (np.arange(0, HD, 2, dtype=np.float64) / HD)
    ang = np.outer(freqs, np.arange(S, dtype=np.float64))
    C = (np.concatenate([np.cos(ang), np.cos(ang)], 0) * sc128).astype(np.float32)
    Ssg = (np.concatenate([-np.sin(ang), np.sin(ang)], 0) * sc128).astype(np.float32)
    kk = np.arange(128)[:, None]
    qq = np.arange(512)[None, :]
    masks = np.concatenate([(128 * s + kk <= qq) for s in range(4)],
                           axis=1).astype(ml_dtypes.bfloat16)
    pswap = np.zeros((128, 128), np.float32)
    for i in range(128):
        pswap[i, (i + 64) % 128] = 1.0
    pswap = pswap.astype(ml_dtypes.bfloat16)
    identm = np.eye(128, dtype=ml_dtypes.bfloat16)
    ones4_np = np.zeros((128, 1), np.float32)
    ones4_np[[0, 32, 64, 96], 0] = 1.0

    To = S // NCORES
    maps = []
    for c in range(NCORES):
        maps.append({
            "xT": xT,
            "xoT": np.ascontiguousarray(xT[:, c * To:(c + 1) * To]),
            "wq": np.ascontiguousarray(wq[:, c * HPC * HD:(c + 1) * HPC * HD]),
            "wk": np.ascontiguousarray(wk[:, c * HD:(c + 1) * HD]),
            "wv": np.ascontiguousarray(np.asarray(wv)[:, c * HD:(c + 1) * HD]),
            "wo": wo,
            "w2T": w2T,
            "bias2d": bias2d,
            "wscale": wsc,
            "cosT": C,
            "sinT": Ssg,
            "masks": masks,
            "pswap": pswap,
            "ident": identm,
            "onescb": np.ones((128, 1), ml_dtypes.bfloat16),
            "onesrb": np.ones((1, 128), ml_dtypes.bfloat16),
            "onescf": np.ones((128, 1), np.float32),
            "onesrf": np.ones((1, 128), np.float32),
            "ones4": ones4_np,
        })
    return maps


def kernel(L0_attn_x, L0_attn_wq, L0_attn_wk, L0_attn_wv, L0_attn_wo,
           L0_ffn_weight_2bit, L0_ffn_weight_scale, L0_ffn_bias):
    x = np.asarray(L0_attn_x)
    S = x.shape[1]
    maps = prep_inputs(L0_attn_x, L0_attn_wq, L0_attn_wk, L0_attn_wv, L0_attn_wo,
                       L0_ffn_weight_2bit, L0_ffn_weight_scale, L0_ffn_bias)
    nc = build_program(S)
    with allow_ici_all_to_all():
        res = bass_utils.run_bass_kernel_spmd(nc, maps, core_ids=list(range(NCORES)))
    outs = [np.asarray(r["outT"]).T for r in res.results]
    return np.ascontiguousarray(np.concatenate(outs, axis=0)).reshape(1, S, DIM).astype(np.float32)


# version 07
# speedup vs baseline: 1.2625x; 1.1249x over previous
import contextlib
import numpy as np
import ml_dtypes

import concourse.bass as bass
import concourse.mybir as mybir
import concourse.bass_utils as bass_utils
from concourse.aot_env import allow_ici_all_to_all

F32 = mybir.dt.float32
BF16 = mybir.dt.bfloat16
I32 = mybir.dt.int32
AF = mybir.ActivationFunctionType
ALU = mybir.AluOpType
AX = mybir.AxisListType

DIM = 4096
NH = 32
NKV = 8
HD = 128
NCORES = 8
HPC = 4
THETA = 10000.0

ENGS = ("tensor", "vector", "scalar", "gpsimd", "sync")


class Ctr:

    def __init__(self, stack, nc, name, step=1):
        self.sem = stack.enter_context(nc.semaphore(name))
        self.n = 0
        self.step = step

    def bump(self):
        self.n += self.step
        return (self.sem, self.step)


class Prog:
    def __init__(self):
        self.ops = {e: [] for e in ENGS}

    def emit(self, eng, fn=None, waits=(), inc=None):
        self.ops[eng].append((fn, tuple((s, v) for (s, v) in waits if v > 0), inc))


def build_program(S, upto='full', p1_tgs=None, p1_lvl=6):
    To = S // NCORES
    NTG = S // 512
    NQC = S // 512
    NTH = (To + 511) // 512
    n_ft = DIM // 128

    nc = bass.Bass("TRN2", target_bir_lowering=False)

    xT = nc.dram_tensor("xT", [DIM, S], F32, kind="ExternalInput").ap()
    xoT = nc.dram_tensor("xoT", [DIM, To], F32, kind="ExternalInput").ap()
    wq = nc.dram_tensor("wq", [DIM, HPC * HD], F32, kind="ExternalInput").ap()
    wk = nc.dram_tensor("wk", [DIM, HD], F32, kind="ExternalInput").ap()
    wv = nc.dram_tensor("wv", [DIM, HD], F32, kind="ExternalInput").ap()
    wo = nc.dram_tensor("wo", [DIM, DIM], F32, kind="ExternalInput").ap()
    w2T = nc.dram_tensor("w2T", [DIM, DIM], I32, kind="ExternalInput").ap()
    bias2d = nc.dram_tensor("bias2d", [128, n_ft], F32, kind="ExternalInput").ap()
    wscale = nc.dram_tensor("wscale", [1, 1], F32, kind="ExternalInput").ap()
    cosT = nc.dram_tensor("cosT", [128, S], F32, kind="ExternalInput").ap()
    sinT = nc.dram_tensor("sinT", [128, S], F32, kind="ExternalInput").ap()
    masks = nc.dram_tensor("masks", [128, 4 * 512], BF16, kind="ExternalInput").ap()
    pswap = nc.dram_tensor("pswap", [128, 128], BF16, kind="ExternalInput").ap()
    ident = nc.dram_tensor("ident", [128, 128], BF16, kind="ExternalInput").ap()
    onescb = nc.dram_tensor("onescb", [128, 1], BF16, kind="ExternalInput").ap()
    onesrb = nc.dram_tensor("onesrb", [1, 128], BF16, kind="ExternalInput").ap()
    onescf = nc.dram_tensor("onescf", [128, 1], F32, kind="ExternalInput").ap()
    onesrf = nc.dram_tensor("onesrf", [1, 128], F32, kind="ExternalInput").ap()
    ones4 = nc.dram_tensor("ones4", [128, 1], F32, kind="ExternalInput").ap()
    outT = nc.dram_tensor("outT", [DIM, To], F32, kind="ExternalOutput").ap()

    a2a_in = nc.dram_tensor("a2a_in", [NCORES, HPC * HD, To], BF16)
    a2a_out = nc.dram_tensor("a2a_out", [NCORES, HPC * HD, To], BF16)
    s_dram = nc.dram_tensor("s_dram", [DIM, To], F32)
    r2_dram = nc.dram_tensor("r2_dram", [DIM, To], F32)

    stk = contextlib.ExitStack()

    sW = Ctr(stk, nc, "sW", 16)
    sX = [Ctr(stk, nc, "sX%d" % i, 16) for i in range(8)]
    sCS = [Ctr(stk, nc, "sCS%d" % i, 16) for i in range(2)]
    sWo = [Ctr(stk, nc, "sWo%d" % i, 16) for i in range(2)]
    sFf = [Ctr(stk, nc, "sFf%d" % i, 16) for i in range(3)]
    sXo = [Ctr(stk, nc, "sXo%d" % i, 16) for i in range(2)]
    sSw = [Ctr(stk, nc, "sSw%d" % i, 16) for i in range(2)]
    sSr = [Ctr(stk, nc, "sSr%d" % i, 16) for i in range(4)]
    sS2 = [Ctr(stk, nc, "sS2%d" % i, 16) for i in range(4)]
    sR2w = [Ctr(stk, nc, "sR2w%d" % i, 16) for i in range(2)]
    sR2r = [Ctr(stk, nc, "sR2r%d" % i, 16) for i in range(4)]
    sA2 = [Ctr(stk, nc, "sA2_%d" % i, 16) for i in range(8)]
    sAo = Ctr(stk, nc, "sAo", 16)
    sOut = [Ctr(stk, nc, "sOut%d" % i, 16) for i in range(2)]
    ccs = Ctr(stk, nc, "ccs", 1)
    peM = Ctr(stk, nc, "peM", 1)
    acM = Ctr(stk, nc, "acM", 1)
    dvM = Ctr(stk, nc, "dvM", 1)

    P = Prog()

    masks_sb = stk.enter_context(nc.sbuf_tensor("masks_sb", [128, 4 * 512], BF16))
    pswap_sb = stk.enter_context(nc.sbuf_tensor("pswap_sb", [128, 128], BF16))
    ident_sb = stk.enter_context(nc.sbuf_tensor("ident_sb", [128, 128], BF16))
    onescb_sb = stk.enter_context(nc.sbuf_tensor("onescb_sb", [128, 1], BF16))
    onesrb_sb = stk.enter_context(nc.sbuf_tensor("onesrb_sb", [1, 128], BF16))
    onescf_sb = stk.enter_context(nc.sbuf_tensor("onescf_sb", [128, 1], F32))
    onesrf_sb = stk.enter_context(nc.sbuf_tensor("onesrf_sb", [1, 128], F32))
    ones4_sb = stk.enter_context(nc.sbuf_tensor("ones4_sb", [128, 1], F32))
    bias2d_sb = stk.enter_context(nc.sbuf_tensor("bias2d_sb", [128, n_ft], F32))
    wscale_sb = stk.enter_context(nc.sbuf_tensor("wscale_sb", [1, 1], F32))
    cmax = stk.enter_context(nc.sbuf_tensor("cmax", [1, 4 * NQC], F32))
    kmaxs = stk.enter_context(nc.sbuf_tensor("kmaxs", [1, NTG + 2], F32))
    cneg = stk.enter_context(nc.sbuf_tensor("cneg", [1, 4 * NQC], BF16))
    rowbuf = stk.enter_context(nc.sbuf_tensor("rowbuf", [1, 3 * 1024], F32))
    consts = stk.enter_context(nc.sbuf_tensor("consts", [1, 8], F32))

    xo_sb = [stk.enter_context(nc.sbuf_tensor("xo_sb%d" % i, [128, To], F32)) for i in range(2)]

    ph12 = contextlib.ExitStack()
    q4 = ph12.enter_context(nc.sbuf_tensor("q4", [128, HPC, S], BF16))
    KT = ph12.enter_context(nc.sbuf_tensor("KT", [128, S], BF16))
    Vtok = ph12.enter_context(nc.sbuf_tensor("Vtok", [128, S], BF16))

    nc.const_aps.register(F32, 1e-5, consts[0:1, 0:1])
    nc.const_aps.register(F32, 0.5, consts[0:1, 1:2])
    nc.const_aps.register(F32, -0.5, consts[0:1, 2:3])
    nc.const_aps.register(F32, -1.0, consts[0:1, 3:4])

    P.emit("vector", lambda v: v.memset(consts[0:1, 0:1], 1e-5), inc=dvM.bump())
    P.emit("vector", lambda v: v.memset(consts[0:1, 1:2], 0.5), inc=dvM.bump())
    P.emit("vector", lambda v: v.memset(consts[0:1, 2:3], -0.5), inc=dvM.bump())
    P.emit("vector", lambda v: v.memset(consts[0:1, 3:4], -1.0), inc=dvM.bump())
    consts_ready = dvM.n

    def g_load(dst, src):
        P.emit("gpsimd", lambda g, d=dst, s=src: g.dma_start(out=d, in_=s), inc=sW.bump())

    g_load(masks_sb[:, :], masks)
    g_load(pswap_sb[:, :], pswap)
    g_load(ident_sb[:, :], ident)
    g_load(onescb_sb[:, :], onescb)
    g_load(onesrb_sb[:, :], onesrb)
    g_load(onescf_sb[:, :], onescf)
    g_load(onesrf_sb[:, :], onesrf)
    g_load(ones4_sb[:, :], ones4)
    g_load(bias2d_sb[:, :], bias2d)
    g_load(wscale_sb[:, :], wscale)

    ph1 = contextlib.ExitStack()
    wq_sb = ph1.enter_context(nc.sbuf_tensor("wq_sb", [128, n_ft, HPC * HD], BF16))
    wk_sb = ph1.enter_context(nc.sbuf_tensor("wk_sb", [128, n_ft, HD], BF16))
    wv_sb = ph1.enter_context(nc.sbuf_tensor("wv_sb", [128, n_ft, HD], BF16))
    xbuf = [ph1.enter_context(nc.sbuf_tensor("xbuf%d" % i, [128, 512], BF16)) for i in range(8)]
    csbuf = [ph1.enter_context(nc.sbuf_tensor("csbuf%d" % i, [128, 2, 512], F32)) for i in range(2)]
    qkstage = ph1.enter_context(nc.sbuf_tensor("qkstage", [128, 512], BF16))
    vstage = ph1.enter_context(nc.sbuf_tensor("vstage", [128, 512], BF16))
    sqstage = ph1.enter_context(nc.sbuf_tensor("sqstage", [128, 512], BF16))
    t1b = ph1.enter_context(nc.sbuf_tensor("t1b", [128, 512], F32))
    t2b = ph1.enter_context(nc.sbuf_tensor("t2b", [128, 512], F32))

    pp1 = contextlib.ExitStack()
    q_ps = pp1.enter_context(nc.psum_tensor("q_ps", [128, 4 * 512], F32))
    kv_ps = pp1.enter_context(nc.psum_tensor("kv_ps", [128, 2 * 512], F32))
    sw_ps = pp1.enter_context(nc.psum_tensor("sw_ps", [128, 512], F32))
    vt_ps = pp1.enter_context(nc.psum_tensor("vt_ps", [128, 128], BF16))

    g_load(wq_sb[:, :, :], wq.rearrange("(i p) f -> p i f", p=128))
    g_load(wk_sb[:, :, :], wk.rearrange("(i p) f -> p i f", p=128))
    g_load(wv_sb[:, :, :], wv.rearrange("(i p) f -> p i f", p=128))
    w_loaded = sW.n

    total_tiles = NTG * n_ft
    xt_consumed_at = []
    cs_consumed_at = []

    if upto == "p0":
        sT = Ctr(stk, nc, "sT", 16)
        P.emit("gpsimd",
               lambda g: g.dma_start(out=outT[0:128, 0:512], in_=wq_sb[:, 0, :]),
               waits=[(sW.sem, sW.n)], inc=sT.bump())
        P.emit("gpsimd", None, waits=[(sT.sem, sT.n)])
        def finish0():
            def runner(eng_ops):
                def _run(eng):
                    for fn, waits, inc in eng_ops:
                        if fn is None:
                            for (sem, val) in waits:
                                eng.wait_ge(sem, val)
                            continue
                        if len(waits) > 1:
                            for (sem, val) in waits[:-1]:
                                eng.wait_ge(sem, val)
                        inst = fn(eng)
                        if waits:
                            inst.wait_op(waits[-1][0], waits[-1][1], "sem-ge")
                        if inc is not None:
                            inst.then_inc(inc[0], inc[1])
                return _run
            with nc.Block() as block:
                block.tensor(runner(P.ops["tensor"]))
                block.vector(runner(P.ops["vector"]))
                block.scalar(runner(P.ops["scalar"]))
                block.gpsimd(runner(P.ops["gpsimd"]))
                block.sync(runner(P.ops["sync"]))
            return nc
        nc_out = finish0()
        ph1.close(); pp1.close(); ph12.close(); stk.close()
        return nc_out

    def emit_xt_load(j):
        tg, d = divmod(j, n_ft)
        b = j % 8
        waits = [(sW.sem, w_loaded)]
        if j >= 8:
            waits.append((peM.sem, xt_consumed_at[j - 8]))
        P.emit("gpsimd",
               lambda g, b=b, tg=tg, d=d: g.dma_start(
                   out=xbuf[b][:, :], in_=xT[d * 128:(d + 1) * 128, tg * 512:(tg + 1) * 512]),
               waits=waits, inc=sX[b].bump())

    def emit_cs_load(tg):
        b = tg % 2
        waits = []
        if tg >= 2:
            waits.append((dvM.sem, cs_consumed_at[tg - 2]))
        P.emit("sync",
               lambda sy, b=b, tg=tg: sy.dma_start(
                   out=csbuf[b][:, 0, :], in_=cosT[:, tg * 512:(tg + 1) * 512]),
               waits=waits, inc=sCS[b].bump())
        P.emit("sync",
               lambda sy, b=b, tg=tg: sy.dma_start(
                   out=csbuf[b][:, 1, :], in_=sinT[:, tg * 512:(tg + 1) * 512]),
               inc=sCS[b].bump())

    NTG_RUN = NTG if p1_tgs is None else min(NTG, p1_tgs)
    total_tiles = NTG_RUN * n_ft
    xt_next = 0
    for j in range(min(8, total_tiles)):
        emit_xt_load(j)
        xt_next += 1
    emit_cs_load(0)
    if NTG > 1:
        emit_cs_load(1)

    prev_ep_act = 0
    prev_ep_dve = 0

    for tg in range(NTG_RUN):
        for d in range(n_ft):
            j = tg * n_ft + d
            b = j % 8
            waits = [(sX[b].sem, 16 * (j // 8 + 1))]
            if d == 0:
                waits.append((acM.sem, prev_ep_act))
                waits.append((dvM.sem, prev_ep_dve))
            st = (d == 0)
            sp = (d == n_ft - 1)
            for jj in range(HPC):
                P.emit("tensor",
                       lambda t, d=d, jj=jj, b=b, st=st, sp=sp: t.matmul(
                           q_ps[:, jj * 512:jj * 512 + 512],
                           wq_sb[:, d, jj * 128:jj * 128 + 128],
                           xbuf[b][:, :], start=st, stop=sp),
                       waits=waits if jj == 0 else ())
            P.emit("tensor",
                   lambda t, d=d, b=b, st=st, sp=sp: t.matmul(
                       kv_ps[:, 0:512], wk_sb[:, d, :], xbuf[b][:, :], start=st, stop=sp))
            P.emit("tensor",
                   lambda t, d=d, b=b, st=st, sp=sp: t.matmul(
                       kv_ps[:, 512:1024], wv_sb[:, d, :], xbuf[b][:, :], start=st, stop=sp),
                   inc=peM.bump())
            xt_consumed_at.append(peM.n)
            if xt_next < total_tiles:
                emit_xt_load(xt_next)
                xt_next += 1
        dloop_done = peM.n

        csb = tg % 2
        cs_ready = 32 * (tg // 2 + 1)
        tgs = slice(tg * 512, (tg + 1) * 512)
        prev_t2 = 0
        prev_red = 0
        prev_norm_pe = 0
        for hx in range(HPC + 1):
            if p1_lvl < 1:
                break
            is_q = hx < HPC
            src_ps = q_ps[:, hx * 512:hx * 512 + 512] if is_q else kv_ps[:, 0:512]
            P.emit("scalar",
                   lambda s, src=src_ps: s.copy(qkstage[:, :], src),
                   waits=[(peM.sem, dloop_done), (dvM.sem, prev_t2), (acM.sem, acM.n)],
                   inc=acM.bump())
            evac_done = acM.n
            if p1_lvl < 2:
                continue
            P.emit("tensor",
                   lambda t: t.matmul(sw_ps[:, :], pswap_sb[:, :], qkstage[:, :],
                                      start=True, stop=True),
                   waits=[(acM.sem, evac_done), (dvM.sem, max(prev_t2, prev_red))],
                   inc=peM.bump())
            swap_done = peM.n
            if p1_lvl < 3:
                continue
            P.emit("vector",
                   lambda v, src=src_ps, csb=csb: v.tensor_mul(t1b[:, :], src, csbuf[csb][:, 0, :]),
                   waits=[(sCS[csb].sem, cs_ready), (peM.sem, dloop_done), (dvM.sem, dvM.n),
                          (acM.sem, evac_done)])
            P.emit("vector",
                   lambda v, csb=csb: v.tensor_mul(t2b[:, :], sw_ps[:, :], csbuf[csb][:, 1, :]),
                   waits=[(peM.sem, swap_done), (dvM.sem, dvM.n)], inc=dvM.bump())
            prev_t2 = dvM.n
            dst = q4[:, hx, tgs] if is_q else KT[:, tgs]
            P.emit("vector",
                   lambda v, dst=dst: v.tensor_add(dst, t1b[:, :], t2b[:, :]),
                   waits=[(dvM.sem, dvM.n)], inc=dvM.bump())
            rope_done = dvM.n
            if p1_lvl < 4:
                continue
            P.emit("scalar",
                   lambda s, dst=dst: s.activation(sqstage[:, :], dst, AF.Square),
                   waits=[(dvM.sem, rope_done), (peM.sem, prev_norm_pe), (acM.sem, acM.n)],
                   inc=acM.bump())
            sq_done = acM.n
            P.emit("tensor",
                   lambda t: t.matmul(sw_ps[0:1, :], onescb_sb[:, :], sqstage[:, :],
                                      start=True, stop=True),
                   waits=[(acM.sem, sq_done), (dvM.sem, prev_t2)],
                   inc=peM.bump())
            prev_norm_pe = peM.n
            if is_q:
                dst_red = cmax[0:1, (tg * 4 + hx):(tg * 4 + hx + 1)]
            else:
                dst_red = kmaxs[0:1, tg:tg + 1]
            P.emit("vector",
                   lambda v, dst_red=dst_red: v.reduce_max(dst_red, sw_ps[0:1, :], axis=AX.X),
                   waits=[(peM.sem, prev_norm_pe), (dvM.sem, dvM.n)], inc=dvM.bump())
            prev_red = dvM.n
        if p1_lvl >= 5:
            P.emit("scalar",
                   lambda s: s.copy(vstage[:, :], kv_ps[:, 512:1024]),
                   waits=[(peM.sem, dloop_done), (acM.sem, acM.n)], inc=acM.bump())
            vevac_done = acM.n
            prev_vt_copy = 0
            for i in range(4):
                P.emit("tensor",
                       lambda t, i=i: t.transpose(vt_ps[:, :], vstage[:, i * 128:i * 128 + 128],
                                                  ident_sb[:, :]),
                       waits=[(acM.sem, vevac_done), (dvM.sem, prev_vt_copy)],
                       inc=peM.bump())
                tr_done = peM.n
                P.emit("vector",
                       lambda v, i=i, tg=tg: v.tensor_copy(
                           Vtok[:, (tg * 4 + i) * 128:(tg * 4 + i) * 128 + 128], vt_ps[:, :]),
                       waits=[(peM.sem, tr_done), (dvM.sem, dvM.n)], inc=dvM.bump())
                prev_vt_copy = dvM.n
        cs_consumed_at.append(dvM.n)
        if tg + 2 < NTG:
            emit_cs_load(tg + 2)
        prev_ep_act = acM.n
        prev_ep_dve = dvM.n

    phase1_dve = dvM.n
    phase1_act = acM.n

    if p1_tgs is not None:
        sT = Ctr(stk, nc, "sT", 16)
        P.emit("gpsimd",
               lambda g: g.dma_start(out=outT[0:128, :], in_=KT[:, 0:To]),
               waits=[(dvM.sem, phase1_dve), (acM.sem, phase1_act)], inc=sT.bump())
        P.emit("gpsimd",
               lambda g: g.dma_start(out=outT[128:256, :], in_=q4[:, 0, 0:To]),
               inc=sT.bump())
        P.emit("gpsimd",
               lambda g: g.dma_start(out=outT[256:384, :], in_=Vtok[:, 0:To]),
               inc=sT.bump())
        P.emit("gpsimd", None, waits=[(sT.sem, sT.n)])
        def finish1():
            def runner(eng_ops):
                def _run(eng):
                    for fn, waits, inc in eng_ops:
                        if fn is None:
                            for (sem, val) in waits:
                                eng.wait_ge(sem, val)
                            continue
                        if len(waits) > 1:
                            for (sem, val) in waits[:-1]:
                                eng.wait_ge(sem, val)
                        inst = fn(eng)
                        if waits:
                            inst.wait_op(waits[-1][0], waits[-1][1], "sem-ge")
                        if inc is not None:
                            inst.then_inc(inc[0], inc[1])
                return _run
            with nc.Block() as block:
                block.tensor(runner(P.ops["tensor"]))
                block.vector(runner(P.ops["vector"]))
                block.scalar(runner(P.ops["scalar"]))
                block.gpsimd(runner(P.ops["gpsimd"]))
                block.sync(runner(P.ops["sync"]))
            return nc
        nc_out = finish1()
        ph1.close(); pp1.close(); ph12.close(); stk.close()
        return nc_out

    P.emit("vector",
           lambda v: v.reduce_max(kmaxs[0:1, NTG:NTG + 1], kmaxs[0:1, 0:NTG], axis=AX.X),
           waits=[(dvM.sem, phase1_dve)], inc=dvM.bump())
    P.emit("vector",
           lambda v: v.tensor_scalar(rowbuf[0:1, 0:4 * NQC], cmax[0:1, :],
                                     kmaxs[0:1, NTG:NTG + 1], 1.21, ALU.mult, ALU.mult),
           waits=[(dvM.sem, dvM.n)], inc=dvM.bump())
    u_done = dvM.n
    P.emit("scalar",
           lambda s: s.activation(rowbuf[0:1, 1024:1024 + 4 * NQC], rowbuf[0:1, 0:4 * NQC],
                                  AF.Ln, bias=1e-5),
           waits=[(dvM.sem, u_done), (sW.sem, sW.n), (dvM.sem, consts_ready),
                  (acM.sem, acM.n)],
           inc=acM.bump())
    P.emit("scalar",
           lambda s: s.activation(rowbuf[0:1, 2048:2048 + 4 * NQC],
                                  rowbuf[0:1, 1024:1024 + 4 * NQC], AF.Exp, scale=0.5),
           waits=[(acM.sem, acM.n)], inc=acM.bump())
    c_done = acM.n
    P.emit("vector",
           lambda v: v.tensor_scalar_mul(cneg[0:1, :], rowbuf[0:1, 2048:2048 + 4 * NQC], -1.0),
           waits=[(acM.sem, c_done), (dvM.sem, dvM.n)], inc=dvM.bump())
    cneg_done = dvM.n

    dbg_tail = []

    def finish():
        def runner(eng_ops):
            def _run(eng):
                for fn, waits, inc in eng_ops:
                    if fn is None:
                        for (sem, val) in waits:
                            eng.wait_ge(sem, val)
                        continue
                    if len(waits) > 1:
                        for (sem, val) in waits[:-1]:
                            eng.wait_ge(sem, val)
                    inst = fn(eng)
                    if waits:
                        inst.wait_op(waits[-1][0], waits[-1][1], "sem-ge")
                    if inc is not None:
                        inst.then_inc(inc[0], inc[1])
            return _run

        with allow_ici_all_to_all():
            with nc.Block() as block:
                block.tensor(runner(P.ops["tensor"]))
                block.vector(runner(P.ops["vector"]))
                block.scalar(runner(P.ops["scalar"]))
                block.gpsimd(runner(P.ops["gpsimd"]))
                block.sync(runner(P.ops["sync"]))
        return nc

    if upto == "p1":
        sT = Ctr(stk, nc, "sT", 16)
        P.emit("gpsimd",
               lambda g: g.dma_start(out=outT[0:128, :], in_=KT[:, 0:To]),
               waits=[(dvM.sem, cneg_done)], inc=sT.bump())
        P.emit("gpsimd",
               lambda g: g.dma_start(out=outT[128:256, :], in_=q4[:, 0, 0:To]),
               inc=sT.bump())
        P.emit("gpsimd",
               lambda g: g.dma_start(out=outT[256:384, :], in_=Vtok[:, 0:To]),
               inc=sT.bump())
        P.emit("gpsimd", None, waits=[(sT.sem, sT.n)])
        nc_out = finish()
        ph1.close(); pp1.close(); ph12.close(); stk.close()
        return nc_out

    ph1.close()
    pp1.close()

    ph2 = contextlib.ExitStack()
    pt = [[ph2.enter_context(nc.sbuf_tensor("pt%d_%d" % (hp, par), [128, 2, 1024], BF16))
           for par in range(2)] for hp in range(2)]
    bias_sb = ph2.enter_context(nc.sbuf_tensor("biasc_sb", [128, 4], F32))
    o_sb = [ph2.enter_context(nc.sbuf_tensor("o_sb%d" % i, [128, 512], F32)) for i in range(2)]
    l_sb = [ph2.enter_context(nc.sbuf_tensor("l_sb%d" % i, [128, 512], F32)) for i in range(2)]
    linv_row = [ph2.enter_context(nc.sbuf_tensor("linv_row%d" % i, [1, 512], F32))
                for i in range(2)]
    linv_bc = [ph2.enter_context(nc.sbuf_tensor("linv_bc%d" % i, [128, 512], F32))
               for i in range(2)]
    atst = [ph2.enter_context(nc.sbuf_tensor("atst%d" % i, [128, 512], BF16)) for i in range(8)]

    pp2 = contextlib.ExitStack()
    s_ps = [pp2.enter_context(nc.psum_tensor("s_ps%d" % i, [128, 1024], F32)) for i in range(2)]
    o_ps = [pp2.enter_context(nc.psum_tensor("o_ps%d" % i, [128, 512], F32)) for i in range(2)]
    l_ps = [pp2.enter_context(nc.psum_tensor("l_ps%d" % i, [128, 512], F32)) for i in range(2)]

    pair_idx = 0
    pending_norm = []
    l_free_at = [0, 0]
    l_zero_at = [0, 0]
    linv_pe_guard = [0, 0]
    exp_half_at = {}
    group_l_done = {}
    ghp_count = [0, 0]
    atst_stored_at = [0] * 8
    o_free_at = [0, 0]
    o_sb_free = [0, 0]
    linv_bc_free = [0, 0]
    bank_guard = [0, 0]

    for qc in range(NQC):
        pro_waits = [(acM.sem, acM.n), (dvM.sem, cneg_done)]
        if qc == 0:
            pro_waits += [(dvM.sem, max(phase1_dve, cneg_done)), (acM.sem, phase1_act)]
        P.emit("tensor",
               lambda t, qc=qc: t.matmul(s_ps[0][:, 0:4], onesrb_sb[:, :],
                                         cneg[0:1, 4 * qc:4 * qc + 4], start=True, stop=True),
               waits=pro_waits, inc=peM.bump())
        bias_mm = peM.n
        P.emit("scalar",
               lambda s: s.copy(bias_sb[:, :], s_ps[0][:, 0:4]),
               waits=[(peM.sem, bias_mm), (acM.sem, acM.n)], inc=acM.bump())
        bias_ready = acM.n
        bank_guard[0] = bias_ready

        for pas in range(2):
            for hp in range(2):
                P.emit("vector",
                       lambda v, hp=hp: v.memset(l_ps[hp][:, :], 0.0),
                       waits=[(acM.sem, l_free_at[hp]), (dvM.sem, dvM.n)],
                       inc=dvM.bump())
                l_zero_at[hp] = dvM.n
            pending_avs = None
            for kc in range(qc + 1):
                diag = (kc == qc)
                for hp in range(2):
                    h = pas * 2 + hp
                    gpar = ghp_count[hp] % 2
                    pair_avs = []
                    for pj in range(2):
                        pg = pair_idx
                        bank = pg % 2
                        for sj in range(2):
                            ks = pj * 2 + sj
                            w = []
                            if sj == 0 and pg >= 2:
                                w.append((acM.sem, max(exp_half_at[(pg - 2, 1)],
                                                       bank_guard[bank])))
                            elif sj == 0:
                                w.append((acM.sem, bank_guard[bank]))
                            P.emit("tensor",
                                   lambda t, kc=kc, ks=ks, h=h, qc=qc, bank=bank, sj=sj:
                                   t.matmul(s_ps[bank][:, sj * 512:sj * 512 + 512],
                                            KT[:, kc * 512 + ks * 128:kc * 512 + ks * 128 + 128],
                                            q4[:, h, qc * 512:qc * 512 + 512],
                                            start=True, stop=True),
                                   waits=w, inc=peM.bump() if sj == 1 else None)
                        sdone = peM.n
                        ew = [(peM.sem, sdone), (acM.sem, acM.n)]
                        if (hp, gpar) in group_l_done:
                            ew.append((peM.sem, group_l_done[(hp, gpar)]))
                        P.emit("scalar",
                               lambda s, bank=bank, hp=hp, gpar=gpar, pj=pj,
                               hcol=pas * 2 + hp:
                               s.activation(pt[hp][gpar][:, pj, :], s_ps[bank][:, :],
                                            AF.Exp, bias=bias_sb[:, hcol:hcol + 1]),
                               waits=ew, inc=acM.bump())
                        exp_half_at[(pg, 1)] = acM.n
                        avw = (acM.sem, acM.n)
                        if diag:
                            P.emit("vector",
                                   lambda v, hp=hp, gpar=gpar, pj=pj:
                                   v.tensor_mul(pt[hp][gpar][:, pj, :],
                                                pt[hp][gpar][:, pj, :],
                                                masks_sb[:, pj * 1024:pj * 1024 + 1024]),
                                   waits=[(acM.sem, acM.n), (dvM.sem, dvM.n)],
                                   inc=dvM.bump())
                            avw = (dvM.sem, dvM.n)
                        pair_avs.append((pj, avw))
                        pair_idx += 1
                    if pending_avs is not None:
                        pending_avs()
                    def _avs(hp=hp, gpar=gpar, kc=kc, diag=diag, pair_avs=pair_avs,
                             first=(kc == 0)):
                        for (ppj, avw) in pair_avs:
                            st0 = (first and ppj == 0)
                            sp0 = (diag and ppj == 1)
                            for sj in range(2):
                                ks = ppj * 2 + sj
                                kslice = slice((kc * 4 + ks) * 128,
                                               (kc * 4 + ks) * 128 + 128)
                                avkw = []
                                if sj == 0:
                                    avkw.append(avw)
                                if st0 and sj == 0:
                                    avkw.append((dvM.sem, o_free_at[hp]))
                                P.emit("tensor",
                                       lambda t, gpar=gpar, ppj=ppj, sj=sj, kslice=kslice,
                                       hp=hp, st=(st0 and sj == 0), sp=(sp0 and sj == 1):
                                       t.matmul(o_ps[hp][:, :], Vtok[:, kslice],
                                                pt[hp][gpar][:, ppj, sj * 512:sj * 512 + 512],
                                                start=st, stop=sp),
                                       waits=avkw)
                        for ksl in range(4):
                            ppj, sj = divmod(ksl, 2)
                            lw = []
                            if ksl == 0 and first:
                                lw.append((dvM.sem, l_zero_at[hp]))
                            P.emit("tensor",
                                   lambda t, gpar=gpar, ppj=ppj, sj=sj, hp=hp, ksl=ksl,
                                   sp=diag:
                                   t.matmul(l_ps[hp][32 * ksl:32 * ksl + 1, :],
                                            onescb_sb[:, :],
                                            pt[hp][gpar][:, ppj, sj * 512:sj * 512 + 512],
                                            start=False, stop=sp, skip_group_check=True,
                                            tile_position=(0, 32 * ksl)),
                                   waits=lw,
                                   inc=peM.bump() if ksl == 3 else None)
                        group_l_done[(hp, gpar)] = peM.n
                    pending_avs = _avs
                    ghp_count[hp] += 1
            pending_avs()
            pending_avs = None
            for fn in pending_norm:
                fn()
            pending_norm = []
            for hp in range(2):
                h = pas * 2 + hp
                gl = group_l_done[(hp, (ghp_count[hp] - 1) % 2)]
                P.emit("vector",
                       lambda v, hp=hp: v.tensor_copy(l_sb[hp][:, :], l_ps[hp][:, :]),
                       waits=[(peM.sem, gl), (dvM.sem, dvM.n)], inc=dvM.bump())
                levac_done = dvM.n
                P.emit("tensor",
                       lambda t, hp=hp: t.matmul(l_ps[hp][0:1, :], ones4_sb[:, :],
                                                 l_sb[hp][:, :], start=True, stop=True),
                       waits=[(dvM.sem, levac_done)], inc=peM.bump())
                lsum_done = peM.n
                P.emit("scalar",
                       lambda s, hp=hp: s.activation(linv_row[hp][:, :], l_ps[hp][0:1, :],
                                                     AF.Ln),
                       waits=[(peM.sem, max(lsum_done, linv_pe_guard[hp])), (acM.sem, acM.n)],
                       inc=acM.bump())
                ln_done = acM.n
                P.emit("scalar",
                       lambda s, hp=hp: s.activation(linv_row[hp][:, :], linv_row[hp][:, :],
                                                     AF.Exp, scale=-1.0),
                       waits=[(acM.sem, acM.n)], inc=acM.bump())
                linv_done = acM.n
                P.emit("vector",
                       lambda v, hp=hp: v.tensor_copy(o_sb[hp][:, :], o_ps[hp][:, :]),
                       waits=[(peM.sem, gl), (dvM.sem, max(dvM.n, o_sb_free[hp]))],
                       inc=dvM.bump())
                o_free_at[hp] = dvM.n
                l_free_at[hp] = ln_done
                def _norm(hp=hp, h=h, qc=qc, linv_done=linv_done):
                    P.emit("tensor",
                           lambda t, hp=hp: t.matmul(s_ps[hp][:, 0:512], onesrf_sb[:, :],
                                                     linv_row[hp][:, :], start=True, stop=True),
                           waits=[(acM.sem, max(linv_done, acM.n))], inc=peM.bump())
                    bc_done = peM.n
                    linv_pe_guard[hp] = bc_done
                    P.emit("scalar",
                           lambda s, hp=hp: s.copy(linv_bc[hp][:, :], s_ps[hp][:, 0:512]),
                           waits=[(peM.sem, bc_done), (acM.sem, acM.n),
                                  (dvM.sem, linv_bc_free[hp])],
                           inc=acM.bump())
                    bank_guard[hp] = acM.n
                    bcev_done = acM.n
                    sti = (qc * 4 + h) % 8
                    P.emit("vector",
                           lambda v, hp=hp, sti=sti: v.tensor_mul(atst[sti][:, :],
                                                                  o_sb[hp][:, :],
                                                                  linv_bc[hp][:, :]),
                           waits=[(acM.sem, bcev_done), (sA2[sti].sem, atst_stored_at[sti]),
                                  (dvM.sem, dvM.n)],
                           inc=dvM.bump())
                    o_sb_free[hp] = dvM.n
                    linv_bc_free[hp] = dvM.n
                    mul_done = dvM.n
                    d_dest = qc // NTH
                    tof = (qc % NTH) * 512
                    P.emit("sync",
                           lambda sy, h=h, sti=sti, d_dest=d_dest, tof=tof: sy.dma_start(
                               out=a2a_in.ap()[d_dest, h * 128:h * 128 + 128, tof:tof + 512],
                               in_=atst[sti][:, :]),
                           waits=[(dvM.sem, mul_done)], inc=sA2[sti].bump())
                    atst_stored_at[sti] = sA2[sti].n
                pending_norm.append(_norm)

    for fn in pending_norm:
        fn()
    pending_norm = []

    P.emit("gpsimd",
           lambda g: g.collective_compute(
               "AllToAll", ALU.bypass, replica_groups=[list(range(NCORES))],
               ins=[a2a_in.ap().opt()], outs=[a2a_out.ap().opt()]),
           waits=[(s.sem, s.n) for s in sA2], inc=ccs.bump())

    if upto == "p2":
        sT = Ctr(stk, nc, "sT", 16)
        P.emit("gpsimd",
               lambda g: g.dma_start(out=outT[:, :],
                                     in_=a2a_out.ap().rearrange("r f t -> (r f) t")),
               waits=[(ccs.sem, 1)], inc=sT.bump())
        P.emit("gpsimd", None, waits=[(sT.sem, sT.n)])
        nc_out = finish()
        ph2.close(); pp2.close(); ph12.close(); stk.close()
        return nc_out

    ph2.close()
    pp2.close()
    ph12.close()

    ph34 = contextlib.ExitStack()
    st_sb = [ph34.enter_context(nc.sbuf_tensor("st_sb%d" % i, [128, To], F32)) for i in range(4)]
    r2st = [ph34.enter_context(nc.sbuf_tensor("r2st%d" % i, [128, To], F32)) for i in range(2)]
    sq_sb = ph34.enter_context(nc.sbuf_tensor("sq_sb", [128, To], F32))
    acc_s = ph34.enter_context(nc.sbuf_tensor("acc_s", [128, To], F32))
    acc_q = ph34.enter_context(nc.sbuf_tensor("acc_q", [128, To], F32))
    m_bc = ph34.enter_context(nc.sbuf_tensor("m_bc", [128, To], F32))
    rstd_bc = ph34.enter_context(nc.sbuf_tensor("rstd_bc", [128, To], F32))
    sc_bc = ph34.enter_context(nc.sbuf_tensor("sc_bc", [128, To], F32))

    ph25 = contextlib.ExitStack()
    at_own = ph25.enter_context(nc.sbuf_tensor("at_own", [128, n_ft, To], BF16))
    wo_sb = [ph25.enter_context(nc.sbuf_tensor("wo_sb%d" % i, [128, n_ft, 256], BF16))
             for i in range(2)]

    pp3 = contextlib.ExitStack()
    psA = [pp3.enter_context(nc.psum_tensor("psA%d" % i, [128, To], F32)) for i in range(4)]

    P.emit("sync",
           lambda sy: sy.dma_start(out=at_own[:, :, :],
                                   in_=a2a_out.ap().rearrange("r (i p) t -> p (r i) t", p=128)),
           waits=[(ccs.sem, 1)], inc=sAo.bump())
    at_own_ready = sAo.n

    n_grp = n_ft // 2
    wo_grp_free_at = [0, 0]
    wo_stripe_free_at = [0, 0]
    st_free_at = [0, 0]
    st_free_acm = [0, 0]
    xo_used_at = [0, 0]
    acc_chain = 0

    def emit_wo_stripe(g):
        b = g % 2
        w = [(ccs.sem, 1)] if g < 2 else []
        if g >= 2:
            w.append((peM.sem, wo_stripe_free_at[b]))
        P.emit("gpsimd",
               lambda gp, g=g, b=b: gp.dma_start(
                   out=wo_sb[b][:, :, :],
                   in_=wo[:, g * 256:(g + 1) * 256].rearrange("(i p) f -> p i f", p=128)),
               waits=w, inc=sWo[b].bump())

    emit_wo_stripe(0)
    emit_wo_stripe(1)
    for ft in range(min(2, n_ft)):
        P.emit("sync",
               lambda sy, ft=ft: sy.dma_start(
                   out=xo_sb[ft % 2][:, :], in_=xoT[ft * 128:(ft + 1) * 128, :]),
               inc=sXo[ft % 2].bump())

    for g in range(n_grp):
        b = g % 2
        mmw = [(sWo[b].sem, 16 * (g // 2 + 1)), (sAo.sem, at_own_ready)]
        if g >= 2:
            mmw.append((dvM.sem, wo_grp_free_at[b]))
        for d in range(n_ft):
            for jj in range(2):
                for th in range(NTH):
                    P.emit("tensor",
                           lambda t, b=b, d=d, jj=jj, th=th, st=(d == 0), sp=(d == n_ft - 1):
                           t.matmul(psA[2 * b + jj][:, th * 512:th * 512 + 512],
                                    wo_sb[b][:, d, jj * 128:jj * 128 + 128],
                                    at_own[:, d, th * 512:th * 512 + 512],
                                    start=st, stop=sp),
                           waits=mmw if (d == 0 and jj == 0 and th == 0) else (),
                           inc=peM.bump() if (d == n_ft - 1 and jj == 1 and th == NTH - 1) else None)
        grp_done = peM.n
        wo_stripe_free_at[b] = grp_done
        if g + 2 < n_grp:
            emit_wo_stripe(g + 2)
        for jj in range(2):
            ft = 2 * g + jj
            sb = ft % 2
            P.emit("vector",
                   lambda v, b=b, jj=jj, sb=sb: v.tensor_add(
                       st_sb[sb][:, :], psA[2 * b + jj][:, 0:To], xo_sb[sb][:, :]),
                   waits=[(peM.sem, grp_done), (sXo[sb].sem, 16 * (ft // 2 + 1)),
                          (sSw[sb].sem, st_free_at[sb]), (acM.sem, st_free_acm[sb]),
                          (dvM.sem, dvM.n)],
                   inc=dvM.bump())
            add_done = dvM.n
            xo_used_at[sb] = add_done
            if jj == 1:
                wo_grp_free_at[b] = add_done
            if ft + 2 < n_ft:
                P.emit("sync",
                       lambda sy, ft=ft, sb=sb: sy.dma_start(
                           out=xo_sb[sb][:, :], in_=xoT[(ft + 2) * 128:(ft + 3) * 128, :]),
                       waits=[(dvM.sem, add_done)], inc=sXo[sb].bump())
            P.emit("sync",
                   lambda sy, ft=ft, sb=sb: sy.dma_start(
                       out=s_dram.ap()[ft * 128:(ft + 1) * 128, :], in_=st_sb[sb][:, :]),
                   waits=[(dvM.sem, add_done)], inc=sSw[sb].bump())
            st_free_at[sb] = sSw[sb].n
            if ft == 0:
                P.emit("vector", lambda v, sb=sb: v.tensor_copy(acc_s[:, :], st_sb[sb][:, :]),
                       waits=[(dvM.sem, add_done)], inc=dvM.bump())
            else:
                P.emit("vector", lambda v, sb=sb: v.tensor_add(acc_s[:, :], acc_s[:, :],
                                                               st_sb[sb][:, :]),
                       waits=[(dvM.sem, max(add_done, acc_chain))], inc=dvM.bump())
            P.emit("scalar", lambda s, sb=sb: s.activation(sq_sb[:, :], st_sb[sb][:, :],
                                                           AF.Square),
                   waits=[(dvM.sem, add_done), (dvM.sem, acc_chain), (acM.sem, acM.n)],
                   inc=acM.bump())
            sqd = acM.n
            st_free_acm[sb] = sqd
            if ft == 0:
                P.emit("vector", lambda v: v.tensor_copy(acc_q[:, :], sq_sb[:, :]),
                       waits=[(acM.sem, sqd), (dvM.sem, dvM.n)], inc=dvM.bump())
            else:
                P.emit("vector", lambda v: v.tensor_add(acc_q[:, :], acc_q[:, :], sq_sb[:, :]),
                       waits=[(acM.sem, sqd), (dvM.sem, dvM.n)], inc=dvM.bump())
            acc_chain = dvM.n

    def emit_ln_finalize(m_dst, r_dst, roff, acc_ready_dvM, sc_dst=None):
        psum, psumq = psA[3], psA[2]
        for half in range(NTH):
            P.emit("tensor",
                   lambda t, half=half: t.matmul(
                       psum[0:1, half * 512:half * 512 + 512], onescf_sb[:, :],
                       acc_s[:, half * 512:half * 512 + 512], start=True, stop=True),
                   waits=[(dvM.sem, acc_ready_dvM), (acM.sem, acM.n)],
                   inc=peM.bump())
            P.emit("tensor",
                   lambda t, half=half: t.matmul(
                       psumq[0:1, half * 512:half * 512 + 512], onescf_sb[:, :],
                       acc_q[:, half * 512:half * 512 + 512], start=True, stop=True),
                   inc=peM.bump())
        mm_done = peM.n
        r0, r1, r2o = roff, roff + 1024, roff + 2048
        P.emit("scalar",
               lambda s: s.copy(rowbuf[0:1, r0:r0 + To], psum[0:1, 0:To]),
               waits=[(peM.sem, mm_done), (acM.sem, acM.n)], inc=acM.bump())
        P.emit("scalar",
               lambda s: s.copy(rowbuf[0:1, r1:r1 + To], psumq[0:1, 0:To]),
               waits=[(acM.sem, acM.n)], inc=acM.bump())
        evr = acM.n
        P.emit("vector",
               lambda v: v.tensor_scalar_mul(rowbuf[0:1, r0:r0 + To],
                                             rowbuf[0:1, r0:r0 + To], 1.0 / DIM),
               waits=[(acM.sem, evr), (dvM.sem, dvM.n)], inc=dvM.bump())
        m_row_done = dvM.n
        P.emit("vector",
               lambda v: v.tensor_scalar_mul(rowbuf[0:1, r1:r1 + To],
                                             rowbuf[0:1, r1:r1 + To], 1.0 / DIM),
               waits=[(dvM.sem, dvM.n)], inc=dvM.bump())
        P.emit("vector",
               lambda v: v.tensor_mul(rowbuf[0:1, r2o:r2o + To],
                                      rowbuf[0:1, r0:r0 + To], rowbuf[0:1, r0:r0 + To]),
               waits=[(dvM.sem, dvM.n)], inc=dvM.bump())
        P.emit("vector",
               lambda v: v.tensor_sub(rowbuf[0:1, r1:r1 + To], rowbuf[0:1, r1:r1 + To],
                                      rowbuf[0:1, r2o:r2o + To]),
               waits=[(dvM.sem, dvM.n)], inc=dvM.bump())
        var_done = dvM.n
        P.emit("scalar",
               lambda s: s.activation(rowbuf[0:1, r2o:r2o + To], rowbuf[0:1, r1:r1 + To],
                                      AF.Ln, bias=1e-5),
               waits=[(dvM.sem, var_done), (acM.sem, acM.n)], inc=acM.bump())
        P.emit("scalar",
               lambda s: s.activation(rowbuf[0:1, r1:r1 + To], rowbuf[0:1, r2o:r2o + To],
                                      AF.Exp, scale=-0.5),
               waits=[(acM.sem, acM.n)], inc=acM.bump())
        rstd_done = acM.n
        sc_done = dvM.n
        if sc_dst is not None:
            P.emit("vector",
                   lambda v: v.tensor_scalar(rowbuf[0:1, r2o:r2o + To],
                                             rowbuf[0:1, r1:r1 + To],
                                             wscale_sb[0:1, 0:1], None, ALU.mult),
                   waits=[(acM.sem, rstd_done), (dvM.sem, dvM.n)], inc=dvM.bump())
            sc_done = dvM.n
        bc_list = [(m_dst, r0), (r_dst, r1)]
        if sc_dst is not None:
            bc_list.append((sc_dst, r2o))
        prev_ev = 0
        for (dst, off) in bc_list:
            for half in range(NTH):
                P.emit("tensor",
                       lambda t, off=off, half=half: t.matmul(
                           psum[:, half * 512:half * 512 + 512], onesrf_sb[:, :],
                           rowbuf[0:1, off + half * 512:off + half * 512 + 512],
                           start=True, stop=True),
                       waits=[(acM.sem, max(rstd_done, prev_ev)), (dvM.sem, sc_done)],
                       inc=peM.bump())
                bmm = peM.n
                P.emit("scalar",
                       lambda s, dst=dst, half=half: s.copy(
                           dst[:, half * 512:half * 512 + 512],
                           psum[:, half * 512:half * 512 + 512]),
                       waits=[(peM.sem, bmm), (acM.sem, acM.n)], inc=acM.bump())
                prev_ev = acM.n
        return acM.n

    bc1_done = emit_ln_finalize(m_bc, rstd_bc, 0, acc_chain, sc_dst=sc_bc)

    if upto == "p25":
        sT = Ctr(stk, nc, "sT", 16)
        P.emit("sync",
               lambda sy: sy.dma_start(out=outT[:, :], in_=s_dram.ap()),
               waits=[(sSw[0].sem, sSw[0].n), (sSw[1].sem, sSw[1].n),
                      (acM.sem, bc1_done)], inc=sT.bump())
        P.emit("sync", None, waits=[(sT.sem, sT.n)])
        nc_out = finish()
        ph25.close(); ph34.close(); pp3.close(); stk.close()
        return nc_out

    ph25.close()
    ph3 = contextlib.ExitStack()
    d1_sb = ph3.enter_context(nc.sbuf_tensor("d1_sb", [128, n_ft, To], BF16))
    w2r = [ph3.enter_context(nc.sbuf_tensor("w2r%d" % i, [128, n_ft, 128], BF16))
           for i in range(3)]
    wf_sb = [ph3.enter_context(nc.sbuf_tensor("wf_sb%d" % i, [128, n_ft, 128], BF16))
             for i in range(3)]

    d1_done_at = []
    for ft in range(n_ft):
        sb = ft % 4
        P.emit("sync",
               lambda sy, ft=ft, sb=sb: sy.dma_start(
                   out=st_sb[sb][:, :], in_=s_dram.ap()[ft * 128:(ft + 1) * 128, :]),
               waits=[(sSw[0].sem, sSw[0].n), (sSw[1].sem, sSw[1].n),
                      (acM.sem, bc1_done),
                      (dvM.sem, d1_done_at[ft - 4] if ft >= 4 else 0)],
               inc=sSr[sb].bump())
        P.emit("vector",
               lambda v, ft=ft, sb=sb: v.tensor_sub(d1_sb[:, ft, :], st_sb[sb][:, :],
                                                    m_bc[:, :]),
               waits=[(sSr[sb].sem, 16 * (ft // 4 + 1)), (acM.sem, bc1_done),
                      (dvM.sem, dvM.n)],
               inc=dvM.bump())
        d1_done_at.append(dvM.n)
    d1_all = dvM.n

    ff_grp_free_at = [0, 0, 0]
    ff_stripe_free_at = [0, 0, 0]
    wf_conv_at = [0, 0, 0]
    st3_free_dv = [0, 0, 0, 0]
    r2st_free_dv = [0, 0]
    r2st_free_ac = [0, 0]
    r2st_spill_at = [0, 0]
    acc2_chain = 0

    for g in range(n_ft):
        b = g % 3
        w = [(sSw[0].sem, sSw[0].n), (sSw[1].sem, sSw[1].n)]
        if g >= 3:
            w.append((dvM.sem, wf_conv_at[b]))
        P.emit("gpsimd",
               lambda gp, g=g, b=b: gp.dma_start(
                   out=w2r[b][:, :, :],
                   in_=w2T[:, g * 128:(g + 1) * 128].rearrange("(i p) f -> p i f", p=128)),
               waits=w, inc=sFf[b].bump())
        P.emit("vector",
               lambda v, b=b: v.tensor_scalar_sub(wf_sb[b][:, :, :], w2r[b][:, :, :], 1.5),
               waits=[(sFf[b].sem, 16 * (g // 3 + 1)), (peM.sem, ff_stripe_free_at[b]),
                      (dvM.sem, dvM.n)],
               inc=dvM.bump())
        wf_conv_at[b] = dvM.n
        conv_done = dvM.n
        mmw = [(dvM.sem, max(conv_done, d1_all))]
        if g >= 3:
            mmw.append((dvM.sem, ff_grp_free_at[b]))
        for d in range(n_ft):
            for th in range(NTH):
                P.emit("tensor",
                       lambda t, b=b, d=d, th=th, st=(d == 0), sp=(d == n_ft - 1):
                       t.matmul(psA[b][:, th * 512:th * 512 + 512],
                                wf_sb[b][:, d, :], d1_sb[:, d, th * 512:th * 512 + 512],
                                start=st, stop=sp),
                       waits=mmw if (d == 0 and th == 0) else (),
                       inc=peM.bump() if (d == n_ft - 1 and th == NTH - 1) else None)
        grp_done = peM.n
        ff_stripe_free_at[b] = grp_done
        sb = g % 4
        sb2 = g % 2
        P.emit("sync",
               lambda sy, g=g, sb=sb: sy.dma_start(
                   out=st_sb[sb][:, :], in_=s_dram.ap()[g * 128:(g + 1) * 128, :]),
               waits=[(dvM.sem, max(d1_all, st3_free_dv[sb]))],
               inc=sS2[sb].bump())
        P.emit("vector",
               lambda v, sb=sb: v.tensor_sub(sq_sb[:, :], st_sb[sb][:, :], m_bc[:, :]),
               waits=[(sS2[sb].sem, 16 * (g // 4 + 1)), (dvM.sem, dvM.n)],
               inc=dvM.bump())
        st3_free_dv[sb] = dvM.n
        P.emit("vector",
               lambda v, sb=sb: v.tensor_mul(sq_sb[:, :], sq_sb[:, :], rstd_bc[:, :]),
               waits=[(dvM.sem, dvM.n)], inc=dvM.bump())
        hh_done = dvM.n
        P.emit("vector",
               lambda v, b=b, sb2=sb2: v.tensor_mul(r2st[sb2][:, :], psA[b][:, 0:To],
                                                    sc_bc[:, :]),
               waits=[(peM.sem, grp_done), (dvM.sem, max(hh_done, r2st_free_dv[sb2])),
                      (acM.sem, r2st_free_ac[sb2]), (sR2w[sb2].sem, r2st_spill_at[sb2])],
               inc=dvM.bump())
        ff_grp_free_at[b] = dvM.n
        P.emit("vector",
               lambda v, g=g, sb2=sb2: v.scalar_tensor_tensor(
                   r2st[sb2][:, :], r2st[sb2][:, :], bias2d_sb[:, g:g + 1], sq_sb[:, :],
                   ALU.add, ALU.add),
               waits=[(dvM.sem, dvM.n)], inc=dvM.bump())
        r2_done = dvM.n
        P.emit("scalar",
               lambda s, g=g, sb2=sb2: s.dma_start(
                   out=r2_dram.ap()[g * 128:(g + 1) * 128, :], in_=r2st[sb2][:, :]),
               waits=[(dvM.sem, r2_done)], inc=sR2w[sb2].bump())
        r2st_spill_at[sb2] = sR2w[sb2].n
        if g == 0:
            P.emit("vector", lambda v, sb2=sb2: v.tensor_copy(acc_s[:, :], r2st[sb2][:, :]),
                   waits=[(dvM.sem, r2_done)], inc=dvM.bump())
        else:
            P.emit("vector", lambda v, sb2=sb2: v.tensor_add(acc_s[:, :], acc_s[:, :],
                                                             r2st[sb2][:, :]),
                   waits=[(dvM.sem, max(r2_done, acc2_chain))], inc=dvM.bump())
        P.emit("scalar", lambda s, sb2=sb2: s.activation(sq_sb[:, :], r2st[sb2][:, :], AF.Square),
               waits=[(dvM.sem, dvM.n), (acM.sem, acM.n)], inc=acM.bump())
        sqd = acM.n
        r2st_free_ac[sb2] = sqd
        if g == 0:
            P.emit("vector", lambda v: v.tensor_copy(acc_q[:, :], sq_sb[:, :]),
                   waits=[(acM.sem, sqd), (dvM.sem, dvM.n)], inc=dvM.bump())
        else:
            P.emit("vector", lambda v: v.tensor_add(acc_q[:, :], acc_q[:, :], sq_sb[:, :]),
                   waits=[(acM.sem, sqd), (dvM.sem, dvM.n)], inc=dvM.bump())
        acc2_chain = dvM.n
        r2st_free_dv[sb2] = acc2_chain

    bc2_done = emit_ln_finalize(m_bc, rstd_bc, 0, acc2_chain)
    dv_after_bc2 = dvM.n

    out_done_at = [dv_after_bc2] * 4
    out_stage_free = [dv_after_bc2, dv_after_bc2]
    out_store_at = [0, 0]
    for ft in range(n_ft):
        sb = ft % 4
        sb2 = ft % 2
        P.emit("sync",
               lambda sy, ft=ft, sb=sb: sy.dma_start(
                   out=st_sb[sb][:, :], in_=r2_dram.ap()[ft * 128:(ft + 1) * 128, :]),
               waits=[(sR2w[0].sem, sR2w[0].n), (sR2w[1].sem, sR2w[1].n),
                      (dvM.sem, out_done_at[sb]), (acM.sem, bc2_done)],
               inc=sR2r[sb].bump())
        P.emit("vector",
               lambda v, sb=sb: v.tensor_sub(sq_sb[:, :], st_sb[sb][:, :], m_bc[:, :]),
               waits=[(sR2r[sb].sem, 16 * (ft // 4 + 1)), (acM.sem, bc2_done),
                      (dvM.sem, dvM.n)],
               inc=dvM.bump())
        out_done_at[sb] = dvM.n
        P.emit("vector",
               lambda v, sb2=sb2: v.tensor_mul(r2st[sb2][:, :], sq_sb[:, :], rstd_bc[:, :]),
               waits=[(dvM.sem, max(dvM.n, out_stage_free[sb2])),
                      (sOut[sb2].sem, out_store_at[sb2])],
               inc=dvM.bump())
        out_stage_free[sb2] = dvM.n
        P.emit("scalar",
               lambda s, ft=ft, sb2=sb2: s.dma_start(
                   out=outT[ft * 128:(ft + 1) * 128, :], in_=r2st[sb2][:, :]),
               waits=[(dvM.sem, dvM.n)], inc=sOut[sb2].bump())
        out_store_at[sb2] = sOut[sb2].n
    P.emit("sync", None, waits=[(sOut[0].sem, sOut[0].n), (sOut[1].sem, sOut[1].n)])

    nc_out = finish()
    ph3.close()
    ph34.close()
    pp3.close()
    stk.close()
    return nc_out


def prep_inputs(L0_attn_x, L0_attn_wq, L0_attn_wk, L0_attn_wv, L0_attn_wo,
                L0_ffn_weight_2bit, L0_ffn_weight_scale, L0_ffn_bias):
    x = np.asarray(L0_attn_x)
    S = x.shape[1]
    x2 = np.ascontiguousarray(x.reshape(S, DIM))
    xT = np.ascontiguousarray(x2.T)

    perm = np.concatenate([np.arange(0, HD, 2), np.arange(1, HD, 2)])
    wq = np.asarray(L0_attn_wq).reshape(DIM, NH, HD)[:, :, perm].reshape(DIM, NH * HD)
    wk = np.asarray(L0_attn_wk).reshape(DIM, NKV, HD)[:, :, perm].reshape(DIM, NKV * HD)
    wv = np.asarray(L0_attn_wv)
    wo = np.ascontiguousarray(np.asarray(L0_attn_wo))
    w2T = np.ascontiguousarray(np.asarray(L0_ffn_weight_2bit).T.astype(np.int32))
    bias2d = np.ascontiguousarray(np.asarray(L0_ffn_bias).reshape(DIM // 128, 128).T)
    wsc = np.ascontiguousarray(np.asarray(L0_ffn_weight_scale).reshape(1, 1).astype(np.float32))

    sc128 = np.float64(HD) ** -0.25
    freqs = 1.0 / THETA ** (np.arange(0, HD, 2, dtype=np.float64) / HD)
    ang = np.outer(freqs, np.arange(S, dtype=np.float64))
    C = (np.concatenate([np.cos(ang), np.cos(ang)], 0) * sc128).astype(np.float32)
    Ssg = (np.concatenate([-np.sin(ang), np.sin(ang)], 0) * sc128).astype(np.float32)
    kk = np.arange(128)[:, None]
    qq = np.arange(512)[None, :]
    masks = np.concatenate([(128 * s + kk <= qq) for s in range(4)],
                           axis=1).astype(ml_dtypes.bfloat16)
    pswap = np.zeros((128, 128), np.float32)
    for i in range(128):
        pswap[i, (i + 64) % 128] = 1.0
    pswap = pswap.astype(ml_dtypes.bfloat16)
    identm = np.eye(128, dtype=ml_dtypes.bfloat16)
    ones4_np = np.zeros((128, 1), np.float32)
    ones4_np[[0, 32, 64, 96], 0] = 1.0

    To = S // NCORES
    maps = []
    for c in range(NCORES):
        maps.append({
            "xT": xT,
            "xoT": np.ascontiguousarray(xT[:, c * To:(c + 1) * To]),
            "wq": np.ascontiguousarray(wq[:, c * HPC * HD:(c + 1) * HPC * HD]),
            "wk": np.ascontiguousarray(wk[:, c * HD:(c + 1) * HD]),
            "wv": np.ascontiguousarray(np.asarray(wv)[:, c * HD:(c + 1) * HD]),
            "wo": wo,
            "w2T": w2T,
            "bias2d": bias2d,
            "wscale": wsc,
            "cosT": C,
            "sinT": Ssg,
            "masks": masks,
            "pswap": pswap,
            "ident": identm,
            "onescb": np.ones((128, 1), ml_dtypes.bfloat16),
            "onesrb": np.ones((1, 128), ml_dtypes.bfloat16),
            "onescf": np.ones((128, 1), np.float32),
            "onesrf": np.ones((1, 128), np.float32),
            "ones4": ones4_np,
        })
    return maps


def kernel(L0_attn_x, L0_attn_wq, L0_attn_wk, L0_attn_wv, L0_attn_wo,
           L0_ffn_weight_2bit, L0_ffn_weight_scale, L0_ffn_bias):
    x = np.asarray(L0_attn_x)
    S = x.shape[1]
    maps = prep_inputs(L0_attn_x, L0_attn_wq, L0_attn_wk, L0_attn_wv, L0_attn_wo,
                       L0_ffn_weight_2bit, L0_ffn_weight_scale, L0_ffn_bias)
    nc = build_program(S)
    with allow_ici_all_to_all():
        res = bass_utils.run_bass_kernel_spmd(nc, maps, core_ids=list(range(NCORES)))
    outs = [np.asarray(r["outT"]).T for r in res.results]
    return np.ascontiguousarray(np.concatenate(outs, axis=0)).reshape(1, S, DIM).astype(np.float32)


# version 09
# speedup vs baseline: 1.3968x; 1.1064x over previous
import contextlib
import numpy as np
import ml_dtypes

import concourse.bass as bass
import concourse.mybir as mybir
import concourse.bass_utils as bass_utils
from concourse.aot_env import allow_ici_all_to_all

F32 = mybir.dt.float32
BF16 = mybir.dt.bfloat16
I32 = mybir.dt.int32
AF = mybir.ActivationFunctionType
ALU = mybir.AluOpType
AX = mybir.AxisListType

DIM = 4096
NH = 32
NKV = 8
HD = 128
NCORES = 8
HPC = 4
THETA = 10000.0

ENGS = ("tensor", "vector", "scalar", "gpsimd", "sync")


class Ctr:

    def __init__(self, stack, nc, name, step=1):
        self.sem = stack.enter_context(nc.semaphore(name))
        self.n = 0
        self.step = step

    def bump(self):
        self.n += self.step
        return (self.sem, self.step)


class Prog:
    def __init__(self):
        self.ops = {e: [] for e in ENGS}

    def emit(self, eng, fn=None, waits=(), inc=None):
        self.ops[eng].append((fn, tuple((s, v) for (s, v) in waits if v > 0), inc))


def build_program(S, upto='full', p1_tgs=None, p1_lvl=6):
    To = S // NCORES
    NTG = S // 512
    NQC = S // 512
    NTH = (To + 511) // 512
    n_ft = DIM // 128

    nc = bass.Bass("TRN2", target_bir_lowering=False)

    xT = nc.dram_tensor("xT", [DIM, S], F32, kind="ExternalInput").ap()
    xoT = nc.dram_tensor("xoT", [DIM, To], F32, kind="ExternalInput").ap()
    wq = nc.dram_tensor("wq", [DIM, HPC * HD], F32, kind="ExternalInput").ap()
    wk = nc.dram_tensor("wk", [DIM, HD], F32, kind="ExternalInput").ap()
    wv = nc.dram_tensor("wv", [DIM, HD], F32, kind="ExternalInput").ap()
    wo = nc.dram_tensor("wo", [DIM, DIM], F32, kind="ExternalInput").ap()
    w2T = nc.dram_tensor("w2T", [DIM, DIM], I32, kind="ExternalInput").ap()
    bias2d = nc.dram_tensor("bias2d", [128, n_ft], F32, kind="ExternalInput").ap()
    wscale = nc.dram_tensor("wscale", [1, 1], F32, kind="ExternalInput").ap()
    cosT = nc.dram_tensor("cosT", [128, S], F32, kind="ExternalInput").ap()
    sinT = nc.dram_tensor("sinT", [128, S], F32, kind="ExternalInput").ap()
    masks = nc.dram_tensor("masks", [128, 4 * 512], BF16, kind="ExternalInput").ap()
    pswap = nc.dram_tensor("pswap", [128, 128], BF16, kind="ExternalInput").ap()
    ident = nc.dram_tensor("ident", [128, 128], BF16, kind="ExternalInput").ap()
    onescb = nc.dram_tensor("onescb", [128, 1], BF16, kind="ExternalInput").ap()
    onesrb = nc.dram_tensor("onesrb", [1, 128], BF16, kind="ExternalInput").ap()
    onescf = nc.dram_tensor("onescf", [128, 1], F32, kind="ExternalInput").ap()
    onesrf = nc.dram_tensor("onesrf", [1, 128], F32, kind="ExternalInput").ap()
    ones4 = nc.dram_tensor("ones4", [128, 1], F32, kind="ExternalInput").ap()
    outT = nc.dram_tensor("outT", [DIM, To], F32, kind="ExternalOutput").ap()

    a2a_in = nc.dram_tensor("a2a_in", [NCORES, HPC * HD, To], BF16)
    a2a_out = nc.dram_tensor("a2a_out", [NCORES, HPC * HD, To], BF16)
    s_dram = nc.dram_tensor("s_dram", [DIM, To], F32)
    r2_dram = nc.dram_tensor("r2_dram", [DIM, To], F32)

    stk = contextlib.ExitStack()

    sW = Ctr(stk, nc, "sW", 16)
    sX = [Ctr(stk, nc, "sX%d" % i, 16) for i in range(8)]
    sCS = [Ctr(stk, nc, "sCS%d" % i, 16) for i in range(2)]
    sWo = [Ctr(stk, nc, "sWo%d" % i, 16) for i in range(2)]
    sFf = [Ctr(stk, nc, "sFf%d" % i, 16) for i in range(3)]
    sXo = [Ctr(stk, nc, "sXo%d" % i, 16) for i in range(2)]
    sSw = [Ctr(stk, nc, "sSw%d" % i, 16) for i in range(2)]
    sSr = [Ctr(stk, nc, "sSr%d" % i, 16) for i in range(4)]
    sS2 = [Ctr(stk, nc, "sS2%d" % i, 16) for i in range(4)]
    sR2w = [Ctr(stk, nc, "sR2w%d" % i, 16) for i in range(2)]
    sR2r = [Ctr(stk, nc, "sR2r%d" % i, 16) for i in range(4)]
    sA2 = [Ctr(stk, nc, "sA2_%d" % i, 16) for i in range(8)]
    sAo = Ctr(stk, nc, "sAo", 16)
    sOut = [Ctr(stk, nc, "sOut%d" % i, 16) for i in range(2)]
    ccs = Ctr(stk, nc, "ccs", 1)
    peM = Ctr(stk, nc, "peM", 1)
    acM = Ctr(stk, nc, "acM", 1)
    dvM = Ctr(stk, nc, "dvM", 1)

    P = Prog()

    masks_sb = stk.enter_context(nc.sbuf_tensor("masks_sb", [128, 4 * 512], BF16))
    pswap_sb = stk.enter_context(nc.sbuf_tensor("pswap_sb", [128, 128], BF16))
    ident_sb = stk.enter_context(nc.sbuf_tensor("ident_sb", [128, 128], BF16))
    onescb_sb = stk.enter_context(nc.sbuf_tensor("onescb_sb", [128, 1], BF16))
    onesrb_sb = stk.enter_context(nc.sbuf_tensor("onesrb_sb", [1, 128], BF16))
    onescf_sb = stk.enter_context(nc.sbuf_tensor("onescf_sb", [128, 1], F32))
    onesrf_sb = stk.enter_context(nc.sbuf_tensor("onesrf_sb", [1, 128], F32))
    ones4_sb = stk.enter_context(nc.sbuf_tensor("ones4_sb", [128, 1], F32))
    bias2d_sb = stk.enter_context(nc.sbuf_tensor("bias2d_sb", [128, n_ft], F32))
    wscale_sb = stk.enter_context(nc.sbuf_tensor("wscale_sb", [1, 1], F32))
    cmax = stk.enter_context(nc.sbuf_tensor("cmax", [1, 4 * NQC], F32))
    kmaxs = stk.enter_context(nc.sbuf_tensor("kmaxs", [1, NTG + 2], F32))
    cneg = stk.enter_context(nc.sbuf_tensor("cneg", [1, 4 * NQC], BF16))
    rowbuf = stk.enter_context(nc.sbuf_tensor("rowbuf", [1, 3 * 1024], F32))
    consts = stk.enter_context(nc.sbuf_tensor("consts", [1, 8], F32))

    ph12 = contextlib.ExitStack()
    q4 = ph12.enter_context(nc.sbuf_tensor("q4", [128, HPC, S], BF16))
    KT = ph12.enter_context(nc.sbuf_tensor("KT", [128, S], BF16))
    Vtok = ph12.enter_context(nc.sbuf_tensor("Vtok", [128, S], BF16))

    nc.const_aps.register(F32, 1e-5, consts[0:1, 0:1])
    nc.const_aps.register(F32, 0.5, consts[0:1, 1:2])
    nc.const_aps.register(F32, -0.5, consts[0:1, 2:3])
    nc.const_aps.register(F32, -1.0, consts[0:1, 3:4])

    P.emit("vector", lambda v: v.memset(consts[0:1, 0:1], 1e-5), inc=dvM.bump())
    P.emit("vector", lambda v: v.memset(consts[0:1, 1:2], 0.5), inc=dvM.bump())
    P.emit("vector", lambda v: v.memset(consts[0:1, 2:3], -0.5), inc=dvM.bump())
    P.emit("vector", lambda v: v.memset(consts[0:1, 3:4], -1.0), inc=dvM.bump())
    consts_ready = dvM.n

    def g_load(dst, src):
        P.emit("gpsimd", lambda g, d=dst, s=src: g.dma_start(out=d, in_=s), inc=sW.bump())

    g_load(masks_sb[:, :], masks)
    g_load(pswap_sb[:, :], pswap)
    g_load(ident_sb[:, :], ident)
    g_load(onescb_sb[:, :], onescb)
    g_load(onesrb_sb[:, :], onesrb)
    g_load(onescf_sb[:, :], onescf)
    g_load(onesrf_sb[:, :], onesrf)
    g_load(ones4_sb[:, :], ones4)
    g_load(bias2d_sb[:, :], bias2d)
    g_load(wscale_sb[:, :], wscale)

    ph1 = contextlib.ExitStack()
    wq_sb = ph1.enter_context(nc.sbuf_tensor("wq_sb", [128, n_ft, HPC * HD], BF16))
    wk_sb = ph1.enter_context(nc.sbuf_tensor("wk_sb", [128, n_ft, HD], BF16))
    wv_sb = ph1.enter_context(nc.sbuf_tensor("wv_sb", [128, n_ft, HD], BF16))
    xbuf = [ph1.enter_context(nc.sbuf_tensor("xbuf%d" % i, [128, 512], BF16)) for i in range(8)]
    csbuf = [ph1.enter_context(nc.sbuf_tensor("csbuf%d" % i, [128, 2, 512], F32)) for i in range(2)]
    qkst = ph1.enter_context(nc.sbuf_tensor("qkst", [128, 6, 512], BF16))
    sqst = ph1.enter_context(nc.sbuf_tensor("sqst", [128, 5, 512], BF16))
    t1b = ph1.enter_context(nc.sbuf_tensor("t1b", [128, 512], F32))
    t2buf = ph1.enter_context(nc.sbuf_tensor("t2buf", [128, 5, 512], F32))

    pp1 = contextlib.ExitStack()
    q_ps = pp1.enter_context(nc.psum_tensor("q_ps", [128, 4 * 512], F32))
    kv_ps = pp1.enter_context(nc.psum_tensor("kv_ps", [128, 2 * 512], F32))
    sw_ps = pp1.enter_context(nc.psum_tensor("sw_ps", [128, 512], F32))
    vt_ps = pp1.enter_context(nc.psum_tensor("vt_ps", [128, 128], BF16))

    g_load(wq_sb[:, :, :], wq.rearrange("(i p) f -> p i f", p=128))
    g_load(wk_sb[:, :, :], wk.rearrange("(i p) f -> p i f", p=128))
    g_load(wv_sb[:, :, :], wv.rearrange("(i p) f -> p i f", p=128))
    w_loaded = sW.n

    total_tiles = NTG * n_ft
    xt_consumed_at = []
    cs_consumed_at = []

    if upto == "p0":
        sT = Ctr(stk, nc, "sT", 16)
        P.emit("gpsimd",
               lambda g: g.dma_start(out=outT[0:128, 0:512], in_=wq_sb[:, 0, :]),
               waits=[(sW.sem, sW.n)], inc=sT.bump())
        P.emit("gpsimd", None, waits=[(sT.sem, sT.n)])
        def finish0():
            def runner(eng_ops):
                def _run(eng):
                    for fn, waits, inc in eng_ops:
                        if fn is None:
                            for (sem, val) in waits:
                                eng.wait_ge(sem, val)
                            continue
                        if len(waits) > 1:
                            for (sem, val) in waits[:-1]:
                                eng.wait_ge(sem, val)
                        inst = fn(eng)
                        if waits:
                            inst.wait_op(waits[-1][0], waits[-1][1], "sem-ge")
                        if inc is not None:
                            inst.then_inc(inc[0], inc[1])
                return _run
            with nc.Block() as block:
                block.tensor(runner(P.ops["tensor"]))
                block.vector(runner(P.ops["vector"]))
                block.scalar(runner(P.ops["scalar"]))
                block.gpsimd(runner(P.ops["gpsimd"]))
                block.sync(runner(P.ops["sync"]))
            return nc
        nc_out = finish0()
        ph1.close(); pp1.close(); ph12.close(); stk.close()
        return nc_out

    def emit_xt_load(j):
        tg, d = divmod(j, n_ft)
        b = j % 8
        waits = [(sW.sem, w_loaded)]
        if j >= 8:
            waits.append((peM.sem, xt_consumed_at[j - 8]))
        P.emit("gpsimd",
               lambda g, b=b, tg=tg, d=d: g.dma_start(
                   out=xbuf[b][:, :], in_=xT[d * 128:(d + 1) * 128, tg * 512:(tg + 1) * 512]),
               waits=waits, inc=sX[b].bump())

    def emit_cs_load(tg):
        b = tg % 2
        waits = []
        if tg >= 2:
            waits.append((dvM.sem, cs_consumed_at[tg - 2]))
        P.emit("sync",
               lambda sy, b=b, tg=tg: sy.dma_start(
                   out=csbuf[b][:, 0, :], in_=cosT[:, tg * 512:(tg + 1) * 512]),
               waits=waits, inc=sCS[b].bump())
        P.emit("sync",
               lambda sy, b=b, tg=tg: sy.dma_start(
                   out=csbuf[b][:, 1, :], in_=sinT[:, tg * 512:(tg + 1) * 512]),
               inc=sCS[b].bump())

    NTG_RUN = NTG if p1_tgs is None else min(NTG, p1_tgs)
    total_tiles = NTG_RUN * n_ft
    xt_next = 0
    for j in range(min(8, total_tiles)):
        emit_xt_load(j)
        xt_next += 1
    emit_cs_load(0)
    if NTG > 1:
        emit_cs_load(1)

    prev_ep_act = 0
    prev_ep_dve = 0

    pending_p1 = None
    red_last = [0]
    vt_copy_last = [0]
    rope_all_prev = 0
    t2_last_prev = 0
    evac5_prev = 0
    vt_pe_prev = 0
    sq_pe_prev = 0

    for tg in range(NTG_RUN):
        for d in range(n_ft):
            j = tg * n_ft + d
            b = j % 8
            waits = [(sX[b].sem, 16 * (j // 8 + 1))]
            if d == 0:
                waits.append((acM.sem, evac5_prev))
                waits.append((dvM.sem, t2_last_prev))
            st = (d == 0)
            sp = (d == n_ft - 1)
            for jj in range(HPC):
                P.emit("tensor",
                       lambda t, d=d, jj=jj, b=b, st=st, sp=sp: t.matmul(
                           q_ps[:, jj * 512:jj * 512 + 512],
                           wq_sb[:, d, jj * 128:jj * 128 + 128],
                           xbuf[b][:, :], start=st, stop=sp),
                       waits=waits if jj == 0 else ())
            P.emit("tensor",
                   lambda t, d=d, b=b, st=st, sp=sp: t.matmul(
                       kv_ps[:, 0:512], wk_sb[:, d, :], xbuf[b][:, :], start=st, stop=sp))
            P.emit("tensor",
                   lambda t, d=d, b=b, st=st, sp=sp: t.matmul(
                       kv_ps[:, 512:1024], wv_sb[:, d, :], xbuf[b][:, :], start=st, stop=sp),
                   inc=peM.bump())
            xt_consumed_at.append(peM.n)
            if xt_next < total_tiles:
                emit_xt_load(xt_next)
                xt_next += 1
        dloop_done = peM.n

        if pending_p1 is not None:
            sq_pe_prev, vt_pe_prev = pending_p1()
            pending_p1 = None

        csb = tg % 2
        cs_ready = 32 * (tg // 2 + 1)
        tgs = slice(tg * 512, (tg + 1) * 512)
        evac_at = []
        for hx in range(6):
            src_ps = q_ps[:, hx * 512:hx * 512 + 512] if hx < 4 else \
                kv_ps[:, (hx - 4) * 512:(hx - 4) * 512 + 512]
            w = [(peM.sem, dloop_done), (acM.sem, acM.n)]
            if hx < 5:
                w.append((dvM.sem, rope_all_prev))
            else:
                w.append((peM.sem, vt_pe_prev))
            P.emit("scalar",
                   lambda s, src=src_ps, hx=hx: s.copy(qkst[:, hx, :], src),
                   waits=w, inc=acM.bump())
            evac_at.append(acM.n)
        evac5 = acM.n
        swap_at = []
        for hx in range(5):
            dst_ps = q_ps[:, hx * 512:hx * 512 + 512] if hx < 4 else kv_ps[:, 0:512]
            P.emit("tensor",
                   lambda t, hx=hx, dst=dst_ps: t.matmul(dst, pswap_sb[:, :],
                                                         qkst[:, hx, :],
                                                         start=True, stop=True),
                   waits=[(acM.sem, evac_at[hx])], inc=peM.bump())
            swap_at.append(peM.n)

        for hx in range(5):
            dst_ps = q_ps[:, hx * 512:hx * 512 + 512] if hx < 4 else kv_ps[:, 0:512]
            P.emit("vector",
                   lambda v, src=dst_ps, csb=csb, hx=hx: v.tensor_mul(
                       t2buf[:, hx, :], src, csbuf[csb][:, 1, :]),
                   waits=[(peM.sem, swap_at[hx]), (dvM.sem, dvM.n)] +
                         ([(sCS[csb].sem, cs_ready)] if hx == 0 else []),
                   inc=dvM.bump())
        t2_last = dvM.n
        rope_at = []
        for hx in range(5):
            dst = q4[:, hx, tgs] if hx < 4 else KT[:, tgs]
            P.emit("vector",
                   lambda v, hx=hx, csb=csb: v.tensor_mul(t1b[:, :], qkst[:, hx, :],
                                                          csbuf[csb][:, 0, :]),
                   waits=[(dvM.sem, dvM.n)], inc=dvM.bump())
            P.emit("vector",
                   lambda v, dst=dst, hx=hx: v.tensor_add(dst, t1b[:, :], t2buf[:, hx, :]),
                   waits=[(dvM.sem, dvM.n)], inc=dvM.bump())
            rope_at.append(dvM.n)
        rope_all = dvM.n
        cs_consumed_at.append(rope_all)
        if tg + 2 < NTG:
            emit_cs_load(tg + 2)

        sq_at = []
        for hx in range(5):
            dst = q4[:, hx, tgs] if hx < 4 else KT[:, tgs]
            P.emit("scalar",
                   lambda s, dst=dst, hx=hx: s.activation(sqst[:, hx, :], dst, AF.Square),
                   waits=[(dvM.sem, rope_at[hx]), (peM.sem, sq_pe_prev), (acM.sem, acM.n)],
                   inc=acM.bump())
            sq_at.append(acM.n)

        def _p1_deferred(tg=tg, sq_at=tuple(sq_at), evac5=evac5):
            for hx in range(4):
                P.emit("tensor",
                       lambda t, hx=hx: t.matmul(sw_ps[32 * hx:32 * hx + 1, :],
                                                 onescb_sb[:, :], sqst[:, hx, :],
                                                 start=True, stop=True,
                                                 skip_group_check=True,
                                                 tile_position=(0, 32 * hx)),
                       waits=[(acM.sem, sq_at[hx]), (dvM.sem, red_last[0])],
                       inc=peM.bump() if hx == 3 else None)
            qn_done = peM.n
            for hx in range(4):
                P.emit("vector",
                       lambda v, hx=hx, tg=tg: v.reduce_max(
                           cmax[0:1, (tg * 4 + hx):(tg * 4 + hx + 1)],
                           sw_ps[32 * hx:32 * hx + 1, :], axis=AX.X),
                       waits=[(peM.sem, qn_done), (dvM.sem, dvM.n)], inc=dvM.bump())
            P.emit("tensor",
                   lambda t: t.matmul(sw_ps[0:1, :], onescb_sb[:, :], sqst[:, 4, :],
                                      start=True, stop=True, skip_group_check=True),
                   waits=[(acM.sem, sq_at[4]), (dvM.sem, dvM.n)], inc=peM.bump())
            kn_done = peM.n
            sq_pe_upd = peM.n
            P.emit("vector",
                   lambda v, tg=tg: v.reduce_max(kmaxs[0:1, tg:tg + 1], sw_ps[0:1, :],
                                                 axis=AX.X),
                   waits=[(peM.sem, kn_done), (dvM.sem, dvM.n)], inc=dvM.bump())
            red_last[0] = dvM.n
            prev_vt_copy = vt_copy_last[0]
            vtp = 0
            for i in range(4):
                P.emit("tensor",
                       lambda t, i=i: t.transpose(vt_ps[:, :], qkst[:, 5, :]
                                                  [:, i * 128:i * 128 + 128],
                                                  ident_sb[:, :]),
                       waits=[(acM.sem, evac5), (dvM.sem, prev_vt_copy)],
                       inc=peM.bump())
                vtp = peM.n
                P.emit("vector",
                       lambda v, i=i, tg=tg: v.tensor_copy(
                           Vtok[:, (tg * 4 + i) * 128:(tg * 4 + i) * 128 + 128],
                           vt_ps[:, :]),
                       waits=[(peM.sem, vtp), (dvM.sem, dvM.n)], inc=dvM.bump())
                prev_vt_copy = dvM.n
            vt_copy_last[0] = prev_vt_copy
            return sq_pe_upd, vtp
        pending_p1 = _p1_deferred

        rope_all_prev = rope_all
        t2_last_prev = t2_last
        evac5_prev = evac5
        if tg == NTG_RUN - 1:
            sq_pe_prev, vt_pe_prev = pending_p1()
            pending_p1 = None

    phase1_dve = dvM.n
    phase1_act = acM.n

    if p1_tgs is not None:
        sT = Ctr(stk, nc, "sT", 16)
        P.emit("gpsimd",
               lambda g: g.dma_start(out=outT[0:128, :], in_=KT[:, 0:To]),
               waits=[(dvM.sem, phase1_dve), (acM.sem, phase1_act)], inc=sT.bump())
        P.emit("gpsimd",
               lambda g: g.dma_start(out=outT[128:256, :], in_=q4[:, 0, 0:To]),
               inc=sT.bump())
        P.emit("gpsimd",
               lambda g: g.dma_start(out=outT[256:384, :], in_=Vtok[:, 0:To]),
               inc=sT.bump())
        P.emit("gpsimd", None, waits=[(sT.sem, sT.n)])
        def finish1():
            def runner(eng_ops):
                def _run(eng):
                    for fn, waits, inc in eng_ops:
                        if fn is None:
                            for (sem, val) in waits:
                                eng.wait_ge(sem, val)
                            continue
                        if len(waits) > 1:
                            for (sem, val) in waits[:-1]:
                                eng.wait_ge(sem, val)
                        inst = fn(eng)
                        if waits:
                            inst.wait_op(waits[-1][0], waits[-1][1], "sem-ge")
                        if inc is not None:
                            inst.then_inc(inc[0], inc[1])
                return _run
            with nc.Block() as block:
                block.tensor(runner(P.ops["tensor"]))
                block.vector(runner(P.ops["vector"]))
                block.scalar(runner(P.ops["scalar"]))
                block.gpsimd(runner(P.ops["gpsimd"]))
                block.sync(runner(P.ops["sync"]))
            return nc
        nc_out = finish1()
        ph1.close(); pp1.close(); ph12.close(); stk.close()
        return nc_out

    P.emit("vector",
           lambda v: v.reduce_max(kmaxs[0:1, NTG:NTG + 1], kmaxs[0:1, 0:NTG], axis=AX.X),
           waits=[(dvM.sem, phase1_dve)], inc=dvM.bump())
    P.emit("vector",
           lambda v: v.tensor_scalar(rowbuf[0:1, 0:4 * NQC], cmax[0:1, :],
                                     kmaxs[0:1, NTG:NTG + 1], 1.21, ALU.mult, ALU.mult),
           waits=[(dvM.sem, dvM.n)], inc=dvM.bump())
    u_done = dvM.n
    P.emit("scalar",
           lambda s: s.activation(rowbuf[0:1, 1024:1024 + 4 * NQC], rowbuf[0:1, 0:4 * NQC],
                                  AF.Ln, bias=1e-5),
           waits=[(dvM.sem, u_done), (sW.sem, sW.n), (dvM.sem, consts_ready),
                  (acM.sem, acM.n)],
           inc=acM.bump())
    P.emit("scalar",
           lambda s: s.activation(rowbuf[0:1, 2048:2048 + 4 * NQC],
                                  rowbuf[0:1, 1024:1024 + 4 * NQC], AF.Exp, scale=0.5),
           waits=[(acM.sem, acM.n)], inc=acM.bump())
    c_done = acM.n
    P.emit("vector",
           lambda v: v.tensor_scalar_mul(cneg[0:1, :], rowbuf[0:1, 2048:2048 + 4 * NQC], -1.0),
           waits=[(acM.sem, c_done), (dvM.sem, dvM.n)], inc=dvM.bump())
    cneg_done = dvM.n

    dbg_tail = []

    def finish():
        def runner(eng_ops):
            def _run(eng):
                for fn, waits, inc in eng_ops:
                    if fn is None:
                        for (sem, val) in waits:
                            eng.wait_ge(sem, val)
                        continue
                    if len(waits) > 1:
                        for (sem, val) in waits[:-1]:
                            eng.wait_ge(sem, val)
                    inst = fn(eng)
                    if waits:
                        inst.wait_op(waits[-1][0], waits[-1][1], "sem-ge")
                    if inc is not None:
                        inst.then_inc(inc[0], inc[1])
            return _run

        with allow_ici_all_to_all():
            with nc.Block() as block:
                block.tensor(runner(P.ops["tensor"]))
                block.vector(runner(P.ops["vector"]))
                block.scalar(runner(P.ops["scalar"]))
                block.gpsimd(runner(P.ops["gpsimd"]))
                block.sync(runner(P.ops["sync"]))
        return nc

    if upto == "p1":
        sT = Ctr(stk, nc, "sT", 16)
        P.emit("gpsimd",
               lambda g: g.dma_start(out=outT[0:128, :], in_=KT[:, 0:To]),
               waits=[(dvM.sem, cneg_done)], inc=sT.bump())
        P.emit("gpsimd",
               lambda g: g.dma_start(out=outT[128:256, :], in_=q4[:, 0, 0:To]),
               inc=sT.bump())
        P.emit("gpsimd",
               lambda g: g.dma_start(out=outT[256:384, :], in_=Vtok[:, 0:To]),
               inc=sT.bump())
        P.emit("gpsimd", None, waits=[(sT.sem, sT.n)])
        nc_out = finish()
        ph1.close(); pp1.close(); ph12.close(); stk.close()
        return nc_out

    ph1.close()
    pp1.close()

    ph2 = contextlib.ExitStack()
    pt = [[ph2.enter_context(nc.sbuf_tensor("pt%d_%d" % (hp, par), [128, 2, 1024], BF16))
           for par in range(2)] for hp in range(2)]
    bias_sb = ph2.enter_context(nc.sbuf_tensor("biasc_sb", [128, 4], F32))
    o_sb = [ph2.enter_context(nc.sbuf_tensor("o_sb%d" % i, [128, 512], F32)) for i in range(2)]
    l_sb = [ph2.enter_context(nc.sbuf_tensor("l_sb%d" % i, [128, 512], F32)) for i in range(2)]
    linv_row = [ph2.enter_context(nc.sbuf_tensor("linv_row%d" % i, [1, 512], F32))
                for i in range(2)]
    linv_bc = [ph2.enter_context(nc.sbuf_tensor("linv_bc%d" % i, [128, 512], F32))
               for i in range(2)]
    atst = [ph2.enter_context(nc.sbuf_tensor("atst%d" % i, [128, 512], BF16)) for i in range(8)]

    pp2 = contextlib.ExitStack()
    s_ps = [pp2.enter_context(nc.psum_tensor("s_ps%d" % i, [128, 1024], F32)) for i in range(2)]
    o_ps = [pp2.enter_context(nc.psum_tensor("o_ps%d" % i, [128, 512], F32)) for i in range(2)]
    l_ps = [pp2.enter_context(nc.psum_tensor("l_ps%d" % i, [128, 512], F32)) for i in range(2)]

    pair_idx = 0
    pending_norm = []
    l_free_at = [0, 0]
    l_zero_at = [0, 0]
    linv_pe_guard = [0, 0]
    exp_half_at = {}
    group_l_done = {}
    ghp_count = [0, 0]
    atst_stored_at = [0] * 8
    o_free_at = [0, 0]
    o_sb_free = [0, 0]
    linv_bc_free = [0, 0]
    bank_guard = [0, 0]

    for qc in range(NQC):
        pro_waits = [(acM.sem, acM.n), (dvM.sem, cneg_done)]
        if qc == 0:
            pro_waits += [(dvM.sem, max(phase1_dve, cneg_done)), (acM.sem, phase1_act)]
        P.emit("tensor",
               lambda t, qc=qc: t.matmul(s_ps[0][:, 0:4], onesrb_sb[:, :],
                                         cneg[0:1, 4 * qc:4 * qc + 4], start=True, stop=True),
               waits=pro_waits, inc=peM.bump())
        bias_mm = peM.n
        P.emit("scalar",
               lambda s: s.copy(bias_sb[:, :], s_ps[0][:, 0:4]),
               waits=[(peM.sem, bias_mm), (acM.sem, acM.n)], inc=acM.bump())
        bias_ready = acM.n
        bank_guard[0] = bias_ready

        for pas in range(2):
            for hp in range(2):
                P.emit("vector",
                       lambda v, hp=hp: v.memset(l_ps[hp][:, :], 0.0),
                       waits=[(acM.sem, l_free_at[hp]), (dvM.sem, dvM.n)],
                       inc=dvM.bump())
                l_zero_at[hp] = dvM.n
            pending_avs = None
            for kc in range(qc + 1):
                diag = (kc == qc)
                for hp in range(2):
                    h = pas * 2 + hp
                    gpar = ghp_count[hp] % 2
                    pair_avs = []
                    for pj in range(2):
                        pg = pair_idx
                        bank = pg % 2
                        for sj in range(2):
                            ks = pj * 2 + sj
                            w = []
                            if sj == 0 and pg >= 2:
                                w.append((acM.sem, max(exp_half_at[(pg - 2, 1)],
                                                       bank_guard[bank])))
                            elif sj == 0:
                                w.append((acM.sem, bank_guard[bank]))
                            P.emit("tensor",
                                   lambda t, kc=kc, ks=ks, h=h, qc=qc, bank=bank, sj=sj:
                                   t.matmul(s_ps[bank][:, sj * 512:sj * 512 + 512],
                                            KT[:, kc * 512 + ks * 128:kc * 512 + ks * 128 + 128],
                                            q4[:, h, qc * 512:qc * 512 + 512],
                                            start=True, stop=True),
                                   waits=w, inc=peM.bump() if sj == 1 else None)
                        sdone = peM.n
                        ew = [(peM.sem, sdone), (acM.sem, acM.n)]
                        if (hp, gpar) in group_l_done:
                            ew.append((peM.sem, group_l_done[(hp, gpar)]))
                        P.emit("scalar",
                               lambda s, bank=bank, hp=hp, gpar=gpar, pj=pj,
                               hcol=pas * 2 + hp:
                               s.activation(pt[hp][gpar][:, pj, :], s_ps[bank][:, :],
                                            AF.Exp, bias=bias_sb[:, hcol:hcol + 1]),
                               waits=ew, inc=acM.bump())
                        exp_half_at[(pg, 1)] = acM.n
                        avw = (acM.sem, acM.n)
                        if diag:
                            P.emit("vector",
                                   lambda v, hp=hp, gpar=gpar, pj=pj:
                                   v.tensor_mul(pt[hp][gpar][:, pj, :],
                                                pt[hp][gpar][:, pj, :],
                                                masks_sb[:, pj * 1024:pj * 1024 + 1024]),
                                   waits=[(acM.sem, acM.n), (dvM.sem, dvM.n)],
                                   inc=dvM.bump())
                            avw = (dvM.sem, dvM.n)
                        pair_avs.append((pj, avw))
                        pair_idx += 1
                    if pending_avs is not None:
                        pending_avs()
                    def _avs(hp=hp, gpar=gpar, kc=kc, diag=diag, pair_avs=pair_avs,
                             first=(kc == 0)):
                        for (ppj, avw) in pair_avs:
                            st0 = (first and ppj == 0)
                            sp0 = (diag and ppj == 1)
                            for sj in range(2):
                                ks = ppj * 2 + sj
                                kslice = slice((kc * 4 + ks) * 128,
                                               (kc * 4 + ks) * 128 + 128)
                                avkw = []
                                if sj == 0:
                                    avkw.append(avw)
                                if st0 and sj == 0:
                                    avkw.append((dvM.sem, o_free_at[hp]))
                                P.emit("tensor",
                                       lambda t, gpar=gpar, ppj=ppj, sj=sj, kslice=kslice,
                                       hp=hp, st=(st0 and sj == 0), sp=(sp0 and sj == 1):
                                       t.matmul(o_ps[hp][:, :], Vtok[:, kslice],
                                                pt[hp][gpar][:, ppj, sj * 512:sj * 512 + 512],
                                                start=st, stop=sp),
                                       waits=avkw)
                        for ksl in range(4):
                            ppj, sj = divmod(ksl, 2)
                            lw = []
                            if ksl == 0 and first:
                                lw.append((dvM.sem, l_zero_at[hp]))
                            P.emit("tensor",
                                   lambda t, gpar=gpar, ppj=ppj, sj=sj, hp=hp, ksl=ksl,
                                   sp=diag:
                                   t.matmul(l_ps[hp][32 * ksl:32 * ksl + 1, :],
                                            onescb_sb[:, :],
                                            pt[hp][gpar][:, ppj, sj * 512:sj * 512 + 512],
                                            start=False, stop=sp, skip_group_check=True,
                                            tile_position=(0, 32 * ksl)),
                                   waits=lw,
                                   inc=peM.bump() if ksl == 3 else None)
                        group_l_done[(hp, gpar)] = peM.n
                    pending_avs = _avs
                    ghp_count[hp] += 1
            pending_avs()
            pending_avs = None
            for fn in pending_norm:
                fn()
            pending_norm = []
            for hp in range(2):
                h = pas * 2 + hp
                gl = group_l_done[(hp, (ghp_count[hp] - 1) % 2)]
                P.emit("vector",
                       lambda v, hp=hp: v.tensor_copy(l_sb[hp][:, :], l_ps[hp][:, :]),
                       waits=[(peM.sem, gl), (dvM.sem, dvM.n)], inc=dvM.bump())
                levac_done = dvM.n
                P.emit("tensor",
                       lambda t, hp=hp: t.matmul(l_ps[hp][0:1, :], ones4_sb[:, :],
                                                 l_sb[hp][:, :], start=True, stop=True),
                       waits=[(dvM.sem, levac_done)], inc=peM.bump())
                lsum_done = peM.n
                P.emit("scalar",
                       lambda s, hp=hp: s.activation(linv_row[hp][:, :], l_ps[hp][0:1, :],
                                                     AF.Ln),
                       waits=[(peM.sem, max(lsum_done, linv_pe_guard[hp])), (acM.sem, acM.n)],
                       inc=acM.bump())
                ln_done = acM.n
                P.emit("scalar",
                       lambda s, hp=hp: s.activation(linv_row[hp][:, :], linv_row[hp][:, :],
                                                     AF.Exp, scale=-1.0),
                       waits=[(acM.sem, acM.n)], inc=acM.bump())
                linv_done = acM.n
                P.emit("vector",
                       lambda v, hp=hp: v.tensor_copy(o_sb[hp][:, :], o_ps[hp][:, :]),
                       waits=[(peM.sem, gl), (dvM.sem, max(dvM.n, o_sb_free[hp]))],
                       inc=dvM.bump())
                o_free_at[hp] = dvM.n
                l_free_at[hp] = ln_done
                def _norm(hp=hp, h=h, qc=qc, linv_done=linv_done):
                    P.emit("tensor",
                           lambda t, hp=hp: t.matmul(s_ps[hp][:, 0:512], onesrf_sb[:, :],
                                                     linv_row[hp][:, :], start=True, stop=True),
                           waits=[(acM.sem, max(linv_done, acM.n))], inc=peM.bump())
                    bc_done = peM.n
                    linv_pe_guard[hp] = bc_done
                    P.emit("scalar",
                           lambda s, hp=hp: s.copy(linv_bc[hp][:, :], s_ps[hp][:, 0:512]),
                           waits=[(peM.sem, bc_done), (acM.sem, acM.n),
                                  (dvM.sem, linv_bc_free[hp])],
                           inc=acM.bump())
                    bank_guard[hp] = acM.n
                    bcev_done = acM.n
                    sti = (qc * 4 + h) % 8
                    P.emit("vector",
                           lambda v, hp=hp, sti=sti: v.tensor_mul(atst[sti][:, :],
                                                                  o_sb[hp][:, :],
                                                                  linv_bc[hp][:, :]),
                           waits=[(acM.sem, bcev_done), (sA2[sti].sem, atst_stored_at[sti]),
                                  (dvM.sem, dvM.n)],
                           inc=dvM.bump())
                    o_sb_free[hp] = dvM.n
                    linv_bc_free[hp] = dvM.n
                    mul_done = dvM.n
                    d_dest = qc // NTH
                    tof = (qc % NTH) * 512
                    P.emit("sync",
                           lambda sy, h=h, sti=sti, d_dest=d_dest, tof=tof: sy.dma_start(
                               out=a2a_in.ap()[d_dest, h * 128:h * 128 + 128, tof:tof + 512],
                               in_=atst[sti][:, :]),
                           waits=[(dvM.sem, mul_done)], inc=sA2[sti].bump())
                    atst_stored_at[sti] = sA2[sti].n
                pending_norm.append(_norm)

    for fn in pending_norm:
        fn()
    pending_norm = []

    P.emit("gpsimd",
           lambda g: g.collective_compute(
               "AllToAll", ALU.bypass, replica_groups=[list(range(NCORES))],
               ins=[a2a_in.ap().opt()], outs=[a2a_out.ap().opt()]),
           waits=[(s.sem, s.n) for s in sA2], inc=ccs.bump())

    if upto == "p2":
        sT = Ctr(stk, nc, "sT", 16)
        P.emit("gpsimd",
               lambda g: g.dma_start(out=outT[:, :],
                                     in_=a2a_out.ap().rearrange("r f t -> (r f) t")),
               waits=[(ccs.sem, 1)], inc=sT.bump())
        P.emit("gpsimd", None, waits=[(sT.sem, sT.n)])
        nc_out = finish()
        ph2.close(); pp2.close(); ph12.close(); stk.close()
        return nc_out

    ph2.close()
    pp2.close()
    ph12.close()

    ph34 = contextlib.ExitStack()
    xo_sb = [ph34.enter_context(nc.sbuf_tensor("xo_sb%d" % i, [128, To], F32)) for i in range(2)]
    st_sb = [ph34.enter_context(nc.sbuf_tensor("st_sb%d" % i, [128, To], F32)) for i in range(4)]
    r2st = [ph34.enter_context(nc.sbuf_tensor("r2st%d" % i, [128, To], F32)) for i in range(2)]
    sq_sb = ph34.enter_context(nc.sbuf_tensor("sq_sb", [128, To], F32))
    acc_s = ph34.enter_context(nc.sbuf_tensor("acc_s", [128, To], F32))
    acc_q = ph34.enter_context(nc.sbuf_tensor("acc_q", [128, To], F32))
    m_bc = ph34.enter_context(nc.sbuf_tensor("m_bc", [128, To], F32))
    rstd_bc = ph34.enter_context(nc.sbuf_tensor("rstd_bc", [128, To], F32))
    sc_bc = ph34.enter_context(nc.sbuf_tensor("sc_bc", [128, To], F32))

    ph25 = contextlib.ExitStack()
    at_own = ph25.enter_context(nc.sbuf_tensor("at_own", [128, n_ft, To], BF16))
    wo_sb = [ph25.enter_context(nc.sbuf_tensor("wo_sb%d" % i, [128, n_ft, 256], BF16))
             for i in range(2)]

    pp3 = contextlib.ExitStack()
    psA = [pp3.enter_context(nc.psum_tensor("psA%d" % i, [128, To], F32)) for i in range(4)]

    P.emit("sync",
           lambda sy: sy.dma_start(out=at_own[:, :, :],
                                   in_=a2a_out.ap().rearrange("r (i p) t -> p (r i) t", p=128)),
           waits=[(ccs.sem, 1)], inc=sAo.bump())
    at_own_ready = sAo.n

    n_grp = n_ft // 2
    wo_grp_free_at = [0, 0]
    wo_stripe_free_at = [0, 0]
    st_free_at = [0, 0]
    st_free_acm = [0, 0]
    xo_used_at = [0, 0]
    acc_chain = 0

    def emit_wo_stripe(g):
        b = g % 2
        w = [(ccs.sem, 1)] if g < 2 else []
        if g >= 2:
            w.append((peM.sem, wo_stripe_free_at[b]))
        P.emit("gpsimd",
               lambda gp, g=g, b=b: gp.dma_start(
                   out=wo_sb[b][:, :, :],
                   in_=wo[:, g * 256:(g + 1) * 256].rearrange("(i p) f -> p i f", p=128)),
               waits=w, inc=sWo[b].bump())

    emit_wo_stripe(0)
    emit_wo_stripe(1)
    for ft in range(min(2, n_ft)):
        P.emit("sync",
               lambda sy, ft=ft: sy.dma_start(
                   out=xo_sb[ft % 2][:, :], in_=xoT[ft * 128:(ft + 1) * 128, :]),
               waits=[(ccs.sem, 1)], inc=sXo[ft % 2].bump())

    for g in range(n_grp):
        b = g % 2
        mmw = [(sWo[b].sem, 16 * (g // 2 + 1)), (sAo.sem, at_own_ready)]
        if g >= 2:
            mmw.append((dvM.sem, wo_grp_free_at[b]))
        for d in range(n_ft):
            for jj in range(2):
                for th in range(NTH):
                    P.emit("tensor",
                           lambda t, b=b, d=d, jj=jj, th=th, st=(d == 0), sp=(d == n_ft - 1):
                           t.matmul(psA[2 * b + jj][:, th * 512:th * 512 + 512],
                                    wo_sb[b][:, d, jj * 128:jj * 128 + 128],
                                    at_own[:, d, th * 512:th * 512 + 512],
                                    start=st, stop=sp),
                           waits=mmw if (d == 0 and jj == 0 and th == 0) else (),
                           inc=peM.bump() if (d == n_ft - 1 and jj == 1 and th == NTH - 1) else None)
        grp_done = peM.n
        wo_stripe_free_at[b] = grp_done
        if g + 2 < n_grp:
            emit_wo_stripe(g + 2)
        for jj in range(2):
            ft = 2 * g + jj
            sb = ft % 2
            P.emit("vector",
                   lambda v, b=b, jj=jj, sb=sb: v.tensor_add(
                       st_sb[sb][:, :], psA[2 * b + jj][:, 0:To], xo_sb[sb][:, :]),
                   waits=[(peM.sem, grp_done), (sXo[sb].sem, 16 * (ft // 2 + 1)),
                          (sSw[sb].sem, st_free_at[sb]), (acM.sem, st_free_acm[sb]),
                          (dvM.sem, dvM.n)],
                   inc=dvM.bump())
            add_done = dvM.n
            xo_used_at[sb] = add_done
            if jj == 1:
                wo_grp_free_at[b] = add_done
            if ft + 2 < n_ft:
                P.emit("sync",
                       lambda sy, ft=ft, sb=sb: sy.dma_start(
                           out=xo_sb[sb][:, :], in_=xoT[(ft + 2) * 128:(ft + 3) * 128, :]),
                       waits=[(dvM.sem, add_done)], inc=sXo[sb].bump())
            P.emit("sync",
                   lambda sy, ft=ft, sb=sb: sy.dma_start(
                       out=s_dram.ap()[ft * 128:(ft + 1) * 128, :], in_=st_sb[sb][:, :]),
                   waits=[(dvM.sem, add_done)], inc=sSw[sb].bump())
            st_free_at[sb] = sSw[sb].n
            if ft == 0:
                P.emit("vector", lambda v, sb=sb: v.tensor_copy(acc_s[:, :], st_sb[sb][:, :]),
                       waits=[(dvM.sem, add_done)], inc=dvM.bump())
            else:
                P.emit("vector", lambda v, sb=sb: v.tensor_add(acc_s[:, :], acc_s[:, :],
                                                               st_sb[sb][:, :]),
                       waits=[(dvM.sem, max(add_done, acc_chain))], inc=dvM.bump())
            P.emit("scalar", lambda s, sb=sb: s.activation(sq_sb[:, :], st_sb[sb][:, :],
                                                           AF.Square),
                   waits=[(dvM.sem, add_done), (dvM.sem, acc_chain), (acM.sem, acM.n)],
                   inc=acM.bump())
            sqd = acM.n
            st_free_acm[sb] = sqd
            if ft == 0:
                P.emit("vector", lambda v: v.tensor_copy(acc_q[:, :], sq_sb[:, :]),
                       waits=[(acM.sem, sqd), (dvM.sem, dvM.n)], inc=dvM.bump())
            else:
                P.emit("vector", lambda v: v.tensor_add(acc_q[:, :], acc_q[:, :], sq_sb[:, :]),
                       waits=[(acM.sem, sqd), (dvM.sem, dvM.n)], inc=dvM.bump())
            acc_chain = dvM.n

    def emit_ln_finalize(m_dst, r_dst, roff, acc_ready_dvM, sc_dst=None):
        psum, psumq = psA[3], psA[2]
        for half in range(NTH):
            P.emit("tensor",
                   lambda t, half=half: t.matmul(
                       psum[0:1, half * 512:half * 512 + 512], onescf_sb[:, :],
                       acc_s[:, half * 512:half * 512 + 512], start=True, stop=True),
                   waits=[(dvM.sem, acc_ready_dvM), (acM.sem, acM.n)],
                   inc=peM.bump())
            P.emit("tensor",
                   lambda t, half=half: t.matmul(
                       psumq[0:1, half * 512:half * 512 + 512], onescf_sb[:, :],
                       acc_q[:, half * 512:half * 512 + 512], start=True, stop=True),
                   inc=peM.bump())
        mm_done = peM.n
        r0, r1, r2o = roff, roff + 1024, roff + 2048
        P.emit("scalar",
               lambda s: s.copy(rowbuf[0:1, r0:r0 + To], psum[0:1, 0:To]),
               waits=[(peM.sem, mm_done), (acM.sem, acM.n)], inc=acM.bump())
        P.emit("scalar",
               lambda s: s.copy(rowbuf[0:1, r1:r1 + To], psumq[0:1, 0:To]),
               waits=[(acM.sem, acM.n)], inc=acM.bump())
        evr = acM.n
        P.emit("vector",
               lambda v: v.tensor_scalar_mul(rowbuf[0:1, r0:r0 + To],
                                             rowbuf[0:1, r0:r0 + To], 1.0 / DIM),
               waits=[(acM.sem, evr), (dvM.sem, dvM.n)], inc=dvM.bump())
        m_row_done = dvM.n
        P.emit("vector",
               lambda v: v.tensor_scalar_mul(rowbuf[0:1, r1:r1 + To],
                                             rowbuf[0:1, r1:r1 + To], 1.0 / DIM),
               waits=[(dvM.sem, dvM.n)], inc=dvM.bump())
        P.emit("vector",
               lambda v: v.tensor_mul(rowbuf[0:1, r2o:r2o + To],
                                      rowbuf[0:1, r0:r0 + To], rowbuf[0:1, r0:r0 + To]),
               waits=[(dvM.sem, dvM.n)], inc=dvM.bump())
        P.emit("vector",
               lambda v: v.tensor_sub(rowbuf[0:1, r1:r1 + To], rowbuf[0:1, r1:r1 + To],
                                      rowbuf[0:1, r2o:r2o + To]),
               waits=[(dvM.sem, dvM.n)], inc=dvM.bump())
        var_done = dvM.n
        P.emit("scalar",
               lambda s: s.activation(rowbuf[0:1, r2o:r2o + To], rowbuf[0:1, r1:r1 + To],
                                      AF.Ln, bias=1e-5),
               waits=[(dvM.sem, var_done), (acM.sem, acM.n)], inc=acM.bump())
        P.emit("scalar",
               lambda s: s.activation(rowbuf[0:1, r1:r1 + To], rowbuf[0:1, r2o:r2o + To],
                                      AF.Exp, scale=-0.5),
               waits=[(acM.sem, acM.n)], inc=acM.bump())
        rstd_done = acM.n
        sc_done = dvM.n
        if sc_dst is not None:
            P.emit("vector",
                   lambda v: v.tensor_scalar(rowbuf[0:1, r2o:r2o + To],
                                             rowbuf[0:1, r1:r1 + To],
                                             wscale_sb[0:1, 0:1], None, ALU.mult),
                   waits=[(acM.sem, rstd_done), (dvM.sem, dvM.n)], inc=dvM.bump())
            sc_done = dvM.n
        bc_list = [(m_dst, r0), (r_dst, r1)]
        if sc_dst is not None:
            bc_list.append((sc_dst, r2o))
        prev_ev = 0
        for (dst, off) in bc_list:
            for half in range(NTH):
                P.emit("tensor",
                       lambda t, off=off, half=half: t.matmul(
                           psum[:, half * 512:half * 512 + 512], onesrf_sb[:, :],
                           rowbuf[0:1, off + half * 512:off + half * 512 + 512],
                           start=True, stop=True),
                       waits=[(acM.sem, max(rstd_done, prev_ev)), (dvM.sem, sc_done)],
                       inc=peM.bump())
                bmm = peM.n
                P.emit("scalar",
                       lambda s, dst=dst, half=half: s.copy(
                           dst[:, half * 512:half * 512 + 512],
                           psum[:, half * 512:half * 512 + 512]),
                       waits=[(peM.sem, bmm), (acM.sem, acM.n)], inc=acM.bump())
                prev_ev = acM.n
        return acM.n

    bc1_done = emit_ln_finalize(m_bc, rstd_bc, 0, acc_chain, sc_dst=sc_bc)

    if upto == "p25":
        sT = Ctr(stk, nc, "sT", 16)
        P.emit("sync",
               lambda sy: sy.dma_start(out=outT[:, :], in_=s_dram.ap()),
               waits=[(sSw[0].sem, sSw[0].n), (sSw[1].sem, sSw[1].n),
                      (acM.sem, bc1_done)], inc=sT.bump())
        P.emit("sync", None, waits=[(sT.sem, sT.n)])
        nc_out = finish()
        ph25.close(); ph34.close(); pp3.close(); stk.close()
        return nc_out

    ph25.close()
    ph3 = contextlib.ExitStack()
    d1_sb = ph3.enter_context(nc.sbuf_tensor("d1_sb", [128, n_ft, To], BF16))
    w2r = [ph3.enter_context(nc.sbuf_tensor("w2r%d" % i, [128, n_ft, 128], BF16))
           for i in range(3)]
    wf_sb = [ph3.enter_context(nc.sbuf_tensor("wf_sb%d" % i, [128, n_ft, 128], BF16))
             for i in range(3)]

    d1_done_at = []
    for ft in range(n_ft):
        sb = ft % 4
        P.emit("sync",
               lambda sy, ft=ft, sb=sb: sy.dma_start(
                   out=st_sb[sb][:, :], in_=s_dram.ap()[ft * 128:(ft + 1) * 128, :]),
               waits=[(sSw[0].sem, sSw[0].n), (sSw[1].sem, sSw[1].n),
                      (acM.sem, bc1_done),
                      (dvM.sem, d1_done_at[ft - 4] if ft >= 4 else 0)],
               inc=sSr[sb].bump())
        P.emit("vector",
               lambda v, ft=ft, sb=sb: v.tensor_sub(d1_sb[:, ft, :], st_sb[sb][:, :],
                                                    m_bc[:, :]),
               waits=[(sSr[sb].sem, 16 * (ft // 4 + 1)), (acM.sem, bc1_done),
                      (dvM.sem, dvM.n)],
               inc=dvM.bump())
        d1_done_at.append(dvM.n)
    d1_all = dvM.n

    ff_grp_free_at = [0, 0, 0]
    ff_stripe_free_at = [0, 0, 0]
    wf_conv_at = [0, 0, 0]
    st3_free_dv = [0, 0, 0, 0]
    r2st_free_dv = [0, 0]
    r2st_free_ac = [0, 0]
    r2st_spill_at = [0, 0]
    acc2_chain = 0

    for g in range(n_ft):
        b = g % 3
        w = [(sSw[0].sem, sSw[0].n), (sSw[1].sem, sSw[1].n)]
        if g >= 3:
            w.append((dvM.sem, wf_conv_at[b]))
        P.emit("gpsimd",
               lambda gp, g=g, b=b: gp.dma_start(
                   out=w2r[b][:, :, :],
                   in_=w2T[:, g * 128:(g + 1) * 128].rearrange("(i p) f -> p i f", p=128)),
               waits=w, inc=sFf[b].bump())
        P.emit("vector",
               lambda v, b=b: v.tensor_scalar_sub(wf_sb[b][:, :, :], w2r[b][:, :, :], 1.5),
               waits=[(sFf[b].sem, 16 * (g // 3 + 1)), (peM.sem, ff_stripe_free_at[b]),
                      (dvM.sem, dvM.n)],
               inc=dvM.bump())
        wf_conv_at[b] = dvM.n
        conv_done = dvM.n
        mmw = [(dvM.sem, max(conv_done, d1_all))]
        if g >= 3:
            mmw.append((dvM.sem, ff_grp_free_at[b]))
        for d in range(n_ft):
            for th in range(NTH):
                P.emit("tensor",
                       lambda t, b=b, d=d, th=th, st=(d == 0), sp=(d == n_ft - 1):
                       t.matmul(psA[b][:, th * 512:th * 512 + 512],
                                wf_sb[b][:, d, :], d1_sb[:, d, th * 512:th * 512 + 512],
                                start=st, stop=sp),
                       waits=mmw if (d == 0 and th == 0) else (),
                       inc=peM.bump() if (d == n_ft - 1 and th == NTH - 1) else None)
        grp_done = peM.n
        ff_stripe_free_at[b] = grp_done
        sb = g % 4
        sb2 = g % 2
        P.emit("sync",
               lambda sy, g=g, sb=sb: sy.dma_start(
                   out=st_sb[sb][:, :], in_=s_dram.ap()[g * 128:(g + 1) * 128, :]),
               waits=[(dvM.sem, max(d1_all, st3_free_dv[sb]))],
               inc=sS2[sb].bump())
        P.emit("vector",
               lambda v, sb=sb: v.tensor_sub(sq_sb[:, :], st_sb[sb][:, :], m_bc[:, :]),
               waits=[(sS2[sb].sem, 16 * (g // 4 + 1)), (dvM.sem, dvM.n)],
               inc=dvM.bump())
        st3_free_dv[sb] = dvM.n
        P.emit("vector",
               lambda v, sb=sb: v.tensor_mul(sq_sb[:, :], sq_sb[:, :], rstd_bc[:, :]),
               waits=[(dvM.sem, dvM.n)], inc=dvM.bump())
        hh_done = dvM.n
        P.emit("vector",
               lambda v, b=b, sb2=sb2: v.tensor_mul(r2st[sb2][:, :], psA[b][:, 0:To],
                                                    sc_bc[:, :]),
               waits=[(peM.sem, grp_done), (dvM.sem, max(hh_done, r2st_free_dv[sb2])),
                      (acM.sem, r2st_free_ac[sb2]), (sR2w[sb2].sem, r2st_spill_at[sb2])],
               inc=dvM.bump())
        ff_grp_free_at[b] = dvM.n
        P.emit("vector",
               lambda v, g=g, sb2=sb2: v.scalar_tensor_tensor(
                   r2st[sb2][:, :], r2st[sb2][:, :], bias2d_sb[:, g:g + 1], sq_sb[:, :],
                   ALU.add, ALU.add),
               waits=[(dvM.sem, dvM.n)], inc=dvM.bump())
        r2_done = dvM.n
        P.emit("scalar",
               lambda s, g=g, sb2=sb2: s.dma_start(
                   out=r2_dram.ap()[g * 128:(g + 1) * 128, :], in_=r2st[sb2][:, :]),
               waits=[(dvM.sem, r2_done)], inc=sR2w[sb2].bump())
        r2st_spill_at[sb2] = sR2w[sb2].n
        if g == 0:
            P.emit("vector", lambda v, sb2=sb2: v.tensor_copy(acc_s[:, :], r2st[sb2][:, :]),
                   waits=[(dvM.sem, r2_done)], inc=dvM.bump())
        else:
            P.emit("vector", lambda v, sb2=sb2: v.tensor_add(acc_s[:, :], acc_s[:, :],
                                                             r2st[sb2][:, :]),
                   waits=[(dvM.sem, max(r2_done, acc2_chain))], inc=dvM.bump())
        P.emit("scalar", lambda s, sb2=sb2: s.activation(sq_sb[:, :], r2st[sb2][:, :], AF.Square),
               waits=[(dvM.sem, dvM.n), (acM.sem, acM.n)], inc=acM.bump())
        sqd = acM.n
        r2st_free_ac[sb2] = sqd
        if g == 0:
            P.emit("vector", lambda v: v.tensor_copy(acc_q[:, :], sq_sb[:, :]),
                   waits=[(acM.sem, sqd), (dvM.sem, dvM.n)], inc=dvM.bump())
        else:
            P.emit("vector", lambda v: v.tensor_add(acc_q[:, :], acc_q[:, :], sq_sb[:, :]),
                   waits=[(acM.sem, sqd), (dvM.sem, dvM.n)], inc=dvM.bump())
        acc2_chain = dvM.n
        r2st_free_dv[sb2] = acc2_chain

    bc2_done = emit_ln_finalize(m_bc, rstd_bc, 0, acc2_chain)
    dv_after_bc2 = dvM.n

    out_done_at = [dv_after_bc2] * 4
    out_stage_free = [dv_after_bc2, dv_after_bc2]
    out_store_at = [0, 0]
    for ft in range(n_ft):
        sb = ft % 4
        sb2 = ft % 2
        P.emit("sync",
               lambda sy, ft=ft, sb=sb: sy.dma_start(
                   out=st_sb[sb][:, :], in_=r2_dram.ap()[ft * 128:(ft + 1) * 128, :]),
               waits=[(sR2w[0].sem, sR2w[0].n), (sR2w[1].sem, sR2w[1].n),
                      (dvM.sem, out_done_at[sb]), (acM.sem, bc2_done)],
               inc=sR2r[sb].bump())
        P.emit("vector",
               lambda v, sb=sb: v.tensor_sub(sq_sb[:, :], st_sb[sb][:, :], m_bc[:, :]),
               waits=[(sR2r[sb].sem, 16 * (ft // 4 + 1)), (acM.sem, bc2_done),
                      (dvM.sem, dvM.n)],
               inc=dvM.bump())
        out_done_at[sb] = dvM.n
        P.emit("vector",
               lambda v, sb2=sb2: v.tensor_mul(r2st[sb2][:, :], sq_sb[:, :], rstd_bc[:, :]),
               waits=[(dvM.sem, max(dvM.n, out_stage_free[sb2])),
                      (sOut[sb2].sem, out_store_at[sb2])],
               inc=dvM.bump())
        out_stage_free[sb2] = dvM.n
        P.emit("scalar",
               lambda s, ft=ft, sb2=sb2: s.dma_start(
                   out=outT[ft * 128:(ft + 1) * 128, :], in_=r2st[sb2][:, :]),
               waits=[(dvM.sem, dvM.n)], inc=sOut[sb2].bump())
        out_store_at[sb2] = sOut[sb2].n
    P.emit("sync", None, waits=[(sOut[0].sem, sOut[0].n), (sOut[1].sem, sOut[1].n)])

    nc_out = finish()
    ph3.close()
    ph34.close()
    pp3.close()
    stk.close()
    return nc_out


def prep_inputs(L0_attn_x, L0_attn_wq, L0_attn_wk, L0_attn_wv, L0_attn_wo,
                L0_ffn_weight_2bit, L0_ffn_weight_scale, L0_ffn_bias):
    x = np.asarray(L0_attn_x)
    S = x.shape[1]
    x2 = np.ascontiguousarray(x.reshape(S, DIM))
    xT = np.ascontiguousarray(x2.T)

    perm = np.concatenate([np.arange(0, HD, 2), np.arange(1, HD, 2)])
    wq = np.asarray(L0_attn_wq).reshape(DIM, NH, HD)[:, :, perm].reshape(DIM, NH * HD)
    wk = np.asarray(L0_attn_wk).reshape(DIM, NKV, HD)[:, :, perm].reshape(DIM, NKV * HD)
    wv = np.asarray(L0_attn_wv)
    wo = np.ascontiguousarray(np.asarray(L0_attn_wo))
    w2T = np.ascontiguousarray(np.asarray(L0_ffn_weight_2bit).T.astype(np.int32))
    bias2d = np.ascontiguousarray(np.asarray(L0_ffn_bias).reshape(DIM // 128, 128).T)
    wsc = np.ascontiguousarray(np.asarray(L0_ffn_weight_scale).reshape(1, 1).astype(np.float32))

    sc128 = np.float64(HD) ** -0.25
    freqs = 1.0 / THETA ** (np.arange(0, HD, 2, dtype=np.float64) / HD)
    ang = np.outer(freqs, np.arange(S, dtype=np.float64))
    C = (np.concatenate([np.cos(ang), np.cos(ang)], 0) * sc128).astype(np.float32)
    Ssg = (np.concatenate([-np.sin(ang), np.sin(ang)], 0) * sc128).astype(np.float32)
    kk = np.arange(128)[:, None]
    qq = np.arange(512)[None, :]
    masks = np.concatenate([(128 * s + kk <= qq) for s in range(4)],
                           axis=1).astype(ml_dtypes.bfloat16)
    pswap = np.zeros((128, 128), np.float32)
    for i in range(128):
        pswap[i, (i + 64) % 128] = 1.0
    pswap = pswap.astype(ml_dtypes.bfloat16)
    identm = np.eye(128, dtype=ml_dtypes.bfloat16)
    ones4_np = np.zeros((128, 1), np.float32)
    ones4_np[[0, 32, 64, 96], 0] = 1.0

    To = S // NCORES
    maps = []
    for c in range(NCORES):
        maps.append({
            "xT": xT,
            "xoT": np.ascontiguousarray(xT[:, c * To:(c + 1) * To]),
            "wq": np.ascontiguousarray(wq[:, c * HPC * HD:(c + 1) * HPC * HD]),
            "wk": np.ascontiguousarray(wk[:, c * HD:(c + 1) * HD]),
            "wv": np.ascontiguousarray(np.asarray(wv)[:, c * HD:(c + 1) * HD]),
            "wo": wo,
            "w2T": w2T,
            "bias2d": bias2d,
            "wscale": wsc,
            "cosT": C,
            "sinT": Ssg,
            "masks": masks,
            "pswap": pswap,
            "ident": identm,
            "onescb": np.ones((128, 1), ml_dtypes.bfloat16),
            "onesrb": np.ones((1, 128), ml_dtypes.bfloat16),
            "onescf": np.ones((128, 1), np.float32),
            "onesrf": np.ones((1, 128), np.float32),
            "ones4": ones4_np,
        })
    return maps


def kernel(L0_attn_x, L0_attn_wq, L0_attn_wk, L0_attn_wv, L0_attn_wo,
           L0_ffn_weight_2bit, L0_ffn_weight_scale, L0_ffn_bias):
    x = np.asarray(L0_attn_x)
    S = x.shape[1]
    maps = prep_inputs(L0_attn_x, L0_attn_wq, L0_attn_wk, L0_attn_wv, L0_attn_wo,
                       L0_ffn_weight_2bit, L0_ffn_weight_scale, L0_ffn_bias)
    nc = build_program(S)
    with allow_ici_all_to_all():
        res = bass_utils.run_bass_kernel_spmd(nc, maps, core_ids=list(range(NCORES)))
    outs = [np.asarray(r["outT"]).T for r in res.results]
    return np.ascontiguousarray(np.concatenate(outs, axis=0)).reshape(1, S, DIM).astype(np.float32)


# version 10
# speedup vs baseline: 1.4346x; 1.0271x over previous
import contextlib
import numpy as np
import ml_dtypes

import concourse.bass as bass
import concourse.mybir as mybir
import concourse.bass_utils as bass_utils
from concourse.aot_env import allow_ici_all_to_all

F32 = mybir.dt.float32
BF16 = mybir.dt.bfloat16
I32 = mybir.dt.int32
AF = mybir.ActivationFunctionType
ALU = mybir.AluOpType
AX = mybir.AxisListType

DIM = 4096
NH = 32
NKV = 8
HD = 128
NCORES = 8
HPC = 4
THETA = 10000.0

ENGS = ("tensor", "vector", "scalar", "gpsimd", "sync")


class Ctr:

    def __init__(self, stack, nc, name, step=1):
        self.sem = stack.enter_context(nc.semaphore(name))
        self.n = 0
        self.step = step

    def bump(self):
        self.n += self.step
        return (self.sem, self.step)


class Prog:
    def __init__(self):
        self.ops = {e: [] for e in ENGS}

    def emit(self, eng, fn=None, waits=(), inc=None):
        self.ops[eng].append((fn, tuple((s, v) for (s, v) in waits if v > 0), inc))


def build_program(S, upto='full', p1_tgs=None, p1_lvl=6):
    To = S // NCORES
    NTG = S // 512
    NQC = S // 512
    NTH = (To + 511) // 512
    n_ft = DIM // 128

    nc = bass.Bass("TRN2", target_bir_lowering=False)

    xT = nc.dram_tensor("xT", [DIM, S], F32, kind="ExternalInput").ap()
    xoT = nc.dram_tensor("xoT", [DIM, To], F32, kind="ExternalInput").ap()
    wq = nc.dram_tensor("wq", [DIM, HPC * HD], F32, kind="ExternalInput").ap()
    wk = nc.dram_tensor("wk", [DIM, HD], F32, kind="ExternalInput").ap()
    wv = nc.dram_tensor("wv", [DIM, HD], F32, kind="ExternalInput").ap()
    wo = nc.dram_tensor("wo", [DIM, DIM], F32, kind="ExternalInput").ap()
    w2T = nc.dram_tensor("w2T", [DIM, DIM], I32, kind="ExternalInput").ap()
    bias2d = nc.dram_tensor("bias2d", [128, n_ft], F32, kind="ExternalInput").ap()
    wscale = nc.dram_tensor("wscale", [1, 1], F32, kind="ExternalInput").ap()
    cosT = nc.dram_tensor("cosT", [128, S], F32, kind="ExternalInput").ap()
    sinT = nc.dram_tensor("sinT", [128, S], F32, kind="ExternalInput").ap()
    masks = nc.dram_tensor("masks", [128, 4 * 512], BF16, kind="ExternalInput").ap()
    pswap = nc.dram_tensor("pswap", [128, 128], BF16, kind="ExternalInput").ap()
    ident = nc.dram_tensor("ident", [128, 128], BF16, kind="ExternalInput").ap()
    onescb = nc.dram_tensor("onescb", [128, 1], BF16, kind="ExternalInput").ap()
    onesrb = nc.dram_tensor("onesrb", [1, 128], BF16, kind="ExternalInput").ap()
    onescf = nc.dram_tensor("onescf", [128, 1], F32, kind="ExternalInput").ap()
    onesrf = nc.dram_tensor("onesrf", [1, 128], F32, kind="ExternalInput").ap()
    ones4 = nc.dram_tensor("ones4", [128, 1], F32, kind="ExternalInput").ap()
    outT = nc.dram_tensor("outT", [DIM, To], F32, kind="ExternalOutput").ap()

    a2a_in = nc.dram_tensor("a2a_in", [NCORES, HPC * HD, To], BF16)
    a2a_out = nc.dram_tensor("a2a_out", [NCORES, HPC * HD, To], BF16)
    s_dram = nc.dram_tensor("s_dram", [DIM, To], F32)
    r2_dram = nc.dram_tensor("r2_dram", [DIM, To], F32)

    stk = contextlib.ExitStack()

    sW = Ctr(stk, nc, "sW", 16)
    sX = [Ctr(stk, nc, "sX%d" % i, 16) for i in range(8)]
    sCS = [Ctr(stk, nc, "sCS%d" % i, 16) for i in range(2)]
    sWo = [Ctr(stk, nc, "sWo%d" % i, 16) for i in range(2)]
    sFf = [Ctr(stk, nc, "sFf%d" % i, 16) for i in range(3)]
    sXo = [Ctr(stk, nc, "sXo%d" % i, 16) for i in range(2)]
    sSw = [Ctr(stk, nc, "sSw%d" % i, 16) for i in range(2)]
    sSr = [Ctr(stk, nc, "sSr%d" % i, 16) for i in range(4)]
    sS2 = [Ctr(stk, nc, "sS2%d" % i, 16) for i in range(4)]
    sR2w = [Ctr(stk, nc, "sR2w%d" % i, 16) for i in range(2)]
    sR2r = [Ctr(stk, nc, "sR2r%d" % i, 16) for i in range(4)]
    sA2 = [Ctr(stk, nc, "sA2_%d" % i, 16) for i in range(8)]
    sAo = Ctr(stk, nc, "sAo", 16)
    sOut = [Ctr(stk, nc, "sOut%d" % i, 16) for i in range(2)]
    ccs = Ctr(stk, nc, "ccs", 1)
    peM = Ctr(stk, nc, "peM", 1)
    acM = Ctr(stk, nc, "acM", 1)
    dvM = Ctr(stk, nc, "dvM", 1)

    P = Prog()

    masks_sb = stk.enter_context(nc.sbuf_tensor("masks_sb", [128, 4 * 512], BF16))
    pswap_sb = stk.enter_context(nc.sbuf_tensor("pswap_sb", [128, 128], BF16))
    ident_sb = stk.enter_context(nc.sbuf_tensor("ident_sb", [128, 128], BF16))
    onescb_sb = stk.enter_context(nc.sbuf_tensor("onescb_sb", [128, 1], BF16))
    onesrb_sb = stk.enter_context(nc.sbuf_tensor("onesrb_sb", [1, 128], BF16))
    onescf_sb = stk.enter_context(nc.sbuf_tensor("onescf_sb", [128, 1], F32))
    onesrf_sb = stk.enter_context(nc.sbuf_tensor("onesrf_sb", [1, 128], F32))
    ones4_sb = stk.enter_context(nc.sbuf_tensor("ones4_sb", [128, 1], F32))
    bias2d_sb = stk.enter_context(nc.sbuf_tensor("bias2d_sb", [128, n_ft], F32))
    wscale_sb = stk.enter_context(nc.sbuf_tensor("wscale_sb", [1, 1], F32))
    cmax = stk.enter_context(nc.sbuf_tensor("cmax", [1, 4 * NQC], F32))
    kmaxs = stk.enter_context(nc.sbuf_tensor("kmaxs", [1, NTG + 2], F32))
    cneg = stk.enter_context(nc.sbuf_tensor("cneg", [1, 4 * NQC], BF16))
    rowbuf = stk.enter_context(nc.sbuf_tensor("rowbuf", [1, 3 * 1024], F32))
    consts = stk.enter_context(nc.sbuf_tensor("consts", [1, 8], F32))

    ph12 = contextlib.ExitStack()
    q4 = ph12.enter_context(nc.sbuf_tensor("q4", [128, HPC, S], BF16))
    KT = ph12.enter_context(nc.sbuf_tensor("KT", [128, S], BF16))
    Vtok = ph12.enter_context(nc.sbuf_tensor("Vtok", [128, S], BF16))

    nc.const_aps.register(F32, 1e-5, consts[0:1, 0:1])
    nc.const_aps.register(F32, 0.5, consts[0:1, 1:2])
    nc.const_aps.register(F32, -0.5, consts[0:1, 2:3])
    nc.const_aps.register(F32, -1.0, consts[0:1, 3:4])

    P.emit("vector", lambda v: v.memset(consts[0:1, 0:1], 1e-5), inc=dvM.bump())
    P.emit("vector", lambda v: v.memset(consts[0:1, 1:2], 0.5), inc=dvM.bump())
    P.emit("vector", lambda v: v.memset(consts[0:1, 2:3], -0.5), inc=dvM.bump())
    P.emit("vector", lambda v: v.memset(consts[0:1, 3:4], -1.0), inc=dvM.bump())
    consts_ready = dvM.n

    def g_load(dst, src):
        P.emit("gpsimd", lambda g, d=dst, s=src: g.dma_start(out=d, in_=s), inc=sW.bump())

    g_load(masks_sb[:, :], masks)
    g_load(pswap_sb[:, :], pswap)
    g_load(ident_sb[:, :], ident)
    g_load(onescb_sb[:, :], onescb)
    g_load(onesrb_sb[:, :], onesrb)
    g_load(onescf_sb[:, :], onescf)
    g_load(onesrf_sb[:, :], onesrf)
    g_load(ones4_sb[:, :], ones4)
    g_load(bias2d_sb[:, :], bias2d)
    g_load(wscale_sb[:, :], wscale)

    ph1 = contextlib.ExitStack()
    wq_sb = ph1.enter_context(nc.sbuf_tensor("wq_sb", [128, n_ft, HPC * HD], BF16))
    wk_sb = ph1.enter_context(nc.sbuf_tensor("wk_sb", [128, n_ft, HD], BF16))
    wv_sb = ph1.enter_context(nc.sbuf_tensor("wv_sb", [128, n_ft, HD], BF16))
    xbuf = [ph1.enter_context(nc.sbuf_tensor("xbuf%d" % i, [128, 512], BF16)) for i in range(8)]
    csbuf = [ph1.enter_context(nc.sbuf_tensor("csbuf%d" % i, [128, 2, 512], F32)) for i in range(2)]
    qkst = ph1.enter_context(nc.sbuf_tensor("qkst", [128, 6, 512], BF16))
    sqst = ph1.enter_context(nc.sbuf_tensor("sqst", [128, 5, 512], BF16))
    t1b = ph1.enter_context(nc.sbuf_tensor("t1b", [128, 512], F32))
    t2buf = ph1.enter_context(nc.sbuf_tensor("t2buf", [128, 5, 512], F32))

    pp1 = contextlib.ExitStack()
    q_ps = pp1.enter_context(nc.psum_tensor("q_ps", [128, 4 * 512], F32))
    kv_ps = pp1.enter_context(nc.psum_tensor("kv_ps", [128, 2 * 512], F32))
    sw_ps = pp1.enter_context(nc.psum_tensor("sw_ps", [128, 512], F32))
    vt_ps = pp1.enter_context(nc.psum_tensor("vt_ps", [128, 128], BF16))

    g_load(wq_sb[:, :, :], wq.rearrange("(i p) f -> p i f", p=128))
    g_load(wk_sb[:, :, :], wk.rearrange("(i p) f -> p i f", p=128))
    g_load(wv_sb[:, :, :], wv.rearrange("(i p) f -> p i f", p=128))
    w_loaded = sW.n

    total_tiles = NTG * n_ft
    xt_consumed_at = []
    cs_consumed_at = []

    if upto == "p0":
        sT = Ctr(stk, nc, "sT", 16)
        P.emit("gpsimd",
               lambda g: g.dma_start(out=outT[0:128, 0:512], in_=wq_sb[:, 0, :]),
               waits=[(sW.sem, sW.n)], inc=sT.bump())
        P.emit("gpsimd", None, waits=[(sT.sem, sT.n)])
        def finish0():
            def runner(eng_ops):
                def _run(eng):
                    for fn, waits, inc in eng_ops:
                        if fn is None:
                            for (sem, val) in waits:
                                eng.wait_ge(sem, val)
                            continue
                        if len(waits) > 1:
                            for (sem, val) in waits[:-1]:
                                eng.wait_ge(sem, val)
                        inst = fn(eng)
                        if waits:
                            inst.wait_op(waits[-1][0], waits[-1][1], "sem-ge")
                        if inc is not None:
                            inst.then_inc(inc[0], inc[1])
                return _run
            with nc.Block() as block:
                block.tensor(runner(P.ops["tensor"]))
                block.vector(runner(P.ops["vector"]))
                block.scalar(runner(P.ops["scalar"]))
                block.gpsimd(runner(P.ops["gpsimd"]))
                block.sync(runner(P.ops["sync"]))
            return nc
        nc_out = finish0()
        ph1.close(); pp1.close(); ph12.close(); stk.close()
        return nc_out

    def emit_xt_load(j):
        tg, d = divmod(j, n_ft)
        b = j % 8
        waits = [(sW.sem, w_loaded)]
        if j >= 8:
            waits.append((peM.sem, xt_consumed_at[j - 8]))
        P.emit("gpsimd",
               lambda g, b=b, tg=tg, d=d: g.dma_start(
                   out=xbuf[b][:, :], in_=xT[d * 128:(d + 1) * 128, tg * 512:(tg + 1) * 512]),
               waits=waits, inc=sX[b].bump())

    def emit_cs_load(tg):
        b = tg % 2
        waits = []
        if tg >= 2:
            waits.append((dvM.sem, cs_consumed_at[tg - 2]))
        P.emit("sync",
               lambda sy, b=b, tg=tg: sy.dma_start(
                   out=csbuf[b][:, 0, :], in_=cosT[:, tg * 512:(tg + 1) * 512]),
               waits=waits, inc=sCS[b].bump())
        P.emit("sync",
               lambda sy, b=b, tg=tg: sy.dma_start(
                   out=csbuf[b][:, 1, :], in_=sinT[:, tg * 512:(tg + 1) * 512]),
               inc=sCS[b].bump())

    NTG_RUN = NTG if p1_tgs is None else min(NTG, p1_tgs)
    total_tiles = NTG_RUN * n_ft
    xt_next = 0
    for j in range(min(8, total_tiles)):
        emit_xt_load(j)
        xt_next += 1
    emit_cs_load(0)
    if NTG > 1:
        emit_cs_load(1)

    prev_ep_act = 0
    prev_ep_dve = 0

    pending_p1 = None
    red_last = [0]
    vt_copy_last = [0]
    rope_all_prev = 0
    t2_last_prev = 0
    evac5_prev = 0
    vt_pe_prev = 0
    sq_pe_prev = 0

    for tg in range(NTG_RUN):
        for d in range(n_ft):
            j = tg * n_ft + d
            b = j % 8
            waits = [(sX[b].sem, 16 * (j // 8 + 1))]
            if d == 0:
                waits.append((acM.sem, evac5_prev))
                waits.append((dvM.sem, t2_last_prev))
            st = (d == 0)
            sp = (d == n_ft - 1)
            for jj in range(HPC):
                P.emit("tensor",
                       lambda t, d=d, jj=jj, b=b, st=st, sp=sp: t.matmul(
                           q_ps[:, jj * 512:jj * 512 + 512],
                           wq_sb[:, d, jj * 128:jj * 128 + 128],
                           xbuf[b][:, :], start=st, stop=sp),
                       waits=waits if jj == 0 else ())
            P.emit("tensor",
                   lambda t, d=d, b=b, st=st, sp=sp: t.matmul(
                       kv_ps[:, 0:512], wk_sb[:, d, :], xbuf[b][:, :], start=st, stop=sp))
            P.emit("tensor",
                   lambda t, d=d, b=b, st=st, sp=sp: t.matmul(
                       kv_ps[:, 512:1024], wv_sb[:, d, :], xbuf[b][:, :], start=st, stop=sp),
                   inc=peM.bump())
            xt_consumed_at.append(peM.n)
            if xt_next < total_tiles:
                emit_xt_load(xt_next)
                xt_next += 1
        dloop_done = peM.n

        if pending_p1 is not None:
            sq_pe_prev, vt_pe_prev = pending_p1()
            pending_p1 = None

        csb = tg % 2
        cs_ready = 32 * (tg // 2 + 1)
        tgs = slice(tg * 512, (tg + 1) * 512)
        evac_at = []
        for hx in range(6):
            src_ps = q_ps[:, hx * 512:hx * 512 + 512] if hx < 4 else \
                kv_ps[:, (hx - 4) * 512:(hx - 4) * 512 + 512]
            w = [(peM.sem, dloop_done), (acM.sem, acM.n)]
            if hx < 5:
                w.append((dvM.sem, rope_all_prev))
            else:
                w.append((peM.sem, vt_pe_prev))
            P.emit("scalar",
                   lambda s, src=src_ps, hx=hx: s.copy(qkst[:, hx, :], src),
                   waits=w, inc=acM.bump())
            evac_at.append(acM.n)
        evac5 = acM.n
        swap_at = []
        for hx in range(5):
            dst_ps = q_ps[:, hx * 512:hx * 512 + 512] if hx < 4 else kv_ps[:, 0:512]
            P.emit("tensor",
                   lambda t, hx=hx, dst=dst_ps: t.matmul(dst, pswap_sb[:, :],
                                                         qkst[:, hx, :],
                                                         start=True, stop=True),
                   waits=[(acM.sem, evac_at[hx])], inc=peM.bump())
            swap_at.append(peM.n)

        for hx in range(5):
            dst_ps = q_ps[:, hx * 512:hx * 512 + 512] if hx < 4 else kv_ps[:, 0:512]
            P.emit("vector",
                   lambda v, src=dst_ps, csb=csb, hx=hx: v.tensor_mul(
                       t2buf[:, hx, :], src, csbuf[csb][:, 1, :]),
                   waits=[(peM.sem, swap_at[hx]), (dvM.sem, dvM.n)] +
                         ([(sCS[csb].sem, cs_ready)] if hx == 0 else []),
                   inc=dvM.bump())
        t2_last = dvM.n
        rope_at = []
        for hx in range(5):
            dst = q4[:, hx, tgs] if hx < 4 else KT[:, tgs]
            P.emit("vector",
                   lambda v, hx=hx, csb=csb: v.tensor_mul(t1b[:, :], qkst[:, hx, :],
                                                          csbuf[csb][:, 0, :]),
                   waits=[(dvM.sem, dvM.n)], inc=dvM.bump())
            P.emit("vector",
                   lambda v, dst=dst, hx=hx: v.tensor_add(dst, t1b[:, :], t2buf[:, hx, :]),
                   waits=[(dvM.sem, dvM.n)], inc=dvM.bump())
            rope_at.append(dvM.n)
        rope_all = dvM.n
        cs_consumed_at.append(rope_all)
        if tg + 2 < NTG:
            emit_cs_load(tg + 2)

        sq_at = []
        for hx in range(5):
            dst = q4[:, hx, tgs] if hx < 4 else KT[:, tgs]
            P.emit("scalar",
                   lambda s, dst=dst, hx=hx: s.activation(sqst[:, hx, :], dst, AF.Square),
                   waits=[(dvM.sem, rope_at[hx]), (peM.sem, sq_pe_prev), (acM.sem, acM.n)],
                   inc=acM.bump())
            sq_at.append(acM.n)

        def _p1_deferred(tg=tg, sq_at=tuple(sq_at), evac5=evac5):
            for hx in range(4):
                P.emit("tensor",
                       lambda t, hx=hx: t.matmul(sw_ps[32 * hx:32 * hx + 1, :],
                                                 onescb_sb[:, :], sqst[:, hx, :],
                                                 start=True, stop=True,
                                                 skip_group_check=True,
                                                 tile_position=(0, 32 * hx)),
                       waits=[(acM.sem, sq_at[hx]), (dvM.sem, red_last[0])],
                       inc=peM.bump() if hx == 3 else None)
            qn_done = peM.n
            for hx in range(4):
                P.emit("vector",
                       lambda v, hx=hx, tg=tg: v.reduce_max(
                           cmax[0:1, (tg * 4 + hx):(tg * 4 + hx + 1)],
                           sw_ps[32 * hx:32 * hx + 1, :], axis=AX.X),
                       waits=[(peM.sem, qn_done), (dvM.sem, dvM.n)], inc=dvM.bump())
            P.emit("tensor",
                   lambda t: t.matmul(sw_ps[0:1, :], onescb_sb[:, :], sqst[:, 4, :],
                                      start=True, stop=True, skip_group_check=True),
                   waits=[(acM.sem, sq_at[4]), (dvM.sem, dvM.n)], inc=peM.bump())
            kn_done = peM.n
            sq_pe_upd = peM.n
            P.emit("vector",
                   lambda v, tg=tg: v.reduce_max(kmaxs[0:1, tg:tg + 1], sw_ps[0:1, :],
                                                 axis=AX.X),
                   waits=[(peM.sem, kn_done), (dvM.sem, dvM.n)], inc=dvM.bump())
            red_last[0] = dvM.n
            prev_vt_copy = vt_copy_last[0]
            vtp = 0
            for i in range(4):
                P.emit("tensor",
                       lambda t, i=i: t.transpose(vt_ps[:, :], qkst[:, 5, :]
                                                  [:, i * 128:i * 128 + 128],
                                                  ident_sb[:, :]),
                       waits=[(acM.sem, evac5), (dvM.sem, prev_vt_copy)],
                       inc=peM.bump())
                vtp = peM.n
                P.emit("vector",
                       lambda v, i=i, tg=tg: v.tensor_copy(
                           Vtok[:, (tg * 4 + i) * 128:(tg * 4 + i) * 128 + 128],
                           vt_ps[:, :]),
                       waits=[(peM.sem, vtp), (dvM.sem, dvM.n)], inc=dvM.bump())
                prev_vt_copy = dvM.n
            vt_copy_last[0] = prev_vt_copy
            return sq_pe_upd, vtp
        pending_p1 = _p1_deferred

        rope_all_prev = rope_all
        t2_last_prev = t2_last
        evac5_prev = evac5
        if tg == NTG_RUN - 1:
            sq_pe_prev, vt_pe_prev = pending_p1()
            pending_p1 = None

    phase1_dve = dvM.n
    phase1_act = acM.n

    if p1_tgs is not None:
        sT = Ctr(stk, nc, "sT", 16)
        P.emit("gpsimd",
               lambda g: g.dma_start(out=outT[0:128, :], in_=KT[:, 0:To]),
               waits=[(dvM.sem, phase1_dve), (acM.sem, phase1_act)], inc=sT.bump())
        P.emit("gpsimd",
               lambda g: g.dma_start(out=outT[128:256, :], in_=q4[:, 0, 0:To]),
               inc=sT.bump())
        P.emit("gpsimd",
               lambda g: g.dma_start(out=outT[256:384, :], in_=Vtok[:, 0:To]),
               inc=sT.bump())
        P.emit("gpsimd", None, waits=[(sT.sem, sT.n)])
        def finish1():
            def runner(eng_ops):
                def _run(eng):
                    for fn, waits, inc in eng_ops:
                        if fn is None:
                            for (sem, val) in waits:
                                eng.wait_ge(sem, val)
                            continue
                        if len(waits) > 1:
                            for (sem, val) in waits[:-1]:
                                eng.wait_ge(sem, val)
                        inst = fn(eng)
                        if waits:
                            inst.wait_op(waits[-1][0], waits[-1][1], "sem-ge")
                        if inc is not None:
                            inst.then_inc(inc[0], inc[1])
                return _run
            with nc.Block() as block:
                block.tensor(runner(P.ops["tensor"]))
                block.vector(runner(P.ops["vector"]))
                block.scalar(runner(P.ops["scalar"]))
                block.gpsimd(runner(P.ops["gpsimd"]))
                block.sync(runner(P.ops["sync"]))
            return nc
        nc_out = finish1()
        ph1.close(); pp1.close(); ph12.close(); stk.close()
        return nc_out

    P.emit("vector",
           lambda v: v.reduce_max(kmaxs[0:1, NTG:NTG + 1], kmaxs[0:1, 0:NTG], axis=AX.X),
           waits=[(dvM.sem, phase1_dve)], inc=dvM.bump())
    P.emit("vector",
           lambda v: v.tensor_scalar(rowbuf[0:1, 0:4 * NQC], cmax[0:1, :],
                                     kmaxs[0:1, NTG:NTG + 1], 1.21, ALU.mult, ALU.mult),
           waits=[(dvM.sem, dvM.n)], inc=dvM.bump())
    u_done = dvM.n
    P.emit("scalar",
           lambda s: s.activation(rowbuf[0:1, 1024:1024 + 4 * NQC], rowbuf[0:1, 0:4 * NQC],
                                  AF.Ln, bias=1e-5),
           waits=[(dvM.sem, u_done), (sW.sem, sW.n), (dvM.sem, consts_ready),
                  (acM.sem, acM.n)],
           inc=acM.bump())
    P.emit("scalar",
           lambda s: s.activation(rowbuf[0:1, 2048:2048 + 4 * NQC],
                                  rowbuf[0:1, 1024:1024 + 4 * NQC], AF.Exp, scale=0.5),
           waits=[(acM.sem, acM.n)], inc=acM.bump())
    c_done = acM.n
    P.emit("vector",
           lambda v: v.tensor_scalar_mul(cneg[0:1, :], rowbuf[0:1, 2048:2048 + 4 * NQC], -1.0),
           waits=[(acM.sem, c_done), (dvM.sem, dvM.n)], inc=dvM.bump())
    cneg_done = dvM.n

    dbg_tail = []

    def finish():
        def runner(eng_ops):
            def _run(eng):
                for fn, waits, inc in eng_ops:
                    if fn is None:
                        for (sem, val) in waits:
                            eng.wait_ge(sem, val)
                        continue
                    if len(waits) > 1:
                        for (sem, val) in waits[:-1]:
                            eng.wait_ge(sem, val)
                    inst = fn(eng)
                    if waits:
                        inst.wait_op(waits[-1][0], waits[-1][1], "sem-ge")
                    if inc is not None:
                        inst.then_inc(inc[0], inc[1])
            return _run

        with allow_ici_all_to_all():
            with nc.Block() as block:
                block.tensor(runner(P.ops["tensor"]))
                block.vector(runner(P.ops["vector"]))
                block.scalar(runner(P.ops["scalar"]))
                block.gpsimd(runner(P.ops["gpsimd"]))
                block.sync(runner(P.ops["sync"]))
        return nc

    if upto == "p1":
        sT = Ctr(stk, nc, "sT", 16)
        P.emit("gpsimd",
               lambda g: g.dma_start(out=outT[0:128, :], in_=KT[:, 0:To]),
               waits=[(dvM.sem, cneg_done)], inc=sT.bump())
        P.emit("gpsimd",
               lambda g: g.dma_start(out=outT[128:256, :], in_=q4[:, 0, 0:To]),
               inc=sT.bump())
        P.emit("gpsimd",
               lambda g: g.dma_start(out=outT[256:384, :], in_=Vtok[:, 0:To]),
               inc=sT.bump())
        P.emit("gpsimd", None, waits=[(sT.sem, sT.n)])
        nc_out = finish()
        ph1.close(); pp1.close(); ph12.close(); stk.close()
        return nc_out

    ph1.close()
    pp1.close()

    ph2 = contextlib.ExitStack()
    pt = [[ph2.enter_context(nc.sbuf_tensor("pt%d_%d" % (hp, par), [128, 2, 1024], BF16))
           for par in range(2)] for hp in range(2)]
    bias_sb = ph2.enter_context(nc.sbuf_tensor("biasc_sb", [128, 4], F32))
    o_sb = [ph2.enter_context(nc.sbuf_tensor("o_sb%d" % i, [128, 512], F32)) for i in range(2)]
    l_sb = [ph2.enter_context(nc.sbuf_tensor("l_sb%d" % i, [128, 512], F32)) for i in range(2)]
    linv_row = [ph2.enter_context(nc.sbuf_tensor("linv_row%d" % i, [1, 512], F32))
                for i in range(2)]
    linv_bc = [ph2.enter_context(nc.sbuf_tensor("linv_bc%d" % i, [128, 512], F32))
               for i in range(2)]
    atst = [ph2.enter_context(nc.sbuf_tensor("atst%d" % i, [128, 512], BF16)) for i in range(8)]

    pp2 = contextlib.ExitStack()
    s_ps = [pp2.enter_context(nc.psum_tensor("s_ps%d" % i, [128, 1024], F32)) for i in range(2)]
    o_ps = [pp2.enter_context(nc.psum_tensor("o_ps%d" % i, [128, 512], F32)) for i in range(2)]
    l_ps = [pp2.enter_context(nc.psum_tensor("l_ps%d" % i, [128, 512], F32)) for i in range(2)]

    pair_idx = 0
    pending_norm = []
    l_free_at = [0, 0]
    l_zero_at = [0, 0]
    linv_pe_guard = [0, 0]
    exp_half_at = {}
    group_l_done = {}
    ghp_count = [0, 0]
    atst_stored_at = [0] * 8
    o_free_at = [0, 0]
    o_sb_free = [0, 0]
    linv_bc_free = [0, 0]
    bank_guard = [0, 0]

    for qc in range(NQC):
        pro_waits = [(acM.sem, acM.n), (dvM.sem, cneg_done)]
        if qc == 0:
            pro_waits += [(dvM.sem, max(phase1_dve, cneg_done)), (acM.sem, phase1_act)]
        P.emit("tensor",
               lambda t, qc=qc: t.matmul(s_ps[0][:, 0:4], onesrb_sb[:, :],
                                         cneg[0:1, 4 * qc:4 * qc + 4], start=True, stop=True),
               waits=pro_waits, inc=peM.bump())
        bias_mm = peM.n
        P.emit("scalar",
               lambda s: s.copy(bias_sb[:, :], s_ps[0][:, 0:4]),
               waits=[(peM.sem, bias_mm), (acM.sem, acM.n)], inc=acM.bump())
        bias_ready = acM.n
        bank_guard[0] = bias_ready

        for pas in range(2):
            for hp in range(2):
                P.emit("vector",
                       lambda v, hp=hp: v.memset(l_ps[hp][:, :], 0.0),
                       waits=[(acM.sem, l_free_at[hp]), (dvM.sem, dvM.n)],
                       inc=dvM.bump())
                l_zero_at[hp] = dvM.n
            pending_avs = None
            for kc in range(qc + 1):
                diag = (kc == qc)
                for hp in range(2):
                    h = pas * 2 + hp
                    gpar = ghp_count[hp] % 2
                    pair_avs = []
                    for pj in range(2):
                        pg = pair_idx
                        bank = pg % 2
                        for sj in range(2):
                            ks = pj * 2 + sj
                            w = []
                            if sj == 0 and pg >= 2:
                                w.append((acM.sem, max(exp_half_at[(pg - 2, 1)],
                                                       bank_guard[bank])))
                            elif sj == 0:
                                w.append((acM.sem, bank_guard[bank]))
                            P.emit("tensor",
                                   lambda t, kc=kc, ks=ks, h=h, qc=qc, bank=bank, sj=sj:
                                   t.matmul(s_ps[bank][:, sj * 512:sj * 512 + 512],
                                            KT[:, kc * 512 + ks * 128:kc * 512 + ks * 128 + 128],
                                            q4[:, h, qc * 512:qc * 512 + 512],
                                            start=True, stop=True),
                                   waits=w, inc=peM.bump() if sj == 1 else None)
                        sdone = peM.n
                        ew = [(peM.sem, sdone), (acM.sem, acM.n)]
                        if (hp, gpar) in group_l_done:
                            ew.append((peM.sem, group_l_done[(hp, gpar)]))
                        P.emit("scalar",
                               lambda s, bank=bank, hp=hp, gpar=gpar, pj=pj,
                               hcol=pas * 2 + hp:
                               s.activation(pt[hp][gpar][:, pj, :], s_ps[bank][:, :],
                                            AF.Exp, bias=bias_sb[:, hcol:hcol + 1]),
                               waits=ew, inc=acM.bump())
                        exp_half_at[(pg, 1)] = acM.n
                        avw = (acM.sem, acM.n)
                        if diag:
                            P.emit("vector",
                                   lambda v, hp=hp, gpar=gpar, pj=pj:
                                   v.tensor_mul(pt[hp][gpar][:, pj, :],
                                                pt[hp][gpar][:, pj, :],
                                                masks_sb[:, pj * 1024:pj * 1024 + 1024]),
                                   waits=[(acM.sem, acM.n), (dvM.sem, dvM.n)],
                                   inc=dvM.bump())
                            avw = (dvM.sem, dvM.n)
                        pair_avs.append((pj, avw))
                        pair_idx += 1
                    if pending_avs is not None:
                        pending_avs()
                    def _avs(hp=hp, gpar=gpar, kc=kc, diag=diag, pair_avs=pair_avs,
                             first=(kc == 0)):
                        for (ppj, avw) in pair_avs:
                            st0 = (first and ppj == 0)
                            sp0 = (diag and ppj == 1)
                            for sj in range(2):
                                ks = ppj * 2 + sj
                                kslice = slice((kc * 4 + ks) * 128,
                                               (kc * 4 + ks) * 128 + 128)
                                avkw = []
                                if sj == 0:
                                    avkw.append(avw)
                                if st0 and sj == 0:
                                    avkw.append((dvM.sem, o_free_at[hp]))
                                P.emit("tensor",
                                       lambda t, gpar=gpar, ppj=ppj, sj=sj, kslice=kslice,
                                       hp=hp, st=(st0 and sj == 0), sp=(sp0 and sj == 1):
                                       t.matmul(o_ps[hp][:, :], Vtok[:, kslice],
                                                pt[hp][gpar][:, ppj, sj * 512:sj * 512 + 512],
                                                start=st, stop=sp),
                                       waits=avkw)
                        for ksl in range(4):
                            ppj, sj = divmod(ksl, 2)
                            lw = []
                            if ksl == 0 and first:
                                lw.append((dvM.sem, l_zero_at[hp]))
                            P.emit("tensor",
                                   lambda t, gpar=gpar, ppj=ppj, sj=sj, hp=hp, ksl=ksl,
                                   sp=diag:
                                   t.matmul(l_ps[hp][32 * ksl:32 * ksl + 1, :],
                                            onescb_sb[:, :],
                                            pt[hp][gpar][:, ppj, sj * 512:sj * 512 + 512],
                                            start=False, stop=sp, skip_group_check=True,
                                            tile_position=(0, 32 * ksl)),
                                   waits=lw,
                                   inc=peM.bump() if ksl == 3 else None)
                        group_l_done[(hp, gpar)] = peM.n
                    pending_avs = _avs
                    ghp_count[hp] += 1
            pending_avs()
            pending_avs = None
            for fn in pending_norm:
                fn()
            pending_norm = []
            for hp in range(2):
                h = pas * 2 + hp
                gl = group_l_done[(hp, (ghp_count[hp] - 1) % 2)]
                P.emit("vector",
                       lambda v, hp=hp: v.tensor_copy(l_sb[hp][:, :], l_ps[hp][:, :]),
                       waits=[(peM.sem, gl), (dvM.sem, dvM.n)], inc=dvM.bump())
                levac_done = dvM.n
                P.emit("tensor",
                       lambda t, hp=hp: t.matmul(l_ps[hp][0:1, :], ones4_sb[:, :],
                                                 l_sb[hp][:, :], start=True, stop=True),
                       waits=[(dvM.sem, levac_done)], inc=peM.bump())
                lsum_done = peM.n
                P.emit("scalar",
                       lambda s, hp=hp: s.activation(linv_row[hp][:, :], l_ps[hp][0:1, :],
                                                     AF.Ln),
                       waits=[(peM.sem, max(lsum_done, linv_pe_guard[hp])), (acM.sem, acM.n)],
                       inc=acM.bump())
                ln_done = acM.n
                P.emit("scalar",
                       lambda s, hp=hp: s.activation(linv_row[hp][:, :], linv_row[hp][:, :],
                                                     AF.Exp, scale=-1.0),
                       waits=[(acM.sem, acM.n)], inc=acM.bump())
                linv_done = acM.n
                P.emit("vector",
                       lambda v, hp=hp: v.tensor_copy(o_sb[hp][:, :], o_ps[hp][:, :]),
                       waits=[(peM.sem, gl), (dvM.sem, max(dvM.n, o_sb_free[hp]))],
                       inc=dvM.bump())
                o_free_at[hp] = dvM.n
                l_free_at[hp] = ln_done
                def _norm(hp=hp, h=h, qc=qc, linv_done=linv_done):
                    P.emit("tensor",
                           lambda t, hp=hp: t.matmul(s_ps[hp][:, 0:512], onesrf_sb[:, :],
                                                     linv_row[hp][:, :], start=True, stop=True),
                           waits=[(acM.sem, max(linv_done, acM.n))], inc=peM.bump())
                    bc_done = peM.n
                    linv_pe_guard[hp] = bc_done
                    P.emit("scalar",
                           lambda s, hp=hp: s.copy(linv_bc[hp][:, :], s_ps[hp][:, 0:512]),
                           waits=[(peM.sem, bc_done), (acM.sem, acM.n),
                                  (dvM.sem, linv_bc_free[hp])],
                           inc=acM.bump())
                    bank_guard[hp] = acM.n
                    bcev_done = acM.n
                    sti = (qc * 4 + h) % 8
                    P.emit("vector",
                           lambda v, hp=hp, sti=sti: v.tensor_mul(atst[sti][:, :],
                                                                  o_sb[hp][:, :],
                                                                  linv_bc[hp][:, :]),
                           waits=[(acM.sem, bcev_done), (sA2[sti].sem, atst_stored_at[sti]),
                                  (dvM.sem, dvM.n)],
                           inc=dvM.bump())
                    o_sb_free[hp] = dvM.n
                    linv_bc_free[hp] = dvM.n
                    mul_done = dvM.n
                    d_dest = qc // NTH
                    tof = (qc % NTH) * 512
                    P.emit("sync",
                           lambda sy, h=h, sti=sti, d_dest=d_dest, tof=tof: sy.dma_start(
                               out=a2a_in.ap()[d_dest, h * 128:h * 128 + 128, tof:tof + 512],
                               in_=atst[sti][:, :]),
                           waits=[(dvM.sem, mul_done)], inc=sA2[sti].bump())
                    atst_stored_at[sti] = sA2[sti].n
                pending_norm.append(_norm)

    for fn in pending_norm:
        fn()
    pending_norm = []

    P.emit("gpsimd",
           lambda g: g.collective_compute(
               "AllToAll", ALU.bypass, replica_groups=[list(range(NCORES))],
               ins=[a2a_in.ap().opt()], outs=[a2a_out.ap().opt()]),
           waits=[(s.sem, s.n) for s in sA2], inc=ccs.bump())

    if upto == "p2":
        sT = Ctr(stk, nc, "sT", 16)
        P.emit("gpsimd",
               lambda g: g.dma_start(out=outT[:, :],
                                     in_=a2a_out.ap().rearrange("r f t -> (r f) t")),
               waits=[(ccs.sem, 1)], inc=sT.bump())
        P.emit("gpsimd", None, waits=[(sT.sem, sT.n)])
        nc_out = finish()
        ph2.close(); pp2.close(); ph12.close(); stk.close()
        return nc_out

    ph2.close()
    pp2.close()
    ph12.close()

    ph34 = contextlib.ExitStack()
    xo_sb = [ph34.enter_context(nc.sbuf_tensor("xo_sb%d" % i, [128, To], F32)) for i in range(2)]
    st_sb = [ph34.enter_context(nc.sbuf_tensor("st_sb%d" % i, [128, To], F32)) for i in range(4)]
    r2st = [ph34.enter_context(nc.sbuf_tensor("r2st%d" % i, [128, To], F32)) for i in range(2)]
    sq_sb = ph34.enter_context(nc.sbuf_tensor("sq_sb", [128, To], F32))
    acc_s = ph34.enter_context(nc.sbuf_tensor("acc_s", [128, To], F32))
    acc_q = ph34.enter_context(nc.sbuf_tensor("acc_q", [128, To], F32))
    m_bc = ph34.enter_context(nc.sbuf_tensor("m_bc", [128, To], F32))
    rstd_bc = ph34.enter_context(nc.sbuf_tensor("rstd_bc", [128, To], F32))
    sc_bc = ph34.enter_context(nc.sbuf_tensor("sc_bc", [128, To], F32))

    ph25 = contextlib.ExitStack()
    at_own = ph25.enter_context(nc.sbuf_tensor("at_own", [128, n_ft, To], BF16))
    wo_sb = [ph25.enter_context(nc.sbuf_tensor("wo_sb%d" % i, [128, n_ft, 256], BF16))
             for i in range(2)]

    pp3 = contextlib.ExitStack()
    psA = [pp3.enter_context(nc.psum_tensor("psA%d" % i, [128, To], F32)) for i in range(4)]

    P.emit("sync",
           lambda sy: sy.dma_start(out=at_own[:, :, :],
                                   in_=a2a_out.ap().rearrange("r (i p) t -> p (r i) t", p=128)),
           waits=[(ccs.sem, 1)], inc=sAo.bump())
    at_own_ready = sAo.n

    n_grp = n_ft // 2
    wo_grp_free_at = [0, 0]
    wo_stripe_free_at = [0, 0]
    st_free_at = [0, 0]
    st_free_acm = [0, 0]
    xo_used_at = [0, 0]
    acc_chain = 0

    def emit_wo_stripe(g):
        b = g % 2
        w = [(ccs.sem, 1)] if g < 2 else []
        if g >= 2:
            w.append((peM.sem, wo_stripe_free_at[b]))
        P.emit("gpsimd",
               lambda gp, g=g, b=b: gp.dma_start(
                   out=wo_sb[b][:, :, :],
                   in_=wo[:, g * 256:(g + 1) * 256].rearrange("(i p) f -> p i f", p=128)),
               waits=w, inc=sWo[b].bump())

    emit_wo_stripe(0)
    emit_wo_stripe(1)
    for ft in range(min(2, n_ft)):
        P.emit("sync",
               lambda sy, ft=ft: sy.dma_start(
                   out=xo_sb[ft % 2][:, :], in_=xoT[ft * 128:(ft + 1) * 128, :]),
               waits=[(ccs.sem, 1)], inc=sXo[ft % 2].bump())

    for g in range(n_grp):
        b = g % 2
        mmw = [(sWo[b].sem, 16 * (g // 2 + 1)), (sAo.sem, at_own_ready)]
        if g >= 2:
            mmw.append((dvM.sem, wo_grp_free_at[b]))
        for d in range(n_ft):
            for jj in range(2):
                for th in range(NTH):
                    P.emit("tensor",
                           lambda t, b=b, d=d, jj=jj, th=th, st=(d == 0), sp=(d == n_ft - 1):
                           t.matmul(psA[2 * b + jj][:, th * 512:th * 512 + 512],
                                    wo_sb[b][:, d, jj * 128:jj * 128 + 128],
                                    at_own[:, d, th * 512:th * 512 + 512],
                                    start=st, stop=sp),
                           waits=mmw if (d == 0 and jj == 0 and th == 0) else (),
                           inc=peM.bump() if (d == n_ft - 1 and jj == 1 and th == NTH - 1) else None)
        grp_done = peM.n
        wo_stripe_free_at[b] = grp_done
        if g + 2 < n_grp:
            emit_wo_stripe(g + 2)
        for jj in range(2):
            ft = 2 * g + jj
            sb = ft % 2
            P.emit("vector",
                   lambda v, b=b, jj=jj, sb=sb: v.tensor_add(
                       st_sb[sb][:, :], psA[2 * b + jj][:, 0:To], xo_sb[sb][:, :]),
                   waits=[(peM.sem, grp_done), (sXo[sb].sem, 16 * (ft // 2 + 1)),
                          (sSw[sb].sem, st_free_at[sb]), (acM.sem, st_free_acm[sb]),
                          (dvM.sem, dvM.n)],
                   inc=dvM.bump())
            add_done = dvM.n
            xo_used_at[sb] = add_done
            if jj == 1:
                wo_grp_free_at[b] = add_done
            if ft + 2 < n_ft:
                P.emit("sync",
                       lambda sy, ft=ft, sb=sb: sy.dma_start(
                           out=xo_sb[sb][:, :], in_=xoT[(ft + 2) * 128:(ft + 3) * 128, :]),
                       waits=[(dvM.sem, add_done)], inc=sXo[sb].bump())
            P.emit("sync",
                   lambda sy, ft=ft, sb=sb: sy.dma_start(
                       out=s_dram.ap()[ft * 128:(ft + 1) * 128, :], in_=st_sb[sb][:, :]),
                   waits=[(dvM.sem, add_done)], inc=sSw[sb].bump())
            st_free_at[sb] = sSw[sb].n
            if ft == 0:
                P.emit("vector", lambda v, sb=sb: v.tensor_copy(acc_s[:, :], st_sb[sb][:, :]),
                       waits=[(dvM.sem, add_done)], inc=dvM.bump())
            else:
                P.emit("vector", lambda v, sb=sb: v.tensor_add(acc_s[:, :], acc_s[:, :],
                                                               st_sb[sb][:, :]),
                       waits=[(dvM.sem, max(add_done, acc_chain))], inc=dvM.bump())
            P.emit("scalar", lambda s, sb=sb: s.activation(sq_sb[:, :], st_sb[sb][:, :],
                                                           AF.Square),
                   waits=[(dvM.sem, add_done), (dvM.sem, acc_chain), (acM.sem, acM.n)],
                   inc=acM.bump())
            sqd = acM.n
            st_free_acm[sb] = sqd
            if ft == 0:
                P.emit("vector", lambda v: v.tensor_copy(acc_q[:, :], sq_sb[:, :]),
                       waits=[(acM.sem, sqd), (dvM.sem, dvM.n)], inc=dvM.bump())
            else:
                P.emit("vector", lambda v: v.tensor_add(acc_q[:, :], acc_q[:, :], sq_sb[:, :]),
                       waits=[(acM.sem, sqd), (dvM.sem, dvM.n)], inc=dvM.bump())
            acc_chain = dvM.n

    def emit_ln_finalize(m_dst, r_dst, roff, acc_ready_dvM, sc_dst=None):
        psum, psumq = psA[3], psA[2]
        for half in range(NTH):
            P.emit("tensor",
                   lambda t, half=half: t.matmul(
                       psum[0:1, half * 512:half * 512 + 512], onescf_sb[:, :],
                       acc_s[:, half * 512:half * 512 + 512], start=True, stop=True),
                   waits=[(dvM.sem, acc_ready_dvM), (acM.sem, acM.n)],
                   inc=peM.bump())
            P.emit("tensor",
                   lambda t, half=half: t.matmul(
                       psumq[0:1, half * 512:half * 512 + 512], onescf_sb[:, :],
                       acc_q[:, half * 512:half * 512 + 512], start=True, stop=True),
                   inc=peM.bump())
        mm_done = peM.n
        r0, r1, r2o = roff, roff + 1024, roff + 2048
        P.emit("scalar",
               lambda s: s.copy(rowbuf[0:1, r0:r0 + To], psum[0:1, 0:To]),
               waits=[(peM.sem, mm_done), (acM.sem, acM.n)], inc=acM.bump())
        P.emit("scalar",
               lambda s: s.copy(rowbuf[0:1, r1:r1 + To], psumq[0:1, 0:To]),
               waits=[(acM.sem, acM.n)], inc=acM.bump())
        evr = acM.n
        P.emit("vector",
               lambda v: v.tensor_scalar_mul(rowbuf[0:1, r0:r0 + To],
                                             rowbuf[0:1, r0:r0 + To], 1.0 / DIM),
               waits=[(acM.sem, evr), (dvM.sem, dvM.n)], inc=dvM.bump())
        m_row_done = dvM.n
        P.emit("vector",
               lambda v: v.tensor_scalar_mul(rowbuf[0:1, r1:r1 + To],
                                             rowbuf[0:1, r1:r1 + To], 1.0 / DIM),
               waits=[(dvM.sem, dvM.n)], inc=dvM.bump())
        P.emit("vector",
               lambda v: v.tensor_mul(rowbuf[0:1, r2o:r2o + To],
                                      rowbuf[0:1, r0:r0 + To], rowbuf[0:1, r0:r0 + To]),
               waits=[(dvM.sem, dvM.n)], inc=dvM.bump())
        P.emit("vector",
               lambda v: v.tensor_sub(rowbuf[0:1, r1:r1 + To], rowbuf[0:1, r1:r1 + To],
                                      rowbuf[0:1, r2o:r2o + To]),
               waits=[(dvM.sem, dvM.n)], inc=dvM.bump())
        var_done = dvM.n
        P.emit("scalar",
               lambda s: s.activation(rowbuf[0:1, r2o:r2o + To], rowbuf[0:1, r1:r1 + To],
                                      AF.Ln, bias=1e-5),
               waits=[(dvM.sem, var_done), (acM.sem, acM.n)], inc=acM.bump())
        P.emit("scalar",
               lambda s: s.activation(rowbuf[0:1, r1:r1 + To], rowbuf[0:1, r2o:r2o + To],
                                      AF.Exp, scale=-0.5),
               waits=[(acM.sem, acM.n)], inc=acM.bump())
        rstd_done = acM.n
        sc_done = dvM.n
        if sc_dst is not None:
            P.emit("vector",
                   lambda v: v.tensor_scalar(rowbuf[0:1, r2o:r2o + To],
                                             rowbuf[0:1, r1:r1 + To],
                                             wscale_sb[0:1, 0:1], None, ALU.mult),
                   waits=[(acM.sem, rstd_done), (dvM.sem, dvM.n)], inc=dvM.bump())
            sc_done = dvM.n
        bc_list = [(m_dst, r0), (r_dst, r1)]
        if sc_dst is not None:
            bc_list.append((sc_dst, r2o))
        prev_ev = 0
        for (dst, off) in bc_list:
            for half in range(NTH):
                P.emit("tensor",
                       lambda t, off=off, half=half: t.matmul(
                           psum[:, half * 512:half * 512 + 512], onesrf_sb[:, :],
                           rowbuf[0:1, off + half * 512:off + half * 512 + 512],
                           start=True, stop=True),
                       waits=[(acM.sem, max(rstd_done, prev_ev)), (dvM.sem, sc_done)],
                       inc=peM.bump())
                bmm = peM.n
                P.emit("scalar",
                       lambda s, dst=dst, half=half: s.copy(
                           dst[:, half * 512:half * 512 + 512],
                           psum[:, half * 512:half * 512 + 512]),
                       waits=[(peM.sem, bmm), (acM.sem, acM.n)], inc=acM.bump())
                prev_ev = acM.n
        return acM.n

    bc1_done = emit_ln_finalize(m_bc, rstd_bc, 0, acc_chain, sc_dst=sc_bc)

    if upto == "p25":
        sT = Ctr(stk, nc, "sT", 16)
        P.emit("sync",
               lambda sy: sy.dma_start(out=outT[:, :], in_=s_dram.ap()),
               waits=[(sSw[0].sem, sSw[0].n), (sSw[1].sem, sSw[1].n),
                      (acM.sem, bc1_done)], inc=sT.bump())
        P.emit("sync", None, waits=[(sT.sem, sT.n)])
        nc_out = finish()
        ph25.close(); ph34.close(); pp3.close(); stk.close()
        return nc_out

    ph25.close()
    ph3 = contextlib.ExitStack()
    d1_sb = ph3.enter_context(nc.sbuf_tensor("d1_sb", [128, n_ft, To], BF16))
    w2r = [ph3.enter_context(nc.sbuf_tensor("w2r%d" % i, [128, n_ft, 128], BF16))
           for i in range(3)]
    wf_sb = [ph3.enter_context(nc.sbuf_tensor("wf_sb%d" % i, [128, n_ft, 128], BF16))
             for i in range(3)]

    d1_done_at = []
    for ft in range(n_ft):
        sb = ft % 4
        P.emit("sync",
               lambda sy, ft=ft, sb=sb: sy.dma_start(
                   out=st_sb[sb][:, :], in_=s_dram.ap()[ft * 128:(ft + 1) * 128, :]),
               waits=[(sSw[0].sem, sSw[0].n), (sSw[1].sem, sSw[1].n),
                      (acM.sem, bc1_done),
                      (dvM.sem, d1_done_at[ft - 4] if ft >= 4 else 0)],
               inc=sSr[sb].bump())
        P.emit("vector",
               lambda v, ft=ft, sb=sb: v.tensor_sub(d1_sb[:, ft, :], st_sb[sb][:, :],
                                                    m_bc[:, :]),
               waits=[(sSr[sb].sem, 16 * (ft // 4 + 1)), (acM.sem, bc1_done),
                      (dvM.sem, dvM.n)],
               inc=dvM.bump())
        d1_done_at.append(dvM.n)
    d1_all = dvM.n

    ff_grp_free_at = [0, 0, 0]
    ff_stripe_free_at = [0, 0, 0]
    wf_conv_at = [0, 0, 0]
    st3_free_dv = [0, 0, 0, 0]
    r2st_free_dv = [0, 0]
    r2st_free_ac = [0, 0]
    r2st_spill_at = [0, 0]
    acc2_chain = 0

    def emit_w2_load(g):
        b = g % 3
        w = [(sSw[0].sem, sSw[0].n), (sSw[1].sem, sSw[1].n)]
        if g >= 3:
            w.append((dvM.sem, wf_conv_at[b]))
        P.emit("gpsimd",
               lambda gp, g=g, b=b: gp.dma_start(
                   out=w2r[b][:, :, :],
                   in_=w2T[:, g * 128:(g + 1) * 128].rearrange("(i p) f -> p i f", p=128)),
               waits=w, inc=sFf[b].bump())

    def emit_conv(g):
        b = g % 3
        P.emit("vector",
               lambda v, b=b: v.tensor_scalar_sub(wf_sb[b][:, :, :], w2r[b][:, :, :], 1.5),
               waits=[(sFf[b].sem, 16 * (g // 3 + 1)), (peM.sem, ff_stripe_free_at[b]),
                      (dvM.sem, dvM.n)],
               inc=dvM.bump())
        wf_conv_at[b] = dvM.n
        return dvM.n

    emit_w2_load(0)
    emit_w2_load(1)
    ff_conv_done = [0] * n_ft
    ff_conv_done[0] = emit_conv(0)

    for g in range(n_ft):
        b = g % 3
        if g + 2 < n_ft:
            emit_w2_load(g + 2)
        if g + 1 < n_ft:
            ff_conv_done[g + 1] = emit_conv(g + 1)
        conv_done = ff_conv_done[g]
        mmw = [(dvM.sem, max(conv_done, d1_all))]
        if g >= 3:
            mmw.append((dvM.sem, ff_grp_free_at[b]))
        for d in range(n_ft):
            for th in range(NTH):
                P.emit("tensor",
                       lambda t, b=b, d=d, th=th, st=(d == 0), sp=(d == n_ft - 1):
                       t.matmul(psA[b][:, th * 512:th * 512 + 512],
                                wf_sb[b][:, d, :], d1_sb[:, d, th * 512:th * 512 + 512],
                                start=st, stop=sp),
                       waits=mmw if (d == 0 and th == 0) else (),
                       inc=peM.bump() if (d == n_ft - 1 and th == NTH - 1) else None)
        grp_done = peM.n
        ff_stripe_free_at[b] = grp_done
        sb = g % 4
        sb2 = g % 2
        P.emit("sync",
               lambda sy, g=g, sb=sb: sy.dma_start(
                   out=st_sb[sb][:, :], in_=s_dram.ap()[g * 128:(g + 1) * 128, :]),
               waits=[(dvM.sem, max(d1_all, st3_free_dv[sb]))],
               inc=sS2[sb].bump())
        P.emit("vector",
               lambda v, sb=sb: v.tensor_sub(sq_sb[:, :], st_sb[sb][:, :], m_bc[:, :]),
               waits=[(sS2[sb].sem, 16 * (g // 4 + 1)), (dvM.sem, dvM.n)],
               inc=dvM.bump())
        st3_free_dv[sb] = dvM.n
        P.emit("vector",
               lambda v, sb=sb: v.tensor_mul(sq_sb[:, :], sq_sb[:, :], rstd_bc[:, :]),
               waits=[(dvM.sem, dvM.n)], inc=dvM.bump())
        hh_done = dvM.n
        P.emit("vector",
               lambda v, b=b, sb2=sb2: v.tensor_mul(r2st[sb2][:, :], psA[b][:, 0:To],
                                                    sc_bc[:, :]),
               waits=[(peM.sem, grp_done), (dvM.sem, max(hh_done, r2st_free_dv[sb2])),
                      (acM.sem, r2st_free_ac[sb2]), (sR2w[sb2].sem, r2st_spill_at[sb2])],
               inc=dvM.bump())
        ff_grp_free_at[b] = dvM.n
        P.emit("vector",
               lambda v, g=g, sb2=sb2: v.scalar_tensor_tensor(
                   r2st[sb2][:, :], r2st[sb2][:, :], bias2d_sb[:, g:g + 1], sq_sb[:, :],
                   ALU.add, ALU.add),
               waits=[(dvM.sem, dvM.n)], inc=dvM.bump())
        r2_done = dvM.n
        P.emit("scalar",
               lambda s, g=g, sb2=sb2: s.dma_start(
                   out=r2_dram.ap()[g * 128:(g + 1) * 128, :], in_=r2st[sb2][:, :]),
               waits=[(dvM.sem, r2_done)], inc=sR2w[sb2].bump())
        r2st_spill_at[sb2] = sR2w[sb2].n
        if g == 0:
            P.emit("vector", lambda v, sb2=sb2: v.tensor_copy(acc_s[:, :], r2st[sb2][:, :]),
                   waits=[(dvM.sem, r2_done)], inc=dvM.bump())
        else:
            P.emit("vector", lambda v, sb2=sb2: v.tensor_add(acc_s[:, :], acc_s[:, :],
                                                             r2st[sb2][:, :]),
                   waits=[(dvM.sem, max(r2_done, acc2_chain))], inc=dvM.bump())
        P.emit("scalar", lambda s, sb2=sb2: s.activation(sq_sb[:, :], r2st[sb2][:, :], AF.Square),
               waits=[(dvM.sem, dvM.n), (acM.sem, acM.n)], inc=acM.bump())
        sqd = acM.n
        r2st_free_ac[sb2] = sqd
        if g == 0:
            P.emit("vector", lambda v: v.tensor_copy(acc_q[:, :], sq_sb[:, :]),
                   waits=[(acM.sem, sqd), (dvM.sem, dvM.n)], inc=dvM.bump())
        else:
            P.emit("vector", lambda v: v.tensor_add(acc_q[:, :], acc_q[:, :], sq_sb[:, :]),
                   waits=[(acM.sem, sqd), (dvM.sem, dvM.n)], inc=dvM.bump())
        acc2_chain = dvM.n
        r2st_free_dv[sb2] = acc2_chain

    bc2_done = emit_ln_finalize(m_bc, rstd_bc, 0, acc2_chain)
    dv_after_bc2 = dvM.n

    out_done_at = [dv_after_bc2] * 4
    out_stage_free = [dv_after_bc2, dv_after_bc2]
    out_store_at = [0, 0]
    for ft in range(n_ft):
        sb = ft % 4
        sb2 = ft % 2
        P.emit("sync",
               lambda sy, ft=ft, sb=sb: sy.dma_start(
                   out=st_sb[sb][:, :], in_=r2_dram.ap()[ft * 128:(ft + 1) * 128, :]),
               waits=[(sR2w[0].sem, sR2w[0].n), (sR2w[1].sem, sR2w[1].n),
                      (dvM.sem, out_done_at[sb]), (acM.sem, bc2_done)],
               inc=sR2r[sb].bump())
        P.emit("vector",
               lambda v, sb=sb: v.tensor_sub(sq_sb[:, :], st_sb[sb][:, :], m_bc[:, :]),
               waits=[(sR2r[sb].sem, 16 * (ft // 4 + 1)), (acM.sem, bc2_done),
                      (dvM.sem, dvM.n)],
               inc=dvM.bump())
        out_done_at[sb] = dvM.n
        P.emit("vector",
               lambda v, sb2=sb2: v.tensor_mul(r2st[sb2][:, :], sq_sb[:, :], rstd_bc[:, :]),
               waits=[(dvM.sem, max(dvM.n, out_stage_free[sb2])),
                      (sOut[sb2].sem, out_store_at[sb2])],
               inc=dvM.bump())
        out_stage_free[sb2] = dvM.n
        P.emit("scalar",
               lambda s, ft=ft, sb2=sb2: s.dma_start(
                   out=outT[ft * 128:(ft + 1) * 128, :], in_=r2st[sb2][:, :]),
               waits=[(dvM.sem, dvM.n)], inc=sOut[sb2].bump())
        out_store_at[sb2] = sOut[sb2].n
    P.emit("sync", None, waits=[(sOut[0].sem, sOut[0].n), (sOut[1].sem, sOut[1].n)])

    nc_out = finish()
    ph3.close()
    ph34.close()
    pp3.close()
    stk.close()
    return nc_out


def prep_inputs(L0_attn_x, L0_attn_wq, L0_attn_wk, L0_attn_wv, L0_attn_wo,
                L0_ffn_weight_2bit, L0_ffn_weight_scale, L0_ffn_bias):
    x = np.asarray(L0_attn_x)
    S = x.shape[1]
    x2 = np.ascontiguousarray(x.reshape(S, DIM))
    xT = np.ascontiguousarray(x2.T)

    perm = np.concatenate([np.arange(0, HD, 2), np.arange(1, HD, 2)])
    wq = np.asarray(L0_attn_wq).reshape(DIM, NH, HD)[:, :, perm].reshape(DIM, NH * HD)
    wk = np.asarray(L0_attn_wk).reshape(DIM, NKV, HD)[:, :, perm].reshape(DIM, NKV * HD)
    wv = np.asarray(L0_attn_wv)
    wo = np.ascontiguousarray(np.asarray(L0_attn_wo))
    w2T = np.ascontiguousarray(np.asarray(L0_ffn_weight_2bit).T.astype(np.int32))
    bias2d = np.ascontiguousarray(np.asarray(L0_ffn_bias).reshape(DIM // 128, 128).T)
    wsc = np.ascontiguousarray(np.asarray(L0_ffn_weight_scale).reshape(1, 1).astype(np.float32))

    sc128 = np.float64(HD) ** -0.25
    freqs = 1.0 / THETA ** (np.arange(0, HD, 2, dtype=np.float64) / HD)
    ang = np.outer(freqs, np.arange(S, dtype=np.float64))
    C = (np.concatenate([np.cos(ang), np.cos(ang)], 0) * sc128).astype(np.float32)
    Ssg = (np.concatenate([-np.sin(ang), np.sin(ang)], 0) * sc128).astype(np.float32)
    kk = np.arange(128)[:, None]
    qq = np.arange(512)[None, :]
    masks = np.concatenate([(128 * s + kk <= qq) for s in range(4)],
                           axis=1).astype(ml_dtypes.bfloat16)
    pswap = np.zeros((128, 128), np.float32)
    for i in range(128):
        pswap[i, (i + 64) % 128] = 1.0
    pswap = pswap.astype(ml_dtypes.bfloat16)
    identm = np.eye(128, dtype=ml_dtypes.bfloat16)
    ones4_np = np.zeros((128, 1), np.float32)
    ones4_np[[0, 32, 64, 96], 0] = 1.0

    To = S // NCORES
    maps = []
    for c in range(NCORES):
        maps.append({
            "xT": xT,
            "xoT": np.ascontiguousarray(xT[:, c * To:(c + 1) * To]),
            "wq": np.ascontiguousarray(wq[:, c * HPC * HD:(c + 1) * HPC * HD]),
            "wk": np.ascontiguousarray(wk[:, c * HD:(c + 1) * HD]),
            "wv": np.ascontiguousarray(np.asarray(wv)[:, c * HD:(c + 1) * HD]),
            "wo": wo,
            "w2T": w2T,
            "bias2d": bias2d,
            "wscale": wsc,
            "cosT": C,
            "sinT": Ssg,
            "masks": masks,
            "pswap": pswap,
            "ident": identm,
            "onescb": np.ones((128, 1), ml_dtypes.bfloat16),
            "onesrb": np.ones((1, 128), ml_dtypes.bfloat16),
            "onescf": np.ones((128, 1), np.float32),
            "onesrf": np.ones((1, 128), np.float32),
            "ones4": ones4_np,
        })
    return maps


def kernel(L0_attn_x, L0_attn_wq, L0_attn_wk, L0_attn_wv, L0_attn_wo,
           L0_ffn_weight_2bit, L0_ffn_weight_scale, L0_ffn_bias):
    x = np.asarray(L0_attn_x)
    S = x.shape[1]
    maps = prep_inputs(L0_attn_x, L0_attn_wq, L0_attn_wk, L0_attn_wv, L0_attn_wo,
                       L0_ffn_weight_2bit, L0_ffn_weight_scale, L0_ffn_bias)
    nc = build_program(S)
    with allow_ici_all_to_all():
        res = bass_utils.run_bass_kernel_spmd(nc, maps, core_ids=list(range(NCORES)))
    outs = [np.asarray(r["outT"]).T for r in res.results]
    return np.ascontiguousarray(np.concatenate(outs, axis=0)).reshape(1, S, DIM).astype(np.float32)
